# version 10
# speedup vs baseline: 1.6848x; 1.0044x over previous
import contextlib
import numpy as np
import ml_dtypes
import concourse.bass as bass
import concourse.mybir as mybir
import concourse.bass_utils as _BU
from concourse.bass_utils import run_bass_kernel_spmd

_orig_run_command = _BU.run_command


def _run_command_ldwopt(cmd, *a, **kw):
    if isinstance(cmd, list) and any(str(c).startswith("--enable-ldw-opt") for c in cmd):
        cmd = ["--enable-ldw-opt=true" if str(c) == "--enable-ldw-opt=false" else c
               for c in cmd]
    return _orig_run_command(cmd, *a, **kw)


_BU.run_command = _run_command_ldwopt

F32 = mybir.dt.float32
BF16 = mybir.dt.bfloat16
AF = mybir.ActivationFunctionType
ALU = mybir.AluOpType
AX = mybir.AxisListType

N = 4096
D = 1024
NT = 8
CH = 512
NCH = N // CH
BF = ml_dtypes.bfloat16


_UID = [0]


def _uid():
    _UID[0] += 1
    return _UID[0]


class Sem:
    def __init__(self, nc, stack, name):
        self.h = stack.enter_context(nc.semaphore(f"{name}_u{_UID[0]}"))
        self.n = 0

    def inc(self, instr, k=1):
        instr.then_inc(self.h, k)
        self.n += k
        return self.n

    def wait(self, eng, val=None):
        eng.wait_ge(self.h, self.n if val is None else val)


class PBank:
    def __init__(self, ap):
        self.ap = ap
        self.fresh = True

    def start(self):
        s = self.fresh
        self.fresh = False
        return s

    def reset(self):
        self.fresh = True



def make_serial(eng, sem, auto_inc=True):
    class _W:
        def __getattr__(self, name):
            fn = getattr(eng, name)
            def call(*a, **kw):
                eng.wait_ge(sem.h, sem.n)
                ins = fn(*a, **kw)
                if auto_inc:
                    sem.inc(ins)
                return ins
            return call
    return _W()


def build_l0_attn(N=N, NCH=NCH, dbg=False, ctx=None):
    if ctx is None:
        nc = bass.Bass("TRN2", target_bir_lowering=False)
        stack = contextlib.ExitStack()
        xT = nc.dram_tensor("xT", [D, N], F32, kind="ExternalInput").ap()
    else:
        nc, stack, xT, _out, _waits = ctx
    wqT = nc.dram_tensor("wqT", [D, D], F32, kind="ExternalInput").ap()
    wkT = nc.dram_tensor("wkT", [D, 256], F32, kind="ExternalInput").ap()
    wvT = nc.dram_tensor("wvT", [D, 256], F32, kind="ExternalInput").ap()
    woT = nc.dram_tensor("woT", [D, D], F32, kind="ExternalInput").ap()
    bq = nc.dram_tensor("bq", [64, 16], F32, kind="ExternalInput").ap()
    bk = nc.dram_tensor("bk", [64, 4], F32, kind="ExternalInput").ap()
    bv = nc.dram_tensor("bv", [1, 256], F32, kind="ExternalInput").ap()
    bo = nc.dram_tensor("bo", [128, 8], F32, kind="ExternalInput").ap()
    sel16x = nc.dram_tensor("sel16x", [1, 256], F32, kind="ExternalInput").ap()
    selcol = nc.dram_tensor("selcol", [16, 1024], F32, kind="ExternalInput").ap()
    stk = nc.dram_tensor("stk", [64, 256], F32, kind="ExternalInput").ap()
    if ctx is None:
        x1T = nc.dram_tensor("x1T", [D, N], F32, kind="ExternalOutput").ap()
    else:
        x1T = _out
    if dbg:
        NSd = N // 128
        d_r = nc.dram_tensor("d_r", [1, N], F32, kind="ExternalOutput").ap()
        d_kh = nc.dram_tensor("d_kh", [65, N], F32, kind="ExternalOutput").ap()
        d_va = nc.dram_tensor("d_va", [128, NSd * 260], F32, kind="ExternalOutput").ap()
        d_ob = nc.dram_tensor("d_ob", [64, 16 * CH], F32, kind="ExternalOutput").ap()
        d_rd = nc.dram_tensor("d_rd", [16, CH], F32, kind="ExternalOutput").ap()
        d_op = nc.dram_tensor("d_op", [128, 8 * CH], F32, kind="ExternalOutput").ap()
        d_km = nc.dram_tensor("d_km", [64, 4], F32, kind="ExternalOutput").ap()
        d_qp = nc.dram_tensor("d_qp", [65, CH], F32, kind="ExternalOutput").ap()

    _p = f"u{_uid()}_"
    sb = lambda name, shape, dt: stack.enter_context(nc.sbuf_tensor(_p + name, shape, dt))
    psa = lambda name, shape: stack.enter_context(nc.psum_tensor(_p + name, shape, F32))

    NS = N // 128
    NP_G = NS // 2

    xbc = [[sb(f"xbc{k}_{t}", [128, CH], BF16) for t in range(NT)] for k in range(2)]
    wqb = [sb(f"wqb{t}", [128, D], BF16) for t in range(NT)]
    wkb = [sb(f"wkb{t}", [128, 256], BF16) for t in range(NT)]
    wvb = [sb(f"wvb{t}", [128, 256], BF16) for t in range(NT)]
    wob = [sb(f"wob{t}", [128, D], BF16) for t in range(NT)]
    khead = [sb(f"khead{h}", [65, N], BF16) for h in range(4)]
    v_all = sb("v_all", [128, NS * 260], BF16)
    qplus = [sb(f"qplus{k}", [65, CH], BF16) for k in range(2)]
    qabs = [sb(f"qabs{k}", [64, CH], BF16) for k in range(2)]
    Ebuf = [sb(f"Ebuf{k}", [128, 1024], BF16) for k in range(2)]
    Obuf = sb("Obuf", [64, 16 * CH], BF16)
    Opair = sb("Opair", [128, 8 * CH], BF16)
    stkb = sb("stkb", [64, 256], BF16)
    den2 = sb("den2", [1, 2 * CH], F32)
    rd16 = sb("rd16", [16, CH], F32)
    denbp = [sb(f"denbp{k}", [64, CH], F32) for k in range(2)]
    kmax4 = sb("kmax4", [64, 4], BF16)
    s1row = sb("s1row", [1, CH], F32)
    s2row = sb("s2row", [1, CH], F32)
    mrow = sb("mrow", [1, CH], F32)
    rrows = sb("rrows", [1, N], F32)
    rb = sb("rb", [128, CH], F32)
    S1b = sb("S1b", [1, N], BF16)
    rcolf = sb("rcolf", [128, 1], F32)
    xsq2 = [sb(f"xsq2{k}", [128, CH], BF16) for k in range(2)]
    tev = sb("tev", [128, CH], F32)
    tevv = sb("tevv", [128, 256], F32)
    eps30 = sb("eps30", [128, 1], F32)
    tev2p = [sb(f"tev2p{k}", [128, CH], F32) for k in range(2)]
    x1sb = [sb(f"x1sb{k}", [128, CH], F32) for k in range(2)]
    xres = [sb(f"xres{k}", [128, CH], F32) for k in range(2)]
    wqcs = sb("wqcs", [1, D], BF16)
    wkcs = sb("wkcs", [1, 256], BF16)
    wvcs = sb("wvcs", [1, 256], BF16)
    bvb = sb("bvb", [128, 256], F32)
    ones128 = sb("ones128", [128, 1], BF16)
    onesrow_f = sb("onesrow_f", [1, 128], F32)
    one1f = sb("one1f", [1, 1], F32)
    eps5 = sb("eps5", [1, 1], F32)
    sel16s = sb("sel16s", [1, 256], F32)
    selcs = sb("selcs", [16, 1024], F32)
    bq64s = sb("bq64s", [64, 16], F32)
    bk64 = sb("bk64", [64, 4], F32)
    bvr = sb("bvr", [1, 256], F32)
    bo_c = sb("bo_c", [128, 8], F32)

    pst = [psa(f"pst{k}", [128, 1024]) for k in range(2)]
    pOt = psa("pO", [128, CH])
    pO = pOt
    pdent = psa("pden", [128, CH])
    pden = pdent
    pm1 = psa("pm1", [128, CH])
    pm2 = psa("pm2", [128, CH])

    lw = Sem(nc, stack, "lw")
    lb = Sem(nc, stack, "lb")
    lxp = [Sem(nc, stack, "lxA"), Sem(nc, stack, "lxB")]
    lrp = [Sem(nc, stack, "lrA"), Sem(nc, stack, "lrB")]
    so = Sem(nc, stack, "so")
    sop = [Sem(nc, stack, "soA"), Sem(nc, stack, "soB")]
    tS = Sem(nc, stack, "tS")
    vS = Sem(nc, stack, "vS")
    sS = Sem(nc, stack, "sS")
    t_st = Sem(nc, stack, "t_st")
    s_ex = Sem(nc, stack, "s_ex")
    t_pv = Sem(nc, stack, "t_pv")
    v_oe = Sem(nc, stack, "v_oe")
    t_pk = Sem(nc, stack, "t_pk")
    v_rd = Sem(nc, stack, "v_rd")
    v_nc = Sem(nc, stack, "v_nc")
    t_cm = Sem(nc, stack, "t_cm")
    v_sq = Sem(nc, stack, "v_sq")
    t_sq = Sem(nc, stack, "t_sq")
    t_std = Sem(nc, stack, "t_std")
    t_pvd = Sem(nc, stack, "t_pvd")
    t_pkd = Sem(nc, stack, "t_pkd")

    T, V, S, G, SY = nc.tensor, nc.vector, nc.scalar, nc.gpsimd, nc.sync
    Vw = make_serial(V, vS)
    Sw = make_serial(S, sS)
    Vn = make_serial(V, vS, auto_inc=False)
    Sn = make_serial(S, sS, auto_inc=False)
    if ctx is not None:
        for ws in _waits:
            G.wait_ge(ws.h, ws.n)
            SY.wait_ge(ws.h, ws.n)

    for t in range(NT):
        G.dma_start(out=wqb[t][:, :], in_=wqT[128 * t:128 * (t + 1), :]).then_inc(lw.h, 16)
        G.dma_start(out=wkb[t][:, :], in_=wkT[128 * t:128 * (t + 1), :]).then_inc(lw.h, 16)
        G.dma_start(out=wvb[t][:, :], in_=wvT[128 * t:128 * (t + 1), :]).then_inc(lw.h, 16)
        G.dma_start(out=wob[t][:, :], in_=woT[128 * t:128 * (t + 1), :]).then_inc(lw.h, 16)
    lw.n += 32 * 16
    for srcap, dst in ((bq, bq64s), (bk, bk64), (bv, bvr), (bo, bo_c),
                       (sel16x, sel16s), (selcol, selcs)):
        SY.dma_start(out=dst[:, :], in_=srcap[:, :]).then_inc(lb.h, 16)
    G.dma_start(out=stkb[:, :], in_=stk[:, :]).then_inc(lw.h, 16)
    lw.n += 16
    lb.n += 6 * 16

    Vw.memset(ones128[:, :], 1.0)
    Vw.memset(onesrow_f[:, :], 1.0)
    Vw.memset(one1f[:, :], 1.0)
    Vw.memset(eps5[:, :], 1e-5)
    Vw.memset(eps30[:, :], 1e-30)
    Vw.memset(v_all[:, :], 1.0)
    for h in range(3):
        Vw.memset(khead[h][64:65, :], 1.0)
    vS.inc(Vn.memset(khead[3][64:65, :], 1.0))
    vS.wait(T)
    vS.wait(S)
    lb.wait(V)
    lb.wait(S)
    vS.inc(Vn.tensor_scalar(out=bq64s[:, :], in0=bq64s[:, :], scalar1=0.125,
                           scalar2=None, op0=ALU.mult))

    lw.wait(T)
    for (wtiles, csb, wid) in ((wqb, wqcs, D), (wkb, wkcs, 256), (wvb, wvcs, 256)):
        for c0 in range(0, wid, CH):
            w = min(CH, wid - c0)
            for t in range(NT):
                T.matmul(pm1[0:1, 0:w], ones128[:, :], wtiles[t][:, c0:c0 + w],
                         start=(t == 0), stop=(t == NT - 1))
            tS.inc(T.drain(fusable=False))
            tS.wait(S)
            sS.inc(Sn.activation(out=csb[0:1, c0:c0 + w], in_=pm1[0:1, 0:w],
                                func=AF.Identity, scale=-1.0 / D))
            sS.wait(T)
    lb.wait(T)
    T.matmul(pm2[:, 0:256], onesrow_f[:, :], bvr[:, :], start=True, stop=True)
    tS.inc(T.drain(fusable=False))
    tS.wait(V)
    vS.inc(Vn.tensor_copy(out=bvb[:, :], in_=pm2[:, 0:256]))
    vS.wait(T)

    t_lastuse = [0, 0]

    def load_chunk(c):
        slot = c % 2
        G.wait_ge(tS.h, t_lastuse[slot])
        for t in range(NT):
            G.dma_start(out=xbc[slot][t][:, :],
                        in_=xT[128 * t:128 * (t + 1), c * CH:(c + 1) * CH]).then_inc(lxp[slot].h, 16)
        lxp[slot].n += NT * 16
        return (slot, lxp[slot].n)

    def stats(c, xc):
        cc = slice(c * CH, (c + 1) * CH)
        lsl, lval = loaded[c]
        V.wait_ge(lxp[lsl].h, lval)
        V.wait_ge(sS.h, sS.n)
        T.wait_ge(lxp[lsl].h, lval)
        sq0, tq0 = v_sq.n, t_sq.n
        for t in range(NT):
            if t >= 2:
                V.wait_ge(t_sq.h, tq0 + t - 1)
            v_sq.inc(Vn.tensor_mul(out=xsq2[t % 2][:, :], in0=xc[t][:, :], in1=xc[t][:, :]))
            T.wait_ge(v_sq.h, sq0 + t + 1)
            T.matmul(pm1[0:1, :], ones128[:, :], xc[t][:, :],
                     start=(t == 0), stop=(t == NT - 1))
            t_sq.inc(T.matmul(pm2[0:1, :], ones128[:, :], xsq2[t % 2][:, :],
                              start=(t == 0), stop=(t == NT - 1)))
        tS.inc(T.drain(fusable=False))
        tS.wait(V)
        Vw.tensor_copy(out=s1row[:, :], in_=pm1[0:1, :])
        Vw.tensor_copy(out=s2row[:, :], in_=pm2[0:1, :])
        Vw.tensor_copy(out=S1b[0:1, cc], in_=s1row[:, :])
        Vw.tensor_scalar(out=mrow[:, :], in0=s1row[:, :], scalar1=1.0 / D,
                        scalar2=None, op0=ALU.mult)
        Vw.tensor_mul(out=mrow[:, :], in0=mrow[:, :], in1=mrow[:, :])
        vS.inc(Vn.scalar_tensor_tensor(out=s1row[:, :], in0=s2row[:, :], scalar=1.0 / D,
                                      in1=mrow[:, :], op0=ALU.mult, op1=ALU.subtract))
        vS.wait(S)
        Sw.activation(out=s2row[:, :], in_=s1row[:, :], func=AF.Ln, bias=eps5[0:1, 0:1], scale=1.0)
        sS.inc(Sn.activation(out=rrows[0:1, cc], in_=s2row[:, :], func=AF.Exp,
                            bias=0.0, scale=-0.5))
        sS.wait(T)
        T.matmul(pm1[:, :], onesrow_f[:, :], rrows[0:1, cc], start=True, stop=True)
        tS.inc(T.drain(fusable=False))
        tS.wait(V)
        vS.inc(Vn.tensor_copy(out=rb[:, :], in_=pm1[:, :]))
        vS.wait(T)

    loaded = [load_chunk(0)]
    for c in range(NCH):
        slot = c % 2
        xc = xbc[slot]
        cc = slice(c * CH, (c + 1) * CH)
        stats(c, xc)

        for h in range(4):
            for t in range(NT):
                T.matmul(pm1[0:64, :], wkb[t][:, 64 * h:64 * (h + 1)], xc[t][:, :],
                         start=(t == 0), stop=False)
            T.matmul(pm1[0:64, :], wkcs[0:1, 64 * h:64 * (h + 1)],
                     S1b[0:1, cc], start=False, stop=True)
            tS.inc(T.drain(fusable=False))
            tS.wait(V)
            V.wait_ge(sS.h, sS.n)
            vS.inc(Vn.tensor_tensor(out=tev[0:64, :], in0=pm1[0:64, :],
                                   in1=rb[0:64, :], op=ALU.mult))
            vS.wait(S)
            sS.inc(Sn.activation(out=khead[h][0:64, cc], in_=tev[0:64, :],
                                func=AF.Identity, bias=bk64[:, h:h + 1], scale=1.0))
            sS.wait(T)

        for j in range(4):
            s0 = c * CH + j * 128
            o0 = j * 128
            T.matmul(pm1[:, 0:1], rrows[0:1, s0:s0 + 128], one1f[:, :],
                     start=True, stop=True)
            for t in range(NT):
                T.matmul(pm2[:, 0:256], xc[t][:, o0:o0 + 128], wvb[t][:, :],
                         start=(t == 0), stop=False)
            T.matmul(pm2[:, 0:256], S1b[0:1, s0:s0 + 128], wvcs[0:1, :],
                     start=False, stop=True)
            tS.inc(T.drain(fusable=False))
            tS.wait(V)
            Vw.tensor_copy(out=rcolf[:, :], in_=pm1[:, 0:1])
            Vw.tensor_copy(out=tevv[:, :], in_=pm2[:, 0:256])
            st = c * 4 + j
            for kvh in range(4):
                Vw.scalar_tensor_tensor(
                    out=v_all[:, st * 260 + 65 * kvh: st * 260 + 65 * kvh + 64],
                    in0=tevv[:, 64 * kvh:64 * (kvh + 1)], scalar=rcolf[:, 0:1],
                    in1=bvb[:, 64 * kvh:64 * (kvh + 1)], op0=ALU.mult, op1=ALU.add)
            vS.wait(T)
        t_lastuse[slot] = tS.n
        if c + 1 < NCH:
            loaded.append(load_chunk(c + 1))

    V.wait_ge(sS.h, sS.n)
    for h in range(4):
        Vw.tensor_reduce(out=kmax4[:, h:h + 1], in_=khead[h][0:64, :], axis=AX.X,
                         op=ALU.max, apply_absolute_value=True)
    vS.wait(T)

    t_lastuse = [tS.n, tS.n]
    loaded = [load_chunk(0)]
    adds_done = 0
    qtt_done = {}

    def emit_qbuild(c, i):
        H = c * 16 + i
        cc = slice(c * CH, (c + 1) * CH)
        xc = xbc[c % 2]
        kv = i // 4
        qp = qplus[H % 2]
        qa = qabs[H % 2]
        lsl, lval = loaded[c]
        T.wait_ge(lxp[lsl].h, lval)
        if H >= 1:
            T.wait_ge(vS.h, qtt_done[H - 1])
        for t in range(NT):
            T.matmul(pm1[0:64, :], wqb[t][:, 64 * i:64 * (i + 1)], xc[t][:, :],
                     start=(t == 0), stop=False)
        T.matmul(pm1[0:64, :], wqcs[0:1, 64 * i:64 * (i + 1)],
                 S1b[0:1, cc], start=False, stop=True)
        tS.inc(T.drain(fusable=False))
        V.wait_ge(tS.h, tS.n)
        V.wait_ge(sS.h, sS.n)
        vS.inc(Vn.tensor_tensor(out=tev[0:64, :], in0=pm1[0:64, :],
                               in1=rb[0:64, :], op=ALU.mult))
        qtt_done[H] = vS.n
        S.wait_ge(vS.h, qtt_done[H])
        if H >= 2:
            S.wait_ge(t_pv.h, NP_G * (H - 1))
            S.wait_ge(t_cm.h, H - 1)
        Sw.activation(out=qp[0:64, :], in_=tev[0:64, :], func=AF.Identity,
                     bias=bq64s[:, i:i + 1], scale=0.125)
        sS.inc(Sn.activation(out=qa[0:64, :], in_=qp[0:64, :], func=AF.Abs))
        sS.wait(T)
        T.wait_ge(v_nc.h, v_nc.n)
        T.matmul(pm2[0:1, :], kmax4[:, kv:kv + 1], qa[0:64, :],
                 start=True, stop=True)
        t_cm.inc(T.drain(fusable=False))
        t_cm.wait(S)
        v_nc.inc(Sn.activation(out=qp[64:65, :], in_=pm2[0:1, :],
                              func=AF.Identity, scale=-1.0))

    qa_done = {}

    def emit_q_tmms(c, i):
        H = c * 16 + i
        cc = slice(c * CH, (c + 1) * CH)
        xc = xbc[c % 2]
        lsl, lval = loaded[c]
        T.wait_ge(lxp[lsl].h, lval)
        if H >= 1:
            T.wait_ge(vS.h, qtt_done[H - 1])
        qcov = None
        for t in range(NT):
            mm = T.matmul(pm1[0:64, :], wqb[t][:, 64 * i:64 * (i + 1)], xc[t][:, :],
                          start=(t == 0), stop=False)
            if t == 1:
                tS.inc(mm)
                qcov = tS.n
        T.matmul(pm1[0:64, :], wqcs[0:1, 64 * i:64 * (i + 1)],
                 S1b[0:1, cc], start=False, stop=True)
        V.wait_ge(t_st.h, t_st.n + 1)
        V.wait_ge(sS.h, sS.n)
        vS.inc(Vn.tensor_tensor(out=tev[0:64, :], in0=pm1[0:64, :],
                               in1=rb[0:64, :], op=ALU.mult))
        qtt_done[H] = vS.n
        return qcov

    def emit_q_sacts(c, i):
        H = c * 16 + i
        qp = qplus[H % 2]
        qa = qabs[H % 2]
        S.wait_ge(vS.h, qtt_done[H])
        if H >= 2:
            S.wait_ge(t_pv.h, NP_G * (H - 1))
            S.wait_ge(t_cm.h, H - 1)
        Sw.activation(out=qp[0:64, :], in_=tev[0:64, :], func=AF.Identity,
                     bias=bq64s[:, i:i + 1], scale=0.125)
        sS.inc(Sn.activation(out=qa[0:64, :], in_=qp[0:64, :], func=AF.Abs))
        qa_done[H] = sS.n

    def emit_q_cmm(c, i):
        H = c * 16 + i
        kv = i // 4
        qp = qplus[H % 2]
        qa = qabs[H % 2]
        T.wait_ge(sS.h, qa_done[H])
        T.wait_ge(v_nc.h, v_nc.n)
        t_cm.inc(T.matmul(pm2[0:1, :], kmax4[:, kv:kv + 1], qa[0:64, :],
                          start=True, stop=True))
        S.wait_ge(t_st.h, t_st.n + 1)
        v_nc.inc(Sn.activation(out=qp[64:65, :], in_=pm2[0:1, :],
                              func=AF.Identity, scale=-1.0))

    for c in range(NCH):
        slot = c % 2
        cc = slice(c * CH, (c + 1) * CH)
        sS.wait(T)
        T.matmul(pm1[:, :], onesrow_f[:, :], rrows[0:1, cc], start=True, stop=True)
        tS.inc(T.drain(fusable=False))
        tS.wait(V)
        vS.inc(Vn.tensor_copy(out=rb[:, :], in_=pm1[:, :]))
        vS.wait(T)
        emit_qbuild(c, 0)

        for i in range(16):
            H = c * 16 + i
            kv = i // 4
            qp = qplus[H % 2]
            trigger = min(4, NP_G - 1)

            split = (NP_G == 16)
            T.wait_ge(v_nc.h, H + 1)
            for p in range(NP_G):
                g = NP_G * H + p
                buf = g % 2
                if g >= 2:
                    T.wait_ge(s_ex.h, g - 1)
                for j in range(2):
                    stile = 2 * p + j
                    mm = T.matmul(pst[buf][:, 512 * j:512 * (j + 1)],
                                  khead[kv][:, stile * 128:(stile + 1) * 128],
                                  qp[:, :], start=True, stop=True)
                t_st.inc(mm)
                if p == NP_G - 1 and not split:
                    t_std.inc(T.drain(fusable=False))
                qcov = None
                if p == trigger:
                    if i >= 1:
                        T.wait_ge(v_oe.h, H)
                        if i == 1:
                            T.wait_ge(v_rd.h, c)
                        t_pk.inc(T.matmul(pden[0:16, :],
                                          sel16s[0:1, 16 * (i - 1):16 * i],
                                          den2[0:1, ((H - 1) % 2) * CH:((H - 1) % 2 + 1) * CH],
                                          start=(i == 1), stop=False))
                    if i + 1 < 16:
                        if split:
                            qcov = emit_q_tmms(c, i + 1)
                        else:
                            emit_qbuild(c, i + 1)
                if split and p == 6 and i + 1 < 16:
                    emit_q_sacts(c, i + 1)
                if qcov is not None:
                    S.wait_ge(tS.h, qcov)
                elif p < NP_G - 1:
                    S.wait_ge(t_st.h, g + 2)
                elif split:
                    S.wait_ge(t_pv.h, NP_G * (H + 1) - 1)
                else:
                    S.wait_ge(t_std.h, t_std.n)
                if g >= 2:
                    S.wait_ge(t_pv.h, g - 1)
                s_ex.inc(Sn.activation(out=Ebuf[buf][:, :], in_=pst[buf][:, :],
                                      func=AF.Exp))
                if p >= 1:
                    gp = g - 1
                    T.wait_ge(s_ex.h, gp + 1)
                    if p == 1:
                        T.wait_ge(v_oe.h, H)
                    for j in range(2):
                        stile = 2 * (p - 1) + j
                        mm = T.matmul(pO[0:65, :],
                                      v_all[:, stile * 260 + 65 * kv:stile * 260 + 65 * (kv + 1)],
                                      Ebuf[gp % 2][:, 512 * j:512 * (j + 1)],
                                      start=(p == 1 and j == 0), stop=False)
                    t_pv.inc(mm)
                if split and p == 8 and i + 1 < 16:
                    emit_q_cmm(c, i + 1)
            g = NP_G * H + (NP_G - 1)
            T.wait_ge(s_ex.h, g + 1)
            if NP_G == 1:
                T.wait_ge(v_oe.h, H)
            for j in range(2):
                stile = 2 * (NP_G - 1) + j
                mm = T.matmul(pO[0:65, :],
                              v_all[:, stile * 260 + 65 * kv:stile * 260 + 65 * (kv + 1)],
                              Ebuf[g % 2][:, 512 * j:512 * (j + 1)],
                              start=(NP_G == 1 and j == 0), stop=(j == 1))
            t_pv.inc(mm)
            if i == 15:
                t_pvd.inc(T.drain(fusable=False))

            V.wait_ge(t_pv.h, NP_G * (H + 1))
            if i == 15:
                V.wait_ge(t_pvd.h, t_pvd.n)
            else:
                V.wait_ge(t_st.h, NP_G * (H + 1) + 1)
            if H >= 2:
                V.wait_ge(t_pk.h, H - 1)
            dslot = H % 2
            Vw.tensor_copy(out=den2[0:1, dslot * CH:(dslot + 1) * CH], in_=pO[64:65, :])
            v_oe.inc(Vn.tensor_copy(out=Obuf[:, i * CH:(i + 1) * CH], in_=pO[0:64, :]))
        T.wait_ge(v_oe.h, v_oe.n)
        t_pk.inc(T.matmul(pden[0:16, :], sel16s[0:1, 240:256],
                          den2[0:1, ((c * 16 + 15) % 2) * CH:((c * 16 + 15) % 2 + 1) * CH],
                          start=False, stop=True))
        t_pkd.inc(T.drain(fusable=False))

        S.wait_ge(t_pkd.h, c + 1)
        sS.inc(Sn.activation(out=denbp[0][0:16, :], in_=pden[0:16, :],
                            func=AF.Identity, bias=eps30[0:16, 0:1], scale=1.0))
        sS.wait(V)
        v_rd.inc(Vn.reciprocal(out=rd16[:, :], in_=denbp[0][0:16, :]))
        nbanks = [pm1, pm2]
        nmm_done = []
        nb_done = []
        mult_done = []
        T.wait_ge(v_rd.h, c + 1)
        T.wait_ge(v_nc.h, v_nc.n)

        def norm_evac(i, cover):
            nslot = i % 2
            V.wait_ge(tS.h, cover)
            if i >= 2:
                V.wait_ge(vS.h, mult_done[i - 2])
            vS.inc(Vn.tensor_copy(out=denbp[nslot][:, :], in_=nbanks[nslot][0:64, :]))
            nb_done.append(vS.n)
            vS.inc(Vn.tensor_tensor(out=Obuf[:, i * CH:(i + 1) * CH],
                                   in0=Obuf[:, i * CH:(i + 1) * CH],
                                   in1=denbp[nslot][:, :], op=ALU.mult))
            mult_done.append(vS.n)

        for i in range(16):
            nslot = i % 2
            if i >= 2:
                T.wait_ge(vS.h, nb_done[i - 2])
            tS.inc(T.matmul(nbanks[nslot][0:64, :], selcs[:, 64 * i:64 * (i + 1)],
                            rd16[:, :], start=True, stop=True))
            nmm_done.append(tS.n)
            if i >= 1:
                norm_evac(i - 1, nmm_done[i])
        tS.inc(T.drain(fusable=False))
        norm_evac(15, tS.n)

        pmm_done = []
        pr_done = []
        for pr in range(8):
            pslot = pr % 2
            if pr < 2:
                T.wait_ge(vS.h, mult_done[15])
            else:
                T.wait_ge(vS.h, pr_done[pr - 2])
            T.matmul(nbanks[pslot][:, :], stkb[:, 0:128],
                     Obuf[:, (2 * pr) * CH:(2 * pr + 1) * CH], start=True, stop=False)
            tS.inc(T.matmul(nbanks[pslot][:, :], stkb[:, 128:256],
                            Obuf[:, (2 * pr + 1) * CH:(2 * pr + 2) * CH],
                            start=False, stop=True))
            pmm_done.append(tS.n)
            if pr >= 1:
                V.wait_ge(tS.h, pmm_done[pr])
                vS.inc(Vn.tensor_copy(out=Opair[:, (pr - 1) * CH:pr * CH],
                                      in_=nbanks[(pr - 1) % 2][:, :]))
                pr_done.append(vS.n)
        tS.inc(T.drain(fusable=False))
        V.wait_ge(tS.h, tS.n)
        vS.inc(Vn.tensor_copy(out=Opair[:, 7 * CH:8 * CH], in_=nbanks[1][:, :]))
        pr_done.append(vS.n)

        omm_done = []
        op_act_done = []
        op_add_done = []
        lr_done = []

        def oproj_tail(dt, cover):
            oslot = dt % 2
            S.wait_ge(tS.h, cover)
            if dt >= 2:
                S.wait_ge(vS.h, op_add_done[dt - 2])
            sS.inc(Sn.activation(out=tev2p[oslot][:, :], in_=nbanks[oslot][:, :],
                                func=AF.Identity, bias=bo_c[:, dt:dt + 1], scale=1.0))
            op_act_done.append(sS.n)
            V.wait_ge(sS.h, op_act_done[dt])
            V.wait_ge(lrp[dt % 2].h, lr_done[dt])
            V.wait_ge(sop[oslot].h, sop[oslot].n)
            vS.inc(Vn.tensor_add(out=x1sb[oslot][:, :], in0=tev2p[oslot][:, :],
                                in1=xres[oslot][:, :]))
            op_add_done.append(vS.n)
            vS.wait(SY)
            SY.dma_start(out=x1T[128 * dt:128 * (dt + 1), cc],
                         in_=x1sb[oslot][:, :]).then_inc(sop[oslot].h, 16)
            sop[oslot].n += 16

        for dt in range(NT):
            oslot = dt % 2
            SY.wait_ge(vS.h, adds_done)
            SY.dma_start(out=xres[oslot][:, :],
                         in_=xT[128 * dt:128 * (dt + 1), cc]).then_inc(lrp[oslot].h, 16)
            lrp[oslot].n += 16
            lr_done.append(lrp[oslot].n)
            if dt == 0:
                T.wait_ge(vS.h, pr_done[7])
                T.wait_ge(sS.h, sS.n)
            if dt >= 2:
                T.wait_ge(sS.h, op_act_done[dt - 2])
            for pr in range(8):
                mm = T.matmul(nbanks[oslot][:, :], wob[pr][:, 128 * dt:128 * (dt + 1)],
                              Opair[:, pr * CH:(pr + 1) * CH],
                              start=(pr == 0), stop=(pr == 7))
            tS.inc(mm)
            omm_done.append(tS.n)
            if dt >= 1:
                oproj_tail(dt - 1, omm_done[dt])
                adds_done = op_add_done[dt - 1]
        tS.inc(T.drain(fusable=False))
        oproj_tail(7, tS.n)
        adds_done = op_add_done[7]
        t_lastuse[slot] = tS.n
        if c + 1 < NCH:
            loaded.append(load_chunk(c + 1))

    SY.wait_ge(sop[0].h, sop[0].n)
    SY.wait_ge(sop[1].h, sop[1].n)
    if ctx is None:
        stack.close()
        return nc
    return sop



def build_l0_ffn(N=N, NCH=NCH, ctx=None):
    if ctx is None:
        nc = bass.Bass("TRN2", target_bir_lowering=False)
        stack = contextlib.ExitStack()
        xT = nc.dram_tensor("xT", [D, N], F32, kind="ExternalInput").ap()
    else:
        nc, stack, xT, _out, _waits = ctx
    m1bd = nc.dram_tensor("m1bd", [8 * 128, 128], F32, kind="ExternalInput").ap()
    m2arr = nc.dram_tensor("m2arr", [32 * 128, 512], F32, kind="ExternalInput").ap()
    brow = nc.dram_tensor("brow", [128, 8], F32, kind="ExternalInput").ap()
    if ctx is None:
        x2T = nc.dram_tensor("x2T", [D, N], F32, kind="ExternalOutput").ap()
    else:
        x2T = _out

    _p = f"u{_uid()}_"
    sb = lambda name, shape, dt: stack.enter_context(nc.sbuf_tensor(_p + name, shape, dt))
    psa = lambda name, shape: stack.enter_context(nc.psum_tensor(_p + name, shape, F32))

    xbc = [[sb(f"xbc{k}_{t}", [128, CH], BF16) for t in range(NT)] for k in range(2)]
    m1b = sb("m1b", [128, 8 * 128], F32)
    m2b = [sb(f"m2b{k}", [128, 512], F32) for k in range(4)]
    wR = sb("wR", [128, 4 * 512], BF16)
    wI = sb("wI", [128, 4 * 512], BF16)
    wIn = sb("wIn", [128, 4 * 512], BF16)
    rsr = sb("rsr", [1, 512], BF16)
    rsi = sb("rsi", [1, 512], BF16)
    t1row = sb("t1row", [1, 512], F32)
    t2row = sb("t2row", [1, 512], F32)
    s1row = sb("s1row", [1, CH], F32)
    s2row = sb("s2row", [1, CH], F32)
    mrow = sb("mrow", [1, CH], F32)
    rrow = sb("rrow", [1, CH], F32)
    rb = sb("rb", [128, CH], F32)
    S1b = sb("S1b", [1, CH], BF16)
    xsq2 = [sb(f"xsq2{k}", [128, CH], BF16) for k in range(2)]
    tev = sb("tev", [128, CH], F32)
    tevp = [sb(f"tevp{k}", [128, CH], F32) for k in range(2)]
    x2sb = [sb(f"x2sb{k}", [128, CH], F32) for k in range(2)]
    xres = [sb(f"xres{k}", [128, CH], F32) for k in range(2)]
    ones128 = sb("ones128", [128, 1], BF16)
    onesrow_f = sb("onesrow_f", [1, 128], F32)
    eps5 = sb("eps5", [1, 1], F32)
    brc = sb("brc", [128, 8], F32)

    pm1 = psa("pm1", [128, CH])
    pm2 = psa("pm2", [128, CH])
    pmw = psa("pmw", [128, CH])

    lw = Sem(nc, stack, "lw")
    lx = Sem(nc, stack, "lx")
    lr = Sem(nc, stack, "lr")
    so = Sem(nc, stack, "so")
    sop = [Sem(nc, stack, "soA"), Sem(nc, stack, "soB")]
    tS = Sem(nc, stack, "tS")
    vS = Sem(nc, stack, "vS")
    sS = Sem(nc, stack, "sS")
    v_sq = Sem(nc, stack, "v_sq")
    t_sq = Sem(nc, stack, "t_sq")

    T, V, S, G, SY = nc.tensor, nc.vector, nc.scalar, nc.gpsimd, nc.sync
    Vw = make_serial(V, vS)
    Sw = make_serial(S, sS)
    Vn = make_serial(V, vS, auto_inc=False)
    Sn = make_serial(S, sS, auto_inc=False)
    if ctx is not None:
        for ws in _waits:
            G.wait_ge(ws.h, ws.n)
            SY.wait_ge(ws.h, ws.n)

    SY.dma_start(out=m1b.rearrange("p (a j) -> p a j", a=8),
                 in_=m1bd.rearrange("(a p) j -> p a j", p=128)).then_inc(lw.h, 16)
    SY.dma_start(out=brc[:, :], in_=brow[:, :]).then_inc(lw.h, 16)
    lw.n += 32
    Vw.memset(ones128[:, :], 1.0)
    Vw.memset(onesrow_f[:, :], 1.0)
    vS.inc(Vn.memset(eps5[:, :], 1e-5))
    vS.wait(T)
    vS.wait(S)

    m2ld = 0
    for RI in range(2):
        for tau in range(4):
            for c in range(4):
                slot = c
                G.wait_ge(tS.h, tS.n)
                G.dma_start(out=m2b[slot][:, :],
                            in_=m2arr[(RI * 16 + c * 4 + tau) * 128:
                                      (RI * 16 + c * 4 + tau) * 128 + 128, :]).then_inc(lx.h, 16)
                lx.n += 16
                T.wait_ge(lx.h, lx.n)
                T.wait_ge(lw.h, 32)
                if c == 0:
                    T.wait_ge(sS.h, sS.n)
                T.matmul(pmw[:, :], m1b[:, (RI * 4 + c) * 128:(RI * 4 + c) * 128 + 128],
                         m2b[slot][:, :], start=(c == 0), stop=(c == 3))
            tS.inc(T.drain(fusable=False))
            tS.wait(S)
            dst = wR if RI == 0 else wI
            Sw.activation(out=dst[:, tau * 512:(tau + 1) * 512], in_=pmw[:, :],
                         func=AF.Identity, scale=1.0)
            if RI == 1:
                sS.inc(Sn.activation(out=wIn[:, tau * 512:(tau + 1) * 512], in_=pmw[:, :],
                                    func=AF.Identity, scale=-1.0))
            else:
                sS.inc(Sn.nop())
    sS.wait(T)
    for tau in range(4):
        T.matmul(pm1[0:1, :], ones128[:, :], wR[:, tau * 512:(tau + 1) * 512],
                 start=(tau == 0), stop=(tau == 3))
    for tau in range(4):
        T.matmul(pm2[0:1, :], ones128[:, :], wI[:, tau * 512:(tau + 1) * 512],
                 start=(tau == 0), stop=(tau == 3))
    tS.inc(T.drain(fusable=False))
    tS.wait(V)
    Vw.tensor_copy(out=t1row[:, :], in_=pm1[0:1, :])
    vS.inc(Vn.tensor_copy(out=t2row[:, :], in_=pm2[0:1, :]))
    vS.wait(S)
    Vw.tensor_sub(out=s1row[0:1, :], in0=t1row[:, :], in1=t2row[:, :])
    Vw.tensor_add(out=s2row[0:1, :], in0=t1row[:, :], in1=t2row[:, :])
    vS.inc(Vn.nop())
    vS.wait(S)
    Sw.activation(out=rsr[0:1, :], in_=s1row[0:1, :], func=AF.Identity, scale=-1.0 / D)
    sS.inc(Sn.activation(out=rsi[0:1, :], in_=s2row[0:1, :],
                        func=AF.Identity, scale=-1.0 / D))
    sS.wait(T)
    sS.wait(V)

    t_lastuse = [tS.n, tS.n]

    def load_chunk(c):
        slot = c % 2
        G.wait_ge(tS.h, t_lastuse[slot])
        for t in range(NT):
            G.dma_start(out=xbc[slot][t][:, :],
                        in_=xT[128 * t:128 * (t + 1), c * CH:(c + 1) * CH]).then_inc(lx.h, 16)
        lx.n += NT * 16
        return lx.n

    loaded = {0: load_chunk(0)}
    adds_done = 0
    for c in range(NCH):
        slot = c % 2
        xc = xbc[slot]
        V.wait_ge(lx.h, loaded[c])
        V.wait_ge(sS.h, sS.n)
        T.wait_ge(lx.h, loaded[c])
        sq0, tq0 = v_sq.n, t_sq.n
        for t in range(NT):
            if t >= 2:
                V.wait_ge(t_sq.h, tq0 + t - 1)
            v_sq.inc(Vn.tensor_mul(out=xsq2[t % 2][:, :], in0=xc[t][:, :], in1=xc[t][:, :]))
            T.wait_ge(v_sq.h, sq0 + t + 1)
            T.matmul(pm1[0:1, :], ones128[:, :], xc[t][:, :], start=(t == 0), stop=False)
            t_sq.inc(T.matmul(pm2[0:1, :], ones128[:, :], xsq2[t % 2][:, :],
                              start=(t == 0), stop=(t == NT - 1)))
        tS.inc(T.drain(fusable=False))
        tS.wait(V)
        Vw.tensor_copy(out=s1row[:, :], in_=pm1[0:1, :])
        Vw.tensor_copy(out=s2row[:, :], in_=pm2[0:1, :])
        Vw.tensor_copy(out=S1b[0:1, :], in_=s1row[:, :])
        Vw.tensor_scalar(out=mrow[:, :], in0=s1row[:, :], scalar1=1.0 / D,
                        scalar2=None, op0=ALU.mult)
        Vw.tensor_mul(out=mrow[:, :], in0=mrow[:, :], in1=mrow[:, :])
        vS.inc(Vn.scalar_tensor_tensor(out=s1row[:, :], in0=s2row[:, :], scalar=1.0 / D,
                                      in1=mrow[:, :], op0=ALU.mult, op1=ALU.subtract))
        vS.wait(S)
        Sw.activation(out=s2row[:, :], in_=s1row[:, :], func=AF.Ln, bias=eps5[0:1, 0:1], scale=1.0)
        sS.inc(Sn.activation(out=rrow[0:1, :], in_=s2row[:, :], func=AF.Exp,
                            bias=0.0, scale=-0.5))
        sS.wait(T)
        T.matmul(pm1[:, :], onesrow_f[:, :], rrow[0:1, :], start=True, stop=True)
        tS.inc(T.drain(fusable=False))
        tS.wait(V)
        vS.inc(Vn.tensor_copy(out=rb[:, :], in_=pm1[:, :]))
        vS.wait(T)

        sbanks = [pm1, pmw]
        tt_done = []
        act_done = []
        add_done_l = []
        for sig in range(8):
            im = sig >= 4
            tpart = sig % 4
            slot = sig % 2
            SY.wait_ge(vS.h, adds_done)
            SY.dma_start(out=xres[slot][:, :],
                         in_=xT[128 * sig:128 * (sig + 1),
                                c * CH:(c + 1) * CH]).then_inc(lr.h, 16)
            lr.n += 16
            if sig >= 2:
                T.wait_ge(vS.h, tt_done[sig - 2])
            wA = wI if im else wR
            wB = wR if im else wIn
            for tau in range(4):
                T.matmul(sbanks[slot][:, :],
                         wA[:, tau * 512 + tpart * 128:tau * 512 + tpart * 128 + 128],
                         xc[tau][:, :], start=(tau == 0), stop=False)
            for tau in range(4):
                T.matmul(sbanks[slot][:, :],
                         wB[:, tau * 512 + tpart * 128:tau * 512 + tpart * 128 + 128],
                         xc[4 + tau][:, :], start=False, stop=False)
            rs = rsi if im else rsr
            T.matmul(sbanks[slot][:, :], rs[0:1, 128 * tpart:128 * (tpart + 1)], S1b[0:1, :],
                     start=False, stop=True)
            tS.inc(T.drain(fusable=False))
            dcount = tS.n
            V.wait_ge(tS.h, dcount)
            if sig >= 2:
                V.wait_ge(sS.h, act_done[sig - 2])
            vS.inc(Vn.tensor_tensor(out=tevp[slot][:, :], in0=sbanks[slot][:, :],
                                    in1=rb[:, :], op=ALU.mult))
            tt_done.append(vS.n)
            S.wait_ge(vS.h, tt_done[sig])
            if sig >= 2:
                S.wait_ge(vS.h, add_done_l[sig - 2])
            sS.inc(Sn.activation(out=xsq2[slot][:, :], in_=tevp[slot][:, :], func=AF.Identity,
                                 bias=brc[:, sig:sig + 1], scale=1.0))
            act_done.append(sS.n)
            V.wait_ge(sS.h, act_done[sig])
            V.wait_ge(lr.h, lr.n)
            V.wait_ge(sop[slot].h, sop[slot].n)
            vS.inc(Vn.tensor_add(out=x2sb[slot][:, :], in0=xsq2[slot][:, :],
                                 in1=xres[slot][:, :]))
            adds_done = vS.n
            add_done_l.append(vS.n)
            vS.wait(SY)
            SY.dma_start(out=x2T[128 * sig:128 * (sig + 1), c * CH:(c + 1) * CH],
                         in_=x2sb[slot][:, :]).then_inc(sop[slot].h, 16)
            sop[slot].n += 16
        t_lastuse[slot] = tS.n
        if c + 1 < NCH:
            loaded[c + 1] = load_chunk(c + 1)

    SY.wait_ge(sop[0].h, sop[0].n)
    SY.wait_ge(sop[1].h, sop[1].n)
    if ctx is None:
        stack.close()
        return nc
    return sop



def build_l1_attn(N=N, NCH=NCH, stage=9, ctx=None):
    if ctx is None:
        nc = bass.Bass("TRN2", target_bir_lowering=False)
        stack = contextlib.ExitStack()
        xT = nc.dram_tensor("xT", [D, N], F32, kind="ExternalInput").ap()
    else:
        nc, stack, xT, _out, _waits = ctx
    Wq = nc.dram_tensor("Wq", [D, D], F32, kind="ExternalInput").ap()
    Wk = nc.dram_tensor("Wk", [D, D], F32, kind="ExternalInput").ap()
    Wv = nc.dram_tensor("Wv", [D, D], F32, kind="ExternalInput").ap()
    Wo = nc.dram_tensor("Wo", [D, D], F32, kind="ExternalInput").ap()
    W1 = nc.dram_tensor("W1", [D, 4096], F32, kind="ExternalInput").ap()
    W2 = nc.dram_tensor("W2", [4096, D], F32, kind="ExternalInput").ap()
    bqc = nc.dram_tensor("bqc", [64, 16], F32, kind="ExternalInput").ap()
    bkr = nc.dram_tensor("bkr", [1, D], F32, kind="ExternalInput").ap()
    bvr = nc.dram_tensor("bvr", [1, D], F32, kind="ExternalInput").ap()
    boc = nc.dram_tensor("boc", [128, 8], F32, kind="ExternalInput").ap()
    b1c = nc.dram_tensor("b1c", [128, 32], F32, kind="ExternalInput").ap()
    b2c = nc.dram_tensor("b2c", [128, 8], F32, kind="ExternalInput").ap()
    g1c = nc.dram_tensor("g1c", [128, 8], F32, kind="ExternalInput").ap()
    be1c = nc.dram_tensor("be1c", [128, 8], F32, kind="ExternalInput").ap()
    g2c = nc.dram_tensor("g2c", [128, 8], F32, kind="ExternalInput").ap()
    be2c = nc.dram_tensor("be2c", [128, 8], F32, kind="ExternalInput").ap()
    sel16x = nc.dram_tensor("sel16x1", [1, 256], F32, kind="ExternalInput").ap()
    selcol = nc.dram_tensor("selcol1", [16, 1024], F32, kind="ExternalInput").ap()
    stk = nc.dram_tensor("stk1", [64, 256], F32, kind="ExternalInput").ap()
    W1c = nc.dram_tensor("W1c", [128, 32 * 1024], BF16, kind="Internal").ap()
    W2c = nc.dram_tensor("W2c", [4096, 1024], BF16, kind="Internal").ap()
    if ctx is None:
        x3T = nc.dram_tensor("x3T", [D, N], F32, kind="ExternalOutput").ap()
    else:
        x3T = _out

    _p = f"u{_uid()}_"
    sb = lambda name, shape, dt: stack.enter_context(nc.sbuf_tensor(_p + name, shape, dt))
    psa = lambda name, shape: stack.enter_context(nc.psum_tensor(_p + name, shape, F32))

    NSC = CH // 128

    wqb = [sb(f"wqb{t}", [128, D], BF16) for t in range(NT)]
    wkb = [sb(f"wkb{t}", [128, D], BF16) for t in range(NT)]
    wvb = [sb(f"wvb{t}", [128, D], BF16) for t in range(NT)]
    wob = [sb(f"wob{t}", [128, D], BF16) for t in range(NT)]
    arena = sb("arena", [128, 32 * CH], BF16)
    hbuf = arena
    xbc = [arena[:, 512 * t:512 * (t + 1)] for t in range(NT)]
    srcb = [sb(f"srcb{t}", [128, CH], BF16) for t in range(NT)]
    w1b = [sb(f"w1b{k}", [128, 1024], BF16) for k in range(4)]
    w2b = [sb(f"w2b{k}", [128, 512], BF16) for k in range(4)]
    kf = sb("kf", [128, 1024], F32)
    ebk = sb("ebk", [128, 1024], BF16)
    Ktok = sb("Ktok", [128, 1024], BF16)
    Vtok = sb("Vtok", [128, 1024], BF16)
    KVb = sb("KVb", [64, 1024], BF16)
    KVacc = sb("KVacc", [64, 1024], F32)
    Ksumr = sb("Ksumr", [1, 1024], F32)
    KsumColb = sb("KsumColb", [64, 16], BF16)
    qe = [sb(f"qe{k}", [64, CH], BF16) for k in range(2)]
    Obuf = arena[0:64, 16 * CH:32 * CH]
    Opair = arena[:, 8 * CH:16 * CH]
    stkb = sb("stkb", [64, 256], BF16)
    den2 = sb("den2", [1, 2 * CH], F32)
    rd16 = sb("rd16", [16, CH], F32)
    denbf = sb("denbf", [64, CH], F32)
    uc = srcb
    xln = [sb(f"xln{t}", [128, CH], BF16) for t in range(NT)]
    u2c = xln
    tev = sb("tev", [128, CH], F32)
    tevb = sb("tevb", [128, CH], F32)
    ta = sb("ta", [64, CH], F32)
    x3sb = [sb(f"x3sb{k}", [128, CH], F32) for k in range(2)]
    xres = [sb(f"xres{k}", [128, CH], F32) for k in range(2)]
    s1row = sb("s1row", [1, CH], F32)
    s2row = sb("s2row", [1, CH], F32)
    mrow = sb("mrow", [1, CH], F32)
    mr2 = sb("mr2", [1, CH], F32)
    rrow = sb("rrow", [1, CH], F32)
    rb = sb("rb", [128, CH], F32)
    mrb = sb("mrb", [128, CH], F32)
    xsq2 = [sb(f"xsq2{k}", [128, CH], BF16) for k in range(2)]
    bkb = sb("bkb", [128, D], BF16)
    bvb = sb("bvb", [128, D], BF16)
    ones128 = sb("ones128", [128, 1], BF16)
    onesrow_f = sb("onesrow_f", [1, 128], F32)
    one1f = sb("one1f", [1, 1], F32)
    eps5 = sb("eps5", [1, 1], F32)
    eps6 = sb("eps6", [128, 1], F32)
    bq64 = sb("bq64", [64, 16], F32)
    bkrr = sb("bkrr", [1, D], F32)
    bvrr = sb("bvrr", [1, D], F32)
    bo_c = sb("bo_c", [128, 8], F32)
    b1_c = sb("b1_c", [128, 32], F32)
    b2_c = sb("b2_c", [128, 8], F32)
    g1_c = sb("g1_c", [128, 8], F32)
    be1_c = sb("be1_c", [128, 8], F32)
    g2_c = sb("g2_c", [128, 8], F32)
    be2_c = sb("be2_c", [128, 8], F32)
    sel16s = sb("sel16s", [1, 256], F32)
    selcs = sb("selcs", [16, 1024], F32)

    pk = [psa(f"pk{k}", [128, CH]) for k in range(2)]
    pkv = [psa(f"pkv{k}", [128, CH]) for k in range(2)]
    pks = psa("pks", [128, CH])
    pm1 = psa("pm1", [128, CH])
    pm2 = psa("pm2", [128, CH])
    pm3 = psa("pm3", [128, CH])

    lw = Sem(nc, stack, "lw")
    lb1 = Sem(nc, stack, "lb1")
    lx = Sem(nc, stack, "lx")
    lfp = [Sem(nc, stack, f"lf{k}") for k in range(4)]
    lfd = [Sem(nc, stack, f"lfd{k}") for k in range(4)]
    w1cw = Sem(nc, stack, "w1cw")
    w2cw = Sem(nc, stack, "w2cw")
    lr = Sem(nc, stack, "lr")
    so = Sem(nc, stack, "so")
    sop = [Sem(nc, stack, "soA"), Sem(nc, stack, "soB")]
    tS = Sem(nc, stack, "tS")
    vS = Sem(nc, stack, "vS")
    sS = Sem(nc, stack, "sS")
    t_pk = Sem(nc, stack, "t_pk")
    t_pkd = Sem(nc, stack, "t_pkd")
    v_rd = Sem(nc, stack, "v_rd")

    T, V, S, G, SY = nc.tensor, nc.vector, nc.scalar, nc.gpsimd, nc.sync
    Vw = make_serial(V, vS)
    Sw = make_serial(S, sS)
    Vn = make_serial(V, vS, auto_inc=False)
    Sn = make_serial(S, sS, auto_inc=False)
    if ctx is not None:
        for ws in _waits:
            G.wait_ge(ws.h, ws.n)
            SY.wait_ge(ws.h, ws.n)

    for t in range(NT):
        G.dma_start(out=wqb[t][:, :], in_=Wq[128 * t:128 * (t + 1), :]).then_inc(lw.h, 16)
        G.dma_start(out=wkb[t][:, :], in_=Wk[128 * t:128 * (t + 1), :]).then_inc(lw.h, 16)
        G.dma_start(out=wvb[t][:, :], in_=Wv[128 * t:128 * (t + 1), :]).then_inc(lw.h, 16)
        G.dma_start(out=wob[t][:, :], in_=Wo[128 * t:128 * (t + 1), :]).then_inc(lw.h, 16)
    G.dma_start(out=stkb[:, :], in_=stk[:, :]).then_inc(lw.h, 16)
    lw.n += 33 * 16
    for srcap, dst in ((bqc, bq64), (bkr, bkrr), (bvr, bvrr), (boc, bo_c), (b1c, b1_c),
                       (b2c, b2_c), (g1c, g1_c), (be1c, be1_c), (g2c, g2_c), (be2c, be2_c),
                       (sel16x, sel16s), (selcol, selcs)):
        SY.dma_start(out=dst[:, :], in_=srcap[:, :]).then_inc(lb1.h, 16)
    lb1.n += 12 * 16
    Vw.memset(ones128[:, :], 1.0)
    Vw.memset(onesrow_f[:, :], 1.0)
    Vw.memset(one1f[:, :], 1.0)
    Vw.memset(eps5[:, :], 1e-5)
    Vw.memset(KVacc[:, :], 0.0)
    vS.inc(Vn.memset(eps6[:, :], 1e-6))
    vS.wait(T)
    vS.wait(S)
    lw.wait(T)
    lb1.wait(T)
    lb1.wait(S)
    lb1.wait(V)
    T.matmul(pm1[:, :], onesrow_f[:, :], bkrr[0:1, 0:512], start=True, stop=True)
    T.matmul(pm2[:, :], onesrow_f[:, :], bkrr[0:1, 512:1024], start=True, stop=True)
    T.matmul(pm3[:, :], onesrow_f[:, :], bvrr[0:1, 0:512], start=True, stop=True)
    T.matmul(pk[0][:, :], onesrow_f[:, :], bvrr[0:1, 512:1024], start=True, stop=True)
    tS.inc(T.drain(fusable=False))
    tS.wait(V)
    Vw.tensor_copy(out=bkb[:, 0:512], in_=pm1[:, :])
    Vw.tensor_copy(out=bkb[:, 512:1024], in_=pm2[:, :])
    Vw.tensor_copy(out=bvb[:, 0:512], in_=pm3[:, :])
    vS.inc(Vn.tensor_copy(out=bvb[:, 512:1024], in_=pk[0][:, :]))
    vS.wait(T)

    def load_chunk(c, gate):
        G.wait_ge(tS.h, gate)
        for t in range(NT):
            G.dma_start(out=xbc[t][:, :],
                        in_=xT[128 * t:128 * (t + 1), c * CH:(c + 1) * CH]).then_inc(lx.h, 16)
        lx.n += NT * 16
        return lx.n

    def stats_rows(tiles, into_s1b=None):
        sq0 = vS.n
        for t in range(NT):
            if t >= 2:
                V.wait_ge(tS.h, tS.n)
            vS.inc(Vn.tensor_mul(out=xsq2[t % 2][:, :], in0=tiles[t][:, :], in1=tiles[t][:, :]))
            V_cnt = vS.n
            T.wait_ge(vS.h, V_cnt)
            T.matmul(pm1[0:1, :], ones128[:, :], tiles[t][:, :],
                     start=(t == 0), stop=(t == NT - 1))
            tS.inc(T.matmul(pm2[0:1, :], ones128[:, :], xsq2[t % 2][:, :],
                            start=(t == 0), stop=(t == NT - 1)))
        tS.inc(T.drain(fusable=False))
        tS.wait(V)
        Vw.tensor_copy(out=s1row[:, :], in_=pm1[0:1, :])
        Vw.tensor_copy(out=s2row[:, :], in_=pm2[0:1, :])
        Vw.tensor_scalar(out=mrow[:, :], in0=s1row[:, :], scalar1=1.0 / D,
                        scalar2=None, op0=ALU.mult)
        Vw.tensor_mul(out=mr2[:, :], in0=mrow[:, :], in1=mrow[:, :])
        vS.inc(Vn.scalar_tensor_tensor(out=s1row[:, :], in0=s2row[:, :], scalar=1.0 / D,
                                      in1=mr2[:, :], op0=ALU.mult, op1=ALU.subtract))
        vS.wait(S)
        Sw.activation(out=s2row[:, :], in_=s1row[:, :], func=AF.Ln, bias=eps5[0:1, 0:1], scale=1.0)
        sS.inc(Sn.activation(out=rrow[0:1, :], in_=s2row[:, :], func=AF.Exp,
                            bias=0.0, scale=-0.5))
        sS.wait(V)
        vS.inc(Vn.tensor_mul(out=mr2[:, :], in0=mrow[:, :], in1=rrow[:, :]))
        vS.wait(T)
        sS.wait(T)
        T.matmul(pm1[:, :], onesrow_f[:, :], rrow[0:1, :], start=True, stop=True)
        T.matmul(pm2[:, :], onesrow_f[:, :], mr2[0:1, :], start=True, stop=True)
        tS.inc(T.drain(fusable=False))
        tS.wait(V)
        Vw.tensor_copy(out=rb[:, :], in_=pm1[:, :])
        vS.inc(Vn.tensor_copy(out=mrb[:, :], in_=pm2[:, :]))
        vS.wait(T)

    def ln_apply(src_tiles, out_tiles, gcol, bcol):
        for t in range(NT):
            V.wait_ge(sS.h, sS.n)
            Vw.tensor_tensor(out=tev[:, :], in0=src_tiles[t][:, :], in1=rb[:, :], op=ALU.mult)
            vS.inc(Vn.tensor_sub(out=tevb[:, :], in0=tev[:, :], in1=mrb[:, :]))
            vS.wait(S)
            sS.inc(Sn.activation(out=out_tiles[t][:, :], in_=tevb[:, :], func=AF.Identity,
                                scale=gcol[:, t:t + 1] if gcol is not None else 1.0,
                                bias=bcol[:, t:t + 1] if bcol is not None else 0.0))
        sS.wait(V)
        sS.wait(T)

    KV_first = True
    adds_done = 0
    w1_last = [0, 0, 0, 0]
    w2_last = [0, 0, 0, 0]
    gate = tS.n
    for c in range(NCH):
        loaded = load_chunk(c, gate)
        V.wait_ge(lx.h, loaded)
        T.wait_ge(lx.h, loaded)
        stats_rows(xbc)
        ln_apply(xbc, srcb, None, None)

        for j in range(NSC if stage >= 2 else 0):
            o0 = j * 128
            for half in range(2):
                for t in range(NT):
                    T.matmul(pk[half][:, :], srcb[t][:, o0:o0 + 128],
                             wkb[t][:, 512 * half:512 * (half + 1)],
                             start=(t == 0), stop=(t == NT - 1))
            tS.inc(T.drain(fusable=False))
            tS.wait(V)
            Vw.tensor_copy(out=kf[:, 0:512], in_=pk[0][:, :])
            vS.inc(Vn.tensor_copy(out=kf[:, 512:1024], in_=pk[1][:, :]))
            vS.wait(T)
            if stage < 2.5:
                continue
            for half in range(2):
                for t in range(NT):
                    T.matmul(pk[half][:, :], srcb[t][:, o0:o0 + 128],
                             wvb[t][:, 512 * half:512 * (half + 1)],
                             start=(t == 0), stop=(t == NT - 1))
            tS.inc(T.drain(fusable=False))
            if stage < 2.7:
                continue
            Vw.tensor_add(out=kf[:, :], in0=kf[:, :], in1=bkb[:, :])
            Vw.tensor_scalar_min(out=ebk[:, :], in0=kf[:, :], scalar1=0.0)
            vS.inc(Vn.nop())
            vS.wait(S)
            sS.inc(Sn.activation(out=ebk[:, :], in_=ebk[:, :], func=AF.Exp))
            Vw.tensor_scalar_max(out=kf[:, :], in0=kf[:, :], scalar1=0.0)
            sS.wait(V)
            vS.inc(Vn.tensor_add(out=Ktok[:, :], in0=kf[:, :], in1=ebk[:, :]))
            tS.wait(V)
            Vw.tensor_copy(out=kf[:, 0:512], in_=pk[0][:, :])
            Vw.tensor_copy(out=kf[:, 512:1024], in_=pk[1][:, :])
            vS.inc(Vn.tensor_add(out=Vtok[:, :], in0=kf[:, :], in1=bvb[:, :]))
            vS.wait(T)
            if stage < 2.9:
                continue
            KV_last = (c == NCH - 1 and j == NSC - 1)
            for i in range(16):
                bank = pkv[i // 8]
                T.matmul(bank[0:64, 64 * (i % 8):64 * (i % 8) + 64],
                         Ktok[:, 64 * i:64 * (i + 1)], Vtok[:, 64 * i:64 * (i + 1)],
                         start=True, stop=True)
            T.matmul(pks[0:1, :], ones128[:, :], Ktok[:, 0:512],
                     start=KV_first, stop=KV_last)
            T.matmul(pm3[0:1, :], ones128[:, :], Ktok[:, 512:1024],
                     start=KV_first, stop=KV_last)
            KV_first = False
            tS.inc(T.drain(fusable=False))
            tS.wait(V)
            for i in range(16):
                bank = pkv[i // 8]
                Vw.tensor_add(out=KVacc[:, 64 * i:64 * (i + 1)],
                              in0=KVacc[:, 64 * i:64 * (i + 1)],
                              in1=bank[0:64, 64 * (i % 8):64 * (i % 8) + 64])
            vS.wait(T)
        gate = tS.n
    if stage < 3:
        SY.wait_ge(so.h, so.n)
        stack.close()
        return nc
    tS.wait(V)
    Vw.tensor_copy(out=KVb[:, :], in_=KVacc[:, :])
    Vw.tensor_copy(out=Ksumr[0:1, 0:512], in_=pks[0:1, :])
    vS.inc(Vn.tensor_copy(out=Ksumr[0:1, 512:1024], in_=pm3[0:1, :]))
    vS.wait(T)
    for i in range(16):
        T.matmul(pm1[0:64, i:i + 1], Ksumr[0:1, 64 * i:64 * (i + 1)], one1f[:, :],
                 start=True, stop=True)
    tS.inc(T.drain(fusable=False))
    tS.wait(S)
    sS.inc(Sn.activation(out=KsumColb[:, :], in_=pm1[0:64, 0:16], func=AF.Identity, scale=1.0))
    sS.wait(T)
    sS.wait(V)

    gate2 = tS.n
    for c in range(NCH):
        loaded = load_chunk(c, gate2)
        V.wait_ge(lx.h, loaded)
        T.wait_ge(lx.h, loaded)
        stats_rows(xbc)
        ln_apply(xbc, srcb, None, None)

        for i in range(16 if stage >= 4 else 0):
            H = c * 16 + i
            T.wait_ge(vS.h, vS.n)
            for t in range(NT):
                T.matmul(pm1[0:64, :], wqb[t][:, 64 * i:64 * (i + 1)], srcb[t][:, :],
                         start=(t == 0), stop=(t == NT - 1))
            tS.inc(T.drain(fusable=False))
            tS.wait(S)
            if H >= 2:
                S.wait_ge(t_pk.h, H - 1)
            Sw.activation(out=tev[0:64, :], in_=pm1[0:64, :], func=AF.Identity,
                         bias=bq64[:, i:i + 1], scale=1.0)
            sS.inc(Sn.nop())
            sS.wait(V)
            Vw.tensor_scalar_min(out=ta[:, :], in0=tev[0:64, :], scalar1=0.0)
            vS.inc(Vn.nop())
            vS.wait(S)
            sS.inc(Sn.activation(out=ta[:, :], in_=ta[:, :], func=AF.Exp))
            Vw.tensor_scalar_max(out=tev[0:64, :], in0=tev[0:64, :], scalar1=0.0)
            sS.wait(V)
            vS.inc(Vn.tensor_add(out=qe[H % 2][:, :], in0=tev[0:64, :], in1=ta[:, :]))
            vS.wait(T)
            dslot = H % 2
            T.matmul(pm2[0:1, :], KsumColb[:, i:i + 1], qe[H % 2][:, :],
                     start=True, stop=True)
            T.matmul(pm3[0:64, :], KVb[:, 64 * i:64 * (i + 1)], qe[H % 2][:, :],
                     start=True, stop=True)
            t_pk.inc(T.drain(fusable=False))
            V.wait_ge(t_pk.h, t_pk.n)
            Vw.tensor_copy(out=den2[0:1, dslot * CH:(dslot + 1) * CH], in_=pm2[0:1, :])
            vS.inc(Vn.tensor_copy(out=Obuf[:, i * CH:(i + 1) * CH], in_=pm3[0:64, :]))
            vS.wait(T)
            if i == 0:
                T.wait_ge(v_rd.h, c)
            T.matmul(pk[0][0:16, :], sel16s[0:1, 16 * i:16 * (i + 1)],
                     den2[0:1, dslot * CH:(dslot + 1) * CH],
                     start=(i == 0), stop=(i == 15))
            if i == 15:
                t_pkd.inc(T.drain(fusable=False))
        if stage < 5:
            gate2 = tS.n
            continue
        S.wait_ge(t_pkd.h, c + 1)
        sS.inc(Sn.activation(out=denbf[0:16, :], in_=pk[0][0:16, :], func=AF.Identity,
                            bias=eps6[0:16, 0:1], scale=1.0))
        sS.wait(V)
        v_rd.inc(Vn.reciprocal(out=rd16[:, :], in_=denbf[0:16, :]))
        for i in range(16):
            T.wait_ge(v_rd.h, c + 1)
            T.wait_ge(vS.h, vS.n)
            T.matmul(pm1[0:64, :], selcs[:, 64 * i:64 * (i + 1)], rd16[:, :],
                     start=True, stop=True)
            tS.inc(T.drain(fusable=False))
            tS.wait(V)
            Vw.tensor_copy(out=denbf[:, :], in_=pm1[0:64, :])
            vS.inc(Vn.tensor_tensor(out=Obuf[:, i * CH:(i + 1) * CH],
                                   in0=Obuf[:, i * CH:(i + 1) * CH],
                                   in1=denbf[:, :], op=ALU.mult))
        for pr in range(8):
            T.wait_ge(vS.h, vS.n)
            T.matmul(pm1[:, :], stkb[:, 0:128], Obuf[:, (2 * pr) * CH:(2 * pr + 1) * CH],
                     start=True, stop=False)
            T.matmul(pm1[:, :], stkb[:, 128:256],
                     Obuf[:, (2 * pr + 1) * CH:(2 * pr + 2) * CH],
                     start=False, stop=True)
            tS.inc(T.drain(fusable=False))
            tS.wait(V)
            vS.inc(Vn.tensor_copy(out=Opair[:, pr * CH:(pr + 1) * CH], in_=pm1[:, :]))
        for dt in range(NT):
            T.wait_ge(vS.h, vS.n)
            for pr in range(8):
                T.matmul(pm2[:, :], wob[pr][:, 128 * dt:128 * (dt + 1)],
                         Opair[:, pr * CH:(pr + 1) * CH],
                         start=(pr == 0), stop=(pr == 7))
            tS.inc(T.drain(fusable=False))
            tS.wait(S)
            sS.inc(Sn.activation(out=tev[:, :], in_=pm2[:, :], func=AF.Identity,
                                bias=bo_c[:, dt:dt + 1], scale=1.0))
            sS.wait(V)
            vS.inc(Vn.tensor_add(out=uc[dt][:, :], in0=tev[:, :], in1=srcb[dt][:, :]))
        vS.wait(T)
        vS.wait(S)

        if stage < 6:
            gate2 = tS.n
            continue
        stats_rows(uc)
        ln_apply(uc, xln, g1_c, be1_c)

        hbanks = [pm1, pk[0]]
        gelu_done = []
        mm_done = []
        for ft in range(32):
            slot = ft % 4
            G.wait_ge(tS.h, mm_done[ft - 4] if ft >= 4 else w1_last[slot])
            if c == 0:
                if ft >= 4:
                    G.wait_ge(w1cw.h, 16 * (ft - 3))
                G.dma_start(out=w1b[slot].rearrange("p (t j) -> p t j", t=8),
                            in_=W1.rearrange("(t p) f -> p t f", p=128)[:, :, 128 * ft:128 * (ft + 1)]
                            ).then_inc(lfp[slot].h, 16)
            else:
                if c == 1 and ft == 0:
                    G.wait_ge(w1cw.h, 16 * 32)
                G.dma_start(out=w1b[slot][:, :],
                            in_=W1c[:, 1024 * ft:1024 * (ft + 1)]).then_inc(lfp[slot].h, 16)
            lfp[slot].n += 16
            if c == 0:
                S.wait_ge(lfp[slot].h, lfp[slot].n)
                S.dma_start(out=W1c[:, 1024 * ft:1024 * (ft + 1)],
                            in_=w1b[slot][:, :]).then_inc(w1cw.h, 16)
            T.wait_ge(lfp[slot].h, lfp[slot].n)
            if ft >= 2:
                T.wait_ge(sS.h, gelu_done[ft - 2])
            for t in range(NT):
                mm = T.matmul(hbanks[ft % 2][:, :], w1b[slot][:, 128 * t:128 * (t + 1)],
                              xln[t][:, :], start=(t == 0), stop=(t == NT - 1))
            tS.inc(mm)
            mm_done.append(tS.n)
            w1_last[slot] = tS.n
            if ft >= 1:
                S.wait_ge(tS.h, mm_done[ft])
                sS.inc(Sn.activation(out=hbuf[:, (ft - 1) * CH:ft * CH],
                                     in_=hbanks[(ft - 1) % 2][:, :],
                                     func=AF.Gelu, bias=b1_c[:, ft - 1:ft], scale=1.0))
                gelu_done.append(sS.n)
        tS.inc(T.drain(fusable=False))
        S.wait_ge(tS.h, tS.n)
        sS.inc(Sn.activation(out=hbuf[:, 31 * CH:32 * CH], in_=hbanks[31 % 2][:, :],
                             func=AF.Gelu, bias=b1_c[:, 31:32], scale=1.0))
        gelu_done.append(sS.n)
        sS.wait(T)
        ybanks = [pm2, pm3, pk[0], pk[1]]
        if stage < 7:
            gate2 = tS.n
            continue
        for half in range(2):
            dn_done = []
            for ft in range(32):
                slot = ft % 4
                G.wait_ge(tS.h, dn_done[ft - 4] if ft >= 4 else w2_last[slot])
                if c == 0:
                    nw = half * 32 + ft
                    if nw >= 4:
                        G.wait_ge(w2cw.h, 16 * (nw - 3))
                    G.dma_start(out=w2b[slot][:, :],
                                in_=W2[128 * ft:128 * (ft + 1),
                                       512 * half:512 * (half + 1)]).then_inc(lfd[slot].h, 16)
                else:
                    if c == 1 and half == 0 and ft == 0:
                        G.wait_ge(w2cw.h, 16 * 64)
                    G.dma_start(out=w2b[slot][:, :],
                                in_=W2c[128 * ft:128 * (ft + 1),
                                        512 * half:512 * (half + 1)]).then_inc(lfd[slot].h, 16)
                lfd[slot].n += 16
                if c == 0:
                    S.wait_ge(lfd[slot].h, lfd[slot].n)
                    S.dma_start(out=W2c[128 * ft:128 * (ft + 1),
                                        512 * half:512 * (half + 1)],
                                in_=w2b[slot][:, :]).then_inc(w2cw.h, 16)
                T.wait_ge(lfd[slot].h, lfd[slot].n)
                for dl in range(4):
                    mm = T.matmul(ybanks[dl][:, :], w2b[slot][:, 128 * dl:128 * (dl + 1)],
                                  hbuf[:, ft * CH:(ft + 1) * CH],
                                  start=(ft == 0), stop=(ft == 31))
                tS.inc(mm)
                dn_done.append(tS.n)
                w2_last[slot] = tS.n
            tS.inc(T.drain(fusable=False))
            tS.wait(S)
            for dl in range(4):
                dt = half * 4 + dl
                S.wait_ge(vS.h, vS.n)
                sS.inc(Sn.activation(out=tev[:, :], in_=ybanks[dl][:, :], func=AF.Identity,
                                    bias=b2_c[:, dt:dt + 1], scale=1.0))
                sS.wait(V)
                vS.inc(Vn.tensor_add(out=u2c[dt][:, :], in0=tev[:, :], in1=xln[dt][:, :]))
            vS.wait(T)
        vS.wait(S)

        if stage < 8:
            gate2 = tS.n
            continue
        stats_rows(u2c)
        for dt in range(NT):
            SY.wait_ge(vS.h, adds_done)
            SY.dma_start(out=xres[dt % 2][:, :],
                         in_=xT[128 * dt:128 * (dt + 1), c * CH:(c + 1) * CH]).then_inc(lr.h, 16)
            lr.n += 16
            V.wait_ge(sS.h, sS.n)
            Vw.tensor_tensor(out=tev[:, :], in0=u2c[dt][:, :], in1=rb[:, :], op=ALU.mult)
            vS.inc(Vn.tensor_sub(out=tevb[:, :], in0=tev[:, :], in1=mrb[:, :]))
            vS.wait(S)
            sS.inc(Sn.activation(out=tev[:, :], in_=tevb[:, :], func=AF.Identity,
                                scale=g2_c[:, dt:dt + 1], bias=be2_c[:, dt:dt + 1]))
            sS.wait(V)
            V.wait_ge(lr.h, lr.n)
            V.wait_ge(sop[dt % 2].h, sop[dt % 2].n)
            vS.inc(Vn.tensor_add(out=x3sb[dt % 2][:, :], in0=tev[:, :], in1=xres[dt % 2][:, :]))
            adds_done = vS.n
            vS.wait(SY)
            SY.dma_start(out=x3T[128 * dt:128 * (dt + 1), c * CH:(c + 1) * CH],
                         in_=x3sb[dt % 2][:, :]).then_inc(sop[dt % 2].h, 16)
            sop[dt % 2].n += 16
        vS.wait(T)
        gate2 = tS.n

    SY.wait_ge(sop[0].h, sop[0].n)
    SY.wait_ge(sop[1].h, sop[1].n)
    if ctx is None:
        stack.close()
        return nc
    return sop



def build_l1_ffn(N=N, NCH=NCH, dbg=False, ctx=None):
    if ctx is None:
        nc = bass.Bass("TRN2", target_bir_lowering=False)
        stack = contextlib.ExitStack()
        xT = nc.dram_tensor("xT", [D, N], F32, kind="ExternalInput").ap()
    else:
        nc, stack, xT, _out, _waits = ctx
    wupT = nc.dram_tensor("wupT", [D, 4096], F32, kind="ExternalInput").ap()
    wdnT = nc.dram_tensor("wdnT", [4096, D], F32, kind="ExternalInput").ap()
    if ctx is None:
        outT = nc.dram_tensor("outT", [D, N], F32, kind="ExternalOutput").ap()
    else:
        outT = _out
    wtup = nc.dram_tensor("wtup", [128, 32 * 1024], BF16, kind="Internal").ap()
    wtdn = nc.dram_tensor("wtdn", [4096, D], BF16, kind="Internal").ap()
    if dbg:
        d_wtu = nc.dram_tensor("d_wtu", [D, 4096], mybir.dt.bfloat16, kind="ExternalOutput").ap()
        d_xn = nc.dram_tensor("d_xn", [128, CH], mybir.dt.bfloat16, kind="ExternalOutput").ap()
        d_h = nc.dram_tensor("d_h", [128, 2 * CH], mybir.dt.bfloat16, kind="ExternalOutput").ap()

    _p = f"u{_uid()}_"
    sb = lambda name, shape, dt: stack.enter_context(nc.sbuf_tensor(_p + name, shape, dt))
    psa = lambda name, shape: stack.enter_context(nc.psum_tensor(_p + name, shape, F32))

    arena = sb("arena", [128, 32 * CH], BF16)
    hbuf = arena
    xbc = [arena[:, 512 * t:512 * (t + 1)] for t in range(NT)]
    xnb = [sb(f"xnb{t}", [128, CH], BF16) for t in range(NT)]
    wq1 = [sb(f"wq1{k}", [128, 1024], F32) for k in range(2)]
    sgn = sb("sgn", [128, 1024], F32)
    ab = sb("ab", [128, 1024], F32)
    wtb = [sb(f"wtb{k}", [128, 1024], BF16) for k in range(2)]
    wupb = [sb(f"wupb{k}", [128, 1024], BF16) for k in range(4)]
    wdnb = [sb(f"wdnb{k}", [128, 512], BF16) for k in range(4)]
    hsq2p = [sb(f"hsq2p{k}", [128, CH], BF16) for k in range(2)]
    acc = sb("acc", [128, 33], F32)
    thr1 = sb("thr1", [1, 1], F32)
    negthrb = sb("negthrb", [128, 1024], F32)
    ab2 = sb("ab2", [128, 1024], F32)
    onescol_f = sb("onescol_f", [128, 1], F32)
    s1row = sb("s1row", [1, CH], F32)
    s2row = sb("s2row", [1, CH], F32)
    mrow = sb("mrow", [1, CH], F32)
    mr2 = sb("mr2", [1, CH], F32)
    rhrow = sb("rhrow", [1, CH], F32)
    r2row = sb("r2row", [1, CH], F32)
    rb = sb("rb", [128, CH], F32)
    mrb = sb("mrb", [128, CH], F32)
    r2b = sb("r2b", [128, CH], F32)
    xsq2 = [sb(f"xsq2{k}", [128, CH], BF16) for k in range(2)]
    tev = sb("tev", [128, CH], F32)
    tevb = sb("tevb", [128, CH], F32)
    x4sb = [sb(f"x4sb{k}", [128, CH], F32) for k in range(2)]
    xres = [sb(f"xres{k}", [128, CH], F32) for k in range(2)]
    ones128 = sb("ones128", [128, 1], BF16)
    onesrow_f = sb("onesrow_f", [1, 128], F32)
    eps5 = sb("eps5", [1, 1], F32)

    pm1 = psa("pm1", [128, CH])
    pm2 = psa("pm2", [128, CH])
    ybanks = [psa(f"py{k}", [128, CH]) for k in range(4)]

    lw = Sem(nc, stack, "lw")
    lx = Sem(nc, stack, "lx")
    lfp = [Sem(nc, stack, f"lf{k}") for k in range(4)]
    lfd = [Sem(nc, stack, f"lfd{k}") for k in range(4)]
    lr = Sem(nc, stack, "lr")
    swp = [Sem(nc, stack, "swA"), Sem(nc, stack, "swB")]
    sop = [Sem(nc, stack, "soA"), Sem(nc, stack, "soB")]
    tS = Sem(nc, stack, "tS")
    vS = Sem(nc, stack, "vS")
    sS = Sem(nc, stack, "sS")

    T, V, S, G, SY = nc.tensor, nc.vector, nc.scalar, nc.gpsimd, nc.sync
    Vw = make_serial(V, vS)
    Sw = make_serial(S, sS)
    Vn = make_serial(V, vS, auto_inc=False)
    Sn = make_serial(S, sS, auto_inc=False)
    if ctx is not None:
        for ws in _waits:
            G.wait_ge(ws.h, ws.n)
            SY.wait_ge(ws.h, ws.n)

    Vw.memset(ones128[:, :], 1.0)
    Vw.memset(onesrow_f[:, :], 1.0)
    Vw.memset(onescol_f[:, :], 1.0)
    Vw.memset(eps5[:, :], 1e-5)
    vS.wait(T)
    vS.wait(S)

    def ternarize(wsrc, wdst, ntiles, width, permuted=False):
        nq = width // 1024
        pieces = []
        for t in range(ntiles):
            for q in range(nq):
                pieces.append((t, q))
        for pi, (t, q) in enumerate(pieces):
            slot = pi % 2
            SY.wait_ge(sS.h, sS.n)
            if pi >= 2:
                SY.wait_ge(vS.h, vS.n)
            SY.dma_start(out=wq1[slot][:, :],
                         in_=wsrc[128 * t:128 * (t + 1), 1024 * q:1024 * (q + 1)]
                         ).then_inc(lw.h, 16)
            lw.n += 16
            V.wait_ge(lw.h, lw.n)
            Vw.tensor_reduce(out=acc[:, pi % 32:pi % 32 + 1], in_=wq1[slot][:, :],
                             axis=AX.X, op=ALU.add, apply_absolute_value=True)
            if (pi + 1) % 32 == 0 or pi == len(pieces) - 1:
                k = (pi % 32) + 1
                if pi < 32:
                    Vw.tensor_reduce(out=acc[:, 32:33], in_=acc[:, 0:k], axis=AX.X,
                                     op=ALU.add)
                else:
                    Vw.tensor_reduce(out=acc[:, 0:1], in_=acc[:, 0:k], axis=AX.X,
                                     op=ALU.add)
                    Vw.tensor_add(out=acc[:, 32:33], in0=acc[:, 32:33], in1=acc[:, 0:1])
        vS.wait(T)
        T.matmul(pm1[0:1, 0:1], acc[:, 32:33], onescol_f[:, :], start=True, stop=True)
        tS.inc(T.drain(fusable=False))
        tS.wait(S)
        Sw.activation(out=thr1[:, :], in_=pm1[0:1, 0:1], func=AF.Identity,
                      scale=-0.5 / (ntiles * 128 * width))
        sS.wait(T)
        T.matmul(pm2[:, 0:1], onesrow_f[:, :], thr1[:, :], start=True, stop=True)
        tS.inc(T.drain(fusable=False))
        tS.wait(V)
        vS.inc(Vn.tensor_copy(out=negthrb[:, 0:1], in_=pm2[:, 0:1]))
        for rep in range(10):
            w0 = 1 << rep
            w1 = min(1024 - w0, w0)
            if w1 <= 0:
                break
            Vw.tensor_copy(out=negthrb[:, w0:w0 + w1], in_=negthrb[:, 0:w1])
        for pi, (t, q) in enumerate(pieces):
            slot = pi % 2
            SY.wait_ge(sS.h, sS.n)
            SY.dma_start(out=wq1[slot][:, :],
                         in_=wsrc[128 * t:128 * (t + 1), 1024 * q:1024 * (q + 1)]
                         ).then_inc(lw.h, 16)
            lw.n += 16
            S.wait_ge(lw.h, lw.n)
            S.wait_ge(vS.h, vS.n)
            Sw.activation(out=sgn[:, :], in_=wq1[slot][:, :], func=AF.Sign)
            Sw.activation(out=ab[:, :], in_=wq1[slot][:, :], func=AF.Abs)
            sS.wait(V)
            Vw.tensor_add(out=ab[:, :], in0=ab[:, :], in1=negthrb[:, :])
            vS.wait(S)
            Sw.activation(out=ab2[:, :], in_=ab[:, :], func=AF.Sign)
            sS.wait(V)
            Vw.tensor_scalar_max(out=ab2[:, :], in0=ab2[:, :], scalar1=0.0)
            V.wait_ge(swp[pi % 2].h, swp[pi % 2].n)
            vS.inc(Vn.tensor_mul(out=wtb[pi % 2][:, :], in0=ab2[:, :], in1=sgn[:, :]))
            vS.wait(G)
            if permuted:
                for fl in range(8):
                    o0 = ((8 * q + fl) * 8 + t) * 128
                    G.dma_start(out=wdst[:, o0:o0 + 128],
                                in_=wtb[pi % 2][:, 128 * fl:128 * (fl + 1)]
                                ).then_inc(swp[pi % 2].h, 16)
                    swp[pi % 2].n += 16
            else:
                G.dma_start(out=wdst[128 * t:128 * (t + 1), 1024 * q:1024 * (q + 1)],
                            in_=wtb[pi % 2][:, :]).then_inc(swp[pi % 2].h, 16)
                swp[pi % 2].n += 16

    ternarize(wupT, wtup, NT, 4096, permuted=True)
    ternarize(wdnT, wtdn, 32, 1024)
    G.wait_ge(swp[0].h, swp[0].n)
    G.wait_ge(swp[1].h, swp[1].n)

    t_lastuse = 0
    adds_done = 0
    wu_last = [0, 0, 0, 0]
    wd_last = [0, 0, 0, 0]
    for c in range(NCH):
        G.wait_ge(tS.h, t_lastuse)
        for t in range(NT):
            G.dma_start(out=xbc[t][:, :],
                        in_=xT[128 * t:128 * (t + 1), c * CH:(c + 1) * CH]).then_inc(lx.h, 16)
        lx.n += NT * 16
        V.wait_ge(lx.h, lx.n)
        T.wait_ge(lx.h, lx.n)
        for t in range(NT):
            if t >= 2:
                V.wait_ge(tS.h, tS.n)
            vS.inc(Vn.tensor_mul(out=xsq2[t % 2][:, :], in0=xbc[t][:, :], in1=xbc[t][:, :]))
            T.wait_ge(vS.h, vS.n)
            T.matmul(pm1[0:1, :], ones128[:, :], xbc[t][:, :],
                     start=(t == 0), stop=(t == NT - 1))
            tS.inc(T.matmul(pm2[0:1, :], ones128[:, :], xsq2[t % 2][:, :],
                            start=(t == 0), stop=(t == NT - 1)))
        tS.inc(T.drain(fusable=False))
        tS.wait(V)
        Vw.tensor_copy(out=s1row[:, :], in_=pm1[0:1, :])
        Vw.tensor_copy(out=s2row[:, :], in_=pm2[0:1, :])
        Vw.tensor_scalar(out=mrow[:, :], in0=s1row[:, :], scalar1=1.0 / D,
                         scalar2=None, op0=ALU.mult)
        Vw.tensor_mul(out=mr2[:, :], in0=mrow[:, :], in1=mrow[:, :])
        Vw.scalar_tensor_tensor(out=s1row[:, :], in0=s2row[:, :], scalar=1.0 / D,
                                in1=mr2[:, :], op0=ALU.mult, op1=ALU.subtract)
        vS.wait(S)
        Sw.activation(out=s2row[:, :], in_=s1row[:, :], func=AF.Ln, bias=eps5[0:1, 0:1], scale=1.0)
        Sw.activation(out=mr2[:, :], in_=s1row[:, :], func=AF.Ln, bias=0.0, scale=1.0)
        sS.wait(V)
        Vw.tensor_sub(out=mr2[:, :], in0=mr2[:, :], in1=s2row[:, :])
        vS.wait(S)
        Sw.activation(out=mr2[:, :], in_=mr2[:, :], func=AF.Exp, bias=0.0, scale=0.5)
        Sw.activation(out=s2row[:, :], in_=s2row[:, :], func=AF.Exp, bias=0.0, scale=0.5)
        sS.wait(V)
        Vw.tensor_scalar(out=mr2[:, :], in0=mr2[:, :], scalar1=1e-8, scalar2=None, op0=ALU.add)
        Vw.tensor_mul(out=mr2[:, :], in0=mr2[:, :], in1=s2row[:, :])
        Vw.reciprocal(out=rhrow[:, :], in_=mr2[:, :])
        Vw.tensor_mul(out=mr2[:, :], in0=mrow[:, :], in1=rhrow[:, :])
        vS.wait(T)
        T.matmul(pm1[:, :], onesrow_f[:, :], rhrow[0:1, :], start=True, stop=True)
        T.matmul(pm2[:, :], onesrow_f[:, :], mr2[0:1, :], start=True, stop=True)
        tS.inc(T.drain(fusable=False))
        tS.wait(V)
        Vw.tensor_copy(out=rb[:, :], in_=pm1[:, :])
        Vw.tensor_copy(out=mrb[:, :], in_=pm2[:, :])
        for t in range(NT):
            Vw.tensor_tensor(out=tev[:, :], in0=xbc[t][:, :], in1=rb[:, :], op=ALU.mult)
            Vw.tensor_sub(out=xnb[t][:, :], in0=tev[:, :], in1=mrb[:, :])
        vS.wait(T)
        vS.wait(S)

        ubanks = [pm1, ybanks[0]]
        gelu_done = []
        mm_done = []
        hsq_mm_done = [0] * 33
        sq_done = []

        def _l1f_gelu(ftc):
            slot2 = ftc % 2
            if ftc >= 2:
                S.wait_ge(vS.h, sq_done[ftc - 2])
            sS.inc(Sn.activation(out=hbuf[:, ftc * CH:(ftc + 1) * CH],
                                 in_=ubanks[slot2][:, :],
                                 func=AF.Gelu, bias=0.0, scale=1.0))
            gelu_done.append(sS.n)
            V.wait_ge(sS.h, gelu_done[ftc])
            if ftc >= 1:
                V.wait_ge(tS.h, hsq_mm_done[ftc - 1])
            vS.inc(Vn.tensor_mul(out=hsq2p[slot2][:, :], in0=hbuf[:, ftc * CH:(ftc + 1) * CH],
                                 in1=hbuf[:, ftc * CH:(ftc + 1) * CH]))
            sq_done.append(vS.n)
            T.wait_ge(vS.h, sq_done[ftc])
            tS.inc(T.matmul(pm2[0:1, :], ones128[:, :], hsq2p[slot2][:, :],
                            start=(ftc == 0), stop=(ftc == 31)))
            hsq_mm_done[ftc] = tS.n

        for ft in range(32):
            slot = ft % 4
            G.wait_ge(tS.h, mm_done[ft - 4] if ft >= 4 else wu_last[slot])
            G.dma_start(out=wupb[slot][:, :],
                        in_=wtup[:, 1024 * ft:1024 * (ft + 1)]).then_inc(lfp[slot].h, 16)
            lfp[slot].n += 16
            T.wait_ge(lfp[slot].h, lfp[slot].n)
            if ft >= 2:
                T.wait_ge(sS.h, gelu_done[ft - 2])
            for t in range(NT):
                mm = T.matmul(ubanks[ft % 2][:, :], wupb[slot][:, 128 * t:128 * (t + 1)],
                              xnb[t][:, :], start=(t == 0), stop=(t == NT - 1))
            tS.inc(mm)
            mm_done.append(tS.n)
            wu_last[slot] = tS.n
            if ft >= 1:
                S.wait_ge(tS.h, mm_done[ft])
                _l1f_gelu(ft - 1)
        tS.inc(T.drain(fusable=False))
        S.wait_ge(tS.h, tS.n)
        _l1f_gelu(31)
        tS.inc(T.drain(fusable=False))
        tS.wait(S)
        Sw.activation(out=s2row[:, :], in_=pm2[0:1, :], func=AF.Ln, bias=0.0, scale=1.0 / 4096)
        Sw.activation(out=s2row[:, :], in_=s2row[:, :], func=AF.Exp, bias=0.0, scale=0.5)
        sS.wait(V)
        Vw.tensor_scalar(out=s2row[:, :], in0=s2row[:, :], scalar1=1e-8, scalar2=None, op0=ALU.add)
        Vw.reciprocal(out=r2row[:, :], in_=s2row[:, :])
        vS.wait(T)
        T.matmul(pm1[:, :], onesrow_f[:, :], r2row[0:1, :], start=True, stop=True)
        tS.inc(T.drain(fusable=False))
        tS.wait(V)
        Vw.tensor_copy(out=r2b[:, :], in_=pm1[:, :])
        vS.wait(T)

        for half in range(2):
            dn_done = []
            for ft in range(32):
                slot = ft % 4
                G.wait_ge(tS.h, dn_done[ft - 4] if ft >= 4 else wd_last[slot])
                G.dma_start(out=wdnb[slot][:, :],
                            in_=wtdn[128 * ft:128 * (ft + 1),
                                     512 * half:512 * (half + 1)]).then_inc(lfd[slot].h, 16)
                lfd[slot].n += 16
                T.wait_ge(lfd[slot].h, lfd[slot].n)
                for dl in range(4):
                    mm = T.matmul(ybanks[dl][:, :], wdnb[slot][:, 128 * dl:128 * (dl + 1)],
                                  hbuf[:, ft * CH:(ft + 1) * CH],
                                  start=(ft == 0), stop=(ft == 31))
                tS.inc(mm)
                dn_done.append(tS.n)
                wd_last[slot] = tS.n
            tS.inc(T.drain(fusable=False))
            tS.wait(V)
            for dl in range(4):
                dt = half * 4 + dl
                SY.wait_ge(vS.h, adds_done)
                SY.dma_start(out=xres[dt % 2][:, :],
                             in_=xT[128 * dt:128 * (dt + 1),
                                    c * CH:(c + 1) * CH]).then_inc(lr.h, 16)
                lr.n += 16
                Vw.tensor_tensor(out=tev[:, :], in0=ybanks[dl][:, :], in1=r2b[:, :],
                                 op=ALU.mult)
                V.wait_ge(lr.h, lr.n)
                V.wait_ge(sop[dt % 2].h, sop[dt % 2].n)
                vS.inc(Vn.tensor_add(out=x4sb[dt % 2][:, :], in0=tev[:, :],
                                     in1=xres[dt % 2][:, :]))
                adds_done = vS.n
                vS.wait(SY)
                SY.dma_start(out=outT[128 * dt:128 * (dt + 1), c * CH:(c + 1) * CH],
                             in_=x4sb[dt % 2][:, :]).then_inc(sop[dt % 2].h, 16)
                sop[dt % 2].n += 16
            vS.wait(T)
        t_lastuse = tS.n

    if dbg:
        SY.wait_ge(vS.h, vS.n)
        SY.dma_start(out=d_wtu[:, :], in_=wtup[:, :]).then_inc(sop[0].h, 16)
        sop[0].n += 16
        G.wait_ge(vS.h, vS.n)
        G.dma_start(out=d_xn[:, :], in_=xnb[0][:, :]).then_inc(sop[0].h, 16)
        sop[0].n += 16
        G.dma_start(out=d_h[:, :], in_=hbuf[:, 0:2 * CH]).then_inc(sop[0].h, 16)
        sop[0].n += 16
    SY.wait_ge(sop[0].h, sop[0].n)
    SY.wait_ge(sop[1].h, sop[1].n)
    if ctx is None:
        stack.close()
        return nc
    return sop


def build_fused():
    nc = bass.Bass("TRN2", target_bir_lowering=False)
    xT = nc.dram_tensor("xT", [D, N], F32, kind="ExternalInput").ap()
    x1T = nc.dram_tensor("x1s", [D, N], F32, kind="Internal").ap()
    x2T = nc.dram_tensor("x2s", [D, N], F32, kind="Internal").ap()
    x3T = nc.dram_tensor("x3s", [D, N], F32, kind="Internal").ap()
    outT = nc.dram_tensor("outT", [D, N], F32, kind="ExternalOutput").ap()
    waits = []
    with contextlib.ExitStack() as st1:
        waits = list(build_l0_attn(ctx=(nc, st1, xT, x1T, waits)))
    with contextlib.ExitStack() as st2:
        waits = list(build_l0_ffn(ctx=(nc, st2, x1T, x2T, waits)))
    with contextlib.ExitStack() as st3:
        waits = list(build_l1_attn(ctx=(nc, st3, x2T, x3T, waits)))
    with contextlib.ExitStack() as st4:
        waits = list(build_l1_ffn(ctx=(nc, st4, x3T, outT, waits)))
    return nc



def _sel_consts():
    sel16x = np.zeros((1, 256), np.float32)
    for h in range(16):
        sel16x[0, 16 * h + h] = 1.0
    selcol = np.zeros((16, 1024), np.float32)
    for h in range(16):
        selcol[h, 64 * h:64 * (h + 1)] = 1.0
    stk = np.zeros((64, 256), np.float32)
    for k in range(64):
        stk[k, k] = 1.0
        stk[k, 128 + 64 + k] = 1.0
    return sel16x, selcol, stk


def l0a_inputs(x_b, wq, bq, wk, bk, wv, bv, wo, bo):
    sel16x, selcol, stk = _sel_consts()
    return {
        "xT": np.ascontiguousarray(x_b.T),
        "wqT": np.ascontiguousarray(wq.T),
        "wkT": np.ascontiguousarray(wk.T),
        "wvT": np.ascontiguousarray(wv.T),
        "woT": np.ascontiguousarray(wo.T),
        "bq": np.ascontiguousarray(bq.reshape(16, 64).T),
        "bk": np.ascontiguousarray(bk.reshape(4, 64).T),
        "bv": np.ascontiguousarray(bv.reshape(1, 256)),
        "bo": np.ascontiguousarray(bo.reshape(8, 128).T),
        "sel16x": sel16x,
        "selcol": selcol,
        "stk": stk,
    }


def l0f_consts(G1_R, G2_R, G1_I, G2_I):
    m1bd = np.zeros((8 * 128, 128), np.float32)
    for RI, G2 in ((0, G2_R), (1, G2_I)):
        for c in range(4):
            B = np.asarray(G2)[c, :, 0, :]
            blk = np.zeros((128, 128), np.float32)
            for dd in range(8):
                blk[dd * 16:(dd + 1) * 16, dd * 16:(dd + 1) * 16] = B
            m1bd[(RI * 4 + c) * 128:(RI * 4 + c) * 128 + 128] = blk
    m2arr = np.zeros((32 * 128, 512), np.float32)
    for RI, G1 in ((0, G1_R), (1, G1_I)):
        for c in range(4):
            A = np.asarray(G1)[0, :, c, :]
            for tau in range(4):
                blk = np.zeros((128, 512), np.float32)
                for dl in range(8):
                    for e in range(16):
                        kk = dl * 16 + e
                        blk[kk, e::16] = A[:, 8 * tau + dl]
                m2arr[(RI * 16 + c * 4 + tau) * 128:(RI * 16 + c * 4 + tau) * 128 + 128] = blk
    return m1bd, m2arr


def l0f_inputs(x1, G1_R, G2_R, G1_I, G2_I, bias_real, bias_imag):
    m1bd, m2arr = l0f_consts(G1_R, G2_R, G1_I, G2_I)
    brow = np.concatenate([np.asarray(bias_real).reshape(4, 128).T,
                           np.asarray(bias_imag).reshape(4, 128).T], axis=1)
    return {
        "xT": np.ascontiguousarray(x1.T),
        "m1bd": m1bd,
        "m2arr": m2arr,
        "brow": np.ascontiguousarray(brow.astype(np.float32)),
    }


def l1a_inputs(x2, Wq, bq, Wk, bk, Wv, bv, Wo, bo, W1, b1, W2, b2, g1, be1, g2, be2):
    sel16x, selcol, stk = _sel_consts()
    col8 = lambda a: np.ascontiguousarray(np.asarray(a).reshape(8, 128).T.astype(np.float32))
    return {
        "xT": np.ascontiguousarray(x2.T),
        "Wq": np.ascontiguousarray(Wq), "Wk": np.ascontiguousarray(Wk),
        "Wv": np.ascontiguousarray(Wv), "Wo": np.ascontiguousarray(Wo),
        "W1": np.ascontiguousarray(W1), "W2": np.ascontiguousarray(W2),
        "bqc": np.ascontiguousarray(np.asarray(bq).reshape(16, 64).T.astype(np.float32)),
        "bkr": np.ascontiguousarray(np.asarray(bk).reshape(1, 1024)),
        "bvr": np.ascontiguousarray(np.asarray(bv).reshape(1, 1024)),
        "boc": col8(bo),
        "b1c": np.ascontiguousarray(np.asarray(b1).reshape(32, 128).T.astype(np.float32)),
        "b2c": col8(b2),
        "g1c": col8(g1), "be1c": col8(be1),
        "g2c": col8(g2), "be2c": col8(be2),
        "sel16x1": sel16x, "selcol1": selcol, "stk1": stk,
    }


def l1f_inputs(x3, w_up, w_down):
    return {
        "xT": np.ascontiguousarray(x3.T),
        "wupT": np.ascontiguousarray(np.asarray(w_up).T),
        "wdnT": np.ascontiguousarray(np.asarray(w_down).T),
    }


_NC_CACHE = {}


def _get_fused():
    if "fused" not in _NC_CACHE:
        _NC_CACHE["fused"] = build_fused()
    return _NC_CACHE["fused"]


def kernel(**inputs):
    ncores = 8
    x = np.asarray(inputs["L0_attn_x"])
    nc = _get_fused()
    zero = np.zeros((N, D), np.float32)
    base = {}
    base.update(l0a_inputs(zero,
                           inputs["L0_attn_wq"], inputs["L0_attn_bq"],
                           inputs["L0_attn_wk"], inputs["L0_attn_bk"],
                           inputs["L0_attn_wv"], inputs["L0_attn_bv"],
                           inputs["L0_attn_wo"], inputs["L0_attn_bo"]))
    base.update(l0f_inputs(zero,
                           inputs["L0_ffn_G1_R"], inputs["L0_ffn_G2_R"],
                           inputs["L0_ffn_G1_I"], inputs["L0_ffn_G2_I"],
                           inputs["L0_ffn_bias_real"], inputs["L0_ffn_bias_imag"]))
    base.update(l1a_inputs(zero,
                           inputs["L1_attn_Wq"], inputs["L1_attn_bq"],
                           inputs["L1_attn_Wk"], inputs["L1_attn_bk"],
                           inputs["L1_attn_Wv"], inputs["L1_attn_bv"],
                           inputs["L1_attn_Wo"], inputs["L1_attn_bo"],
                           inputs["L1_attn_W1"], inputs["L1_attn_b1"],
                           inputs["L1_attn_W2"], inputs["L1_attn_b2"],
                           inputs["L1_attn_g1"], inputs["L1_attn_be1"],
                           inputs["L1_attn_g2"], inputs["L1_attn_be2"]))
    base.update(l1f_inputs(zero,
                           inputs["L1_ffn_w_up"], inputs["L1_ffn_w_down"]))
    in_maps = []
    for b in range(ncores):
        m = dict(base)
        m["xT"] = np.ascontiguousarray(np.asarray(x[b]).T)
        in_maps.append(m)
    res = run_bass_kernel_spmd(nc, in_maps, core_ids=list(range(ncores)))
    out = np.stack([res.results[b]["outT"].T for b in range(ncores)], axis=0)
    return np.ascontiguousarray(out.astype(np.float32))


# version 13
# speedup vs baseline: 1.7669x; 1.0487x over previous
import contextlib
import numpy as np
import ml_dtypes
import concourse.bass as bass
import concourse.mybir as mybir
import concourse.bass_utils as _BU
from concourse.bass_utils import run_bass_kernel_spmd

_orig_run_command = _BU.run_command


def _run_command_ldwopt(cmd, *a, **kw):
    if isinstance(cmd, list) and any(str(c).startswith("--enable-ldw-opt") for c in cmd):
        cmd = ["--enable-ldw-opt=true" if str(c) == "--enable-ldw-opt=false" else c
               for c in cmd]
    return _orig_run_command(cmd, *a, **kw)


_BU.run_command = _run_command_ldwopt

F32 = mybir.dt.float32
BF16 = mybir.dt.bfloat16
AF = mybir.ActivationFunctionType
ALU = mybir.AluOpType
AX = mybir.AxisListType

N = 4096
D = 1024
NT = 8
CH = 512
NCH = N // CH
BF = ml_dtypes.bfloat16


_UID = [0]


def _uid():
    _UID[0] += 1
    return _UID[0]


class Sem:
    def __init__(self, nc, stack, name):
        self.h = stack.enter_context(nc.semaphore(f"{name}_u{_UID[0]}"))
        self.n = 0

    def inc(self, instr, k=1):
        instr.then_inc(self.h, k)
        self.n += k
        return self.n

    def wait(self, eng, val=None):
        eng.wait_ge(self.h, self.n if val is None else val)


class PBank:
    def __init__(self, ap):
        self.ap = ap
        self.fresh = True

    def start(self):
        s = self.fresh
        self.fresh = False
        return s

    def reset(self):
        self.fresh = True



def make_serial(eng, sem, auto_inc=True):
    class _W:
        def __getattr__(self, name):
            fn = getattr(eng, name)
            def call(*a, **kw):
                eng.wait_ge(sem.h, sem.n)
                ins = fn(*a, **kw)
                if auto_inc:
                    sem.inc(ins)
                return ins
            return call
    return _W()

class TernBG:

    def __init__(self, nc, stack):
        self.nc = nc
        self._p = f"tbg{_uid()}_"
        self.wupT = nc.dram_tensor("wupT", [D, 4096], F32, kind="ExternalInput").ap()
        self.wdnT = nc.dram_tensor("wdnT", [4096, D], F32, kind="ExternalInput").ap()
        self.wtup = nc.dram_tensor("wtup", [128, 32 * 1024], BF16, kind="Internal").ap()
        self.wtdn = nc.dram_tensor("wtdn", [4096, D], BF16, kind="Internal").ap()
        self.scr = [nc.dram_tensor(f"{self._p}scr{k}", [1, 128], F32, kind="Internal").ap()
                    for k in range(4)]
        self.lwT = Sem(nc, stack, "tbg_lw")
        self.vT = Sem(nc, stack, "tbg_v")
        self.sT = Sem(nc, stack, "tbg_s")
        self.fz = Sem(nc, stack, "tbg_fz")
        self.swp = [Sem(nc, stack, "tbg_swA"), Sem(nc, stack, "tbg_swB")]
        self._gen = None
        self.done = False

    def alloc(self, stack):
        nc = self.nc
        sb = lambda name, shape, dt: stack.enter_context(nc.sbuf_tensor(self._p + name, shape, dt))
        self.wq = sb("wq", [128, 1024], F32)
        self.ab = self.wq[:, 0:512]
        self.sgn = sb("sgn", [128, 512], BF16)
        self.m = sb("m", [128, 512], BF16)
        self.wt = sb("wt", [128, 512], BF16)
        self.acc = sb("acc", [128, 66], F32)
        self.row = sb("row", [1, 128], F32)
        self.thr = [sb(f"thr{k}", [128, 1], F32) for k in range(2)]
        self._gen = self._emit()

    def _src(self, pi):
        if pi < 32:
            t, q = pi // 4, pi % 4
            return self.wupT[128 * t:128 * (t + 1), 1024 * q:1024 * (q + 1)]
        p = pi - 32
        return self.wdnT[128 * p:128 * (p + 1), :]

    def _emit(self):
        nc = self.nc
        V, S, SY = nc.vector, nc.scalar, nc.sync
        lwT, vT, sT, fz = self.lwT, self.vT, self.sT, self.fz
        red_done = {}
        for k in range(65):
            if k >= 1:
                pi = k - 1
                V.wait_ge(lwT.h, 16 * k)
                ins = V.tensor_reduce(out=self.acc[:, pi:pi + 1], in_=self.wq[:, :],
                                      axis=AX.X, op=ALU.add, apply_absolute_value=True)
                vT.inc(ins)
                red_done[pi] = vT.n
            if k < 64:
                if k >= 1:
                    SY.wait_ge(vT.h, red_done[k - 1])
                SY.dma_start(out=self.wq[:, :], in_=self._src(k)).then_inc(lwT.h, 16)
                lwT.n += 16
            yield
        for wi, nel in ((0, float(D) * 4096), (1, 4096.0 * float(D))):
            V.wait_ge(vT.h, vT.n)
            vT.inc(V.tensor_reduce(out=self.acc[:, 64 + wi:65 + wi],
                                   in_=self.acc[:, 32 * wi:32 * (wi + 1)],
                                   axis=AX.X, op=ALU.add))
            SY.wait_ge(vT.h, vT.n)
            SY.dma_start(out=self.scr[wi].rearrange("a b -> b a"),
                         in_=self.acc[:, 64 + wi:65 + wi]).then_inc(fz.h, 16)
            fz.n += 16
            SY.wait_ge(fz.h, fz.n)
            SY.dma_start(out=self.row[:, :], in_=self.scr[wi][:, :]).then_inc(fz.h, 16)
            fz.n += 16
            yield
            V.wait_ge(fz.h, fz.n)
            vT.inc(V.tensor_reduce(out=self.acc[0:1, 64 + wi:65 + wi], in_=self.row[:, :],
                                   axis=AX.X, op=ALU.add))
            S.wait_ge(vT.h, vT.n)
            sT.inc(S.activation(out=self.row[0:1, 0:1], in_=self.acc[0:1, 64 + wi:65 + wi],
                                func=AF.Identity, scale=-0.5 / nel))
            V.wait_ge(sT.h, sT.n)
            for rep in range(7):
                w0 = 1 << rep
                V.wait_ge(vT.h, vT.n)
                vT.inc(V.tensor_copy(out=self.row[0:1, w0:2 * w0], in_=self.row[0:1, 0:w0]))
            SY.wait_ge(vT.h, vT.n)
            SY.dma_start(out=self.scr[2 + wi][:, :], in_=self.row[:, :]).then_inc(fz.h, 16)
            fz.n += 16
            SY.wait_ge(fz.h, fz.n)
            SY.dma_start(out=self.thr[wi][:, :],
                         in_=self.scr[2 + wi].rearrange("a b -> b a")).then_inc(fz.h, 16)
            fz.n += 16
            yield
        thr_gate = fz.n
        ab_done = {}
        for k in range(65):
            if k >= 1:
                pi = k - 1
                wslot = pi % 2
                S.wait_ge(lwT.h, 16 * (64 + k))
                if pi == 0:
                    S.wait_ge(fz.h, thr_gate)
                S.wait_ge(vT.h, vT.n)
                wi = 0 if pi < 32 else 1
                for hh in range(2):
                    if hh == 1:
                        S.wait_ge(vT.h, vT.n)
                    sT.inc(S.activation(out=self.sgn[:, :],
                                        in_=self.wq[:, 512 * hh:512 * (hh + 1)], func=AF.Sign))
                    sT.inc(S.activation(out=self.ab[:, :], in_=self.wq[:, 512 * hh:512 * (hh + 1)],
                                        func=AF.Abs))
                    S.wait_ge(sT.h, sT.n)
                    sT.inc(S.activation(out=self.m[:, :], in_=self.ab[:, :], func=AF.Sign,
                                        bias=self.thr[wi][:, 0:1], scale=1.0))
                    if hh == 1:
                        ab_done[pi] = sT.n
                    V.wait_ge(sT.h, sT.n)
                    vT.inc(V.tensor_scalar_max(out=self.m[:, :], in0=self.m[:, :], scalar1=0.0))
                    V.wait_ge(vT.h, vT.n)
                    V.wait_ge(self.swp[0].h, self.swp[0].n)
                    vT.inc(V.tensor_mul(out=self.wt[:, :], in0=self.m[:, :],
                                        in1=self.sgn[:, :]))
                    SY.wait_ge(vT.h, vT.n)
                    if pi < 32:
                        t, q = pi // 4, pi % 4
                        for fl in range(4 * hh, 4 * hh + 4):
                            o0 = ((8 * q + fl) * 8 + t) * 128
                            SY.dma_start(out=self.wtup[:, o0:o0 + 128],
                                         in_=self.wt[:, 128 * (fl - 4 * hh):128 * (fl - 4 * hh + 1)]
                                         ).then_inc(self.swp[0].h, 16)
                            self.swp[0].n += 16
                    else:
                        p = pi - 32
                        SY.dma_start(out=self.wtdn[128 * p:128 * (p + 1),
                                                   512 * hh:512 * (hh + 1)],
                                     in_=self.wt[:, :]).then_inc(self.swp[0].h, 16)
                        self.swp[0].n += 16
            if k < 64:
                if k >= 1:
                    SY.wait_ge(sT.h, ab_done[k - 1])
                SY.dma_start(out=self.wq[:, :], in_=self._src(k)).then_inc(lwT.h, 16)
                lwT.n += 16
            yield

    def step(self):
        if not self.done and self._gen is not None:
            try:
                next(self._gen)
            except StopIteration:
                self.done = True

    def finish(self):
        while not self.done:
            self.step()



def build_l0_attn(N=N, NCH=NCH, dbg=False, ctx=None, bg=None):
    bg_step = (lambda: None) if bg is None else bg.step
    if ctx is None:
        nc = bass.Bass("TRN2", target_bir_lowering=False)
        stack = contextlib.ExitStack()
        xT = nc.dram_tensor("xT", [D, N], F32, kind="ExternalInput").ap()
    else:
        nc, stack, xT, _out, _waits = ctx
    wqT = nc.dram_tensor("wqT", [D, D], F32, kind="ExternalInput").ap()
    wkT = nc.dram_tensor("wkT", [D, 256], F32, kind="ExternalInput").ap()
    wvT = nc.dram_tensor("wvT", [D, 256], F32, kind="ExternalInput").ap()
    woT = nc.dram_tensor("woT", [D, D], F32, kind="ExternalInput").ap()
    bq = nc.dram_tensor("bq", [64, 16], F32, kind="ExternalInput").ap()
    bk = nc.dram_tensor("bk", [64, 4], F32, kind="ExternalInput").ap()
    bv = nc.dram_tensor("bv", [1, 256], F32, kind="ExternalInput").ap()
    bo = nc.dram_tensor("bo", [128, 8], F32, kind="ExternalInput").ap()
    sel16x = nc.dram_tensor("sel16x", [1, 256], F32, kind="ExternalInput").ap()
    selcol = nc.dram_tensor("selcol", [16, 1024], F32, kind="ExternalInput").ap()
    stk = nc.dram_tensor("stk", [64, 256], F32, kind="ExternalInput").ap()
    if ctx is None:
        x1T = nc.dram_tensor("x1T", [D, N], F32, kind="ExternalOutput").ap()
    else:
        x1T = _out
    if dbg:
        NSd = N // 128
        d_r = nc.dram_tensor("d_r", [1, N], F32, kind="ExternalOutput").ap()
        d_kh = nc.dram_tensor("d_kh", [65, N], F32, kind="ExternalOutput").ap()
        d_va = nc.dram_tensor("d_va", [128, NSd * 260], F32, kind="ExternalOutput").ap()
        d_ob = nc.dram_tensor("d_ob", [64, 16 * CH], F32, kind="ExternalOutput").ap()
        d_rd = nc.dram_tensor("d_rd", [16, CH], F32, kind="ExternalOutput").ap()
        d_op = nc.dram_tensor("d_op", [128, 8 * CH], F32, kind="ExternalOutput").ap()
        d_km = nc.dram_tensor("d_km", [64, 4], F32, kind="ExternalOutput").ap()
        d_qp = nc.dram_tensor("d_qp", [65, CH], F32, kind="ExternalOutput").ap()

    _p = f"u{_uid()}_"
    sb = lambda name, shape, dt: stack.enter_context(nc.sbuf_tensor(_p + name, shape, dt))
    psa = lambda name, shape: stack.enter_context(nc.psum_tensor(_p + name, shape, F32))

    NS = N // 128
    NP_G = NS // 2

    xbc = [[sb(f"xbc{k}_{t}", [128, CH], BF16) for t in range(NT)] for k in range(2)]
    wqb = [sb(f"wqb{t}", [128, D], BF16) for t in range(NT)]
    wkb = [sb(f"wkb{t}", [128, 256], BF16) for t in range(NT)]
    wvb = [sb(f"wvb{t}", [128, 256], BF16) for t in range(NT)]
    wob = [sb(f"wob{t}", [128, D], BF16) for t in range(NT)]
    khead = [sb(f"khead{h}", [65, N], BF16) for h in range(4)]
    v_all = sb("v_all", [128, NS * 260], BF16)
    qplus = [sb(f"qplus{k}", [65, CH], BF16) for k in range(2)]
    qabs = [sb(f"qabs{k}", [64, CH], BF16) for k in range(2)]
    Ebuf = [sb(f"Ebuf{k}", [128, 1024], BF16) for k in range(2)]
    Obuf = sb("Obuf", [64, 16 * CH], BF16)
    Opair = sb("Opair", [128, 8 * CH], BF16)
    stkb = sb("stkb", [64, 256], BF16)
    den2 = sb("den2", [1, 2 * CH], F32)
    rd16 = sb("rd16", [16, CH], F32)
    denbp = [sb(f"denbp{k}", [64, CH], F32) for k in range(2)]
    kmax4 = sb("kmax4", [64, 4], BF16)
    s1row = sb("s1row", [1, CH], F32)
    s2row = sb("s2row", [1, CH], F32)
    mrow = sb("mrow", [1, CH], F32)
    rrows = sb("rrows", [1, N], F32)
    rb = sb("rb", [128, CH], F32)
    S1b = sb("S1b", [1, N], BF16)
    rcolf = sb("rcolf", [128, 1], F32)
    xsq2 = [sb(f"xsq2{k}", [128, CH], BF16) for k in range(2)]
    tev = sb("tev", [128, CH], F32)
    tevv = sb("tevv", [128, 256], F32)
    eps30 = sb("eps30", [128, 1], F32)
    tev2p = [sb(f"tev2p{k}", [128, CH], F32) for k in range(2)]
    x1sb = [sb(f"x1sb{k}", [128, CH], F32) for k in range(2)]
    xres = [sb(f"xres{k}", [128, CH], F32) for k in range(2)]
    wqcs = sb("wqcs", [1, D], BF16)
    wkcs = sb("wkcs", [1, 256], BF16)
    wvcs = sb("wvcs", [1, 256], BF16)
    bvb = sb("bvb", [128, 256], F32)
    ones128 = sb("ones128", [128, 1], BF16)
    onesrow_f = sb("onesrow_f", [1, 128], F32)
    one1f = sb("one1f", [1, 1], F32)
    eps5 = sb("eps5", [1, 1], F32)
    sel16s = sb("sel16s", [1, 256], F32)
    selcs = sb("selcs", [16, 1024], F32)
    bq64s = sb("bq64s", [64, 16], F32)
    bk64 = sb("bk64", [64, 4], F32)
    bvr = sb("bvr", [1, 256], F32)
    bo_c = sb("bo_c", [128, 8], F32)

    pst = [psa(f"pst{k}", [128, 1024]) for k in range(2)]
    pOt = psa("pO", [128, CH])
    pO = pOt
    pdent = psa("pden", [128, CH])
    pden = pdent
    pm1 = psa("pm1", [128, CH])
    pm2 = psa("pm2", [128, CH])

    lw = Sem(nc, stack, "lw")
    lb = Sem(nc, stack, "lb")
    lxp = [Sem(nc, stack, "lxA"), Sem(nc, stack, "lxB")]
    lrp = [Sem(nc, stack, "lrA"), Sem(nc, stack, "lrB")]
    so = Sem(nc, stack, "so")
    sop = [Sem(nc, stack, "soA"), Sem(nc, stack, "soB")]
    tS = Sem(nc, stack, "tS")
    vS = Sem(nc, stack, "vS")
    sS = Sem(nc, stack, "sS")
    t_st = Sem(nc, stack, "t_st")
    s_ex = Sem(nc, stack, "s_ex")
    t_pv = Sem(nc, stack, "t_pv")
    v_oe = Sem(nc, stack, "v_oe")
    t_pk = Sem(nc, stack, "t_pk")
    v_rd = Sem(nc, stack, "v_rd")
    v_nc = Sem(nc, stack, "v_nc")
    t_cm = Sem(nc, stack, "t_cm")
    v_sq = Sem(nc, stack, "v_sq")
    t_sq = Sem(nc, stack, "t_sq")
    t_std = Sem(nc, stack, "t_std")
    t_pvd = Sem(nc, stack, "t_pvd")
    t_pkd = Sem(nc, stack, "t_pkd")

    T, V, S, G, SY = nc.tensor, nc.vector, nc.scalar, nc.gpsimd, nc.sync
    Vw = make_serial(V, vS)
    Sw = make_serial(S, sS)
    Vn = make_serial(V, vS, auto_inc=False)
    Sn = make_serial(S, sS, auto_inc=False)
    if ctx is not None:
        for ws in _waits:
            G.wait_ge(ws.h, ws.n)
            SY.wait_ge(ws.h, ws.n)

    for t in range(NT):
        G.dma_start(out=wqb[t][:, :], in_=wqT[128 * t:128 * (t + 1), :]).then_inc(lw.h, 16)
        G.dma_start(out=wkb[t][:, :], in_=wkT[128 * t:128 * (t + 1), :]).then_inc(lw.h, 16)
        G.dma_start(out=wvb[t][:, :], in_=wvT[128 * t:128 * (t + 1), :]).then_inc(lw.h, 16)
        G.dma_start(out=wob[t][:, :], in_=woT[128 * t:128 * (t + 1), :]).then_inc(lw.h, 16)
    lw.n += 32 * 16
    for srcap, dst in ((bq, bq64s), (bk, bk64), (bv, bvr), (bo, bo_c),
                       (sel16x, sel16s), (selcol, selcs)):
        SY.dma_start(out=dst[:, :], in_=srcap[:, :]).then_inc(lb.h, 16)
    G.dma_start(out=stkb[:, :], in_=stk[:, :]).then_inc(lw.h, 16)
    lw.n += 16
    lb.n += 6 * 16

    Vw.memset(ones128[:, :], 1.0)
    Vw.memset(onesrow_f[:, :], 1.0)
    Vw.memset(one1f[:, :], 1.0)
    Vw.memset(eps5[:, :], 1e-5)
    Vw.memset(eps30[:, :], 1e-30)
    Vw.memset(v_all[:, :], 1.0)
    for h in range(3):
        Vw.memset(khead[h][64:65, :], 1.0)
    vS.inc(Vn.memset(khead[3][64:65, :], 1.0))
    vS.wait(T)
    vS.wait(S)
    lb.wait(V)
    lb.wait(S)
    vS.inc(Vn.tensor_scalar(out=bq64s[:, :], in0=bq64s[:, :], scalar1=0.125,
                           scalar2=None, op0=ALU.mult))

    lw.wait(T)
    for (wtiles, csb, wid) in ((wqb, wqcs, D), (wkb, wkcs, 256), (wvb, wvcs, 256)):
        for c0 in range(0, wid, CH):
            w = min(CH, wid - c0)
            for t in range(NT):
                T.matmul(pm1[0:1, 0:w], ones128[:, :], wtiles[t][:, c0:c0 + w],
                         start=(t == 0), stop=(t == NT - 1))
            tS.inc(T.drain(fusable=False))
            tS.wait(S)
            sS.inc(Sn.activation(out=csb[0:1, c0:c0 + w], in_=pm1[0:1, 0:w],
                                func=AF.Identity, scale=-1.0 / D))
            sS.wait(T)
    lb.wait(T)
    T.matmul(pm2[:, 0:256], onesrow_f[:, :], bvr[:, :], start=True, stop=True)
    tS.inc(T.drain(fusable=False))
    tS.wait(V)
    vS.inc(Vn.tensor_copy(out=bvb[:, :], in_=pm2[:, 0:256]))
    vS.wait(T)

    t_lastuse = [0, 0]

    def load_chunk(c):
        slot = c % 2
        G.wait_ge(tS.h, t_lastuse[slot])
        for t in range(NT):
            G.dma_start(out=xbc[slot][t][:, :],
                        in_=xT[128 * t:128 * (t + 1), c * CH:(c + 1) * CH]).then_inc(lxp[slot].h, 16)
        lxp[slot].n += NT * 16
        return (slot, lxp[slot].n)

    def stats(c, xc):
        cc = slice(c * CH, (c + 1) * CH)
        lsl, lval = loaded[c]
        V.wait_ge(lxp[lsl].h, lval)
        V.wait_ge(sS.h, sS.n)
        T.wait_ge(lxp[lsl].h, lval)
        sq0, tq0 = v_sq.n, t_sq.n
        for t in range(NT):
            if t >= 2:
                V.wait_ge(t_sq.h, tq0 + t - 1)
            v_sq.inc(Vn.tensor_mul(out=xsq2[t % 2][:, :], in0=xc[t][:, :], in1=xc[t][:, :]))
            T.wait_ge(v_sq.h, sq0 + t + 1)
            T.matmul(pm1[0:1, :], ones128[:, :], xc[t][:, :],
                     start=(t == 0), stop=(t == NT - 1))
            t_sq.inc(T.matmul(pm2[0:1, :], ones128[:, :], xsq2[t % 2][:, :],
                              start=(t == 0), stop=(t == NT - 1)))
        tS.inc(T.drain(fusable=False))
        tS.wait(V)
        Vw.tensor_copy(out=s1row[:, :], in_=pm1[0:1, :])
        Vw.tensor_copy(out=s2row[:, :], in_=pm2[0:1, :])
        Vw.tensor_copy(out=S1b[0:1, cc], in_=s1row[:, :])
        Vw.tensor_scalar(out=mrow[:, :], in0=s1row[:, :], scalar1=1.0 / D,
                        scalar2=None, op0=ALU.mult)
        Vw.tensor_mul(out=mrow[:, :], in0=mrow[:, :], in1=mrow[:, :])
        vS.inc(Vn.scalar_tensor_tensor(out=s1row[:, :], in0=s2row[:, :], scalar=1.0 / D,
                                      in1=mrow[:, :], op0=ALU.mult, op1=ALU.subtract))
        vS.wait(S)
        Sw.activation(out=s2row[:, :], in_=s1row[:, :], func=AF.Ln, bias=eps5[0:1, 0:1], scale=1.0)
        sS.inc(Sn.activation(out=rrows[0:1, cc], in_=s2row[:, :], func=AF.Exp,
                            bias=0.0, scale=-0.5))
        sS.wait(T)
        T.matmul(pm1[:, :], onesrow_f[:, :], rrows[0:1, cc], start=True, stop=True)
        tS.inc(T.drain(fusable=False))
        tS.wait(V)
        vS.inc(Vn.tensor_copy(out=rb[:, :], in_=pm1[:, :]))
        vS.wait(T)

    loaded = [load_chunk(0)]
    for c in range(NCH):
        slot = c % 2
        xc = xbc[slot]
        cc = slice(c * CH, (c + 1) * CH)
        stats(c, xc)

        for h in range(4):
            bg_step()
            for t in range(NT):
                T.matmul(pm1[0:64, :], wkb[t][:, 64 * h:64 * (h + 1)], xc[t][:, :],
                         start=(t == 0), stop=False)
            T.matmul(pm1[0:64, :], wkcs[0:1, 64 * h:64 * (h + 1)],
                     S1b[0:1, cc], start=False, stop=True)
            tS.inc(T.drain(fusable=False))
            tS.wait(V)
            V.wait_ge(sS.h, sS.n)
            vS.inc(Vn.tensor_tensor(out=tev[0:64, :], in0=pm1[0:64, :],
                                   in1=rb[0:64, :], op=ALU.mult))
            vS.wait(S)
            sS.inc(Sn.activation(out=khead[h][0:64, cc], in_=tev[0:64, :],
                                func=AF.Identity, bias=bk64[:, h:h + 1], scale=1.0))
            sS.wait(T)

        for j in range(4):
            s0 = c * CH + j * 128
            o0 = j * 128
            T.matmul(pm1[:, 0:1], rrows[0:1, s0:s0 + 128], one1f[:, :],
                     start=True, stop=True)
            for t in range(NT):
                T.matmul(pm2[:, 0:256], xc[t][:, o0:o0 + 128], wvb[t][:, :],
                         start=(t == 0), stop=False)
            T.matmul(pm2[:, 0:256], S1b[0:1, s0:s0 + 128], wvcs[0:1, :],
                     start=False, stop=True)
            tS.inc(T.drain(fusable=False))
            tS.wait(V)
            Vw.tensor_copy(out=rcolf[:, :], in_=pm1[:, 0:1])
            Vw.tensor_copy(out=tevv[:, :], in_=pm2[:, 0:256])
            st = c * 4 + j
            for kvh in range(4):
                Vw.scalar_tensor_tensor(
                    out=v_all[:, st * 260 + 65 * kvh: st * 260 + 65 * kvh + 64],
                    in0=tevv[:, 64 * kvh:64 * (kvh + 1)], scalar=rcolf[:, 0:1],
                    in1=bvb[:, 64 * kvh:64 * (kvh + 1)], op0=ALU.mult, op1=ALU.add)
            vS.wait(T)
        t_lastuse[slot] = tS.n
        if c + 1 < NCH:
            loaded.append(load_chunk(c + 1))

    V.wait_ge(sS.h, sS.n)
    for h in range(4):
        Vw.tensor_reduce(out=kmax4[:, h:h + 1], in_=khead[h][0:64, :], axis=AX.X,
                         op=ALU.max, apply_absolute_value=True)
    vS.wait(T)

    t_lastuse = [tS.n, tS.n]
    loaded = [load_chunk(0)]
    adds_done = 0
    qtt_done = {}

    def emit_qbuild(c, i):
        H = c * 16 + i
        cc = slice(c * CH, (c + 1) * CH)
        xc = xbc[c % 2]
        kv = i // 4
        qp = qplus[H % 2]
        qa = qabs[H % 2]
        lsl, lval = loaded[c]
        T.wait_ge(lxp[lsl].h, lval)
        if H >= 1:
            T.wait_ge(vS.h, qtt_done[H - 1])
        for t in range(NT):
            T.matmul(pm1[0:64, :], wqb[t][:, 64 * i:64 * (i + 1)], xc[t][:, :],
                     start=(t == 0), stop=False)
        T.matmul(pm1[0:64, :], wqcs[0:1, 64 * i:64 * (i + 1)],
                 S1b[0:1, cc], start=False, stop=True)
        tS.inc(T.drain(fusable=False))
        V.wait_ge(tS.h, tS.n)
        V.wait_ge(sS.h, sS.n)
        vS.inc(Vn.tensor_tensor(out=tev[0:64, :], in0=pm1[0:64, :],
                               in1=rb[0:64, :], op=ALU.mult))
        qtt_done[H] = vS.n
        S.wait_ge(vS.h, qtt_done[H])
        if H >= 2:
            S.wait_ge(t_pv.h, NP_G * (H - 1))
            S.wait_ge(t_cm.h, H - 1)
        Sw.activation(out=qp[0:64, :], in_=tev[0:64, :], func=AF.Identity,
                     bias=bq64s[:, i:i + 1], scale=0.125)
        sS.inc(Sn.activation(out=qa[0:64, :], in_=qp[0:64, :], func=AF.Abs))
        sS.wait(T)
        T.wait_ge(v_nc.h, v_nc.n)
        T.matmul(pm2[0:1, :], kmax4[:, kv:kv + 1], qa[0:64, :],
                 start=True, stop=True)
        t_cm.inc(T.drain(fusable=False))
        t_cm.wait(S)
        v_nc.inc(Sn.activation(out=qp[64:65, :], in_=pm2[0:1, :],
                              func=AF.Identity, scale=-1.0))

    qa_done = {}

    def emit_q_tmms(c, i):
        H = c * 16 + i
        cc = slice(c * CH, (c + 1) * CH)
        xc = xbc[c % 2]
        lsl, lval = loaded[c]
        T.wait_ge(lxp[lsl].h, lval)
        if H >= 1:
            T.wait_ge(vS.h, qtt_done[H - 1])
        qcov = None
        for t in range(NT):
            mm = T.matmul(pm1[0:64, :], wqb[t][:, 64 * i:64 * (i + 1)], xc[t][:, :],
                          start=(t == 0), stop=False)
            if t == 1:
                tS.inc(mm)
                qcov = tS.n
        T.matmul(pm1[0:64, :], wqcs[0:1, 64 * i:64 * (i + 1)],
                 S1b[0:1, cc], start=False, stop=True)
        V.wait_ge(t_st.h, t_st.n + 1)
        V.wait_ge(sS.h, sS.n)
        vS.inc(Vn.tensor_tensor(out=tev[0:64, :], in0=pm1[0:64, :],
                               in1=rb[0:64, :], op=ALU.mult))
        qtt_done[H] = vS.n
        return qcov

    def emit_q_sacts(c, i):
        H = c * 16 + i
        qp = qplus[H % 2]
        qa = qabs[H % 2]
        S.wait_ge(vS.h, qtt_done[H])
        if H >= 2:
            S.wait_ge(t_pv.h, NP_G * (H - 1))
            S.wait_ge(t_cm.h, H - 1)
        Sw.activation(out=qp[0:64, :], in_=tev[0:64, :], func=AF.Identity,
                     bias=bq64s[:, i:i + 1], scale=0.125)
        sS.inc(Sn.activation(out=qa[0:64, :], in_=qp[0:64, :], func=AF.Abs))
        qa_done[H] = sS.n

    def emit_q_cmm(c, i):
        H = c * 16 + i
        kv = i // 4
        qp = qplus[H % 2]
        qa = qabs[H % 2]
        T.wait_ge(sS.h, qa_done[H])
        T.wait_ge(v_nc.h, v_nc.n)
        t_cm.inc(T.matmul(pm2[0:1, :], kmax4[:, kv:kv + 1], qa[0:64, :],
                          start=True, stop=True))
        S.wait_ge(t_st.h, t_st.n + 1)
        v_nc.inc(Sn.activation(out=qp[64:65, :], in_=pm2[0:1, :],
                              func=AF.Identity, scale=-1.0))

    for c in range(NCH):
        slot = c % 2
        cc = slice(c * CH, (c + 1) * CH)
        sS.wait(T)
        T.matmul(pm1[:, :], onesrow_f[:, :], rrows[0:1, cc], start=True, stop=True)
        tS.inc(T.drain(fusable=False))
        tS.wait(V)
        vS.inc(Vn.tensor_copy(out=rb[:, :], in_=pm1[:, :]))
        vS.wait(T)
        emit_qbuild(c, 0)

        for i in range(16):
            bg_step()
            H = c * 16 + i
            kv = i // 4
            qp = qplus[H % 2]
            trigger = min(4, NP_G - 1)

            split = (NP_G == 16)
            T.wait_ge(v_nc.h, H + 1)
            for p in range(NP_G):
                g = NP_G * H + p
                buf = g % 2
                if g >= 2:
                    T.wait_ge(s_ex.h, g - 1)
                for j in range(2):
                    stile = 2 * p + j
                    mm = T.matmul(pst[buf][:, 512 * j:512 * (j + 1)],
                                  khead[kv][:, stile * 128:(stile + 1) * 128],
                                  qp[:, :], start=True, stop=True)
                t_st.inc(mm)
                if p == NP_G - 1 and not split:
                    t_std.inc(T.drain(fusable=False))
                qcov = None
                if p == trigger:
                    if i >= 1:
                        T.wait_ge(v_oe.h, H)
                        if i == 1:
                            T.wait_ge(v_rd.h, c)
                        t_pk.inc(T.matmul(pden[0:16, :],
                                          sel16s[0:1, 16 * (i - 1):16 * i],
                                          den2[0:1, ((H - 1) % 2) * CH:((H - 1) % 2 + 1) * CH],
                                          start=(i == 1), stop=False))
                    if i + 1 < 16:
                        if split:
                            qcov = emit_q_tmms(c, i + 1)
                        else:
                            emit_qbuild(c, i + 1)
                if split and p == 6 and i + 1 < 16:
                    emit_q_sacts(c, i + 1)
                if qcov is not None:
                    S.wait_ge(tS.h, qcov)
                elif p < NP_G - 1:
                    S.wait_ge(t_st.h, g + 2)
                elif split:
                    S.wait_ge(t_pv.h, NP_G * (H + 1) - 1)
                else:
                    S.wait_ge(t_std.h, t_std.n)
                if g >= 2:
                    S.wait_ge(t_pv.h, g - 1)
                s_ex.inc(Sn.activation(out=Ebuf[buf][:, :], in_=pst[buf][:, :],
                                      func=AF.Exp))
                if p >= 1:
                    gp = g - 1
                    T.wait_ge(s_ex.h, gp + 1)
                    if p == 1:
                        T.wait_ge(v_oe.h, H)
                    for j in range(2):
                        stile = 2 * (p - 1) + j
                        mm = T.matmul(pO[0:65, :],
                                      v_all[:, stile * 260 + 65 * kv:stile * 260 + 65 * (kv + 1)],
                                      Ebuf[gp % 2][:, 512 * j:512 * (j + 1)],
                                      start=(p == 1 and j == 0), stop=False)
                    t_pv.inc(mm)
                if split and p == 8 and i + 1 < 16:
                    emit_q_cmm(c, i + 1)
            g = NP_G * H + (NP_G - 1)
            T.wait_ge(s_ex.h, g + 1)
            if NP_G == 1:
                T.wait_ge(v_oe.h, H)
            for j in range(2):
                stile = 2 * (NP_G - 1) + j
                mm = T.matmul(pO[0:65, :],
                              v_all[:, stile * 260 + 65 * kv:stile * 260 + 65 * (kv + 1)],
                              Ebuf[g % 2][:, 512 * j:512 * (j + 1)],
                              start=(NP_G == 1 and j == 0), stop=(j == 1))
            t_pv.inc(mm)
            if i == 15:
                t_pvd.inc(T.drain(fusable=False))

            V.wait_ge(t_pv.h, NP_G * (H + 1))
            if i == 15:
                V.wait_ge(t_pvd.h, t_pvd.n)
            else:
                V.wait_ge(t_st.h, NP_G * (H + 1) + 1)
            if H >= 2:
                V.wait_ge(t_pk.h, H - 1)
            dslot = H % 2
            Vw.tensor_copy(out=den2[0:1, dslot * CH:(dslot + 1) * CH], in_=pO[64:65, :])
            v_oe.inc(Vn.tensor_copy(out=Obuf[:, i * CH:(i + 1) * CH], in_=pO[0:64, :]))
        T.wait_ge(v_oe.h, v_oe.n)
        t_pk.inc(T.matmul(pden[0:16, :], sel16s[0:1, 240:256],
                          den2[0:1, ((c * 16 + 15) % 2) * CH:((c * 16 + 15) % 2 + 1) * CH],
                          start=False, stop=True))
        t_pkd.inc(T.drain(fusable=False))

        S.wait_ge(t_pkd.h, c + 1)
        sS.inc(Sn.activation(out=denbp[0][0:16, :], in_=pden[0:16, :],
                            func=AF.Identity, bias=eps30[0:16, 0:1], scale=1.0))
        sS.wait(V)
        v_rd.inc(Vn.reciprocal(out=rd16[:, :], in_=denbp[0][0:16, :]))
        nbanks = [pm1, pm2]
        nmm_done = []
        nb_done = []
        mult_done = []
        T.wait_ge(v_rd.h, c + 1)
        T.wait_ge(v_nc.h, v_nc.n)

        def norm_evac(i, cover):
            nslot = i % 2
            V.wait_ge(tS.h, cover)
            if i >= 2:
                V.wait_ge(vS.h, mult_done[i - 2])
            vS.inc(Vn.tensor_copy(out=denbp[nslot][:, :], in_=nbanks[nslot][0:64, :]))
            nb_done.append(vS.n)
            vS.inc(Vn.tensor_tensor(out=Obuf[:, i * CH:(i + 1) * CH],
                                   in0=Obuf[:, i * CH:(i + 1) * CH],
                                   in1=denbp[nslot][:, :], op=ALU.mult))
            mult_done.append(vS.n)

        for i in range(16):
            nslot = i % 2
            if i >= 2:
                T.wait_ge(vS.h, nb_done[i - 2])
            tS.inc(T.matmul(nbanks[nslot][0:64, :], selcs[:, 64 * i:64 * (i + 1)],
                            rd16[:, :], start=True, stop=True))
            nmm_done.append(tS.n)
            if i >= 1:
                norm_evac(i - 1, nmm_done[i])
        tS.inc(T.drain(fusable=False))
        norm_evac(15, tS.n)

        pmm_done = []
        pr_done = []
        for pr in range(8):
            pslot = pr % 2
            if pr < 2:
                T.wait_ge(vS.h, mult_done[15])
            else:
                T.wait_ge(vS.h, pr_done[pr - 2])
            T.matmul(nbanks[pslot][:, :], stkb[:, 0:128],
                     Obuf[:, (2 * pr) * CH:(2 * pr + 1) * CH], start=True, stop=False)
            tS.inc(T.matmul(nbanks[pslot][:, :], stkb[:, 128:256],
                            Obuf[:, (2 * pr + 1) * CH:(2 * pr + 2) * CH],
                            start=False, stop=True))
            pmm_done.append(tS.n)
            if pr >= 1:
                V.wait_ge(tS.h, pmm_done[pr])
                vS.inc(Vn.tensor_copy(out=Opair[:, (pr - 1) * CH:pr * CH],
                                      in_=nbanks[(pr - 1) % 2][:, :]))
                pr_done.append(vS.n)
        tS.inc(T.drain(fusable=False))
        V.wait_ge(tS.h, tS.n)
        vS.inc(Vn.tensor_copy(out=Opair[:, 7 * CH:8 * CH], in_=nbanks[1][:, :]))
        pr_done.append(vS.n)

        omm_done = []
        op_act_done = []
        op_add_done = []
        lr_done = []

        def oproj_tail(dt, cover):
            oslot = dt % 2
            S.wait_ge(tS.h, cover)
            if dt >= 2:
                S.wait_ge(vS.h, op_add_done[dt - 2])
            sS.inc(Sn.activation(out=tev2p[oslot][:, :], in_=nbanks[oslot][:, :],
                                func=AF.Identity, bias=bo_c[:, dt:dt + 1], scale=1.0))
            op_act_done.append(sS.n)
            V.wait_ge(sS.h, op_act_done[dt])
            V.wait_ge(lrp[dt % 2].h, lr_done[dt])
            V.wait_ge(sop[oslot].h, sop[oslot].n)
            vS.inc(Vn.tensor_add(out=x1sb[oslot][:, :], in0=tev2p[oslot][:, :],
                                in1=xres[oslot][:, :]))
            op_add_done.append(vS.n)
            vS.wait(SY)
            SY.dma_start(out=x1T[128 * dt:128 * (dt + 1), cc],
                         in_=x1sb[oslot][:, :]).then_inc(sop[oslot].h, 16)
            sop[oslot].n += 16

        for dt in range(NT):
            oslot = dt % 2
            SY.wait_ge(vS.h, adds_done)
            SY.dma_start(out=xres[oslot][:, :],
                         in_=xT[128 * dt:128 * (dt + 1), cc]).then_inc(lrp[oslot].h, 16)
            lrp[oslot].n += 16
            lr_done.append(lrp[oslot].n)
            if dt == 0:
                T.wait_ge(vS.h, pr_done[7])
                T.wait_ge(sS.h, sS.n)
            if dt >= 2:
                T.wait_ge(sS.h, op_act_done[dt - 2])
            for pr in range(8):
                mm = T.matmul(nbanks[oslot][:, :], wob[pr][:, 128 * dt:128 * (dt + 1)],
                              Opair[:, pr * CH:(pr + 1) * CH],
                              start=(pr == 0), stop=(pr == 7))
            tS.inc(mm)
            omm_done.append(tS.n)
            if dt >= 1:
                oproj_tail(dt - 1, omm_done[dt])
                adds_done = op_add_done[dt - 1]
        tS.inc(T.drain(fusable=False))
        oproj_tail(7, tS.n)
        adds_done = op_add_done[7]
        t_lastuse[slot] = tS.n
        if c + 1 < NCH:
            loaded.append(load_chunk(c + 1))

    SY.wait_ge(sop[0].h, sop[0].n)
    SY.wait_ge(sop[1].h, sop[1].n)
    if ctx is None:
        stack.close()
        return nc
    return sop



def build_l0_ffn(N=N, NCH=NCH, ctx=None):
    if ctx is None:
        nc = bass.Bass("TRN2", target_bir_lowering=False)
        stack = contextlib.ExitStack()
        xT = nc.dram_tensor("xT", [D, N], F32, kind="ExternalInput").ap()
    else:
        nc, stack, xT, _out, _waits = ctx
    m1bd = nc.dram_tensor("m1bd", [8 * 128, 128], F32, kind="ExternalInput").ap()
    m2arr = nc.dram_tensor("m2arr", [32 * 128, 512], F32, kind="ExternalInput").ap()
    brow = nc.dram_tensor("brow", [128, 8], F32, kind="ExternalInput").ap()
    if ctx is None:
        x2T = nc.dram_tensor("x2T", [D, N], F32, kind="ExternalOutput").ap()
    else:
        x2T = _out

    _p = f"u{_uid()}_"
    sb = lambda name, shape, dt: stack.enter_context(nc.sbuf_tensor(_p + name, shape, dt))
    psa = lambda name, shape: stack.enter_context(nc.psum_tensor(_p + name, shape, F32))

    xbc = [[sb(f"xbc{k}_{t}", [128, CH], BF16) for t in range(NT)] for k in range(2)]
    m1b = sb("m1b", [128, 8 * 128], F32)
    m2b = [sb(f"m2b{k}", [128, 512], F32) for k in range(4)]
    wR = sb("wR", [128, 4 * 512], BF16)
    wI = sb("wI", [128, 4 * 512], BF16)
    wIn = sb("wIn", [128, 4 * 512], BF16)
    rsr = sb("rsr", [1, 512], BF16)
    rsi = sb("rsi", [1, 512], BF16)
    t1row = sb("t1row", [1, 512], F32)
    t2row = sb("t2row", [1, 512], F32)
    s1row = sb("s1row", [1, CH], F32)
    s2row = sb("s2row", [1, CH], F32)
    mrow = sb("mrow", [1, CH], F32)
    rrow = sb("rrow", [1, CH], F32)
    rb = sb("rb", [128, CH], F32)
    S1b = sb("S1b", [1, CH], BF16)
    xsq2 = [sb(f"xsq2{k}", [128, CH], BF16) for k in range(2)]
    tev = sb("tev", [128, CH], F32)
    tevp = [sb(f"tevp{k}", [128, CH], F32) for k in range(2)]
    x2sb = [sb(f"x2sb{k}", [128, CH], F32) for k in range(2)]
    xres = [sb(f"xres{k}", [128, CH], F32) for k in range(2)]
    ones128 = sb("ones128", [128, 1], BF16)
    onesrow_f = sb("onesrow_f", [1, 128], F32)
    eps5 = sb("eps5", [1, 1], F32)
    brc = sb("brc", [128, 8], F32)

    pm1 = psa("pm1", [128, CH])
    pm2 = psa("pm2", [128, CH])
    pmw = psa("pmw", [128, CH])

    lw = Sem(nc, stack, "lw")
    lx = Sem(nc, stack, "lx")
    lr = Sem(nc, stack, "lr")
    so = Sem(nc, stack, "so")
    sop = [Sem(nc, stack, "soA"), Sem(nc, stack, "soB")]
    tS = Sem(nc, stack, "tS")
    vS = Sem(nc, stack, "vS")
    sS = Sem(nc, stack, "sS")
    v_sq = Sem(nc, stack, "v_sq")
    t_sq = Sem(nc, stack, "t_sq")

    T, V, S, G, SY = nc.tensor, nc.vector, nc.scalar, nc.gpsimd, nc.sync
    Vw = make_serial(V, vS)
    Sw = make_serial(S, sS)
    Vn = make_serial(V, vS, auto_inc=False)
    Sn = make_serial(S, sS, auto_inc=False)
    if ctx is not None:
        for ws in _waits:
            G.wait_ge(ws.h, ws.n)
            SY.wait_ge(ws.h, ws.n)

    SY.dma_start(out=m1b.rearrange("p (a j) -> p a j", a=8),
                 in_=m1bd.rearrange("(a p) j -> p a j", p=128)).then_inc(lw.h, 16)
    SY.dma_start(out=brc[:, :], in_=brow[:, :]).then_inc(lw.h, 16)
    lw.n += 32
    Vw.memset(ones128[:, :], 1.0)
    Vw.memset(onesrow_f[:, :], 1.0)
    vS.inc(Vn.memset(eps5[:, :], 1e-5))
    vS.wait(T)
    vS.wait(S)

    m2ld = 0
    for RI in range(2):
        for tau in range(4):
            for c in range(4):
                slot = c
                G.wait_ge(tS.h, tS.n)
                G.dma_start(out=m2b[slot][:, :],
                            in_=m2arr[(RI * 16 + c * 4 + tau) * 128:
                                      (RI * 16 + c * 4 + tau) * 128 + 128, :]).then_inc(lx.h, 16)
                lx.n += 16
                T.wait_ge(lx.h, lx.n)
                T.wait_ge(lw.h, 32)
                if c == 0:
                    T.wait_ge(sS.h, sS.n)
                T.matmul(pmw[:, :], m1b[:, (RI * 4 + c) * 128:(RI * 4 + c) * 128 + 128],
                         m2b[slot][:, :], start=(c == 0), stop=(c == 3))
            tS.inc(T.drain(fusable=False))
            tS.wait(S)
            dst = wR if RI == 0 else wI
            Sw.activation(out=dst[:, tau * 512:(tau + 1) * 512], in_=pmw[:, :],
                         func=AF.Identity, scale=1.0)
            if RI == 1:
                sS.inc(Sn.activation(out=wIn[:, tau * 512:(tau + 1) * 512], in_=pmw[:, :],
                                    func=AF.Identity, scale=-1.0))
            else:
                sS.inc(Sn.nop())
    sS.wait(T)
    for tau in range(4):
        T.matmul(pm1[0:1, :], ones128[:, :], wR[:, tau * 512:(tau + 1) * 512],
                 start=(tau == 0), stop=(tau == 3))
    for tau in range(4):
        T.matmul(pm2[0:1, :], ones128[:, :], wI[:, tau * 512:(tau + 1) * 512],
                 start=(tau == 0), stop=(tau == 3))
    tS.inc(T.drain(fusable=False))
    tS.wait(V)
    Vw.tensor_copy(out=t1row[:, :], in_=pm1[0:1, :])
    vS.inc(Vn.tensor_copy(out=t2row[:, :], in_=pm2[0:1, :]))
    vS.wait(S)
    Vw.tensor_sub(out=s1row[0:1, :], in0=t1row[:, :], in1=t2row[:, :])
    Vw.tensor_add(out=s2row[0:1, :], in0=t1row[:, :], in1=t2row[:, :])
    vS.inc(Vn.nop())
    vS.wait(S)
    Sw.activation(out=rsr[0:1, :], in_=s1row[0:1, :], func=AF.Identity, scale=-1.0 / D)
    sS.inc(Sn.activation(out=rsi[0:1, :], in_=s2row[0:1, :],
                        func=AF.Identity, scale=-1.0 / D))
    sS.wait(T)
    sS.wait(V)

    t_lastuse = [tS.n, tS.n]

    def load_chunk(c):
        slot = c % 2
        G.wait_ge(tS.h, t_lastuse[slot])
        for t in range(NT):
            G.dma_start(out=xbc[slot][t][:, :],
                        in_=xT[128 * t:128 * (t + 1), c * CH:(c + 1) * CH]).then_inc(lx.h, 16)
        lx.n += NT * 16
        return lx.n

    loaded = {0: load_chunk(0)}
    adds_done = 0
    for c in range(NCH):
        slot = c % 2
        xc = xbc[slot]
        V.wait_ge(lx.h, loaded[c])
        V.wait_ge(sS.h, sS.n)
        T.wait_ge(lx.h, loaded[c])
        sq0, tq0 = v_sq.n, t_sq.n
        for t in range(NT):
            if t >= 2:
                V.wait_ge(t_sq.h, tq0 + t - 1)
            v_sq.inc(Vn.tensor_mul(out=xsq2[t % 2][:, :], in0=xc[t][:, :], in1=xc[t][:, :]))
            T.wait_ge(v_sq.h, sq0 + t + 1)
            T.matmul(pm1[0:1, :], ones128[:, :], xc[t][:, :], start=(t == 0), stop=False)
            t_sq.inc(T.matmul(pm2[0:1, :], ones128[:, :], xsq2[t % 2][:, :],
                              start=(t == 0), stop=(t == NT - 1)))
        tS.inc(T.drain(fusable=False))
        tS.wait(V)
        Vw.tensor_copy(out=s1row[:, :], in_=pm1[0:1, :])
        Vw.tensor_copy(out=s2row[:, :], in_=pm2[0:1, :])
        Vw.tensor_copy(out=S1b[0:1, :], in_=s1row[:, :])
        Vw.tensor_scalar(out=mrow[:, :], in0=s1row[:, :], scalar1=1.0 / D,
                        scalar2=None, op0=ALU.mult)
        Vw.tensor_mul(out=mrow[:, :], in0=mrow[:, :], in1=mrow[:, :])
        vS.inc(Vn.scalar_tensor_tensor(out=s1row[:, :], in0=s2row[:, :], scalar=1.0 / D,
                                      in1=mrow[:, :], op0=ALU.mult, op1=ALU.subtract))
        vS.wait(S)
        Sw.activation(out=s2row[:, :], in_=s1row[:, :], func=AF.Ln, bias=eps5[0:1, 0:1], scale=1.0)
        sS.inc(Sn.activation(out=rrow[0:1, :], in_=s2row[:, :], func=AF.Exp,
                            bias=0.0, scale=-0.5))
        sS.wait(T)
        T.matmul(pm1[:, :], onesrow_f[:, :], rrow[0:1, :], start=True, stop=True)
        tS.inc(T.drain(fusable=False))
        tS.wait(V)
        vS.inc(Vn.tensor_copy(out=rb[:, :], in_=pm1[:, :]))
        vS.wait(T)

        sbanks = [pm1, pmw]
        tt_done = []
        act_done = []
        add_done_l = []
        for sig in range(8):
            im = sig >= 4
            tpart = sig % 4
            slot = sig % 2
            SY.wait_ge(vS.h, adds_done)
            SY.dma_start(out=xres[slot][:, :],
                         in_=xT[128 * sig:128 * (sig + 1),
                                c * CH:(c + 1) * CH]).then_inc(lr.h, 16)
            lr.n += 16
            if sig >= 2:
                T.wait_ge(vS.h, tt_done[sig - 2])
            wA = wI if im else wR
            wB = wR if im else wIn
            for tau in range(4):
                T.matmul(sbanks[slot][:, :],
                         wA[:, tau * 512 + tpart * 128:tau * 512 + tpart * 128 + 128],
                         xc[tau][:, :], start=(tau == 0), stop=False)
            for tau in range(4):
                T.matmul(sbanks[slot][:, :],
                         wB[:, tau * 512 + tpart * 128:tau * 512 + tpart * 128 + 128],
                         xc[4 + tau][:, :], start=False, stop=False)
            rs = rsi if im else rsr
            T.matmul(sbanks[slot][:, :], rs[0:1, 128 * tpart:128 * (tpart + 1)], S1b[0:1, :],
                     start=False, stop=True)
            tS.inc(T.drain(fusable=False))
            dcount = tS.n
            V.wait_ge(tS.h, dcount)
            if sig >= 2:
                V.wait_ge(sS.h, act_done[sig - 2])
            vS.inc(Vn.tensor_tensor(out=tevp[slot][:, :], in0=sbanks[slot][:, :],
                                    in1=rb[:, :], op=ALU.mult))
            tt_done.append(vS.n)
            S.wait_ge(vS.h, tt_done[sig])
            if sig >= 2:
                S.wait_ge(vS.h, add_done_l[sig - 2])
            sS.inc(Sn.activation(out=xsq2[slot][:, :], in_=tevp[slot][:, :], func=AF.Identity,
                                 bias=brc[:, sig:sig + 1], scale=1.0))
            act_done.append(sS.n)
            V.wait_ge(sS.h, act_done[sig])
            V.wait_ge(lr.h, lr.n)
            V.wait_ge(sop[slot].h, sop[slot].n)
            vS.inc(Vn.tensor_add(out=x2sb[slot][:, :], in0=xsq2[slot][:, :],
                                 in1=xres[slot][:, :]))
            adds_done = vS.n
            add_done_l.append(vS.n)
            vS.wait(SY)
            SY.dma_start(out=x2T[128 * sig:128 * (sig + 1), c * CH:(c + 1) * CH],
                         in_=x2sb[slot][:, :]).then_inc(sop[slot].h, 16)
            sop[slot].n += 16
        t_lastuse[slot] = tS.n
        if c + 1 < NCH:
            loaded[c + 1] = load_chunk(c + 1)

    SY.wait_ge(sop[0].h, sop[0].n)
    SY.wait_ge(sop[1].h, sop[1].n)
    if ctx is None:
        stack.close()
        return nc
    return sop



def build_l1_attn(N=N, NCH=NCH, stage=9, ctx=None, bg=None):
    bg_step = (lambda: None) if bg is None else bg.step
    if ctx is None:
        nc = bass.Bass("TRN2", target_bir_lowering=False)
        stack = contextlib.ExitStack()
        xT = nc.dram_tensor("xT", [D, N], F32, kind="ExternalInput").ap()
    else:
        nc, stack, xT, _out, _waits = ctx
    Wq = nc.dram_tensor("Wq", [D, D], F32, kind="ExternalInput").ap()
    Wk = nc.dram_tensor("Wk", [D, D], F32, kind="ExternalInput").ap()
    Wv = nc.dram_tensor("Wv", [D, D], F32, kind="ExternalInput").ap()
    Wo = nc.dram_tensor("Wo", [D, D], F32, kind="ExternalInput").ap()
    W1 = nc.dram_tensor("W1", [D, 4096], F32, kind="ExternalInput").ap()
    W2 = nc.dram_tensor("W2", [4096, D], F32, kind="ExternalInput").ap()
    bqc = nc.dram_tensor("bqc", [64, 16], F32, kind="ExternalInput").ap()
    bkr = nc.dram_tensor("bkr", [1, D], F32, kind="ExternalInput").ap()
    bvr = nc.dram_tensor("bvr", [1, D], F32, kind="ExternalInput").ap()
    boc = nc.dram_tensor("boc", [128, 8], F32, kind="ExternalInput").ap()
    b1c = nc.dram_tensor("b1c", [128, 32], F32, kind="ExternalInput").ap()
    b2c = nc.dram_tensor("b2c", [128, 8], F32, kind="ExternalInput").ap()
    g1c = nc.dram_tensor("g1c", [128, 8], F32, kind="ExternalInput").ap()
    be1c = nc.dram_tensor("be1c", [128, 8], F32, kind="ExternalInput").ap()
    g2c = nc.dram_tensor("g2c", [128, 8], F32, kind="ExternalInput").ap()
    be2c = nc.dram_tensor("be2c", [128, 8], F32, kind="ExternalInput").ap()
    sel16x = nc.dram_tensor("sel16x1", [1, 256], F32, kind="ExternalInput").ap()
    selcol = nc.dram_tensor("selcol1", [16, 1024], F32, kind="ExternalInput").ap()
    stk = nc.dram_tensor("stk1", [64, 256], F32, kind="ExternalInput").ap()
    W1c = nc.dram_tensor("W1c", [128, 32 * 1024], BF16, kind="Internal").ap()
    W2c = nc.dram_tensor("W2c", [4096, 1024], BF16, kind="Internal").ap()
    if ctx is None:
        x3T = nc.dram_tensor("x3T", [D, N], F32, kind="ExternalOutput").ap()
    else:
        x3T = _out

    _p = f"u{_uid()}_"
    sb = lambda name, shape, dt: stack.enter_context(nc.sbuf_tensor(_p + name, shape, dt))
    psa = lambda name, shape: stack.enter_context(nc.psum_tensor(_p + name, shape, F32))
    if bg is not None:
        bg.alloc(stack)

    NSC = CH // 128

    wqb = [sb(f"wqb{t}", [128, D], BF16) for t in range(NT)]
    wkb = [sb(f"wkb{t}", [128, D], BF16) for t in range(NT)]
    wvb = [sb(f"wvb{t}", [128, D], BF16) for t in range(NT)]
    wob = [sb(f"wob{t}", [128, D], BF16) for t in range(NT)]
    arena = sb("arena", [128, 32 * CH], BF16)
    hbuf = arena
    xbc = [arena[:, 512 * t:512 * (t + 1)] for t in range(NT)]
    srcb = [sb(f"srcb{t}", [128, CH], BF16) for t in range(NT)]
    w1b = [sb(f"w1b{k}", [128, 1024], BF16) for k in range(4)]
    w2b = [sb(f"w2b{k}", [128, 512], BF16) for k in range(4)]
    kf = sb("kf", [128, 1024], F32)
    ebk = sb("ebk", [128, 1024], BF16)
    Ktok = sb("Ktok", [128, 1024], BF16)
    Vtok = sb("Vtok", [128, 1024], BF16)
    KVb = sb("KVb", [64, 1024], BF16)
    KVacc = sb("KVacc", [64, 1024], F32)
    Ksumr = sb("Ksumr", [1, 1024], F32)
    KsumColb = sb("KsumColb", [64, 16], BF16)
    qe = [sb(f"qe{k}", [64, CH], BF16) for k in range(2)]
    Obuf = arena[0:64, 16 * CH:32 * CH]
    Opair = arena[:, 8 * CH:16 * CH]
    stkb = sb("stkb", [64, 256], BF16)
    den2 = sb("den2", [1, 2 * CH], F32)
    rd16 = sb("rd16", [16, CH], F32)
    denbf = sb("denbf", [64, CH], F32)
    uc = srcb
    xln = [sb(f"xln{t}", [128, CH], BF16) for t in range(NT)]
    u2c = xln
    tev = sb("tev", [128, CH], F32)
    tevb = sb("tevb", [128, CH], F32)
    ta = sb("ta", [64, CH], F32)
    x3sb = [sb(f"x3sb{k}", [128, CH], F32) for k in range(2)]
    xres = [sb(f"xres{k}", [128, CH], F32) for k in range(2)]
    s1row = sb("s1row", [1, CH], F32)
    s2row = sb("s2row", [1, CH], F32)
    mrow = sb("mrow", [1, CH], F32)
    mr2 = sb("mr2", [1, CH], F32)
    rrow = sb("rrow", [1, CH], F32)
    rb = sb("rb", [128, CH], F32)
    mrb = sb("mrb", [128, CH], F32)
    xsq2 = [sb(f"xsq2{k}", [128, CH], BF16) for k in range(2)]
    bkb = sb("bkb", [128, D], BF16)
    bvb = sb("bvb", [128, D], BF16)
    ones128 = sb("ones128", [128, 1], BF16)
    onesrow_f = sb("onesrow_f", [1, 128], F32)
    one1f = sb("one1f", [1, 1], F32)
    eps5 = sb("eps5", [1, 1], F32)
    eps6 = sb("eps6", [128, 1], F32)
    bq64 = sb("bq64", [64, 16], F32)
    bkrr = kf[0:1, :]
    bo_c = sb("bo_c", [128, 8], F32)
    b1_c = sb("b1_c", [128, 32], F32)
    b2_c = sb("b2_c", [128, 8], F32)
    g1_c = sb("g1_c", [128, 8], F32)
    be1_c = sb("be1_c", [128, 8], F32)
    g2_c = sb("g2_c", [128, 8], F32)
    be2_c = sb("be2_c", [128, 8], F32)
    sel16s = sb("sel16s", [1, 256], F32)
    selcs = sb("selcs", [16, 1024], F32)

    pk = [psa(f"pk{k}", [128, CH]) for k in range(2)]
    pkv = [psa(f"pkv{k}", [128, CH]) for k in range(2)]
    pks = psa("pks", [128, CH])
    pm1 = psa("pm1", [128, CH])
    pm2 = psa("pm2", [128, CH])
    pm3 = psa("pm3", [128, CH])

    lw = Sem(nc, stack, "lw")
    lb1 = Sem(nc, stack, "lb1")
    lx = Sem(nc, stack, "lx")
    lfp = [Sem(nc, stack, f"lf{k}") for k in range(4)]
    lfd = [Sem(nc, stack, f"lfd{k}") for k in range(4)]
    w1cw = Sem(nc, stack, "w1cw")
    w2cw = Sem(nc, stack, "w2cw")
    lr = Sem(nc, stack, "lr")
    so = Sem(nc, stack, "so")
    sop = [Sem(nc, stack, "soA"), Sem(nc, stack, "soB")]
    tS = Sem(nc, stack, "tS")
    vS = Sem(nc, stack, "vS")
    sS = Sem(nc, stack, "sS")
    t_pk = Sem(nc, stack, "t_pk")
    t_pkd = Sem(nc, stack, "t_pkd")
    v_rd = Sem(nc, stack, "v_rd")

    T, V, S, G, SY = nc.tensor, nc.vector, nc.scalar, nc.gpsimd, nc.sync
    Vw = make_serial(V, vS)
    Sw = make_serial(S, sS)
    Vn = make_serial(V, vS, auto_inc=False)
    Sn = make_serial(S, sS, auto_inc=False)
    if ctx is not None:
        for ws in _waits:
            G.wait_ge(ws.h, ws.n)
            SY.wait_ge(ws.h, ws.n)

    for t in range(NT):
        G.dma_start(out=wqb[t][:, :], in_=Wq[128 * t:128 * (t + 1), :]).then_inc(lw.h, 16)
        G.dma_start(out=wkb[t][:, :], in_=Wk[128 * t:128 * (t + 1), :]).then_inc(lw.h, 16)
        G.dma_start(out=wvb[t][:, :], in_=Wv[128 * t:128 * (t + 1), :]).then_inc(lw.h, 16)
        G.dma_start(out=wob[t][:, :], in_=Wo[128 * t:128 * (t + 1), :]).then_inc(lw.h, 16)
    G.dma_start(out=stkb[:, :], in_=stk[:, :]).then_inc(lw.h, 16)
    lw.n += 33 * 16
    for srcap, dst in ((bqc, bq64), (bkr, bkrr), (boc, bo_c), (b1c, b1_c),
                       (b2c, b2_c), (g1c, g1_c), (be1c, be1_c), (g2c, g2_c), (be2c, be2_c),
                       (sel16x, sel16s), (selcol, selcs)):
        SY.dma_start(out=dst[:, :], in_=srcap[:, :]).then_inc(lb1.h, 16)
    lb1.n += 11 * 16
    Vw.memset(ones128[:, :], 1.0)
    Vw.memset(onesrow_f[:, :], 1.0)
    Vw.memset(one1f[:, :], 1.0)
    Vw.memset(eps5[:, :], 1e-5)
    Vw.memset(KVacc[:, :], 0.0)
    vS.inc(Vn.memset(eps6[:, :], 1e-6))
    vS.wait(T)
    vS.wait(S)
    lw.wait(T)
    lb1.wait(T)
    lb1.wait(S)
    lb1.wait(V)
    T.matmul(pm1[:, :], onesrow_f[:, :], bkrr[0:1, 0:512], start=True, stop=True)
    tS.inc(T.matmul(pm2[:, :], onesrow_f[:, :], bkrr[0:1, 512:1024], start=True, stop=True))
    SY.wait_ge(tS.h, tS.n)
    SY.dma_start(out=bkrr[:, :], in_=bvr[:, :]).then_inc(lb1.h, 16)
    lb1.n += 16
    lb1.wait(T)
    T.matmul(pm3[:, :], onesrow_f[:, :], bkrr[0:1, 0:512], start=True, stop=True)
    T.matmul(pk[0][:, :], onesrow_f[:, :], bkrr[0:1, 512:1024], start=True, stop=True)
    tS.inc(T.drain(fusable=False))
    tS.wait(V)
    Vw.tensor_copy(out=bkb[:, 0:512], in_=pm1[:, :])
    Vw.tensor_copy(out=bkb[:, 512:1024], in_=pm2[:, :])
    Vw.tensor_copy(out=bvb[:, 0:512], in_=pm3[:, :])
    vS.inc(Vn.tensor_copy(out=bvb[:, 512:1024], in_=pk[0][:, :]))
    vS.wait(T)

    def load_chunk(c, gate):
        G.wait_ge(tS.h, gate)
        for t in range(NT):
            G.dma_start(out=xbc[t][:, :],
                        in_=xT[128 * t:128 * (t + 1), c * CH:(c + 1) * CH]).then_inc(lx.h, 16)
        lx.n += NT * 16
        return lx.n

    def stats_rows(tiles, into_s1b=None):
        sq0 = vS.n
        for t in range(NT):
            if t >= 2:
                V.wait_ge(tS.h, tS.n)
            vS.inc(Vn.tensor_mul(out=xsq2[t % 2][:, :], in0=tiles[t][:, :], in1=tiles[t][:, :]))
            V_cnt = vS.n
            T.wait_ge(vS.h, V_cnt)
            T.matmul(pm1[0:1, :], ones128[:, :], tiles[t][:, :],
                     start=(t == 0), stop=(t == NT - 1))
            tS.inc(T.matmul(pm2[0:1, :], ones128[:, :], xsq2[t % 2][:, :],
                            start=(t == 0), stop=(t == NT - 1)))
        tS.inc(T.drain(fusable=False))
        tS.wait(V)
        Vw.tensor_copy(out=s1row[:, :], in_=pm1[0:1, :])
        Vw.tensor_copy(out=s2row[:, :], in_=pm2[0:1, :])
        Vw.tensor_scalar(out=mrow[:, :], in0=s1row[:, :], scalar1=1.0 / D,
                        scalar2=None, op0=ALU.mult)
        Vw.tensor_mul(out=mr2[:, :], in0=mrow[:, :], in1=mrow[:, :])
        vS.inc(Vn.scalar_tensor_tensor(out=s1row[:, :], in0=s2row[:, :], scalar=1.0 / D,
                                      in1=mr2[:, :], op0=ALU.mult, op1=ALU.subtract))
        vS.wait(S)
        Sw.activation(out=s2row[:, :], in_=s1row[:, :], func=AF.Ln, bias=eps5[0:1, 0:1], scale=1.0)
        sS.inc(Sn.activation(out=rrow[0:1, :], in_=s2row[:, :], func=AF.Exp,
                            bias=0.0, scale=-0.5))
        sS.wait(V)
        vS.inc(Vn.tensor_mul(out=mr2[:, :], in0=mrow[:, :], in1=rrow[:, :]))
        vS.wait(T)
        sS.wait(T)
        T.matmul(pm1[:, :], onesrow_f[:, :], rrow[0:1, :], start=True, stop=True)
        T.matmul(pm2[:, :], onesrow_f[:, :], mr2[0:1, :], start=True, stop=True)
        tS.inc(T.drain(fusable=False))
        tS.wait(V)
        Vw.tensor_copy(out=rb[:, :], in_=pm1[:, :])
        vS.inc(Vn.tensor_copy(out=mrb[:, :], in_=pm2[:, :]))
        vS.wait(T)

    def ln_apply(src_tiles, out_tiles, gcol, bcol):
        for t in range(NT):
            V.wait_ge(sS.h, sS.n)
            Vw.tensor_tensor(out=tev[:, :], in0=src_tiles[t][:, :], in1=rb[:, :], op=ALU.mult)
            vS.inc(Vn.tensor_sub(out=tevb[:, :], in0=tev[:, :], in1=mrb[:, :]))
            vS.wait(S)
            sS.inc(Sn.activation(out=out_tiles[t][:, :], in_=tevb[:, :], func=AF.Identity,
                                scale=gcol[:, t:t + 1] if gcol is not None else 1.0,
                                bias=bcol[:, t:t + 1] if bcol is not None else 0.0))
        sS.wait(V)
        sS.wait(T)

    KV_first = True
    adds_done = 0
    w1_last = [0, 0, 0, 0]
    w2_last = [0, 0, 0, 0]
    gate = tS.n
    for c in range(NCH):
        loaded = load_chunk(c, gate)
        V.wait_ge(lx.h, loaded)
        T.wait_ge(lx.h, loaded)
        stats_rows(xbc)
        ln_apply(xbc, srcb, None, None)

        for j in range(NSC if stage >= 2 else 0):
            bg_step()
            o0 = j * 128
            for half in range(2):
                for t in range(NT):
                    T.matmul(pk[half][:, :], srcb[t][:, o0:o0 + 128],
                             wkb[t][:, 512 * half:512 * (half + 1)],
                             start=(t == 0), stop=(t == NT - 1))
            tS.inc(T.drain(fusable=False))
            tS.wait(V)
            Vw.tensor_copy(out=kf[:, 0:512], in_=pk[0][:, :])
            vS.inc(Vn.tensor_copy(out=kf[:, 512:1024], in_=pk[1][:, :]))
            vS.wait(T)
            if stage < 2.5:
                continue
            for half in range(2):
                for t in range(NT):
                    T.matmul(pk[half][:, :], srcb[t][:, o0:o0 + 128],
                             wvb[t][:, 512 * half:512 * (half + 1)],
                             start=(t == 0), stop=(t == NT - 1))
            tS.inc(T.drain(fusable=False))
            if stage < 2.7:
                continue
            Vw.tensor_add(out=kf[:, :], in0=kf[:, :], in1=bkb[:, :])
            Vw.tensor_scalar_min(out=ebk[:, :], in0=kf[:, :], scalar1=0.0)
            vS.inc(Vn.nop())
            vS.wait(S)
            sS.inc(Sn.activation(out=ebk[:, :], in_=ebk[:, :], func=AF.Exp))
            Vw.tensor_scalar_max(out=kf[:, :], in0=kf[:, :], scalar1=0.0)
            sS.wait(V)
            vS.inc(Vn.tensor_add(out=Ktok[:, :], in0=kf[:, :], in1=ebk[:, :]))
            tS.wait(V)
            Vw.tensor_copy(out=kf[:, 0:512], in_=pk[0][:, :])
            Vw.tensor_copy(out=kf[:, 512:1024], in_=pk[1][:, :])
            vS.inc(Vn.tensor_add(out=Vtok[:, :], in0=kf[:, :], in1=bvb[:, :]))
            vS.wait(T)
            if stage < 2.9:
                continue
            KV_last = (c == NCH - 1 and j == NSC - 1)
            for i in range(16):
                bank = pkv[i // 8]
                T.matmul(bank[0:64, 64 * (i % 8):64 * (i % 8) + 64],
                         Ktok[:, 64 * i:64 * (i + 1)], Vtok[:, 64 * i:64 * (i + 1)],
                         start=True, stop=True)
            T.matmul(pks[0:1, :], ones128[:, :], Ktok[:, 0:512],
                     start=KV_first, stop=KV_last)
            T.matmul(pm3[0:1, :], ones128[:, :], Ktok[:, 512:1024],
                     start=KV_first, stop=KV_last)
            KV_first = False
            tS.inc(T.drain(fusable=False))
            tS.wait(V)
            for i in range(16):
                bank = pkv[i // 8]
                Vw.tensor_add(out=KVacc[:, 64 * i:64 * (i + 1)],
                              in0=KVacc[:, 64 * i:64 * (i + 1)],
                              in1=bank[0:64, 64 * (i % 8):64 * (i % 8) + 64])
            vS.wait(T)
        gate = tS.n
    if stage < 3:
        SY.wait_ge(so.h, so.n)
        stack.close()
        return nc
    tS.wait(V)
    Vw.tensor_copy(out=KVb[:, :], in_=KVacc[:, :])
    Vw.tensor_copy(out=Ksumr[0:1, 0:512], in_=pks[0:1, :])
    vS.inc(Vn.tensor_copy(out=Ksumr[0:1, 512:1024], in_=pm3[0:1, :]))
    vS.wait(T)
    for i in range(16):
        T.matmul(pm1[0:64, i:i + 1], Ksumr[0:1, 64 * i:64 * (i + 1)], one1f[:, :],
                 start=True, stop=True)
    tS.inc(T.drain(fusable=False))
    tS.wait(S)
    sS.inc(Sn.activation(out=KsumColb[:, :], in_=pm1[0:64, 0:16], func=AF.Identity, scale=1.0))
    sS.wait(T)
    sS.wait(V)

    gate2 = tS.n
    for c in range(NCH):
        loaded = load_chunk(c, gate2)
        V.wait_ge(lx.h, loaded)
        T.wait_ge(lx.h, loaded)
        stats_rows(xbc)
        ln_apply(xbc, srcb, None, None)

        for i in range(16 if stage >= 4 else 0):
            bg_step()
            H = c * 16 + i
            T.wait_ge(vS.h, vS.n)
            for t in range(NT):
                T.matmul(pm1[0:64, :], wqb[t][:, 64 * i:64 * (i + 1)], srcb[t][:, :],
                         start=(t == 0), stop=(t == NT - 1))
            tS.inc(T.drain(fusable=False))
            tS.wait(S)
            if H >= 2:
                S.wait_ge(t_pk.h, H - 1)
            Sw.activation(out=tev[0:64, :], in_=pm1[0:64, :], func=AF.Identity,
                         bias=bq64[:, i:i + 1], scale=1.0)
            sS.inc(Sn.nop())
            sS.wait(V)
            Vw.tensor_scalar_min(out=ta[:, :], in0=tev[0:64, :], scalar1=0.0)
            vS.inc(Vn.nop())
            vS.wait(S)
            sS.inc(Sn.activation(out=ta[:, :], in_=ta[:, :], func=AF.Exp))
            Vw.tensor_scalar_max(out=tev[0:64, :], in0=tev[0:64, :], scalar1=0.0)
            sS.wait(V)
            vS.inc(Vn.tensor_add(out=qe[H % 2][:, :], in0=tev[0:64, :], in1=ta[:, :]))
            vS.wait(T)
            dslot = H % 2
            T.matmul(pm2[0:1, :], KsumColb[:, i:i + 1], qe[H % 2][:, :],
                     start=True, stop=True)
            T.matmul(pm3[0:64, :], KVb[:, 64 * i:64 * (i + 1)], qe[H % 2][:, :],
                     start=True, stop=True)
            t_pk.inc(T.drain(fusable=False))
            V.wait_ge(t_pk.h, t_pk.n)
            Vw.tensor_copy(out=den2[0:1, dslot * CH:(dslot + 1) * CH], in_=pm2[0:1, :])
            vS.inc(Vn.tensor_copy(out=Obuf[:, i * CH:(i + 1) * CH], in_=pm3[0:64, :]))
            vS.wait(T)
            if i == 0:
                T.wait_ge(v_rd.h, c)
            T.matmul(pk[0][0:16, :], sel16s[0:1, 16 * i:16 * (i + 1)],
                     den2[0:1, dslot * CH:(dslot + 1) * CH],
                     start=(i == 0), stop=(i == 15))
            if i == 15:
                t_pkd.inc(T.drain(fusable=False))
        if stage < 5:
            gate2 = tS.n
            continue
        S.wait_ge(t_pkd.h, c + 1)
        sS.inc(Sn.activation(out=denbf[0:16, :], in_=pk[0][0:16, :], func=AF.Identity,
                            bias=eps6[0:16, 0:1], scale=1.0))
        sS.wait(V)
        v_rd.inc(Vn.reciprocal(out=rd16[:, :], in_=denbf[0:16, :]))
        for i in range(16):
            T.wait_ge(v_rd.h, c + 1)
            T.wait_ge(vS.h, vS.n)
            T.matmul(pm1[0:64, :], selcs[:, 64 * i:64 * (i + 1)], rd16[:, :],
                     start=True, stop=True)
            tS.inc(T.drain(fusable=False))
            tS.wait(V)
            Vw.tensor_copy(out=denbf[:, :], in_=pm1[0:64, :])
            vS.inc(Vn.tensor_tensor(out=Obuf[:, i * CH:(i + 1) * CH],
                                   in0=Obuf[:, i * CH:(i + 1) * CH],
                                   in1=denbf[:, :], op=ALU.mult))
        for pr in range(8):
            T.wait_ge(vS.h, vS.n)
            T.matmul(pm1[:, :], stkb[:, 0:128], Obuf[:, (2 * pr) * CH:(2 * pr + 1) * CH],
                     start=True, stop=False)
            T.matmul(pm1[:, :], stkb[:, 128:256],
                     Obuf[:, (2 * pr + 1) * CH:(2 * pr + 2) * CH],
                     start=False, stop=True)
            tS.inc(T.drain(fusable=False))
            tS.wait(V)
            vS.inc(Vn.tensor_copy(out=Opair[:, pr * CH:(pr + 1) * CH], in_=pm1[:, :]))
        for dt in range(NT):
            T.wait_ge(vS.h, vS.n)
            for pr in range(8):
                T.matmul(pm2[:, :], wob[pr][:, 128 * dt:128 * (dt + 1)],
                         Opair[:, pr * CH:(pr + 1) * CH],
                         start=(pr == 0), stop=(pr == 7))
            tS.inc(T.drain(fusable=False))
            tS.wait(S)
            sS.inc(Sn.activation(out=tev[:, :], in_=pm2[:, :], func=AF.Identity,
                                bias=bo_c[:, dt:dt + 1], scale=1.0))
            sS.wait(V)
            vS.inc(Vn.tensor_add(out=uc[dt][:, :], in0=tev[:, :], in1=srcb[dt][:, :]))
        vS.wait(T)
        vS.wait(S)

        if stage < 6:
            gate2 = tS.n
            continue
        stats_rows(uc)
        ln_apply(uc, xln, g1_c, be1_c)

        hbanks = [pm1, pk[0]]
        gelu_done = []
        mm_done = []
        for ft in range(32):
            slot = ft % 4
            G.wait_ge(tS.h, mm_done[ft - 4] if ft >= 4 else w1_last[slot])
            if c == 0:
                if ft >= 4:
                    G.wait_ge(w1cw.h, 16 * (ft - 3))
                G.dma_start(out=w1b[slot].rearrange("p (t j) -> p t j", t=8),
                            in_=W1.rearrange("(t p) f -> p t f", p=128)[:, :, 128 * ft:128 * (ft + 1)]
                            ).then_inc(lfp[slot].h, 16)
            else:
                if c == 1 and ft == 0:
                    G.wait_ge(w1cw.h, 16 * 32)
                G.dma_start(out=w1b[slot][:, :],
                            in_=W1c[:, 1024 * ft:1024 * (ft + 1)]).then_inc(lfp[slot].h, 16)
            lfp[slot].n += 16
            if c == 0:
                S.wait_ge(lfp[slot].h, lfp[slot].n)
                S.dma_start(out=W1c[:, 1024 * ft:1024 * (ft + 1)],
                            in_=w1b[slot][:, :]).then_inc(w1cw.h, 16)
            T.wait_ge(lfp[slot].h, lfp[slot].n)
            if ft >= 2:
                T.wait_ge(sS.h, gelu_done[ft - 2])
            for t in range(NT):
                mm = T.matmul(hbanks[ft % 2][:, :], w1b[slot][:, 128 * t:128 * (t + 1)],
                              xln[t][:, :], start=(t == 0), stop=(t == NT - 1))
            tS.inc(mm)
            mm_done.append(tS.n)
            w1_last[slot] = tS.n
            if ft >= 1:
                S.wait_ge(tS.h, mm_done[ft])
                sS.inc(Sn.activation(out=hbuf[:, (ft - 1) * CH:ft * CH],
                                     in_=hbanks[(ft - 1) % 2][:, :],
                                     func=AF.Gelu, bias=b1_c[:, ft - 1:ft], scale=1.0))
                gelu_done.append(sS.n)
        tS.inc(T.drain(fusable=False))
        S.wait_ge(tS.h, tS.n)
        sS.inc(Sn.activation(out=hbuf[:, 31 * CH:32 * CH], in_=hbanks[31 % 2][:, :],
                             func=AF.Gelu, bias=b1_c[:, 31:32], scale=1.0))
        gelu_done.append(sS.n)
        sS.wait(T)
        ybanks = [pm2, pm3, pk[0], pk[1]]
        if stage < 7:
            gate2 = tS.n
            continue
        for half in range(2):
            dn_done = []
            for ft in range(32):
                slot = ft % 4
                G.wait_ge(tS.h, dn_done[ft - 4] if ft >= 4 else w2_last[slot])
                if c == 0:
                    nw = half * 32 + ft
                    if nw >= 4:
                        G.wait_ge(w2cw.h, 16 * (nw - 3))
                    G.dma_start(out=w2b[slot][:, :],
                                in_=W2[128 * ft:128 * (ft + 1),
                                       512 * half:512 * (half + 1)]).then_inc(lfd[slot].h, 16)
                else:
                    if c == 1 and half == 0 and ft == 0:
                        G.wait_ge(w2cw.h, 16 * 64)
                    G.dma_start(out=w2b[slot][:, :],
                                in_=W2c[128 * ft:128 * (ft + 1),
                                        512 * half:512 * (half + 1)]).then_inc(lfd[slot].h, 16)
                lfd[slot].n += 16
                if c == 0:
                    S.wait_ge(lfd[slot].h, lfd[slot].n)
                    S.dma_start(out=W2c[128 * ft:128 * (ft + 1),
                                        512 * half:512 * (half + 1)],
                                in_=w2b[slot][:, :]).then_inc(w2cw.h, 16)
                T.wait_ge(lfd[slot].h, lfd[slot].n)
                for dl in range(4):
                    mm = T.matmul(ybanks[dl][:, :], w2b[slot][:, 128 * dl:128 * (dl + 1)],
                                  hbuf[:, ft * CH:(ft + 1) * CH],
                                  start=(ft == 0), stop=(ft == 31))
                tS.inc(mm)
                dn_done.append(tS.n)
                w2_last[slot] = tS.n
            tS.inc(T.drain(fusable=False))
            tS.wait(S)
            for dl in range(4):
                dt = half * 4 + dl
                S.wait_ge(vS.h, vS.n)
                sS.inc(Sn.activation(out=tev[:, :], in_=ybanks[dl][:, :], func=AF.Identity,
                                    bias=b2_c[:, dt:dt + 1], scale=1.0))
                sS.wait(V)
                vS.inc(Vn.tensor_add(out=u2c[dt][:, :], in0=tev[:, :], in1=xln[dt][:, :]))
            vS.wait(T)
        vS.wait(S)

        if stage < 8:
            gate2 = tS.n
            continue
        stats_rows(u2c)
        for dt in range(NT):
            SY.wait_ge(vS.h, adds_done)
            SY.dma_start(out=xres[dt % 2][:, :],
                         in_=xT[128 * dt:128 * (dt + 1), c * CH:(c + 1) * CH]).then_inc(lr.h, 16)
            lr.n += 16
            V.wait_ge(sS.h, sS.n)
            Vw.tensor_tensor(out=tev[:, :], in0=u2c[dt][:, :], in1=rb[:, :], op=ALU.mult)
            vS.inc(Vn.tensor_sub(out=tevb[:, :], in0=tev[:, :], in1=mrb[:, :]))
            vS.wait(S)
            sS.inc(Sn.activation(out=tev[:, :], in_=tevb[:, :], func=AF.Identity,
                                scale=g2_c[:, dt:dt + 1], bias=be2_c[:, dt:dt + 1]))
            sS.wait(V)
            V.wait_ge(lr.h, lr.n)
            V.wait_ge(sop[dt % 2].h, sop[dt % 2].n)
            vS.inc(Vn.tensor_add(out=x3sb[dt % 2][:, :], in0=tev[:, :], in1=xres[dt % 2][:, :]))
            adds_done = vS.n
            vS.wait(SY)
            SY.dma_start(out=x3T[128 * dt:128 * (dt + 1), c * CH:(c + 1) * CH],
                         in_=x3sb[dt % 2][:, :]).then_inc(sop[dt % 2].h, 16)
            sop[dt % 2].n += 16
        vS.wait(T)
        gate2 = tS.n

    if bg is not None:
        bg.finish()
    SY.wait_ge(sop[0].h, sop[0].n)
    SY.wait_ge(sop[1].h, sop[1].n)
    if ctx is None:
        stack.close()
        return nc
    return sop



def build_l1_ffn(N=N, NCH=NCH, dbg=False, ctx=None, bg=None):
    if ctx is None:
        nc = bass.Bass("TRN2", target_bir_lowering=False)
        stack = contextlib.ExitStack()
        xT = nc.dram_tensor("xT", [D, N], F32, kind="ExternalInput").ap()
    else:
        nc, stack, xT, _out, _waits = ctx
    if bg is None:
        wupT = nc.dram_tensor("wupT", [D, 4096], F32, kind="ExternalInput").ap()
        wdnT = nc.dram_tensor("wdnT", [4096, D], F32, kind="ExternalInput").ap()
    else:
        wupT, wdnT = bg.wupT, bg.wdnT
    if ctx is None:
        outT = nc.dram_tensor("outT", [D, N], F32, kind="ExternalOutput").ap()
    else:
        outT = _out
    if bg is None:
        wtup = nc.dram_tensor("wtup", [128, 32 * 1024], BF16, kind="Internal").ap()
        wtdn = nc.dram_tensor("wtdn", [4096, D], BF16, kind="Internal").ap()
    else:
        wtup, wtdn = bg.wtup, bg.wtdn
    if dbg:
        d_wtu = nc.dram_tensor("d_wtu", [D, 4096], mybir.dt.bfloat16, kind="ExternalOutput").ap()
        d_xn = nc.dram_tensor("d_xn", [128, CH], mybir.dt.bfloat16, kind="ExternalOutput").ap()
        d_h = nc.dram_tensor("d_h", [128, 2 * CH], mybir.dt.bfloat16, kind="ExternalOutput").ap()

    _p = f"u{_uid()}_"
    sb = lambda name, shape, dt: stack.enter_context(nc.sbuf_tensor(_p + name, shape, dt))
    psa = lambda name, shape: stack.enter_context(nc.psum_tensor(_p + name, shape, F32))

    arena = sb("arena", [128, 32 * CH], BF16)
    hbuf = arena
    xbc = [arena[:, 512 * t:512 * (t + 1)] for t in range(NT)]
    xnb = [sb(f"xnb{t}", [128, CH], BF16) for t in range(NT)]
    if bg is None:
        wq1 = [sb(f"wq1{k}", [128, 1024], F32) for k in range(2)]
        sgn = sb("sgn", [128, 1024], F32)
        ab = sb("ab", [128, 1024], F32)
        wtb = [sb(f"wtb{k}", [128, 1024], BF16) for k in range(2)]
        acc = sb("acc", [128, 33], F32)
        thr1 = sb("thr1", [1, 1], F32)
        negthrb = sb("negthrb", [128, 1024], F32)
        ab2 = sb("ab2", [128, 1024], F32)
    wupb = [sb(f"wupb{k}", [128, 1024], BF16) for k in range(4)]
    wdnb = [sb(f"wdnb{k}", [128, 512], BF16) for k in range(4)]
    hsq2p = [sb(f"hsq2p{k}", [128, CH], BF16) for k in range(2)]
    onescol_f = sb("onescol_f", [128, 1], F32)
    s1row = sb("s1row", [1, CH], F32)
    s2row = sb("s2row", [1, CH], F32)
    mrow = sb("mrow", [1, CH], F32)
    mr2 = sb("mr2", [1, CH], F32)
    rhrow = sb("rhrow", [1, CH], F32)
    r2row = sb("r2row", [1, CH], F32)
    rb = sb("rb", [128, CH], F32)
    mrb = sb("mrb", [128, CH], F32)
    r2b = sb("r2b", [128, CH], F32)
    xsq2 = [sb(f"xsq2{k}", [128, CH], BF16) for k in range(2)]
    tev = sb("tev", [128, CH], F32)
    tevb = sb("tevb", [128, CH], F32)
    x4sb = [sb(f"x4sb{k}", [128, CH], F32) for k in range(2)]
    xres = [sb(f"xres{k}", [128, CH], F32) for k in range(2)]
    ones128 = sb("ones128", [128, 1], BF16)
    onesrow_f = sb("onesrow_f", [1, 128], F32)
    eps5 = sb("eps5", [1, 1], F32)

    pm1 = psa("pm1", [128, CH])
    pm2 = psa("pm2", [128, CH])
    ybanks = [psa(f"py{k}", [128, CH]) for k in range(4)]

    lw = Sem(nc, stack, "lw")
    lx = Sem(nc, stack, "lx")
    lfp = [Sem(nc, stack, f"lf{k}") for k in range(4)]
    lfd = [Sem(nc, stack, f"lfd{k}") for k in range(4)]
    lr = Sem(nc, stack, "lr")
    swp = [Sem(nc, stack, "swA"), Sem(nc, stack, "swB")]
    sop = [Sem(nc, stack, "soA"), Sem(nc, stack, "soB")]
    tS = Sem(nc, stack, "tS")
    vS = Sem(nc, stack, "vS")
    sS = Sem(nc, stack, "sS")

    T, V, S, G, SY = nc.tensor, nc.vector, nc.scalar, nc.gpsimd, nc.sync
    Vw = make_serial(V, vS)
    Sw = make_serial(S, sS)
    Vn = make_serial(V, vS, auto_inc=False)
    Sn = make_serial(S, sS, auto_inc=False)
    if ctx is not None:
        for ws in _waits:
            G.wait_ge(ws.h, ws.n)
            SY.wait_ge(ws.h, ws.n)

    Vw.memset(ones128[:, :], 1.0)
    Vw.memset(onesrow_f[:, :], 1.0)
    Vw.memset(onescol_f[:, :], 1.0)
    Vw.memset(eps5[:, :], 1e-5)
    vS.wait(T)
    vS.wait(S)

    def ternarize(wsrc, wdst, ntiles, width, permuted=False):
        nq = width // 1024
        pieces = []
        for t in range(ntiles):
            for q in range(nq):
                pieces.append((t, q))
        for pi, (t, q) in enumerate(pieces):
            slot = pi % 2
            SY.wait_ge(sS.h, sS.n)
            if pi >= 2:
                SY.wait_ge(vS.h, vS.n)
            SY.dma_start(out=wq1[slot][:, :],
                         in_=wsrc[128 * t:128 * (t + 1), 1024 * q:1024 * (q + 1)]
                         ).then_inc(lw.h, 16)
            lw.n += 16
            V.wait_ge(lw.h, lw.n)
            Vw.tensor_reduce(out=acc[:, pi % 32:pi % 32 + 1], in_=wq1[slot][:, :],
                             axis=AX.X, op=ALU.add, apply_absolute_value=True)
            if (pi + 1) % 32 == 0 or pi == len(pieces) - 1:
                k = (pi % 32) + 1
                if pi < 32:
                    Vw.tensor_reduce(out=acc[:, 32:33], in_=acc[:, 0:k], axis=AX.X,
                                     op=ALU.add)
                else:
                    Vw.tensor_reduce(out=acc[:, 0:1], in_=acc[:, 0:k], axis=AX.X,
                                     op=ALU.add)
                    Vw.tensor_add(out=acc[:, 32:33], in0=acc[:, 32:33], in1=acc[:, 0:1])
        vS.wait(T)
        T.matmul(pm1[0:1, 0:1], acc[:, 32:33], onescol_f[:, :], start=True, stop=True)
        tS.inc(T.drain(fusable=False))
        tS.wait(S)
        Sw.activation(out=thr1[:, :], in_=pm1[0:1, 0:1], func=AF.Identity,
                      scale=-0.5 / (ntiles * 128 * width))
        sS.wait(T)
        T.matmul(pm2[:, 0:1], onesrow_f[:, :], thr1[:, :], start=True, stop=True)
        tS.inc(T.drain(fusable=False))
        tS.wait(V)
        vS.inc(Vn.tensor_copy(out=negthrb[:, 0:1], in_=pm2[:, 0:1]))
        for rep in range(10):
            w0 = 1 << rep
            w1 = min(1024 - w0, w0)
            if w1 <= 0:
                break
            Vw.tensor_copy(out=negthrb[:, w0:w0 + w1], in_=negthrb[:, 0:w1])
        for pi, (t, q) in enumerate(pieces):
            slot = pi % 2
            SY.wait_ge(sS.h, sS.n)
            SY.dma_start(out=wq1[slot][:, :],
                         in_=wsrc[128 * t:128 * (t + 1), 1024 * q:1024 * (q + 1)]
                         ).then_inc(lw.h, 16)
            lw.n += 16
            S.wait_ge(lw.h, lw.n)
            S.wait_ge(vS.h, vS.n)
            Sw.activation(out=sgn[:, :], in_=wq1[slot][:, :], func=AF.Sign)
            Sw.activation(out=ab[:, :], in_=wq1[slot][:, :], func=AF.Abs)
            sS.wait(V)
            Vw.tensor_add(out=ab[:, :], in0=ab[:, :], in1=negthrb[:, :])
            vS.wait(S)
            Sw.activation(out=ab2[:, :], in_=ab[:, :], func=AF.Sign)
            sS.wait(V)
            Vw.tensor_scalar_max(out=ab2[:, :], in0=ab2[:, :], scalar1=0.0)
            V.wait_ge(swp[pi % 2].h, swp[pi % 2].n)
            vS.inc(Vn.tensor_mul(out=wtb[pi % 2][:, :], in0=ab2[:, :], in1=sgn[:, :]))
            vS.wait(G)
            if permuted:
                for fl in range(8):
                    o0 = ((8 * q + fl) * 8 + t) * 128
                    G.dma_start(out=wdst[:, o0:o0 + 128],
                                in_=wtb[pi % 2][:, 128 * fl:128 * (fl + 1)]
                                ).then_inc(swp[pi % 2].h, 16)
                    swp[pi % 2].n += 16
            else:
                G.dma_start(out=wdst[128 * t:128 * (t + 1), 1024 * q:1024 * (q + 1)],
                            in_=wtb[pi % 2][:, :]).then_inc(swp[pi % 2].h, 16)
                swp[pi % 2].n += 16

    if bg is None:
        ternarize(wupT, wtup, NT, 4096, permuted=True)
        ternarize(wdnT, wtdn, 32, 1024)
        G.wait_ge(swp[0].h, swp[0].n)
        G.wait_ge(swp[1].h, swp[1].n)
    else:
        bg.finish()
        G.wait_ge(bg.swp[0].h, bg.swp[0].n)
        G.wait_ge(bg.swp[1].h, bg.swp[1].n)

    t_lastuse = 0
    adds_done = 0
    wu_last = [0, 0, 0, 0]
    wd_last = [0, 0, 0, 0]
    for c in range(NCH):
        G.wait_ge(tS.h, t_lastuse)
        for t in range(NT):
            G.dma_start(out=xbc[t][:, :],
                        in_=xT[128 * t:128 * (t + 1), c * CH:(c + 1) * CH]).then_inc(lx.h, 16)
        lx.n += NT * 16
        V.wait_ge(lx.h, lx.n)
        T.wait_ge(lx.h, lx.n)
        for t in range(NT):
            if t >= 2:
                V.wait_ge(tS.h, tS.n)
            vS.inc(Vn.tensor_mul(out=xsq2[t % 2][:, :], in0=xbc[t][:, :], in1=xbc[t][:, :]))
            T.wait_ge(vS.h, vS.n)
            T.matmul(pm1[0:1, :], ones128[:, :], xbc[t][:, :],
                     start=(t == 0), stop=(t == NT - 1))
            tS.inc(T.matmul(pm2[0:1, :], ones128[:, :], xsq2[t % 2][:, :],
                            start=(t == 0), stop=(t == NT - 1)))
        tS.inc(T.drain(fusable=False))
        tS.wait(V)
        Vw.tensor_copy(out=s1row[:, :], in_=pm1[0:1, :])
        Vw.tensor_copy(out=s2row[:, :], in_=pm2[0:1, :])
        Vw.tensor_scalar(out=mrow[:, :], in0=s1row[:, :], scalar1=1.0 / D,
                         scalar2=None, op0=ALU.mult)
        Vw.tensor_mul(out=mr2[:, :], in0=mrow[:, :], in1=mrow[:, :])
        Vw.scalar_tensor_tensor(out=s1row[:, :], in0=s2row[:, :], scalar=1.0 / D,
                                in1=mr2[:, :], op0=ALU.mult, op1=ALU.subtract)
        vS.wait(S)
        Sw.activation(out=s2row[:, :], in_=s1row[:, :], func=AF.Ln, bias=eps5[0:1, 0:1], scale=1.0)
        Sw.activation(out=mr2[:, :], in_=s1row[:, :], func=AF.Ln, bias=0.0, scale=1.0)
        sS.wait(V)
        Vw.tensor_sub(out=mr2[:, :], in0=mr2[:, :], in1=s2row[:, :])
        vS.wait(S)
        Sw.activation(out=mr2[:, :], in_=mr2[:, :], func=AF.Exp, bias=0.0, scale=0.5)
        Sw.activation(out=s2row[:, :], in_=s2row[:, :], func=AF.Exp, bias=0.0, scale=0.5)
        sS.wait(V)
        Vw.tensor_scalar(out=mr2[:, :], in0=mr2[:, :], scalar1=1e-8, scalar2=None, op0=ALU.add)
        Vw.tensor_mul(out=mr2[:, :], in0=mr2[:, :], in1=s2row[:, :])
        Vw.reciprocal(out=rhrow[:, :], in_=mr2[:, :])
        Vw.tensor_mul(out=mr2[:, :], in0=mrow[:, :], in1=rhrow[:, :])
        vS.wait(T)
        T.matmul(pm1[:, :], onesrow_f[:, :], rhrow[0:1, :], start=True, stop=True)
        T.matmul(pm2[:, :], onesrow_f[:, :], mr2[0:1, :], start=True, stop=True)
        tS.inc(T.drain(fusable=False))
        tS.wait(V)
        Vw.tensor_copy(out=rb[:, :], in_=pm1[:, :])
        Vw.tensor_copy(out=mrb[:, :], in_=pm2[:, :])
        for t in range(NT):
            Vw.tensor_tensor(out=tev[:, :], in0=xbc[t][:, :], in1=rb[:, :], op=ALU.mult)
            Vw.tensor_sub(out=xnb[t][:, :], in0=tev[:, :], in1=mrb[:, :])
        vS.wait(T)
        vS.wait(S)

        ubanks = [pm1, ybanks[0]]
        gelu_done = []
        mm_done = []
        hsq_mm_done = [0] * 33
        sq_done = []

        def _l1f_gelu(ftc):
            slot2 = ftc % 2
            if ftc >= 2:
                S.wait_ge(vS.h, sq_done[ftc - 2])
            sS.inc(Sn.activation(out=hbuf[:, ftc * CH:(ftc + 1) * CH],
                                 in_=ubanks[slot2][:, :],
                                 func=AF.Gelu, bias=0.0, scale=1.0))
            gelu_done.append(sS.n)
            V.wait_ge(sS.h, gelu_done[ftc])
            if ftc >= 1:
                V.wait_ge(tS.h, hsq_mm_done[ftc - 1])
            vS.inc(Vn.tensor_mul(out=hsq2p[slot2][:, :], in0=hbuf[:, ftc * CH:(ftc + 1) * CH],
                                 in1=hbuf[:, ftc * CH:(ftc + 1) * CH]))
            sq_done.append(vS.n)
            T.wait_ge(vS.h, sq_done[ftc])
            tS.inc(T.matmul(pm2[0:1, :], ones128[:, :], hsq2p[slot2][:, :],
                            start=(ftc == 0), stop=(ftc == 31)))
            hsq_mm_done[ftc] = tS.n

        for ft in range(32):
            slot = ft % 4
            G.wait_ge(tS.h, mm_done[ft - 4] if ft >= 4 else wu_last[slot])
            G.dma_start(out=wupb[slot][:, :],
                        in_=wtup[:, 1024 * ft:1024 * (ft + 1)]).then_inc(lfp[slot].h, 16)
            lfp[slot].n += 16
            T.wait_ge(lfp[slot].h, lfp[slot].n)
            if ft >= 2:
                T.wait_ge(sS.h, gelu_done[ft - 2])
            for t in range(NT):
                mm = T.matmul(ubanks[ft % 2][:, :], wupb[slot][:, 128 * t:128 * (t + 1)],
                              xnb[t][:, :], start=(t == 0), stop=(t == NT - 1))
            tS.inc(mm)
            mm_done.append(tS.n)
            wu_last[slot] = tS.n
            if ft >= 1:
                S.wait_ge(tS.h, mm_done[ft])
                _l1f_gelu(ft - 1)
        tS.inc(T.drain(fusable=False))
        S.wait_ge(tS.h, tS.n)
        _l1f_gelu(31)
        tS.inc(T.drain(fusable=False))
        tS.wait(S)
        Sw.activation(out=s2row[:, :], in_=pm2[0:1, :], func=AF.Ln, bias=0.0, scale=1.0 / 4096)
        Sw.activation(out=s2row[:, :], in_=s2row[:, :], func=AF.Exp, bias=0.0, scale=0.5)
        sS.wait(V)
        Vw.tensor_scalar(out=s2row[:, :], in0=s2row[:, :], scalar1=1e-8, scalar2=None, op0=ALU.add)
        Vw.reciprocal(out=r2row[:, :], in_=s2row[:, :])
        vS.wait(T)
        T.matmul(pm1[:, :], onesrow_f[:, :], r2row[0:1, :], start=True, stop=True)
        tS.inc(T.drain(fusable=False))
        tS.wait(V)
        Vw.tensor_copy(out=r2b[:, :], in_=pm1[:, :])
        vS.wait(T)

        for half in range(2):
            dn_done = []
            for ft in range(32):
                slot = ft % 4
                G.wait_ge(tS.h, dn_done[ft - 4] if ft >= 4 else wd_last[slot])
                G.dma_start(out=wdnb[slot][:, :],
                            in_=wtdn[128 * ft:128 * (ft + 1),
                                     512 * half:512 * (half + 1)]).then_inc(lfd[slot].h, 16)
                lfd[slot].n += 16
                T.wait_ge(lfd[slot].h, lfd[slot].n)
                for dl in range(4):
                    mm = T.matmul(ybanks[dl][:, :], wdnb[slot][:, 128 * dl:128 * (dl + 1)],
                                  hbuf[:, ft * CH:(ft + 1) * CH],
                                  start=(ft == 0), stop=(ft == 31))
                tS.inc(mm)
                dn_done.append(tS.n)
                wd_last[slot] = tS.n
            tS.inc(T.drain(fusable=False))
            tS.wait(V)
            for dl in range(4):
                dt = half * 4 + dl
                SY.wait_ge(vS.h, adds_done)
                SY.dma_start(out=xres[dt % 2][:, :],
                             in_=xT[128 * dt:128 * (dt + 1),
                                    c * CH:(c + 1) * CH]).then_inc(lr.h, 16)
                lr.n += 16
                Vw.tensor_tensor(out=tev[:, :], in0=ybanks[dl][:, :], in1=r2b[:, :],
                                 op=ALU.mult)
                V.wait_ge(lr.h, lr.n)
                V.wait_ge(sop[dt % 2].h, sop[dt % 2].n)
                vS.inc(Vn.tensor_add(out=x4sb[dt % 2][:, :], in0=tev[:, :],
                                     in1=xres[dt % 2][:, :]))
                adds_done = vS.n
                vS.wait(SY)
                SY.dma_start(out=outT[128 * dt:128 * (dt + 1), c * CH:(c + 1) * CH],
                             in_=x4sb[dt % 2][:, :]).then_inc(sop[dt % 2].h, 16)
                sop[dt % 2].n += 16
            vS.wait(T)
        t_lastuse = tS.n

    if dbg:
        SY.wait_ge(vS.h, vS.n)
        SY.dma_start(out=d_wtu[:, :], in_=wtup[:, :]).then_inc(sop[0].h, 16)
        sop[0].n += 16
        G.wait_ge(vS.h, vS.n)
        G.dma_start(out=d_xn[:, :], in_=xnb[0][:, :]).then_inc(sop[0].h, 16)
        sop[0].n += 16
        G.dma_start(out=d_h[:, :], in_=hbuf[:, 0:2 * CH]).then_inc(sop[0].h, 16)
        sop[0].n += 16
    SY.wait_ge(sop[0].h, sop[0].n)
    SY.wait_ge(sop[1].h, sop[1].n)
    if ctx is None:
        stack.close()
        return nc
    return sop


def build_fused():
    nc = bass.Bass("TRN2", target_bir_lowering=False)
    xT = nc.dram_tensor("xT", [D, N], F32, kind="ExternalInput").ap()
    x1T = nc.dram_tensor("x1s", [D, N], F32, kind="Internal").ap()
    x2T = nc.dram_tensor("x2s", [D, N], F32, kind="Internal").ap()
    x3T = nc.dram_tensor("x3s", [D, N], F32, kind="Internal").ap()
    outT = nc.dram_tensor("outT", [D, N], F32, kind="ExternalOutput").ap()
    waits = []
    with contextlib.ExitStack() as bgst:
        bg = TernBG(nc, bgst)
        with contextlib.ExitStack() as st1:
            waits = list(build_l0_attn(ctx=(nc, st1, xT, x1T, waits)))
        with contextlib.ExitStack() as st2:
            waits = list(build_l0_ffn(ctx=(nc, st2, x1T, x2T, waits)))
        with contextlib.ExitStack() as st3:
            waits = list(build_l1_attn(ctx=(nc, st3, x2T, x3T, waits), bg=bg))
        with contextlib.ExitStack() as st4:
            waits = list(build_l1_ffn(ctx=(nc, st4, x3T, outT, waits), bg=bg))
    return nc



def _sel_consts():
    sel16x = np.zeros((1, 256), np.float32)
    for h in range(16):
        sel16x[0, 16 * h + h] = 1.0
    selcol = np.zeros((16, 1024), np.float32)
    for h in range(16):
        selcol[h, 64 * h:64 * (h + 1)] = 1.0
    stk = np.zeros((64, 256), np.float32)
    for k in range(64):
        stk[k, k] = 1.0
        stk[k, 128 + 64 + k] = 1.0
    return sel16x, selcol, stk


def l0a_inputs(x_b, wq, bq, wk, bk, wv, bv, wo, bo):
    sel16x, selcol, stk = _sel_consts()
    return {
        "xT": np.ascontiguousarray(x_b.T),
        "wqT": np.ascontiguousarray(wq.T),
        "wkT": np.ascontiguousarray(wk.T),
        "wvT": np.ascontiguousarray(wv.T),
        "woT": np.ascontiguousarray(wo.T),
        "bq": np.ascontiguousarray(bq.reshape(16, 64).T),
        "bk": np.ascontiguousarray(bk.reshape(4, 64).T),
        "bv": np.ascontiguousarray(bv.reshape(1, 256)),
        "bo": np.ascontiguousarray(bo.reshape(8, 128).T),
        "sel16x": sel16x,
        "selcol": selcol,
        "stk": stk,
    }


def l0f_consts(G1_R, G2_R, G1_I, G2_I):
    m1bd = np.zeros((8 * 128, 128), np.float32)
    for RI, G2 in ((0, G2_R), (1, G2_I)):
        for c in range(4):
            B = np.asarray(G2)[c, :, 0, :]
            blk = np.zeros((128, 128), np.float32)
            for dd in range(8):
                blk[dd * 16:(dd + 1) * 16, dd * 16:(dd + 1) * 16] = B
            m1bd[(RI * 4 + c) * 128:(RI * 4 + c) * 128 + 128] = blk
    m2arr = np.zeros((32 * 128, 512), np.float32)
    for RI, G1 in ((0, G1_R), (1, G1_I)):
        for c in range(4):
            A = np.asarray(G1)[0, :, c, :]
            for tau in range(4):
                blk = np.zeros((128, 512), np.float32)
                for dl in range(8):
                    for e in range(16):
                        kk = dl * 16 + e
                        blk[kk, e::16] = A[:, 8 * tau + dl]
                m2arr[(RI * 16 + c * 4 + tau) * 128:(RI * 16 + c * 4 + tau) * 128 + 128] = blk
    return m1bd, m2arr


def l0f_inputs(x1, G1_R, G2_R, G1_I, G2_I, bias_real, bias_imag):
    m1bd, m2arr = l0f_consts(G1_R, G2_R, G1_I, G2_I)
    brow = np.concatenate([np.asarray(bias_real).reshape(4, 128).T,
                           np.asarray(bias_imag).reshape(4, 128).T], axis=1)
    return {
        "xT": np.ascontiguousarray(x1.T),
        "m1bd": m1bd,
        "m2arr": m2arr,
        "brow": np.ascontiguousarray(brow.astype(np.float32)),
    }


def l1a_inputs(x2, Wq, bq, Wk, bk, Wv, bv, Wo, bo, W1, b1, W2, b2, g1, be1, g2, be2):
    sel16x, selcol, stk = _sel_consts()
    col8 = lambda a: np.ascontiguousarray(np.asarray(a).reshape(8, 128).T.astype(np.float32))
    return {
        "xT": np.ascontiguousarray(x2.T),
        "Wq": np.ascontiguousarray(Wq), "Wk": np.ascontiguousarray(Wk),
        "Wv": np.ascontiguousarray(Wv), "Wo": np.ascontiguousarray(Wo),
        "W1": np.ascontiguousarray(W1), "W2": np.ascontiguousarray(W2),
        "bqc": np.ascontiguousarray(np.asarray(bq).reshape(16, 64).T.astype(np.float32)),
        "bkr": np.ascontiguousarray(np.asarray(bk).reshape(1, 1024)),
        "bvr": np.ascontiguousarray(np.asarray(bv).reshape(1, 1024)),
        "boc": col8(bo),
        "b1c": np.ascontiguousarray(np.asarray(b1).reshape(32, 128).T.astype(np.float32)),
        "b2c": col8(b2),
        "g1c": col8(g1), "be1c": col8(be1),
        "g2c": col8(g2), "be2c": col8(be2),
        "sel16x1": sel16x, "selcol1": selcol, "stk1": stk,
    }


def l1f_inputs(x3, w_up, w_down):
    return {
        "xT": np.ascontiguousarray(x3.T),
        "wupT": np.ascontiguousarray(np.asarray(w_up).T),
        "wdnT": np.ascontiguousarray(np.asarray(w_down).T),
    }


_NC_CACHE = {}


def _get_fused():
    if "fused" not in _NC_CACHE:
        _NC_CACHE["fused"] = build_fused()
    return _NC_CACHE["fused"]


def kernel(**inputs):
    ncores = 8
    x = np.asarray(inputs["L0_attn_x"])
    nc = _get_fused()
    zero = np.zeros((N, D), np.float32)
    base = {}
    base.update(l0a_inputs(zero,
                           inputs["L0_attn_wq"], inputs["L0_attn_bq"],
                           inputs["L0_attn_wk"], inputs["L0_attn_bk"],
                           inputs["L0_attn_wv"], inputs["L0_attn_bv"],
                           inputs["L0_attn_wo"], inputs["L0_attn_bo"]))
    base.update(l0f_inputs(zero,
                           inputs["L0_ffn_G1_R"], inputs["L0_ffn_G2_R"],
                           inputs["L0_ffn_G1_I"], inputs["L0_ffn_G2_I"],
                           inputs["L0_ffn_bias_real"], inputs["L0_ffn_bias_imag"]))
    base.update(l1a_inputs(zero,
                           inputs["L1_attn_Wq"], inputs["L1_attn_bq"],
                           inputs["L1_attn_Wk"], inputs["L1_attn_bk"],
                           inputs["L1_attn_Wv"], inputs["L1_attn_bv"],
                           inputs["L1_attn_Wo"], inputs["L1_attn_bo"],
                           inputs["L1_attn_W1"], inputs["L1_attn_b1"],
                           inputs["L1_attn_W2"], inputs["L1_attn_b2"],
                           inputs["L1_attn_g1"], inputs["L1_attn_be1"],
                           inputs["L1_attn_g2"], inputs["L1_attn_be2"]))
    base.update(l1f_inputs(zero,
                           inputs["L1_ffn_w_up"], inputs["L1_ffn_w_down"]))
    in_maps = []
    for b in range(ncores):
        m = dict(base)
        m["xT"] = np.ascontiguousarray(np.asarray(x[b]).T)
        in_maps.append(m)
    res = run_bass_kernel_spmd(nc, in_maps, core_ids=list(range(ncores)))
    out = np.stack([res.results[b]["outT"].T for b in range(ncores)], axis=0)
    return np.ascontiguousarray(out.astype(np.float32))


# version 15
# speedup vs baseline: 1.8924x; 1.0710x over previous
import contextlib
import numpy as np
import ml_dtypes
import concourse.bass as bass
import concourse.mybir as mybir
import concourse.bass_utils as _BU
from concourse.bass_utils import run_bass_kernel_spmd

_orig_run_command = _BU.run_command


def _run_command_ldwopt(cmd, *a, **kw):
    if isinstance(cmd, list) and any(str(c).startswith("--enable-ldw-opt") for c in cmd):
        cmd = ["--enable-ldw-opt=true" if str(c) == "--enable-ldw-opt=false" else c
               for c in cmd]
    return _orig_run_command(cmd, *a, **kw)


_BU.run_command = _run_command_ldwopt

F32 = mybir.dt.float32
BF16 = mybir.dt.bfloat16
AF = mybir.ActivationFunctionType
ALU = mybir.AluOpType
AX = mybir.AxisListType

N = 4096
D = 1024
NT = 8
CH = 512
NCH = N // CH
BF = ml_dtypes.bfloat16


_UID = [0]


def _uid():
    _UID[0] += 1
    return _UID[0]


class Sem:
    def __init__(self, nc, stack, name):
        self.h = stack.enter_context(nc.semaphore(f"{name}_u{_UID[0]}"))
        self.n = 0

    def inc(self, instr, k=1):
        instr.then_inc(self.h, k)
        self.n += k
        return self.n

    def wait(self, eng, val=None):
        eng.wait_ge(self.h, self.n if val is None else val)


class PBank:
    def __init__(self, ap):
        self.ap = ap
        self.fresh = True

    def start(self):
        s = self.fresh
        self.fresh = False
        return s

    def reset(self):
        self.fresh = True



def make_serial(eng, sem, auto_inc=True):
    class _W:
        def __getattr__(self, name):
            fn = getattr(eng, name)
            def call(*a, **kw):
                eng.wait_ge(sem.h, sem.n)
                ins = fn(*a, **kw)
                if auto_inc:
                    sem.inc(ins)
                return ins
            return call
    return _W()

class TernBG:

    def __init__(self, nc, stack):
        self.nc = nc
        self._p = f"tbg{_uid()}_"
        self.wupT = nc.dram_tensor("wupT", [D, 4096], F32, kind="ExternalInput").ap()
        self.wdnT = nc.dram_tensor("wdnT", [4096, D], F32, kind="ExternalInput").ap()
        self.wtup = nc.dram_tensor("wtup", [128, 32 * 1024], BF16, kind="Internal").ap()
        self.wtdn = nc.dram_tensor("wtdn", [4096, D], BF16, kind="Internal").ap()
        self.scr = [nc.dram_tensor(f"{self._p}scr{k}", [1, 128], F32, kind="Internal").ap()
                    for k in range(4)]
        self.lwT = Sem(nc, stack, "tbg_lw")
        self.vT = Sem(nc, stack, "tbg_v")
        self.sT = Sem(nc, stack, "tbg_s")
        self.fz = Sem(nc, stack, "tbg_fz")
        self.swp = [Sem(nc, stack, "tbg_swA"), Sem(nc, stack, "tbg_swB")]
        self._gen = None
        self.done = False

    def alloc(self, stack):
        nc = self.nc
        sb = lambda name, shape, dt: stack.enter_context(nc.sbuf_tensor(self._p + name, shape, dt))
        self.wq = sb("wq", [128, 1024], F32)
        self.ab = self.wq[:, 0:512]
        self.sgn = sb("sgn", [128, 512], BF16)
        self.m = sb("m", [128, 512], BF16)
        self.wt = sb("wt", [128, 512], BF16)
        self.acc = sb("acc", [128, 66], F32)
        self.row = sb("row", [1, 128], F32)
        self.thr = [sb(f"thr{k}", [128, 1], F32) for k in range(2)]
        self._gen = self._emit()

    def _src(self, pi):
        if pi < 32:
            t, q = pi // 4, pi % 4
            return self.wupT[128 * t:128 * (t + 1), 1024 * q:1024 * (q + 1)]
        p = pi - 32
        return self.wdnT[128 * p:128 * (p + 1), :]

    def _emit(self):
        nc = self.nc
        V, S, SY = nc.vector, nc.scalar, nc.sync
        lwT, vT, sT, fz = self.lwT, self.vT, self.sT, self.fz
        red_done = {}
        for k in range(65):
            if k >= 1:
                pi = k - 1
                V.wait_ge(lwT.h, 16 * k)
                ins = V.tensor_reduce(out=self.acc[:, pi:pi + 1], in_=self.wq[:, :],
                                      axis=AX.X, op=ALU.add, apply_absolute_value=True)
                vT.inc(ins)
                red_done[pi] = vT.n
            if k < 64:
                if k >= 1:
                    SY.wait_ge(vT.h, red_done[k - 1])
                SY.dma_start(out=self.wq[:, :], in_=self._src(k)).then_inc(lwT.h, 16)
                lwT.n += 16
            yield
        for wi, nel in ((0, float(D) * 4096), (1, 4096.0 * float(D))):
            V.wait_ge(vT.h, vT.n)
            vT.inc(V.tensor_reduce(out=self.acc[:, 64 + wi:65 + wi],
                                   in_=self.acc[:, 32 * wi:32 * (wi + 1)],
                                   axis=AX.X, op=ALU.add))
            SY.wait_ge(vT.h, vT.n)
            SY.dma_start(out=self.scr[wi].rearrange("a b -> b a"),
                         in_=self.acc[:, 64 + wi:65 + wi]).then_inc(fz.h, 16)
            fz.n += 16
            SY.wait_ge(fz.h, fz.n)
            SY.dma_start(out=self.row[:, :], in_=self.scr[wi][:, :]).then_inc(fz.h, 16)
            fz.n += 16
            yield
            V.wait_ge(fz.h, fz.n)
            vT.inc(V.tensor_reduce(out=self.acc[0:1, 64 + wi:65 + wi], in_=self.row[:, :],
                                   axis=AX.X, op=ALU.add))
            S.wait_ge(vT.h, vT.n)
            sT.inc(S.activation(out=self.row[0:1, 0:1], in_=self.acc[0:1, 64 + wi:65 + wi],
                                func=AF.Identity, scale=-0.5 / nel))
            V.wait_ge(sT.h, sT.n)
            for rep in range(7):
                w0 = 1 << rep
                V.wait_ge(vT.h, vT.n)
                vT.inc(V.tensor_copy(out=self.row[0:1, w0:2 * w0], in_=self.row[0:1, 0:w0]))
            SY.wait_ge(vT.h, vT.n)
            SY.dma_start(out=self.scr[2 + wi][:, :], in_=self.row[:, :]).then_inc(fz.h, 16)
            fz.n += 16
            SY.wait_ge(fz.h, fz.n)
            SY.dma_start(out=self.thr[wi][:, :],
                         in_=self.scr[2 + wi].rearrange("a b -> b a")).then_inc(fz.h, 16)
            fz.n += 16
            yield
        thr_gate = fz.n
        ab_done = {}
        for k in range(65):
            if k >= 1:
                pi = k - 1
                wslot = pi % 2
                S.wait_ge(lwT.h, 16 * (64 + k))
                if pi == 0:
                    S.wait_ge(fz.h, thr_gate)
                S.wait_ge(vT.h, vT.n)
                wi = 0 if pi < 32 else 1
                for hh in range(2):
                    if hh == 1:
                        S.wait_ge(vT.h, vT.n)
                    sT.inc(S.activation(out=self.sgn[:, :],
                                        in_=self.wq[:, 512 * hh:512 * (hh + 1)], func=AF.Sign))
                    sT.inc(S.activation(out=self.ab[:, :], in_=self.wq[:, 512 * hh:512 * (hh + 1)],
                                        func=AF.Abs))
                    S.wait_ge(sT.h, sT.n)
                    sT.inc(S.activation(out=self.m[:, :], in_=self.ab[:, :], func=AF.Sign,
                                        bias=self.thr[wi][:, 0:1], scale=1.0))
                    if hh == 1:
                        ab_done[pi] = sT.n
                    V.wait_ge(sT.h, sT.n)
                    vT.inc(V.tensor_scalar_max(out=self.m[:, :], in0=self.m[:, :], scalar1=0.0))
                    V.wait_ge(vT.h, vT.n)
                    V.wait_ge(self.swp[0].h, self.swp[0].n)
                    vT.inc(V.tensor_mul(out=self.wt[:, :], in0=self.m[:, :],
                                        in1=self.sgn[:, :]))
                    SY.wait_ge(vT.h, vT.n)
                    if pi < 32:
                        t, q = pi // 4, pi % 4
                        for fl in range(4 * hh, 4 * hh + 4):
                            o0 = ((8 * q + fl) * 8 + t) * 128
                            SY.dma_start(out=self.wtup[:, o0:o0 + 128],
                                         in_=self.wt[:, 128 * (fl - 4 * hh):128 * (fl - 4 * hh + 1)]
                                         ).then_inc(self.swp[0].h, 16)
                            self.swp[0].n += 16
                    else:
                        p = pi - 32
                        SY.dma_start(out=self.wtdn[128 * p:128 * (p + 1),
                                                   512 * hh:512 * (hh + 1)],
                                     in_=self.wt[:, :]).then_inc(self.swp[0].h, 16)
                        self.swp[0].n += 16
            if k < 64:
                if k >= 1:
                    SY.wait_ge(sT.h, ab_done[k - 1])
                SY.dma_start(out=self.wq[:, :], in_=self._src(k)).then_inc(lwT.h, 16)
                lwT.n += 16
            yield

    def step(self):
        if not self.done and self._gen is not None:
            try:
                next(self._gen)
            except StopIteration:
                self.done = True

    def finish(self):
        while not self.done:
            self.step()



def build_l0_attn(N=N, NCH=NCH, dbg=False, ctx=None, bg=None):
    bg_step = (lambda: None) if bg is None else bg.step
    if ctx is None:
        nc = bass.Bass("TRN2", target_bir_lowering=False)
        stack = contextlib.ExitStack()
        xT = nc.dram_tensor("xT", [D, N], F32, kind="ExternalInput").ap()
    else:
        nc, stack, xT, _out, _waits = ctx
    wqT = nc.dram_tensor("wqT", [D, D], F32, kind="ExternalInput").ap()
    wkT = nc.dram_tensor("wkT", [D, 256], F32, kind="ExternalInput").ap()
    wvT = nc.dram_tensor("wvT", [D, 256], F32, kind="ExternalInput").ap()
    woT = nc.dram_tensor("woT", [D, D], F32, kind="ExternalInput").ap()
    bq = nc.dram_tensor("bq", [64, 16], F32, kind="ExternalInput").ap()
    bk = nc.dram_tensor("bk", [64, 4], F32, kind="ExternalInput").ap()
    bv = nc.dram_tensor("bv", [1, 256], F32, kind="ExternalInput").ap()
    bo = nc.dram_tensor("bo", [128, 8], F32, kind="ExternalInput").ap()
    sel16x = nc.dram_tensor("sel16x", [1, 256], F32, kind="ExternalInput").ap()
    selcol = nc.dram_tensor("selcol", [16, 1024], F32, kind="ExternalInput").ap()
    stk = nc.dram_tensor("stk", [64, 256], F32, kind="ExternalInput").ap()
    if ctx is None:
        x1T = nc.dram_tensor("x1T", [D, N], F32, kind="ExternalOutput").ap()
    else:
        x1T = _out
    if dbg:
        NSd = N // 128
        d_r = nc.dram_tensor("d_r", [1, N], F32, kind="ExternalOutput").ap()
        d_kh = nc.dram_tensor("d_kh", [65, N], F32, kind="ExternalOutput").ap()
        d_va = nc.dram_tensor("d_va", [128, NSd * 260], F32, kind="ExternalOutput").ap()
        d_ob = nc.dram_tensor("d_ob", [64, 16 * CH], F32, kind="ExternalOutput").ap()
        d_rd = nc.dram_tensor("d_rd", [16, CH], F32, kind="ExternalOutput").ap()
        d_op = nc.dram_tensor("d_op", [128, 8 * CH], F32, kind="ExternalOutput").ap()
        d_km = nc.dram_tensor("d_km", [64, 4], F32, kind="ExternalOutput").ap()
        d_qp = nc.dram_tensor("d_qp", [65, CH], F32, kind="ExternalOutput").ap()

    _p = f"u{_uid()}_"
    sb = lambda name, shape, dt: stack.enter_context(nc.sbuf_tensor(_p + name, shape, dt))
    psa = lambda name, shape: stack.enter_context(nc.psum_tensor(_p + name, shape, F32))

    NS = N // 128
    NP_G = NS // 2

    xbc = [[sb(f"xbc{k}_{t}", [128, CH], BF16) for t in range(NT)] for k in range(2)]
    wqb = [sb(f"wqb{t}", [128, D], BF16) for t in range(NT)]
    wkb = [sb(f"wkb{t}", [128, 256], BF16) for t in range(NT)]
    wvb = [sb(f"wvb{t}", [128, 256], BF16) for t in range(NT)]
    wob = [sb(f"wob{t}", [128, D], BF16) for t in range(NT)]
    khead = [sb(f"khead{h}", [65, N], BF16) for h in range(4)]
    v_all = sb("v_all", [128, NS * 260], BF16)
    qplus = [sb(f"qplus{k}", [65, CH], BF16) for k in range(2)]
    qabs = [sb(f"qabs{k}", [64, CH], BF16) for k in range(2)]
    Ebuf = [sb(f"Ebuf{k}", [128, 1024], BF16) for k in range(2)]
    Obuf = sb("Obuf", [64, 16 * CH], BF16)
    Opair = sb("Opair", [128, 8 * CH], BF16)
    stkb = sb("stkb", [64, 256], BF16)
    den2 = sb("den2", [1, 2 * CH], F32)
    rd16 = sb("rd16", [16, CH], F32)
    denbp = [sb(f"denbp{k}", [64, CH], F32) for k in range(2)]
    kmax4 = sb("kmax4", [64, 4], BF16)
    s1row = sb("s1row", [1, CH], F32)
    s2row = sb("s2row", [1, CH], F32)
    mrow = sb("mrow", [1, CH], F32)
    rrows = sb("rrows", [1, N], F32)
    rb = sb("rb", [128, CH], F32)
    S1b = sb("S1b", [1, N], BF16)
    rcolf = sb("rcolf", [128, 1], F32)
    xsq2 = [sb(f"xsq2{k}", [128, CH], BF16) for k in range(2)]
    tev = sb("tev", [128, CH], F32)
    tevv = sb("tevv", [128, 256], F32)
    eps30 = sb("eps30", [128, 1], F32)
    tev2p = [sb(f"tev2p{k}", [128, CH], F32) for k in range(2)]
    x1sb = [sb(f"x1sb{k}", [128, CH], F32) for k in range(2)]
    xres = [sb(f"xres{k}", [128, CH], F32) for k in range(2)]
    wqcs = sb("wqcs", [1, D], BF16)
    wkcs = sb("wkcs", [1, 256], BF16)
    wvcs = sb("wvcs", [1, 256], BF16)
    bvb = sb("bvb", [128, 256], F32)
    ones128 = sb("ones128", [128, 1], BF16)
    onesrow_f = sb("onesrow_f", [1, 128], F32)
    one1f = sb("one1f", [1, 1], F32)
    eps5 = sb("eps5", [1, 1], F32)
    sel16s = sb("sel16s", [1, 256], F32)
    selcs = sb("selcs", [16, 1024], F32)
    bq64s = sb("bq64s", [64, 16], F32)
    bk64 = sb("bk64", [64, 4], F32)
    bvr = sb("bvr", [1, 256], F32)
    bo_c = sb("bo_c", [128, 8], F32)

    pst = [psa(f"pst{k}", [128, 1024]) for k in range(2)]
    pOt = psa("pO", [128, CH])
    pO = pOt
    pdent = psa("pden", [128, CH])
    pden = pdent
    pm1 = psa("pm1", [128, CH])
    pm2 = psa("pm2", [128, CH])

    lw = Sem(nc, stack, "lw")
    lb = Sem(nc, stack, "lb")
    lxp = [Sem(nc, stack, "lxA"), Sem(nc, stack, "lxB")]
    lrp = [Sem(nc, stack, "lrA"), Sem(nc, stack, "lrB")]
    so = Sem(nc, stack, "so")
    sop = [Sem(nc, stack, "soA"), Sem(nc, stack, "soB")]
    tS = Sem(nc, stack, "tS")
    vS = Sem(nc, stack, "vS")
    sS = Sem(nc, stack, "sS")
    t_st = Sem(nc, stack, "t_st")
    s_ex = Sem(nc, stack, "s_ex")
    t_pv = Sem(nc, stack, "t_pv")
    t_pvh = Sem(nc, stack, "t_pvh")
    v_oe = Sem(nc, stack, "v_oe")
    t_pk = Sem(nc, stack, "t_pk")
    v_rd = Sem(nc, stack, "v_rd")
    v_nc = Sem(nc, stack, "v_nc")
    t_cm = Sem(nc, stack, "t_cm")
    v_sq = Sem(nc, stack, "v_sq")
    t_sq = Sem(nc, stack, "t_sq")
    t_std = Sem(nc, stack, "t_std")
    t_pvd = Sem(nc, stack, "t_pvd")
    t_pkd = Sem(nc, stack, "t_pkd")

    T, V, S, G, SY = nc.tensor, nc.vector, nc.scalar, nc.gpsimd, nc.sync
    Vw = make_serial(V, vS)
    Sw = make_serial(S, sS)
    Vn = make_serial(V, vS, auto_inc=False)
    Sn = make_serial(S, sS, auto_inc=False)
    if ctx is not None:
        for ws in _waits:
            G.wait_ge(ws.h, ws.n)
            SY.wait_ge(ws.h, ws.n)

    for t in range(NT):
        G.dma_start(out=wqb[t][:, :], in_=wqT[128 * t:128 * (t + 1), :]).then_inc(lw.h, 16)
        G.dma_start(out=wkb[t][:, :], in_=wkT[128 * t:128 * (t + 1), :]).then_inc(lw.h, 16)
        G.dma_start(out=wvb[t][:, :], in_=wvT[128 * t:128 * (t + 1), :]).then_inc(lw.h, 16)
        G.dma_start(out=wob[t][:, :], in_=woT[128 * t:128 * (t + 1), :]).then_inc(lw.h, 16)
    lw.n += 32 * 16
    for srcap, dst in ((bq, bq64s), (bk, bk64), (bv, bvr), (bo, bo_c),
                       (sel16x, sel16s), (selcol, selcs)):
        SY.dma_start(out=dst[:, :], in_=srcap[:, :]).then_inc(lb.h, 16)
    G.dma_start(out=stkb[:, :], in_=stk[:, :]).then_inc(lw.h, 16)
    lw.n += 16
    lb.n += 6 * 16

    Vw.memset(ones128[:, :], 1.0)
    Vw.memset(onesrow_f[:, :], 1.0)
    Vw.memset(one1f[:, :], 1.0)
    Vw.memset(eps5[:, :], 1e-5)
    Vw.memset(eps30[:, :], 1e-30)
    Vw.memset(v_all[:, :], 1.0)
    for h in range(3):
        Vw.memset(khead[h][64:65, :], 1.0)
    vS.inc(Vn.memset(khead[3][64:65, :], 1.0))
    vS.wait(T)
    vS.wait(S)
    lb.wait(V)
    lb.wait(S)
    vS.inc(Vn.tensor_scalar(out=bq64s[:, :], in0=bq64s[:, :], scalar1=0.125,
                           scalar2=None, op0=ALU.mult))

    lw.wait(T)
    for (wtiles, csb, wid) in ((wqb, wqcs, D), (wkb, wkcs, 256), (wvb, wvcs, 256)):
        for c0 in range(0, wid, CH):
            w = min(CH, wid - c0)
            for t in range(NT):
                T.matmul(pm1[0:1, 0:w], ones128[:, :], wtiles[t][:, c0:c0 + w],
                         start=(t == 0), stop=(t == NT - 1))
            tS.inc(T.drain(fusable=False))
            tS.wait(S)
            sS.inc(Sn.activation(out=csb[0:1, c0:c0 + w], in_=pm1[0:1, 0:w],
                                func=AF.Identity, scale=-1.0 / D))
            sS.wait(T)
    lb.wait(T)
    T.matmul(pm2[:, 0:256], onesrow_f[:, :], bvr[:, :], start=True, stop=True)
    tS.inc(T.drain(fusable=False))
    tS.wait(V)
    vS.inc(Vn.tensor_copy(out=bvb[:, :], in_=pm2[:, 0:256]))
    vS.wait(T)

    t_lastuse = [0, 0]

    def load_chunk(c):
        slot = c % 2
        G.wait_ge(tS.h, t_lastuse[slot])
        for t in range(NT):
            G.dma_start(out=xbc[slot][t][:, :],
                        in_=xT[128 * t:128 * (t + 1), c * CH:(c + 1) * CH]).then_inc(lxp[slot].h, 16)
        lxp[slot].n += NT * 16
        return (slot, lxp[slot].n)

    def stats(c, xc):
        cc = slice(c * CH, (c + 1) * CH)
        lsl, lval = loaded[c]
        V.wait_ge(lxp[lsl].h, lval)
        V.wait_ge(sS.h, sS.n)
        T.wait_ge(lxp[lsl].h, lval)
        sq0, tq0 = v_sq.n, t_sq.n
        for t in range(NT):
            if t >= 2:
                V.wait_ge(t_sq.h, tq0 + t - 1)
            v_sq.inc(Vn.tensor_mul(out=xsq2[t % 2][:, :], in0=xc[t][:, :], in1=xc[t][:, :]))
            T.wait_ge(v_sq.h, sq0 + t + 1)
            T.matmul(pm1[0:1, :], ones128[:, :], xc[t][:, :],
                     start=(t == 0), stop=(t == NT - 1))
            t_sq.inc(T.matmul(pm2[0:1, :], ones128[:, :], xsq2[t % 2][:, :],
                              start=(t == 0), stop=(t == NT - 1)))
        tS.inc(T.drain(fusable=False))
        tS.wait(V)
        Vw.tensor_copy(out=s1row[:, :], in_=pm1[0:1, :])
        Vw.tensor_copy(out=s2row[:, :], in_=pm2[0:1, :])
        Vw.tensor_copy(out=S1b[0:1, cc], in_=s1row[:, :])
        Vw.tensor_scalar(out=mrow[:, :], in0=s1row[:, :], scalar1=1.0 / D,
                        scalar2=None, op0=ALU.mult)
        Vw.tensor_mul(out=mrow[:, :], in0=mrow[:, :], in1=mrow[:, :])
        vS.inc(Vn.scalar_tensor_tensor(out=s1row[:, :], in0=s2row[:, :], scalar=1.0 / D,
                                      in1=mrow[:, :], op0=ALU.mult, op1=ALU.subtract))
        vS.wait(S)
        Sw.activation(out=s2row[:, :], in_=s1row[:, :], func=AF.Ln, bias=eps5[0:1, 0:1], scale=1.0)
        sS.inc(Sn.activation(out=rrows[0:1, cc], in_=s2row[:, :], func=AF.Exp,
                            bias=0.0, scale=-0.5))
        sS.wait(T)
        T.matmul(pm1[:, :], onesrow_f[:, :], rrows[0:1, cc], start=True, stop=True)
        tS.inc(T.drain(fusable=False))
        tS.wait(V)
        vS.inc(Vn.tensor_copy(out=rb[:, :], in_=pm1[:, :]))
        vS.wait(T)

    loaded = [load_chunk(0)]
    for c in range(NCH):
        slot = c % 2
        xc = xbc[slot]
        cc = slice(c * CH, (c + 1) * CH)
        stats(c, xc)

        for h in range(4):
            bg_step()
            for t in range(NT):
                T.matmul(pm1[0:64, :], wkb[t][:, 64 * h:64 * (h + 1)], xc[t][:, :],
                         start=(t == 0), stop=False)
            T.matmul(pm1[0:64, :], wkcs[0:1, 64 * h:64 * (h + 1)],
                     S1b[0:1, cc], start=False, stop=True)
            tS.inc(T.drain(fusable=False))
            tS.wait(V)
            V.wait_ge(sS.h, sS.n)
            vS.inc(Vn.tensor_tensor(out=tev[0:64, :], in0=pm1[0:64, :],
                                   in1=rb[0:64, :], op=ALU.mult))
            vS.wait(S)
            sS.inc(Sn.activation(out=khead[h][0:64, cc], in_=tev[0:64, :],
                                func=AF.Identity, bias=bk64[:, h:h + 1], scale=1.0))
            sS.wait(T)

        for j in range(4):
            s0 = c * CH + j * 128
            o0 = j * 128
            T.matmul(pm1[:, 0:1], rrows[0:1, s0:s0 + 128], one1f[:, :],
                     start=True, stop=True)
            for t in range(NT):
                T.matmul(pm2[:, 0:256], xc[t][:, o0:o0 + 128], wvb[t][:, :],
                         start=(t == 0), stop=False)
            T.matmul(pm2[:, 0:256], S1b[0:1, s0:s0 + 128], wvcs[0:1, :],
                     start=False, stop=True)
            tS.inc(T.drain(fusable=False))
            tS.wait(V)
            Vw.tensor_copy(out=rcolf[:, :], in_=pm1[:, 0:1])
            Vw.tensor_copy(out=tevv[:, :], in_=pm2[:, 0:256])
            st = c * 4 + j
            for kvh in range(4):
                Vw.scalar_tensor_tensor(
                    out=v_all[:, st * 260 + 65 * kvh: st * 260 + 65 * kvh + 64],
                    in0=tevv[:, 64 * kvh:64 * (kvh + 1)], scalar=rcolf[:, 0:1],
                    in1=bvb[:, 64 * kvh:64 * (kvh + 1)], op0=ALU.mult, op1=ALU.add)
            vS.wait(T)
        t_lastuse[slot] = tS.n
        if c + 1 < NCH:
            loaded.append(load_chunk(c + 1))

    V.wait_ge(sS.h, sS.n)
    for h in range(4):
        Vw.tensor_reduce(out=kmax4[:, h:h + 1], in_=khead[h][0:64, :], axis=AX.X,
                         op=ALU.max, apply_absolute_value=True)
    vS.wait(T)

    t_lastuse = [tS.n, tS.n]
    loaded = [load_chunk(0)]
    adds_done = 0
    qtt_done = {}

    def emit_qbuild(c, i):
        H = c * 16 + i
        cc = slice(c * CH, (c + 1) * CH)
        xc = xbc[c % 2]
        kv = i // 4
        qp = qplus[H % 2]
        qa = qabs[H % 2]
        lsl, lval = loaded[c]
        T.wait_ge(lxp[lsl].h, lval)
        if H >= 1:
            T.wait_ge(vS.h, qtt_done[H - 1])
        for t in range(NT):
            T.matmul(pm1[0:64, :], wqb[t][:, 64 * i:64 * (i + 1)], xc[t][:, :],
                     start=(t == 0), stop=False)
        T.matmul(pm1[0:64, :], wqcs[0:1, 64 * i:64 * (i + 1)],
                 S1b[0:1, cc], start=False, stop=True)
        tS.inc(T.drain(fusable=False))
        V.wait_ge(tS.h, tS.n)
        V.wait_ge(sS.h, sS.n)
        vS.inc(Vn.tensor_tensor(out=tev[0:64, :], in0=pm1[0:64, :],
                               in1=rb[0:64, :], op=ALU.mult))
        qtt_done[H] = vS.n
        S.wait_ge(vS.h, qtt_done[H])
        if H >= 2:
            S.wait_ge(t_pv.h, NP_G * (H - 1))
            S.wait_ge(t_cm.h, H - 1)
        Sw.activation(out=qp[0:64, :], in_=tev[0:64, :], func=AF.Identity,
                     bias=bq64s[:, i:i + 1], scale=0.125)
        sS.inc(Sn.activation(out=qa[0:64, :], in_=qp[0:64, :], func=AF.Abs))
        sS.wait(T)
        T.wait_ge(v_nc.h, v_nc.n)
        T.matmul(pm2[0:1, :], kmax4[:, kv:kv + 1], qa[0:64, :],
                 start=True, stop=True)
        t_cm.inc(T.drain(fusable=False))
        t_cm.wait(S)
        v_nc.inc(Sn.activation(out=qp[64:65, :], in_=pm2[0:1, :],
                              func=AF.Identity, scale=-1.0))

    qa_done = {}

    def emit_q_tmms(c, i):
        H = c * 16 + i
        cc = slice(c * CH, (c + 1) * CH)
        xc = xbc[c % 2]
        lsl, lval = loaded[c]
        T.wait_ge(lxp[lsl].h, lval)
        if H >= 1:
            T.wait_ge(vS.h, qtt_done[H - 1])
        qcov = None
        for t in range(NT):
            mm = T.matmul(pm1[0:64, :], wqb[t][:, 64 * i:64 * (i + 1)], xc[t][:, :],
                          start=(t == 0), stop=False)
            if t == 1:
                tS.inc(mm)
                qcov = tS.n
        T.matmul(pm1[0:64, :], wqcs[0:1, 64 * i:64 * (i + 1)],
                 S1b[0:1, cc], start=False, stop=True)
        V.wait_ge(t_st.h, t_st.n + 1)
        V.wait_ge(sS.h, sS.n)
        vS.inc(Vn.tensor_tensor(out=tev[0:64, :], in0=pm1[0:64, :],
                               in1=rb[0:64, :], op=ALU.mult))
        qtt_done[H] = vS.n
        return qcov

    def emit_q_sacts(c, i):
        H = c * 16 + i
        qp = qplus[H % 2]
        qa = qabs[H % 2]
        S.wait_ge(vS.h, qtt_done[H])
        if H >= 2:
            S.wait_ge(t_pv.h, NP_G * (H - 1))
            S.wait_ge(t_cm.h, H - 1)
        Sw.activation(out=qp[0:64, :], in_=tev[0:64, :], func=AF.Identity,
                     bias=bq64s[:, i:i + 1], scale=0.125)
        sS.inc(Sn.activation(out=qa[0:64, :], in_=qp[0:64, :], func=AF.Abs))
        qa_done[H] = sS.n

    def emit_q_cmm(c, i):
        H = c * 16 + i
        kv = i // 4
        qp = qplus[H % 2]
        qa = qabs[H % 2]
        T.wait_ge(sS.h, qa_done[H])
        T.wait_ge(v_nc.h, v_nc.n)
        t_cm.inc(T.matmul(pm2[0:1, :], kmax4[:, kv:kv + 1], qa[0:64, :],
                          start=True, stop=True))
        S.wait_ge(t_st.h, t_st.n + 1)
        v_nc.inc(Sn.activation(out=qp[64:65, :], in_=pm2[0:1, :],
                              func=AF.Identity, scale=-1.0))

    for c in range(NCH):
        slot = c % 2
        cc = slice(c * CH, (c + 1) * CH)
        sS.wait(T)
        T.matmul(pm1[:, :], onesrow_f[:, :], rrows[0:1, cc], start=True, stop=True)
        tS.inc(T.drain(fusable=False))
        tS.wait(V)
        vS.inc(Vn.tensor_copy(out=rb[:, :], in_=pm1[:, :]))
        vS.wait(T)
        emit_qbuild(c, 0)

        for i in range(16):
            bg_step()
            H = c * 16 + i
            kv = i // 4
            qp = qplus[H % 2]
            trigger = min(4, NP_G - 1)

            split = (NP_G == 16)
            T.wait_ge(v_nc.h, H + 1)
            for p in range(NP_G):
                g = NP_G * H + p
                buf = g % 2
                if g >= 2:
                    T.wait_ge(s_ex.h, g - 1)
                for j in range(2):
                    stile = 2 * p + j
                    mm = T.matmul(pst[buf][:, 512 * j:512 * (j + 1)],
                                  khead[kv][:, stile * 128:(stile + 1) * 128],
                                  qp[:, :], start=True, stop=True)
                t_st.inc(mm)
                if p == NP_G - 1 and not split:
                    t_std.inc(T.drain(fusable=False))
                if p == trigger:
                    if i >= 1:
                        T.wait_ge(v_oe.h, H)
                        if i == 1:
                            T.wait_ge(v_rd.h, c)
                        t_pk.inc(T.matmul(pden[0:16, :],
                                          sel16s[0:1, 16 * (i - 1):16 * i],
                                          den2[0:1, ((H - 1) % 2) * CH:((H - 1) % 2 + 1) * CH],
                                          start=(i == 1), stop=False))
                    if i + 1 < 16:
                        if split:
                            emit_q_tmms(c, i + 1)
                        else:
                            emit_qbuild(c, i + 1)
                if split and p == 6 and i + 1 < 16:
                    emit_q_sacts(c, i + 1)
                if p == 0:
                    S.wait_ge(t_st.h, g + 2)
                else:
                    S.wait_ge(t_pv.h, g)
                if g >= 2:
                    S.wait_ge(t_pv.h, g - 1)
                s_ex.inc(Sn.activation(out=Ebuf[buf][:, :], in_=pst[buf][:, :],
                                      func=AF.Exp))
                if p >= 1:
                    gp = g - 1
                    T.wait_ge(s_ex.h, gp + 1)
                    if p == 1:
                        T.wait_ge(v_oe.h, H)
                    for j in range(2):
                        stile = 2 * (p - 1) + j
                        mm = T.matmul(pO[0:65, :],
                                      v_all[:, stile * 260 + 65 * kv:stile * 260 + 65 * (kv + 1)],
                                      Ebuf[gp % 2][:, 512 * j:512 * (j + 1)],
                                      start=(p == 1 and j == 0), stop=False)
                    t_pv.inc(mm)
                if split and p == 8 and i + 1 < 16:
                    emit_q_cmm(c, i + 1)
            g = NP_G * H + (NP_G - 1)
            T.wait_ge(s_ex.h, g + 1)
            if NP_G == 1:
                T.wait_ge(v_oe.h, H)
            for j in range(2):
                stile = 2 * (NP_G - 1) + j
                mm = T.matmul(pO[0:65, :],
                              v_all[:, stile * 260 + 65 * kv:stile * 260 + 65 * (kv + 1)],
                              Ebuf[g % 2][:, 512 * j:512 * (j + 1)],
                              start=(NP_G == 1 and j == 0), stop=(j == 1))
            t_pv.inc(mm)
            if i == 15:
                t_pvd.inc(T.drain(fusable=False))

            V.wait_ge(t_pv.h, NP_G * (H + 1))
            if i == 15:
                V.wait_ge(t_pvd.h, t_pvd.n)
            else:
                V.wait_ge(t_st.h, NP_G * (H + 1) + 1)
            if H >= 2:
                V.wait_ge(t_pk.h, H - 1)
            dslot = H % 2
            Vw.tensor_copy(out=den2[0:1, dslot * CH:(dslot + 1) * CH], in_=pO[64:65, :])
            v_oe.inc(Vn.tensor_copy(out=Obuf[:, i * CH:(i + 1) * CH], in_=pO[0:64, :]))
        T.wait_ge(v_oe.h, v_oe.n)
        t_pk.inc(T.matmul(pden[0:16, :], sel16s[0:1, 240:256],
                          den2[0:1, ((c * 16 + 15) % 2) * CH:((c * 16 + 15) % 2 + 1) * CH],
                          start=False, stop=True))
        t_pkd.inc(T.drain(fusable=False))

        S.wait_ge(t_pkd.h, c + 1)
        sS.inc(Sn.activation(out=denbp[0][0:16, :], in_=pden[0:16, :],
                            func=AF.Identity, bias=eps30[0:16, 0:1], scale=1.0))
        sS.wait(V)
        v_rd.inc(Vn.reciprocal(out=rd16[:, :], in_=denbp[0][0:16, :]))
        nbanks = [pm1, pm2]
        nmm_done = []
        nb_done = []
        mult_done = []
        T.wait_ge(v_rd.h, c + 1)
        T.wait_ge(v_nc.h, v_nc.n)

        def norm_evac(i, cover):
            nslot = i % 2
            V.wait_ge(tS.h, cover)
            if i >= 2:
                V.wait_ge(vS.h, mult_done[i - 2])
            vS.inc(Vn.tensor_copy(out=denbp[nslot][:, :], in_=nbanks[nslot][0:64, :]))
            nb_done.append(vS.n)
            vS.inc(Vn.tensor_tensor(out=Obuf[:, i * CH:(i + 1) * CH],
                                   in0=Obuf[:, i * CH:(i + 1) * CH],
                                   in1=denbp[nslot][:, :], op=ALU.mult))
            mult_done.append(vS.n)

        for i in range(16):
            nslot = i % 2
            if i >= 2:
                T.wait_ge(vS.h, nb_done[i - 2])
            tS.inc(T.matmul(nbanks[nslot][0:64, :], selcs[:, 64 * i:64 * (i + 1)],
                            rd16[:, :], start=True, stop=True))
            nmm_done.append(tS.n)
            if i >= 1:
                norm_evac(i - 1, nmm_done[i])
        tS.inc(T.drain(fusable=False))
        norm_evac(15, tS.n)

        pmm_done = []
        pr_done = []
        for pr in range(8):
            pslot = pr % 2
            if pr < 2:
                T.wait_ge(vS.h, mult_done[15])
            else:
                T.wait_ge(vS.h, pr_done[pr - 2])
            T.matmul(nbanks[pslot][:, :], stkb[:, 0:128],
                     Obuf[:, (2 * pr) * CH:(2 * pr + 1) * CH], start=True, stop=False)
            tS.inc(T.matmul(nbanks[pslot][:, :], stkb[:, 128:256],
                            Obuf[:, (2 * pr + 1) * CH:(2 * pr + 2) * CH],
                            start=False, stop=True))
            pmm_done.append(tS.n)
            if pr >= 1:
                V.wait_ge(tS.h, pmm_done[pr])
                vS.inc(Vn.tensor_copy(out=Opair[:, (pr - 1) * CH:pr * CH],
                                      in_=nbanks[(pr - 1) % 2][:, :]))
                pr_done.append(vS.n)
        tS.inc(T.drain(fusable=False))
        V.wait_ge(tS.h, tS.n)
        vS.inc(Vn.tensor_copy(out=Opair[:, 7 * CH:8 * CH], in_=nbanks[1][:, :]))
        pr_done.append(vS.n)

        omm_done = []
        op_act_done = []
        op_add_done = []
        lr_done = []

        def oproj_tail(dt, cover):
            oslot = dt % 2
            S.wait_ge(tS.h, cover)
            if dt >= 2:
                S.wait_ge(vS.h, op_add_done[dt - 2])
            sS.inc(Sn.activation(out=tev2p[oslot][:, :], in_=nbanks[oslot][:, :],
                                func=AF.Identity, bias=bo_c[:, dt:dt + 1], scale=1.0))
            op_act_done.append(sS.n)
            V.wait_ge(sS.h, op_act_done[dt])
            V.wait_ge(lrp[dt % 2].h, lr_done[dt])
            V.wait_ge(sop[oslot].h, sop[oslot].n)
            vS.inc(Vn.tensor_add(out=x1sb[oslot][:, :], in0=tev2p[oslot][:, :],
                                in1=xres[oslot][:, :]))
            op_add_done.append(vS.n)
            vS.wait(SY)
            SY.dma_start(out=x1T[128 * dt:128 * (dt + 1), cc],
                         in_=x1sb[oslot][:, :]).then_inc(sop[oslot].h, 16)
            sop[oslot].n += 16

        for dt in range(NT):
            oslot = dt % 2
            SY.wait_ge(vS.h, adds_done)
            SY.dma_start(out=xres[oslot][:, :],
                         in_=xT[128 * dt:128 * (dt + 1), cc]).then_inc(lrp[oslot].h, 16)
            lrp[oslot].n += 16
            lr_done.append(lrp[oslot].n)
            if dt == 0:
                T.wait_ge(vS.h, pr_done[7])
                T.wait_ge(sS.h, sS.n)
            if dt >= 2:
                T.wait_ge(sS.h, op_act_done[dt - 2])
            for pr in range(8):
                mm = T.matmul(nbanks[oslot][:, :], wob[pr][:, 128 * dt:128 * (dt + 1)],
                              Opair[:, pr * CH:(pr + 1) * CH],
                              start=(pr == 0), stop=(pr == 7))
            tS.inc(mm)
            omm_done.append(tS.n)
            if dt >= 1:
                oproj_tail(dt - 1, omm_done[dt])
                adds_done = op_add_done[dt - 1]
        tS.inc(T.drain(fusable=False))
        oproj_tail(7, tS.n)
        adds_done = op_add_done[7]
        t_lastuse[slot] = tS.n
        if c + 1 < NCH:
            loaded.append(load_chunk(c + 1))

    SY.wait_ge(sop[0].h, sop[0].n)
    SY.wait_ge(sop[1].h, sop[1].n)
    if ctx is None:
        stack.close()
        return nc
    return sop



def build_l0_ffn(N=N, NCH=NCH, ctx=None):
    if ctx is None:
        nc = bass.Bass("TRN2", target_bir_lowering=False)
        stack = contextlib.ExitStack()
        xT = nc.dram_tensor("xT", [D, N], F32, kind="ExternalInput").ap()
    else:
        nc, stack, xT, _out, _waits = ctx
    m1bd = nc.dram_tensor("m1bd", [8 * 128, 128], F32, kind="ExternalInput").ap()
    m2arr = nc.dram_tensor("m2arr", [32 * 128, 512], F32, kind="ExternalInput").ap()
    brow = nc.dram_tensor("brow", [128, 8], F32, kind="ExternalInput").ap()
    if ctx is None:
        x2T = nc.dram_tensor("x2T", [D, N], F32, kind="ExternalOutput").ap()
    else:
        x2T = _out

    _p = f"u{_uid()}_"
    sb = lambda name, shape, dt: stack.enter_context(nc.sbuf_tensor(_p + name, shape, dt))
    psa = lambda name, shape: stack.enter_context(nc.psum_tensor(_p + name, shape, F32))

    xbc = [[sb(f"xbc{k}_{t}", [128, CH], BF16) for t in range(NT)] for k in range(2)]
    m1b = sb("m1b", [128, 8 * 128], F32)
    m2b = [sb(f"m2b{k}", [128, 512], F32) for k in range(4)]
    wR = sb("wR", [128, 4 * 512], BF16)
    wI = sb("wI", [128, 4 * 512], BF16)
    wIn = sb("wIn", [128, 4 * 512], BF16)
    rsr = sb("rsr", [1, 512], BF16)
    rsi = sb("rsi", [1, 512], BF16)
    t1row = sb("t1row", [1, 512], F32)
    t2row = sb("t2row", [1, 512], F32)
    s1row = sb("s1row", [1, CH], F32)
    s2row = sb("s2row", [1, CH], F32)
    mrow = sb("mrow", [1, CH], F32)
    rrow = sb("rrow", [1, CH], F32)
    rb = sb("rb", [128, CH], F32)
    S1b = sb("S1b", [1, CH], BF16)
    xsq2 = [sb(f"xsq2{k}", [128, CH], BF16) for k in range(2)]
    tev = sb("tev", [128, CH], F32)
    tevp = [sb(f"tevp{k}", [128, CH], F32) for k in range(2)]
    x2sb = [sb(f"x2sb{k}", [128, CH], F32) for k in range(2)]
    xres = [sb(f"xres{k}", [128, CH], F32) for k in range(2)]
    ones128 = sb("ones128", [128, 1], BF16)
    onesrow_f = sb("onesrow_f", [1, 128], F32)
    eps5 = sb("eps5", [1, 1], F32)
    brc = sb("brc", [128, 8], F32)

    pm1 = psa("pm1", [128, CH])
    pm2 = psa("pm2", [128, CH])
    pmw = psa("pmw", [128, CH])

    lw = Sem(nc, stack, "lw")
    lx = Sem(nc, stack, "lx")
    lr = Sem(nc, stack, "lr")
    so = Sem(nc, stack, "so")
    sop = [Sem(nc, stack, "soA"), Sem(nc, stack, "soB")]
    tS = Sem(nc, stack, "tS")
    vS = Sem(nc, stack, "vS")
    sS = Sem(nc, stack, "sS")
    v_sq = Sem(nc, stack, "v_sq")
    t_sq = Sem(nc, stack, "t_sq")

    T, V, S, G, SY = nc.tensor, nc.vector, nc.scalar, nc.gpsimd, nc.sync
    Vw = make_serial(V, vS)
    Sw = make_serial(S, sS)
    Vn = make_serial(V, vS, auto_inc=False)
    Sn = make_serial(S, sS, auto_inc=False)
    if ctx is not None:
        for ws in _waits:
            G.wait_ge(ws.h, ws.n)
            SY.wait_ge(ws.h, ws.n)

    SY.dma_start(out=m1b.rearrange("p (a j) -> p a j", a=8),
                 in_=m1bd.rearrange("(a p) j -> p a j", p=128)).then_inc(lw.h, 16)
    SY.dma_start(out=brc[:, :], in_=brow[:, :]).then_inc(lw.h, 16)
    lw.n += 32
    Vw.memset(ones128[:, :], 1.0)
    Vw.memset(onesrow_f[:, :], 1.0)
    vS.inc(Vn.memset(eps5[:, :], 1e-5))
    vS.wait(T)
    vS.wait(S)

    m2ld = 0
    for RI in range(2):
        for tau in range(4):
            for c in range(4):
                slot = c
                G.wait_ge(tS.h, tS.n)
                G.dma_start(out=m2b[slot][:, :],
                            in_=m2arr[(RI * 16 + c * 4 + tau) * 128:
                                      (RI * 16 + c * 4 + tau) * 128 + 128, :]).then_inc(lx.h, 16)
                lx.n += 16
                T.wait_ge(lx.h, lx.n)
                T.wait_ge(lw.h, 32)
                if c == 0:
                    T.wait_ge(sS.h, sS.n)
                T.matmul(pmw[:, :], m1b[:, (RI * 4 + c) * 128:(RI * 4 + c) * 128 + 128],
                         m2b[slot][:, :], start=(c == 0), stop=(c == 3))
            tS.inc(T.drain(fusable=False))
            tS.wait(S)
            dst = wR if RI == 0 else wI
            Sw.activation(out=dst[:, tau * 512:(tau + 1) * 512], in_=pmw[:, :],
                         func=AF.Identity, scale=1.0)
            if RI == 1:
                sS.inc(Sn.activation(out=wIn[:, tau * 512:(tau + 1) * 512], in_=pmw[:, :],
                                    func=AF.Identity, scale=-1.0))
            else:
                sS.inc(Sn.nop())
    sS.wait(T)
    for tau in range(4):
        T.matmul(pm1[0:1, :], ones128[:, :], wR[:, tau * 512:(tau + 1) * 512],
                 start=(tau == 0), stop=(tau == 3))
    for tau in range(4):
        T.matmul(pm2[0:1, :], ones128[:, :], wI[:, tau * 512:(tau + 1) * 512],
                 start=(tau == 0), stop=(tau == 3))
    tS.inc(T.drain(fusable=False))
    tS.wait(V)
    Vw.tensor_copy(out=t1row[:, :], in_=pm1[0:1, :])
    vS.inc(Vn.tensor_copy(out=t2row[:, :], in_=pm2[0:1, :]))
    vS.wait(S)
    Vw.tensor_sub(out=s1row[0:1, :], in0=t1row[:, :], in1=t2row[:, :])
    Vw.tensor_add(out=s2row[0:1, :], in0=t1row[:, :], in1=t2row[:, :])
    vS.inc(Vn.nop())
    vS.wait(S)
    Sw.activation(out=rsr[0:1, :], in_=s1row[0:1, :], func=AF.Identity, scale=-1.0 / D)
    sS.inc(Sn.activation(out=rsi[0:1, :], in_=s2row[0:1, :],
                        func=AF.Identity, scale=-1.0 / D))
    sS.wait(T)
    sS.wait(V)

    t_lastuse = [tS.n, tS.n]

    def load_chunk(c):
        slot = c % 2
        G.wait_ge(tS.h, t_lastuse[slot])
        for t in range(NT):
            G.dma_start(out=xbc[slot][t][:, :],
                        in_=xT[128 * t:128 * (t + 1), c * CH:(c + 1) * CH]).then_inc(lx.h, 16)
        lx.n += NT * 16
        return lx.n

    loaded = {0: load_chunk(0)}
    adds_done = 0
    for c in range(NCH):
        slot = c % 2
        xc = xbc[slot]
        V.wait_ge(lx.h, loaded[c])
        V.wait_ge(sS.h, sS.n)
        T.wait_ge(lx.h, loaded[c])
        sq0, tq0 = v_sq.n, t_sq.n
        for t in range(NT):
            if t >= 2:
                V.wait_ge(t_sq.h, tq0 + t - 1)
            v_sq.inc(Vn.tensor_mul(out=xsq2[t % 2][:, :], in0=xc[t][:, :], in1=xc[t][:, :]))
            T.wait_ge(v_sq.h, sq0 + t + 1)
            T.matmul(pm1[0:1, :], ones128[:, :], xc[t][:, :], start=(t == 0), stop=False)
            t_sq.inc(T.matmul(pm2[0:1, :], ones128[:, :], xsq2[t % 2][:, :],
                              start=(t == 0), stop=(t == NT - 1)))
        tS.inc(T.drain(fusable=False))
        tS.wait(V)
        Vw.tensor_copy(out=s1row[:, :], in_=pm1[0:1, :])
        Vw.tensor_copy(out=s2row[:, :], in_=pm2[0:1, :])
        Vw.tensor_copy(out=S1b[0:1, :], in_=s1row[:, :])
        Vw.tensor_scalar(out=mrow[:, :], in0=s1row[:, :], scalar1=1.0 / D,
                        scalar2=None, op0=ALU.mult)
        Vw.tensor_mul(out=mrow[:, :], in0=mrow[:, :], in1=mrow[:, :])
        vS.inc(Vn.scalar_tensor_tensor(out=s1row[:, :], in0=s2row[:, :], scalar=1.0 / D,
                                      in1=mrow[:, :], op0=ALU.mult, op1=ALU.subtract))
        vS.wait(S)
        Sw.activation(out=s2row[:, :], in_=s1row[:, :], func=AF.Ln, bias=eps5[0:1, 0:1], scale=1.0)
        sS.inc(Sn.activation(out=rrow[0:1, :], in_=s2row[:, :], func=AF.Exp,
                            bias=0.0, scale=-0.5))
        sS.wait(T)
        T.matmul(pm1[:, :], onesrow_f[:, :], rrow[0:1, :], start=True, stop=True)
        tS.inc(T.drain(fusable=False))
        tS.wait(V)
        vS.inc(Vn.tensor_copy(out=rb[:, :], in_=pm1[:, :]))
        vS.wait(T)

        sbanks = [pm1, pmw]
        tt_done = []
        act_done = []
        add_done_l = []
        for sig in range(8):
            im = sig >= 4
            tpart = sig % 4
            slot = sig % 2
            SY.wait_ge(vS.h, adds_done)
            SY.dma_start(out=xres[slot][:, :],
                         in_=xT[128 * sig:128 * (sig + 1),
                                c * CH:(c + 1) * CH]).then_inc(lr.h, 16)
            lr.n += 16
            if sig >= 2:
                T.wait_ge(vS.h, tt_done[sig - 2])
            wA = wI if im else wR
            wB = wR if im else wIn
            for tau in range(4):
                T.matmul(sbanks[slot][:, :],
                         wA[:, tau * 512 + tpart * 128:tau * 512 + tpart * 128 + 128],
                         xc[tau][:, :], start=(tau == 0), stop=False)
            for tau in range(4):
                T.matmul(sbanks[slot][:, :],
                         wB[:, tau * 512 + tpart * 128:tau * 512 + tpart * 128 + 128],
                         xc[4 + tau][:, :], start=False, stop=False)
            rs = rsi if im else rsr
            T.matmul(sbanks[slot][:, :], rs[0:1, 128 * tpart:128 * (tpart + 1)], S1b[0:1, :],
                     start=False, stop=True)
            tS.inc(T.drain(fusable=False))
            dcount = tS.n
            V.wait_ge(tS.h, dcount)
            if sig >= 2:
                V.wait_ge(sS.h, act_done[sig - 2])
            vS.inc(Vn.tensor_tensor(out=tevp[slot][:, :], in0=sbanks[slot][:, :],
                                    in1=rb[:, :], op=ALU.mult))
            tt_done.append(vS.n)
            S.wait_ge(vS.h, tt_done[sig])
            if sig >= 2:
                S.wait_ge(vS.h, add_done_l[sig - 2])
            sS.inc(Sn.activation(out=xsq2[slot][:, :], in_=tevp[slot][:, :], func=AF.Identity,
                                 bias=brc[:, sig:sig + 1], scale=1.0))
            act_done.append(sS.n)
            V.wait_ge(sS.h, act_done[sig])
            V.wait_ge(lr.h, lr.n)
            V.wait_ge(sop[slot].h, sop[slot].n)
            vS.inc(Vn.tensor_add(out=x2sb[slot][:, :], in0=xsq2[slot][:, :],
                                 in1=xres[slot][:, :]))
            adds_done = vS.n
            add_done_l.append(vS.n)
            vS.wait(SY)
            SY.dma_start(out=x2T[128 * sig:128 * (sig + 1), c * CH:(c + 1) * CH],
                         in_=x2sb[slot][:, :]).then_inc(sop[slot].h, 16)
            sop[slot].n += 16
        t_lastuse[slot] = tS.n
        if c + 1 < NCH:
            loaded[c + 1] = load_chunk(c + 1)

    SY.wait_ge(sop[0].h, sop[0].n)
    SY.wait_ge(sop[1].h, sop[1].n)
    if ctx is None:
        stack.close()
        return nc
    return sop



def build_l1_attn(N=N, NCH=NCH, stage=9, ctx=None, bg=None):
    bg_step = (lambda: None) if bg is None else bg.step
    if ctx is None:
        nc = bass.Bass("TRN2", target_bir_lowering=False)
        stack = contextlib.ExitStack()
        xT = nc.dram_tensor("xT", [D, N], F32, kind="ExternalInput").ap()
    else:
        nc, stack, xT, _out, _waits = ctx
    Wq = nc.dram_tensor("Wq", [D, D], F32, kind="ExternalInput").ap()
    Wk = nc.dram_tensor("Wk", [D, D], F32, kind="ExternalInput").ap()
    Wv = nc.dram_tensor("Wv", [D, D], F32, kind="ExternalInput").ap()
    Wo = nc.dram_tensor("Wo", [D, D], F32, kind="ExternalInput").ap()
    W1 = nc.dram_tensor("W1", [D, 4096], F32, kind="ExternalInput").ap()
    W2 = nc.dram_tensor("W2", [4096, D], F32, kind="ExternalInput").ap()
    bqc = nc.dram_tensor("bqc", [64, 16], F32, kind="ExternalInput").ap()
    bkr = nc.dram_tensor("bkr", [1, D], F32, kind="ExternalInput").ap()
    bvr = nc.dram_tensor("bvr", [1, D], F32, kind="ExternalInput").ap()
    boc = nc.dram_tensor("boc", [128, 8], F32, kind="ExternalInput").ap()
    b1c = nc.dram_tensor("b1c", [128, 32], F32, kind="ExternalInput").ap()
    b2c = nc.dram_tensor("b2c", [128, 8], F32, kind="ExternalInput").ap()
    g1c = nc.dram_tensor("g1c", [128, 8], F32, kind="ExternalInput").ap()
    be1c = nc.dram_tensor("be1c", [128, 8], F32, kind="ExternalInput").ap()
    g2c = nc.dram_tensor("g2c", [128, 8], F32, kind="ExternalInput").ap()
    be2c = nc.dram_tensor("be2c", [128, 8], F32, kind="ExternalInput").ap()
    sel16x = nc.dram_tensor("sel16x1", [1, 256], F32, kind="ExternalInput").ap()
    selcol = nc.dram_tensor("selcol1", [16, 1024], F32, kind="ExternalInput").ap()
    stk = nc.dram_tensor("stk1", [64, 256], F32, kind="ExternalInput").ap()
    W1c = nc.dram_tensor("W1c", [128, 32 * 1024], BF16, kind="Internal").ap()
    W2c = nc.dram_tensor("W2c", [4096, 1024], BF16, kind="Internal").ap()
    if ctx is None:
        x3T = nc.dram_tensor("x3T", [D, N], F32, kind="ExternalOutput").ap()
    else:
        x3T = _out

    _p = f"u{_uid()}_"
    sb = lambda name, shape, dt: stack.enter_context(nc.sbuf_tensor(_p + name, shape, dt))
    psa = lambda name, shape: stack.enter_context(nc.psum_tensor(_p + name, shape, F32))
    if bg is not None:
        bg.alloc(stack)

    NSC = CH // 128

    wqb = [sb(f"wqb{t}", [128, D], BF16) for t in range(NT)]
    wkb = [sb(f"wkb{t}", [128, D], BF16) for t in range(NT)]
    wvb = [sb(f"wvb{t}", [128, D], BF16) for t in range(NT)]
    wob = [sb(f"wob{t}", [128, D], BF16) for t in range(NT)]
    arena = sb("arena", [128, 32 * CH], BF16)
    hbuf = arena
    xbc = [arena[:, 512 * t:512 * (t + 1)] for t in range(NT)]
    srcb = [sb(f"srcb{t}", [128, CH], BF16) for t in range(NT)]
    w1b = [sb(f"w1b{k}", [128, 1024], BF16) for k in range(4)]
    w2b = [sb(f"w2b{k}", [128, 512], BF16) for k in range(4)]
    kf = sb("kf", [128, 1024], F32)
    ebk = sb("ebk", [128, 1024], BF16)
    Ktok = sb("Ktok", [128, 1024], BF16)
    Vtok = sb("Vtok", [128, 1024], BF16)
    KVb = sb("KVb", [64, 1024], BF16)
    KVacc = sb("KVacc", [64, 1024], F32)
    Ksumr = sb("Ksumr", [1, 1024], F32)
    KsumColb = sb("KsumColb", [64, 16], BF16)
    qe = [sb(f"qe{k}", [64, CH], BF16) for k in range(2)]
    Obuf = arena[0:64, 16 * CH:32 * CH]
    Opair = arena[:, 8 * CH:16 * CH]
    stkb = sb("stkb", [64, 256], BF16)
    den2 = sb("den2", [1, 2 * CH], F32)
    rd16 = sb("rd16", [16, CH], F32)
    denbf = sb("denbf", [64, CH], F32)
    uc = srcb
    xln = [sb(f"xln{t}", [128, CH], BF16) for t in range(NT)]
    u2c = xln
    tev = sb("tev", [128, CH], F32)
    tevb = sb("tevb", [128, CH], F32)
    ta = sb("ta", [64, CH], F32)
    x3sb = [sb(f"x3sb{k}", [128, CH], F32) for k in range(2)]
    xres = [sb(f"xres{k}", [128, CH], F32) for k in range(2)]
    s1row = sb("s1row", [1, CH], F32)
    s2row = sb("s2row", [1, CH], F32)
    mrow = sb("mrow", [1, CH], F32)
    mr2 = sb("mr2", [1, CH], F32)
    rrow = sb("rrow", [1, CH], F32)
    rb = sb("rb", [128, CH], F32)
    mrb = sb("mrb", [128, CH], F32)
    xsq2 = [sb(f"xsq2{k}", [128, CH], BF16) for k in range(2)]
    bkb = sb("bkb", [128, D], BF16)
    bvb = sb("bvb", [128, D], BF16)
    ones128 = sb("ones128", [128, 1], BF16)
    onesrow_f = sb("onesrow_f", [1, 128], F32)
    one1f = sb("one1f", [1, 1], F32)
    eps5 = sb("eps5", [1, 1], F32)
    eps6 = sb("eps6", [128, 1], F32)
    bq64 = sb("bq64", [64, 16], F32)
    bkrr = kf[0:1, :]
    bo_c = sb("bo_c", [128, 8], F32)
    b1_c = sb("b1_c", [128, 32], F32)
    b2_c = sb("b2_c", [128, 8], F32)
    g1_c = sb("g1_c", [128, 8], F32)
    be1_c = sb("be1_c", [128, 8], F32)
    g2_c = sb("g2_c", [128, 8], F32)
    be2_c = sb("be2_c", [128, 8], F32)
    sel16s = sb("sel16s", [1, 256], F32)
    selcs = sb("selcs", [16, 1024], F32)

    pk = [psa(f"pk{k}", [128, CH]) for k in range(2)]
    pkv = [psa(f"pkv{k}", [128, CH]) for k in range(2)]
    pks = psa("pks", [128, CH])
    pm1 = psa("pm1", [128, CH])
    pm2 = psa("pm2", [128, CH])
    pm3 = psa("pm3", [128, CH])

    lw = Sem(nc, stack, "lw")
    lb1 = Sem(nc, stack, "lb1")
    lx = Sem(nc, stack, "lx")
    lfp = [Sem(nc, stack, f"lf{k}") for k in range(4)]
    lfd = [Sem(nc, stack, f"lfd{k}") for k in range(4)]
    w1cw = Sem(nc, stack, "w1cw")
    w2cw = Sem(nc, stack, "w2cw")
    lr = Sem(nc, stack, "lr")
    so = Sem(nc, stack, "so")
    sop = [Sem(nc, stack, "soA"), Sem(nc, stack, "soB")]
    tS = Sem(nc, stack, "tS")
    vS = Sem(nc, stack, "vS")
    sS = Sem(nc, stack, "sS")
    t_pk = Sem(nc, stack, "t_pk")
    t_pkd = Sem(nc, stack, "t_pkd")
    v_rd = Sem(nc, stack, "v_rd")

    T, V, S, G, SY = nc.tensor, nc.vector, nc.scalar, nc.gpsimd, nc.sync
    Vw = make_serial(V, vS)
    Sw = make_serial(S, sS)
    Vn = make_serial(V, vS, auto_inc=False)
    Sn = make_serial(S, sS, auto_inc=False)
    if ctx is not None:
        for ws in _waits:
            G.wait_ge(ws.h, ws.n)
            SY.wait_ge(ws.h, ws.n)

    for t in range(NT):
        G.dma_start(out=wqb[t][:, :], in_=Wq[128 * t:128 * (t + 1), :]).then_inc(lw.h, 16)
        G.dma_start(out=wkb[t][:, :], in_=Wk[128 * t:128 * (t + 1), :]).then_inc(lw.h, 16)
        G.dma_start(out=wvb[t][:, :], in_=Wv[128 * t:128 * (t + 1), :]).then_inc(lw.h, 16)
        G.dma_start(out=wob[t][:, :], in_=Wo[128 * t:128 * (t + 1), :]).then_inc(lw.h, 16)
    G.dma_start(out=stkb[:, :], in_=stk[:, :]).then_inc(lw.h, 16)
    lw.n += 33 * 16
    for srcap, dst in ((bqc, bq64), (bkr, bkrr), (boc, bo_c), (b1c, b1_c),
                       (b2c, b2_c), (g1c, g1_c), (be1c, be1_c), (g2c, g2_c), (be2c, be2_c),
                       (sel16x, sel16s), (selcol, selcs)):
        SY.dma_start(out=dst[:, :], in_=srcap[:, :]).then_inc(lb1.h, 16)
    lb1.n += 11 * 16
    Vw.memset(ones128[:, :], 1.0)
    Vw.memset(onesrow_f[:, :], 1.0)
    Vw.memset(one1f[:, :], 1.0)
    Vw.memset(eps5[:, :], 1e-5)
    Vw.memset(KVacc[:, :], 0.0)
    vS.inc(Vn.memset(eps6[:, :], 1e-6))
    vS.wait(T)
    vS.wait(S)
    lw.wait(T)
    lb1.wait(T)
    lb1.wait(S)
    lb1.wait(V)
    T.matmul(pm1[:, :], onesrow_f[:, :], bkrr[0:1, 0:512], start=True, stop=True)
    tS.inc(T.matmul(pm2[:, :], onesrow_f[:, :], bkrr[0:1, 512:1024], start=True, stop=True))
    SY.wait_ge(tS.h, tS.n)
    SY.dma_start(out=bkrr[:, :], in_=bvr[:, :]).then_inc(lb1.h, 16)
    lb1.n += 16
    lb1.wait(T)
    T.matmul(pm3[:, :], onesrow_f[:, :], bkrr[0:1, 0:512], start=True, stop=True)
    T.matmul(pk[0][:, :], onesrow_f[:, :], bkrr[0:1, 512:1024], start=True, stop=True)
    tS.inc(T.drain(fusable=False))
    tS.wait(V)
    Vw.tensor_copy(out=bkb[:, 0:512], in_=pm1[:, :])
    Vw.tensor_copy(out=bkb[:, 512:1024], in_=pm2[:, :])
    Vw.tensor_copy(out=bvb[:, 0:512], in_=pm3[:, :])
    vS.inc(Vn.tensor_copy(out=bvb[:, 512:1024], in_=pk[0][:, :]))
    vS.wait(T)

    def load_chunk(c, gate):
        G.wait_ge(tS.h, gate)
        for t in range(NT):
            G.dma_start(out=xbc[t][:, :],
                        in_=xT[128 * t:128 * (t + 1), c * CH:(c + 1) * CH]).then_inc(lx.h, 16)
        lx.n += NT * 16
        return lx.n

    def stats_rows(tiles, into_s1b=None):
        sq0 = vS.n
        for t in range(NT):
            if t >= 2:
                V.wait_ge(tS.h, tS.n)
            vS.inc(Vn.tensor_mul(out=xsq2[t % 2][:, :], in0=tiles[t][:, :], in1=tiles[t][:, :]))
            V_cnt = vS.n
            T.wait_ge(vS.h, V_cnt)
            T.matmul(pm1[0:1, :], ones128[:, :], tiles[t][:, :],
                     start=(t == 0), stop=(t == NT - 1))
            tS.inc(T.matmul(pm2[0:1, :], ones128[:, :], xsq2[t % 2][:, :],
                            start=(t == 0), stop=(t == NT - 1)))
        tS.inc(T.drain(fusable=False))
        tS.wait(V)
        Vw.tensor_copy(out=s1row[:, :], in_=pm1[0:1, :])
        Vw.tensor_copy(out=s2row[:, :], in_=pm2[0:1, :])
        Vw.tensor_scalar(out=mrow[:, :], in0=s1row[:, :], scalar1=1.0 / D,
                        scalar2=None, op0=ALU.mult)
        Vw.tensor_mul(out=mr2[:, :], in0=mrow[:, :], in1=mrow[:, :])
        vS.inc(Vn.scalar_tensor_tensor(out=s1row[:, :], in0=s2row[:, :], scalar=1.0 / D,
                                      in1=mr2[:, :], op0=ALU.mult, op1=ALU.subtract))
        vS.wait(S)
        Sw.activation(out=s2row[:, :], in_=s1row[:, :], func=AF.Ln, bias=eps5[0:1, 0:1], scale=1.0)
        sS.inc(Sn.activation(out=rrow[0:1, :], in_=s2row[:, :], func=AF.Exp,
                            bias=0.0, scale=-0.5))
        sS.wait(V)
        vS.inc(Vn.tensor_mul(out=mr2[:, :], in0=mrow[:, :], in1=rrow[:, :]))
        vS.wait(T)
        sS.wait(T)
        T.matmul(pm1[:, :], onesrow_f[:, :], rrow[0:1, :], start=True, stop=True)
        T.matmul(pm2[:, :], onesrow_f[:, :], mr2[0:1, :], start=True, stop=True)
        tS.inc(T.drain(fusable=False))
        tS.wait(V)
        Vw.tensor_copy(out=rb[:, :], in_=pm1[:, :])
        vS.inc(Vn.tensor_copy(out=mrb[:, :], in_=pm2[:, :]))
        vS.wait(T)

    def ln_apply(src_tiles, out_tiles, gcol, bcol):
        for t in range(NT):
            V.wait_ge(sS.h, sS.n)
            Vw.tensor_tensor(out=tev[:, :], in0=src_tiles[t][:, :], in1=rb[:, :], op=ALU.mult)
            vS.inc(Vn.tensor_sub(out=tevb[:, :], in0=tev[:, :], in1=mrb[:, :]))
            vS.wait(S)
            sS.inc(Sn.activation(out=out_tiles[t][:, :], in_=tevb[:, :], func=AF.Identity,
                                scale=gcol[:, t:t + 1] if gcol is not None else 1.0,
                                bias=bcol[:, t:t + 1] if bcol is not None else 0.0))
        sS.wait(V)
        sS.wait(T)

    KV_first = True
    adds_done = 0
    w1_last = [0, 0, 0, 0]
    w2_last = [0, 0, 0, 0]
    gate = tS.n
    for c in range(NCH):
        loaded = load_chunk(c, gate)
        V.wait_ge(lx.h, loaded)
        T.wait_ge(lx.h, loaded)
        stats_rows(xbc)
        ln_apply(xbc, srcb, None, None)

        for j in range(NSC if stage >= 2 else 0):
            bg_step()
            o0 = j * 128
            for half in range(2):
                for t in range(NT):
                    T.matmul(pk[half][:, :], srcb[t][:, o0:o0 + 128],
                             wkb[t][:, 512 * half:512 * (half + 1)],
                             start=(t == 0), stop=(t == NT - 1))
            tS.inc(T.drain(fusable=False))
            tS.wait(V)
            Vw.tensor_copy(out=kf[:, 0:512], in_=pk[0][:, :])
            vS.inc(Vn.tensor_copy(out=kf[:, 512:1024], in_=pk[1][:, :]))
            vS.wait(T)
            if stage < 2.5:
                continue
            for half in range(2):
                for t in range(NT):
                    T.matmul(pk[half][:, :], srcb[t][:, o0:o0 + 128],
                             wvb[t][:, 512 * half:512 * (half + 1)],
                             start=(t == 0), stop=(t == NT - 1))
            tS.inc(T.drain(fusable=False))
            if stage < 2.7:
                continue
            Vw.tensor_add(out=kf[:, :], in0=kf[:, :], in1=bkb[:, :])
            Vw.tensor_scalar_min(out=ebk[:, :], in0=kf[:, :], scalar1=0.0)
            vS.inc(Vn.nop())
            vS.wait(S)
            sS.inc(Sn.activation(out=ebk[:, :], in_=ebk[:, :], func=AF.Exp))
            Vw.tensor_scalar_max(out=kf[:, :], in0=kf[:, :], scalar1=0.0)
            sS.wait(V)
            vS.inc(Vn.tensor_add(out=Ktok[:, :], in0=kf[:, :], in1=ebk[:, :]))
            tS.wait(V)
            Vw.tensor_copy(out=kf[:, 0:512], in_=pk[0][:, :])
            Vw.tensor_copy(out=kf[:, 512:1024], in_=pk[1][:, :])
            vS.inc(Vn.tensor_add(out=Vtok[:, :], in0=kf[:, :], in1=bvb[:, :]))
            vS.wait(T)
            if stage < 2.9:
                continue
            KV_last = (c == NCH - 1 and j == NSC - 1)
            for i in range(16):
                bank = pkv[i // 8]
                T.matmul(bank[0:64, 64 * (i % 8):64 * (i % 8) + 64],
                         Ktok[:, 64 * i:64 * (i + 1)], Vtok[:, 64 * i:64 * (i + 1)],
                         start=True, stop=True)
            T.matmul(pks[0:1, :], ones128[:, :], Ktok[:, 0:512],
                     start=KV_first, stop=KV_last)
            T.matmul(pm3[0:1, :], ones128[:, :], Ktok[:, 512:1024],
                     start=KV_first, stop=KV_last)
            KV_first = False
            tS.inc(T.drain(fusable=False))
            tS.wait(V)
            for i in range(16):
                bank = pkv[i // 8]
                Vw.tensor_add(out=KVacc[:, 64 * i:64 * (i + 1)],
                              in0=KVacc[:, 64 * i:64 * (i + 1)],
                              in1=bank[0:64, 64 * (i % 8):64 * (i % 8) + 64])
            vS.wait(T)
        gate = tS.n
    if stage < 3:
        SY.wait_ge(so.h, so.n)
        stack.close()
        return nc
    tS.wait(V)
    Vw.tensor_copy(out=KVb[:, :], in_=KVacc[:, :])
    Vw.tensor_copy(out=Ksumr[0:1, 0:512], in_=pks[0:1, :])
    vS.inc(Vn.tensor_copy(out=Ksumr[0:1, 512:1024], in_=pm3[0:1, :]))
    vS.wait(T)
    for i in range(16):
        T.matmul(pm1[0:64, i:i + 1], Ksumr[0:1, 64 * i:64 * (i + 1)], one1f[:, :],
                 start=True, stop=True)
    tS.inc(T.drain(fusable=False))
    tS.wait(S)
    sS.inc(Sn.activation(out=KsumColb[:, :], in_=pm1[0:64, 0:16], func=AF.Identity, scale=1.0))
    sS.wait(T)
    sS.wait(V)

    gate2 = tS.n
    for c in range(NCH):
        loaded = load_chunk(c, gate2)
        V.wait_ge(lx.h, loaded)
        T.wait_ge(lx.h, loaded)
        stats_rows(xbc)
        ln_apply(xbc, srcb, None, None)

        for i in range(16 if stage >= 4 else 0):
            bg_step()
            H = c * 16 + i
            T.wait_ge(vS.h, vS.n)
            for t in range(NT):
                T.matmul(pm1[0:64, :], wqb[t][:, 64 * i:64 * (i + 1)], srcb[t][:, :],
                         start=(t == 0), stop=(t == NT - 1))
            tS.inc(T.drain(fusable=False))
            tS.wait(S)
            if H >= 2:
                S.wait_ge(t_pk.h, H - 1)
            Sw.activation(out=tev[0:64, :], in_=pm1[0:64, :], func=AF.Identity,
                         bias=bq64[:, i:i + 1], scale=1.0)
            sS.inc(Sn.nop())
            sS.wait(V)
            Vw.tensor_scalar_min(out=ta[:, :], in0=tev[0:64, :], scalar1=0.0)
            vS.inc(Vn.nop())
            vS.wait(S)
            sS.inc(Sn.activation(out=ta[:, :], in_=ta[:, :], func=AF.Exp))
            Vw.tensor_scalar_max(out=tev[0:64, :], in0=tev[0:64, :], scalar1=0.0)
            sS.wait(V)
            vS.inc(Vn.tensor_add(out=qe[H % 2][:, :], in0=tev[0:64, :], in1=ta[:, :]))
            vS.wait(T)
            dslot = H % 2
            T.matmul(pm2[0:1, :], KsumColb[:, i:i + 1], qe[H % 2][:, :],
                     start=True, stop=True)
            T.matmul(pm3[0:64, :], KVb[:, 64 * i:64 * (i + 1)], qe[H % 2][:, :],
                     start=True, stop=True)
            t_pk.inc(T.drain(fusable=False))
            V.wait_ge(t_pk.h, t_pk.n)
            Vw.tensor_copy(out=den2[0:1, dslot * CH:(dslot + 1) * CH], in_=pm2[0:1, :])
            vS.inc(Vn.tensor_copy(out=Obuf[:, i * CH:(i + 1) * CH], in_=pm3[0:64, :]))
            vS.wait(T)
            if i == 0:
                T.wait_ge(v_rd.h, c)
            T.matmul(pk[0][0:16, :], sel16s[0:1, 16 * i:16 * (i + 1)],
                     den2[0:1, dslot * CH:(dslot + 1) * CH],
                     start=(i == 0), stop=(i == 15))
            if i == 15:
                t_pkd.inc(T.drain(fusable=False))
        if stage < 5:
            gate2 = tS.n
            continue
        S.wait_ge(t_pkd.h, c + 1)
        sS.inc(Sn.activation(out=denbf[0:16, :], in_=pk[0][0:16, :], func=AF.Identity,
                            bias=eps6[0:16, 0:1], scale=1.0))
        sS.wait(V)
        v_rd.inc(Vn.reciprocal(out=rd16[:, :], in_=denbf[0:16, :]))
        for i in range(16):
            T.wait_ge(v_rd.h, c + 1)
            T.wait_ge(vS.h, vS.n)
            T.matmul(pm1[0:64, :], selcs[:, 64 * i:64 * (i + 1)], rd16[:, :],
                     start=True, stop=True)
            tS.inc(T.drain(fusable=False))
            tS.wait(V)
            Vw.tensor_copy(out=denbf[:, :], in_=pm1[0:64, :])
            vS.inc(Vn.tensor_tensor(out=Obuf[:, i * CH:(i + 1) * CH],
                                   in0=Obuf[:, i * CH:(i + 1) * CH],
                                   in1=denbf[:, :], op=ALU.mult))
        for pr in range(8):
            T.wait_ge(vS.h, vS.n)
            T.matmul(pm1[:, :], stkb[:, 0:128], Obuf[:, (2 * pr) * CH:(2 * pr + 1) * CH],
                     start=True, stop=False)
            T.matmul(pm1[:, :], stkb[:, 128:256],
                     Obuf[:, (2 * pr + 1) * CH:(2 * pr + 2) * CH],
                     start=False, stop=True)
            tS.inc(T.drain(fusable=False))
            tS.wait(V)
            vS.inc(Vn.tensor_copy(out=Opair[:, pr * CH:(pr + 1) * CH], in_=pm1[:, :]))
        for dt in range(NT):
            T.wait_ge(vS.h, vS.n)
            for pr in range(8):
                T.matmul(pm2[:, :], wob[pr][:, 128 * dt:128 * (dt + 1)],
                         Opair[:, pr * CH:(pr + 1) * CH],
                         start=(pr == 0), stop=(pr == 7))
            tS.inc(T.drain(fusable=False))
            tS.wait(S)
            sS.inc(Sn.activation(out=tev[:, :], in_=pm2[:, :], func=AF.Identity,
                                bias=bo_c[:, dt:dt + 1], scale=1.0))
            sS.wait(V)
            vS.inc(Vn.tensor_add(out=uc[dt][:, :], in0=tev[:, :], in1=srcb[dt][:, :]))
        vS.wait(T)
        vS.wait(S)

        if stage < 6:
            gate2 = tS.n
            continue
        stats_rows(uc)
        ln_apply(uc, xln, g1_c, be1_c)

        hbanks = [pm1, pk[0]]
        gelu_done = []
        mm_done = []
        for ft in range(32):
            slot = ft % 4
            G.wait_ge(tS.h, mm_done[ft - 4] if ft >= 4 else w1_last[slot])
            if c == 0:
                if ft >= 4:
                    G.wait_ge(w1cw.h, 16 * (ft - 3))
                G.dma_start(out=w1b[slot].rearrange("p (t j) -> p t j", t=8),
                            in_=W1.rearrange("(t p) f -> p t f", p=128)[:, :, 128 * ft:128 * (ft + 1)]
                            ).then_inc(lfp[slot].h, 16)
            else:
                if c == 1 and ft == 0:
                    G.wait_ge(w1cw.h, 16 * 32)
                G.dma_start(out=w1b[slot][:, :],
                            in_=W1c[:, 1024 * ft:1024 * (ft + 1)]).then_inc(lfp[slot].h, 16)
            lfp[slot].n += 16
            if c == 0:
                S.wait_ge(lfp[slot].h, lfp[slot].n)
                S.dma_start(out=W1c[:, 1024 * ft:1024 * (ft + 1)],
                            in_=w1b[slot][:, :]).then_inc(w1cw.h, 16)
            T.wait_ge(lfp[slot].h, lfp[slot].n)
            if ft >= 2:
                T.wait_ge(sS.h, gelu_done[ft - 2])
            for t in range(NT):
                mm = T.matmul(hbanks[ft % 2][:, :], w1b[slot][:, 128 * t:128 * (t + 1)],
                              xln[t][:, :], start=(t == 0), stop=(t == NT - 1))
            tS.inc(mm)
            mm_done.append(tS.n)
            w1_last[slot] = tS.n
            if ft >= 1:
                S.wait_ge(tS.h, mm_done[ft])
                sS.inc(Sn.activation(out=hbuf[:, (ft - 1) * CH:ft * CH],
                                     in_=hbanks[(ft - 1) % 2][:, :],
                                     func=AF.Gelu, bias=b1_c[:, ft - 1:ft], scale=1.0))
                gelu_done.append(sS.n)
        tS.inc(T.drain(fusable=False))
        S.wait_ge(tS.h, tS.n)
        sS.inc(Sn.activation(out=hbuf[:, 31 * CH:32 * CH], in_=hbanks[31 % 2][:, :],
                             func=AF.Gelu, bias=b1_c[:, 31:32], scale=1.0))
        gelu_done.append(sS.n)
        sS.wait(T)
        ybanks = [pm2, pm3, pk[0], pk[1]]
        if stage < 7:
            gate2 = tS.n
            continue
        for half in range(2):
            dn_done = []
            for ft in range(32):
                slot = ft % 4
                G.wait_ge(tS.h, dn_done[ft - 4] if ft >= 4 else w2_last[slot])
                if c == 0:
                    nw = half * 32 + ft
                    if nw >= 4:
                        G.wait_ge(w2cw.h, 16 * (nw - 3))
                    G.dma_start(out=w2b[slot][:, :],
                                in_=W2[128 * ft:128 * (ft + 1),
                                       512 * half:512 * (half + 1)]).then_inc(lfd[slot].h, 16)
                else:
                    if c == 1 and half == 0 and ft == 0:
                        G.wait_ge(w2cw.h, 16 * 64)
                    G.dma_start(out=w2b[slot][:, :],
                                in_=W2c[128 * ft:128 * (ft + 1),
                                        512 * half:512 * (half + 1)]).then_inc(lfd[slot].h, 16)
                lfd[slot].n += 16
                if c == 0:
                    S.wait_ge(lfd[slot].h, lfd[slot].n)
                    S.dma_start(out=W2c[128 * ft:128 * (ft + 1),
                                        512 * half:512 * (half + 1)],
                                in_=w2b[slot][:, :]).then_inc(w2cw.h, 16)
                T.wait_ge(lfd[slot].h, lfd[slot].n)
                for dl in range(4):
                    mm = T.matmul(ybanks[dl][:, :], w2b[slot][:, 128 * dl:128 * (dl + 1)],
                                  hbuf[:, ft * CH:(ft + 1) * CH],
                                  start=(ft == 0), stop=(ft == 31))
                tS.inc(mm)
                dn_done.append(tS.n)
                w2_last[slot] = tS.n
            tS.inc(T.drain(fusable=False))
            tS.wait(S)
            for dl in range(4):
                dt = half * 4 + dl
                S.wait_ge(vS.h, vS.n)
                sS.inc(Sn.activation(out=tev[:, :], in_=ybanks[dl][:, :], func=AF.Identity,
                                    bias=b2_c[:, dt:dt + 1], scale=1.0))
                sS.wait(V)
                vS.inc(Vn.tensor_add(out=u2c[dt][:, :], in0=tev[:, :], in1=xln[dt][:, :]))
            vS.wait(T)
        vS.wait(S)

        if stage < 8:
            gate2 = tS.n
            continue
        stats_rows(u2c)
        for dt in range(NT):
            SY.wait_ge(vS.h, adds_done)
            SY.dma_start(out=xres[dt % 2][:, :],
                         in_=xT[128 * dt:128 * (dt + 1), c * CH:(c + 1) * CH]).then_inc(lr.h, 16)
            lr.n += 16
            V.wait_ge(sS.h, sS.n)
            Vw.tensor_tensor(out=tev[:, :], in0=u2c[dt][:, :], in1=rb[:, :], op=ALU.mult)
            vS.inc(Vn.tensor_sub(out=tevb[:, :], in0=tev[:, :], in1=mrb[:, :]))
            vS.wait(S)
            sS.inc(Sn.activation(out=tev[:, :], in_=tevb[:, :], func=AF.Identity,
                                scale=g2_c[:, dt:dt + 1], bias=be2_c[:, dt:dt + 1]))
            sS.wait(V)
            V.wait_ge(lr.h, lr.n)
            V.wait_ge(sop[dt % 2].h, sop[dt % 2].n)
            vS.inc(Vn.tensor_add(out=x3sb[dt % 2][:, :], in0=tev[:, :], in1=xres[dt % 2][:, :]))
            adds_done = vS.n
            vS.wait(SY)
            SY.dma_start(out=x3T[128 * dt:128 * (dt + 1), c * CH:(c + 1) * CH],
                         in_=x3sb[dt % 2][:, :]).then_inc(sop[dt % 2].h, 16)
            sop[dt % 2].n += 16
        vS.wait(T)
        gate2 = tS.n

    if bg is not None:
        bg.finish()
    SY.wait_ge(sop[0].h, sop[0].n)
    SY.wait_ge(sop[1].h, sop[1].n)
    if ctx is None:
        stack.close()
        return nc
    return sop



def build_l1_ffn(N=N, NCH=NCH, dbg=False, ctx=None, bg=None):
    if ctx is None:
        nc = bass.Bass("TRN2", target_bir_lowering=False)
        stack = contextlib.ExitStack()
        xT = nc.dram_tensor("xT", [D, N], F32, kind="ExternalInput").ap()
    else:
        nc, stack, xT, _out, _waits = ctx
    if bg is None:
        wupT = nc.dram_tensor("wupT", [D, 4096], F32, kind="ExternalInput").ap()
        wdnT = nc.dram_tensor("wdnT", [4096, D], F32, kind="ExternalInput").ap()
    else:
        wupT, wdnT = bg.wupT, bg.wdnT
    if ctx is None:
        outT = nc.dram_tensor("outT", [D, N], F32, kind="ExternalOutput").ap()
    else:
        outT = _out
    if bg is None:
        wtup = nc.dram_tensor("wtup", [128, 32 * 1024], BF16, kind="Internal").ap()
        wtdn = nc.dram_tensor("wtdn", [4096, D], BF16, kind="Internal").ap()
    else:
        wtup, wtdn = bg.wtup, bg.wtdn
    if dbg:
        d_wtu = nc.dram_tensor("d_wtu", [D, 4096], mybir.dt.bfloat16, kind="ExternalOutput").ap()
        d_xn = nc.dram_tensor("d_xn", [128, CH], mybir.dt.bfloat16, kind="ExternalOutput").ap()
        d_h = nc.dram_tensor("d_h", [128, 2 * CH], mybir.dt.bfloat16, kind="ExternalOutput").ap()

    _p = f"u{_uid()}_"
    sb = lambda name, shape, dt: stack.enter_context(nc.sbuf_tensor(_p + name, shape, dt))
    psa = lambda name, shape: stack.enter_context(nc.psum_tensor(_p + name, shape, F32))

    arena = sb("arena", [128, 32 * CH], BF16)
    hbuf = arena
    xbc = [arena[:, 512 * t:512 * (t + 1)] for t in range(NT)]
    xnb = [sb(f"xnb{t}", [128, CH], BF16) for t in range(NT)]
    if bg is None:
        wq1 = [sb(f"wq1{k}", [128, 1024], F32) for k in range(2)]
        sgn = sb("sgn", [128, 1024], F32)
        ab = sb("ab", [128, 1024], F32)
        wtb = [sb(f"wtb{k}", [128, 1024], BF16) for k in range(2)]
        acc = sb("acc", [128, 33], F32)
        thr1 = sb("thr1", [1, 1], F32)
        negthrb = sb("negthrb", [128, 1024], F32)
        ab2 = sb("ab2", [128, 1024], F32)
    wupb = [sb(f"wupb{k}", [128, 1024], BF16) for k in range(4)]
    wdnb = [sb(f"wdnb{k}", [128, 512], BF16) for k in range(4)]
    hsq2p = [sb(f"hsq2p{k}", [128, CH], BF16) for k in range(2)]
    onescol_f = sb("onescol_f", [128, 1], F32)
    s1row = sb("s1row", [1, CH], F32)
    s2row = sb("s2row", [1, CH], F32)
    mrow = sb("mrow", [1, CH], F32)
    mr2 = sb("mr2", [1, CH], F32)
    rhrow = sb("rhrow", [1, CH], F32)
    r2row = sb("r2row", [1, CH], F32)
    rb = sb("rb", [128, CH], F32)
    mrb = sb("mrb", [128, CH], F32)
    r2b = sb("r2b", [128, CH], F32)
    xsq2 = [sb(f"xsq2{k}", [128, CH], BF16) for k in range(2)]
    tev = sb("tev", [128, CH], F32)
    tevb = sb("tevb", [128, CH], F32)
    x4sb = [sb(f"x4sb{k}", [128, CH], F32) for k in range(2)]
    xres = [sb(f"xres{k}", [128, CH], F32) for k in range(2)]
    ones128 = sb("ones128", [128, 1], BF16)
    onesrow_f = sb("onesrow_f", [1, 128], F32)
    eps5 = sb("eps5", [1, 1], F32)

    pm1 = psa("pm1", [128, CH])
    pm2 = psa("pm2", [128, CH])
    ybanks = [psa(f"py{k}", [128, CH]) for k in range(4)]

    lw = Sem(nc, stack, "lw")
    lx = Sem(nc, stack, "lx")
    lfp = [Sem(nc, stack, f"lf{k}") for k in range(4)]
    lfd = [Sem(nc, stack, f"lfd{k}") for k in range(4)]
    lr = Sem(nc, stack, "lr")
    swp = [Sem(nc, stack, "swA"), Sem(nc, stack, "swB")]
    sop = [Sem(nc, stack, "soA"), Sem(nc, stack, "soB")]
    tS = Sem(nc, stack, "tS")
    vS = Sem(nc, stack, "vS")
    sS = Sem(nc, stack, "sS")

    T, V, S, G, SY = nc.tensor, nc.vector, nc.scalar, nc.gpsimd, nc.sync
    Vw = make_serial(V, vS)
    Sw = make_serial(S, sS)
    Vn = make_serial(V, vS, auto_inc=False)
    Sn = make_serial(S, sS, auto_inc=False)
    if ctx is not None:
        for ws in _waits:
            G.wait_ge(ws.h, ws.n)
            SY.wait_ge(ws.h, ws.n)

    Vw.memset(ones128[:, :], 1.0)
    Vw.memset(onesrow_f[:, :], 1.0)
    Vw.memset(onescol_f[:, :], 1.0)
    Vw.memset(eps5[:, :], 1e-5)
    vS.wait(T)
    vS.wait(S)

    def ternarize(wsrc, wdst, ntiles, width, permuted=False):
        nq = width // 1024
        pieces = []
        for t in range(ntiles):
            for q in range(nq):
                pieces.append((t, q))
        for pi, (t, q) in enumerate(pieces):
            slot = pi % 2
            SY.wait_ge(sS.h, sS.n)
            if pi >= 2:
                SY.wait_ge(vS.h, vS.n)
            SY.dma_start(out=wq1[slot][:, :],
                         in_=wsrc[128 * t:128 * (t + 1), 1024 * q:1024 * (q + 1)]
                         ).then_inc(lw.h, 16)
            lw.n += 16
            V.wait_ge(lw.h, lw.n)
            Vw.tensor_reduce(out=acc[:, pi % 32:pi % 32 + 1], in_=wq1[slot][:, :],
                             axis=AX.X, op=ALU.add, apply_absolute_value=True)
            if (pi + 1) % 32 == 0 or pi == len(pieces) - 1:
                k = (pi % 32) + 1
                if pi < 32:
                    Vw.tensor_reduce(out=acc[:, 32:33], in_=acc[:, 0:k], axis=AX.X,
                                     op=ALU.add)
                else:
                    Vw.tensor_reduce(out=acc[:, 0:1], in_=acc[:, 0:k], axis=AX.X,
                                     op=ALU.add)
                    Vw.tensor_add(out=acc[:, 32:33], in0=acc[:, 32:33], in1=acc[:, 0:1])
        vS.wait(T)
        T.matmul(pm1[0:1, 0:1], acc[:, 32:33], onescol_f[:, :], start=True, stop=True)
        tS.inc(T.drain(fusable=False))
        tS.wait(S)
        Sw.activation(out=thr1[:, :], in_=pm1[0:1, 0:1], func=AF.Identity,
                      scale=-0.5 / (ntiles * 128 * width))
        sS.wait(T)
        T.matmul(pm2[:, 0:1], onesrow_f[:, :], thr1[:, :], start=True, stop=True)
        tS.inc(T.drain(fusable=False))
        tS.wait(V)
        vS.inc(Vn.tensor_copy(out=negthrb[:, 0:1], in_=pm2[:, 0:1]))
        for rep in range(10):
            w0 = 1 << rep
            w1 = min(1024 - w0, w0)
            if w1 <= 0:
                break
            Vw.tensor_copy(out=negthrb[:, w0:w0 + w1], in_=negthrb[:, 0:w1])
        for pi, (t, q) in enumerate(pieces):
            slot = pi % 2
            SY.wait_ge(sS.h, sS.n)
            SY.dma_start(out=wq1[slot][:, :],
                         in_=wsrc[128 * t:128 * (t + 1), 1024 * q:1024 * (q + 1)]
                         ).then_inc(lw.h, 16)
            lw.n += 16
            S.wait_ge(lw.h, lw.n)
            S.wait_ge(vS.h, vS.n)
            Sw.activation(out=sgn[:, :], in_=wq1[slot][:, :], func=AF.Sign)
            Sw.activation(out=ab[:, :], in_=wq1[slot][:, :], func=AF.Abs)
            sS.wait(V)
            Vw.tensor_add(out=ab[:, :], in0=ab[:, :], in1=negthrb[:, :])
            vS.wait(S)
            Sw.activation(out=ab2[:, :], in_=ab[:, :], func=AF.Sign)
            sS.wait(V)
            Vw.tensor_scalar_max(out=ab2[:, :], in0=ab2[:, :], scalar1=0.0)
            V.wait_ge(swp[pi % 2].h, swp[pi % 2].n)
            vS.inc(Vn.tensor_mul(out=wtb[pi % 2][:, :], in0=ab2[:, :], in1=sgn[:, :]))
            vS.wait(G)
            if permuted:
                for fl in range(8):
                    o0 = ((8 * q + fl) * 8 + t) * 128
                    G.dma_start(out=wdst[:, o0:o0 + 128],
                                in_=wtb[pi % 2][:, 128 * fl:128 * (fl + 1)]
                                ).then_inc(swp[pi % 2].h, 16)
                    swp[pi % 2].n += 16
            else:
                G.dma_start(out=wdst[128 * t:128 * (t + 1), 1024 * q:1024 * (q + 1)],
                            in_=wtb[pi % 2][:, :]).then_inc(swp[pi % 2].h, 16)
                swp[pi % 2].n += 16

    if bg is None:
        ternarize(wupT, wtup, NT, 4096, permuted=True)
        ternarize(wdnT, wtdn, 32, 1024)
        G.wait_ge(swp[0].h, swp[0].n)
        G.wait_ge(swp[1].h, swp[1].n)
    else:
        bg.finish()
        G.wait_ge(bg.swp[0].h, bg.swp[0].n)
        G.wait_ge(bg.swp[1].h, bg.swp[1].n)

    t_lastuse = 0
    adds_done = 0
    wu_last = [0, 0, 0, 0]
    wd_last = [0, 0, 0, 0]
    for c in range(NCH):
        G.wait_ge(tS.h, t_lastuse)
        for t in range(NT):
            G.dma_start(out=xbc[t][:, :],
                        in_=xT[128 * t:128 * (t + 1), c * CH:(c + 1) * CH]).then_inc(lx.h, 16)
        lx.n += NT * 16
        V.wait_ge(lx.h, lx.n)
        T.wait_ge(lx.h, lx.n)
        for t in range(NT):
            if t >= 2:
                V.wait_ge(tS.h, tS.n)
            vS.inc(Vn.tensor_mul(out=xsq2[t % 2][:, :], in0=xbc[t][:, :], in1=xbc[t][:, :]))
            T.wait_ge(vS.h, vS.n)
            T.matmul(pm1[0:1, :], ones128[:, :], xbc[t][:, :],
                     start=(t == 0), stop=(t == NT - 1))
            tS.inc(T.matmul(pm2[0:1, :], ones128[:, :], xsq2[t % 2][:, :],
                            start=(t == 0), stop=(t == NT - 1)))
        tS.inc(T.drain(fusable=False))
        tS.wait(V)
        Vw.tensor_copy(out=s1row[:, :], in_=pm1[0:1, :])
        Vw.tensor_copy(out=s2row[:, :], in_=pm2[0:1, :])
        Vw.tensor_scalar(out=mrow[:, :], in0=s1row[:, :], scalar1=1.0 / D,
                         scalar2=None, op0=ALU.mult)
        Vw.tensor_mul(out=mr2[:, :], in0=mrow[:, :], in1=mrow[:, :])
        Vw.scalar_tensor_tensor(out=s1row[:, :], in0=s2row[:, :], scalar=1.0 / D,
                                in1=mr2[:, :], op0=ALU.mult, op1=ALU.subtract)
        vS.wait(S)
        Sw.activation(out=s2row[:, :], in_=s1row[:, :], func=AF.Ln, bias=eps5[0:1, 0:1], scale=1.0)
        Sw.activation(out=mr2[:, :], in_=s1row[:, :], func=AF.Ln, bias=0.0, scale=1.0)
        sS.wait(V)
        Vw.tensor_sub(out=mr2[:, :], in0=mr2[:, :], in1=s2row[:, :])
        vS.wait(S)
        Sw.activation(out=mr2[:, :], in_=mr2[:, :], func=AF.Exp, bias=0.0, scale=0.5)
        Sw.activation(out=s2row[:, :], in_=s2row[:, :], func=AF.Exp, bias=0.0, scale=0.5)
        sS.wait(V)
        Vw.tensor_scalar(out=mr2[:, :], in0=mr2[:, :], scalar1=1e-8, scalar2=None, op0=ALU.add)
        Vw.tensor_mul(out=mr2[:, :], in0=mr2[:, :], in1=s2row[:, :])
        Vw.reciprocal(out=rhrow[:, :], in_=mr2[:, :])
        Vw.tensor_mul(out=mr2[:, :], in0=mrow[:, :], in1=rhrow[:, :])
        vS.wait(T)
        T.matmul(pm1[:, :], onesrow_f[:, :], rhrow[0:1, :], start=True, stop=True)
        T.matmul(pm2[:, :], onesrow_f[:, :], mr2[0:1, :], start=True, stop=True)
        tS.inc(T.drain(fusable=False))
        tS.wait(V)
        Vw.tensor_copy(out=rb[:, :], in_=pm1[:, :])
        Vw.tensor_copy(out=mrb[:, :], in_=pm2[:, :])
        for t in range(NT):
            Vw.tensor_tensor(out=tev[:, :], in0=xbc[t][:, :], in1=rb[:, :], op=ALU.mult)
            Vw.tensor_sub(out=xnb[t][:, :], in0=tev[:, :], in1=mrb[:, :])
        vS.wait(T)
        vS.wait(S)

        ubanks = [pm1, ybanks[0]]
        gelu_done = []
        mm_done = []
        hsq_mm_done = [0] * 33
        sq_done = []

        def _l1f_gelu(ftc):
            slot2 = ftc % 2
            if ftc >= 2:
                S.wait_ge(vS.h, sq_done[ftc - 2])
            sS.inc(Sn.activation(out=hbuf[:, ftc * CH:(ftc + 1) * CH],
                                 in_=ubanks[slot2][:, :],
                                 func=AF.Gelu, bias=0.0, scale=1.0))
            gelu_done.append(sS.n)
            V.wait_ge(sS.h, gelu_done[ftc])
            if ftc >= 1:
                V.wait_ge(tS.h, hsq_mm_done[ftc - 1])
            vS.inc(Vn.tensor_mul(out=hsq2p[slot2][:, :], in0=hbuf[:, ftc * CH:(ftc + 1) * CH],
                                 in1=hbuf[:, ftc * CH:(ftc + 1) * CH]))
            sq_done.append(vS.n)
            T.wait_ge(vS.h, sq_done[ftc])
            tS.inc(T.matmul(pm2[0:1, :], ones128[:, :], hsq2p[slot2][:, :],
                            start=(ftc == 0), stop=(ftc == 31)))
            hsq_mm_done[ftc] = tS.n

        for ft in range(32):
            slot = ft % 4
            G.wait_ge(tS.h, mm_done[ft - 4] if ft >= 4 else wu_last[slot])
            G.dma_start(out=wupb[slot][:, :],
                        in_=wtup[:, 1024 * ft:1024 * (ft + 1)]).then_inc(lfp[slot].h, 16)
            lfp[slot].n += 16
            T.wait_ge(lfp[slot].h, lfp[slot].n)
            if ft >= 2:
                T.wait_ge(sS.h, gelu_done[ft - 2])
            for t in range(NT):
                mm = T.matmul(ubanks[ft % 2][:, :], wupb[slot][:, 128 * t:128 * (t + 1)],
                              xnb[t][:, :], start=(t == 0), stop=(t == NT - 1))
            tS.inc(mm)
            mm_done.append(tS.n)
            wu_last[slot] = tS.n
            if ft >= 1:
                S.wait_ge(tS.h, mm_done[ft])
                _l1f_gelu(ft - 1)
        tS.inc(T.drain(fusable=False))
        S.wait_ge(tS.h, tS.n)
        _l1f_gelu(31)
        tS.inc(T.drain(fusable=False))
        tS.wait(S)
        Sw.activation(out=s2row[:, :], in_=pm2[0:1, :], func=AF.Ln, bias=0.0, scale=1.0 / 4096)
        Sw.activation(out=s2row[:, :], in_=s2row[:, :], func=AF.Exp, bias=0.0, scale=0.5)
        sS.wait(V)
        Vw.tensor_scalar(out=s2row[:, :], in0=s2row[:, :], scalar1=1e-8, scalar2=None, op0=ALU.add)
        Vw.reciprocal(out=r2row[:, :], in_=s2row[:, :])
        vS.wait(T)
        T.matmul(pm1[:, :], onesrow_f[:, :], r2row[0:1, :], start=True, stop=True)
        tS.inc(T.drain(fusable=False))
        tS.wait(V)
        Vw.tensor_copy(out=r2b[:, :], in_=pm1[:, :])
        vS.wait(T)

        for half in range(2):
            dn_done = []
            for ft in range(32):
                slot = ft % 4
                G.wait_ge(tS.h, dn_done[ft - 4] if ft >= 4 else wd_last[slot])
                G.dma_start(out=wdnb[slot][:, :],
                            in_=wtdn[128 * ft:128 * (ft + 1),
                                     512 * half:512 * (half + 1)]).then_inc(lfd[slot].h, 16)
                lfd[slot].n += 16
                T.wait_ge(lfd[slot].h, lfd[slot].n)
                for dl in range(4):
                    mm = T.matmul(ybanks[dl][:, :], wdnb[slot][:, 128 * dl:128 * (dl + 1)],
                                  hbuf[:, ft * CH:(ft + 1) * CH],
                                  start=(ft == 0), stop=(ft == 31))
                tS.inc(mm)
                dn_done.append(tS.n)
                wd_last[slot] = tS.n
            tS.inc(T.drain(fusable=False))
            tS.wait(V)
            for dl in range(4):
                dt = half * 4 + dl
                SY.wait_ge(vS.h, adds_done)
                SY.dma_start(out=xres[dt % 2][:, :],
                             in_=xT[128 * dt:128 * (dt + 1),
                                    c * CH:(c + 1) * CH]).then_inc(lr.h, 16)
                lr.n += 16
                Vw.tensor_tensor(out=tev[:, :], in0=ybanks[dl][:, :], in1=r2b[:, :],
                                 op=ALU.mult)
                V.wait_ge(lr.h, lr.n)
                V.wait_ge(sop[dt % 2].h, sop[dt % 2].n)
                vS.inc(Vn.tensor_add(out=x4sb[dt % 2][:, :], in0=tev[:, :],
                                     in1=xres[dt % 2][:, :]))
                adds_done = vS.n
                vS.wait(SY)
                SY.dma_start(out=outT[128 * dt:128 * (dt + 1), c * CH:(c + 1) * CH],
                             in_=x4sb[dt % 2][:, :]).then_inc(sop[dt % 2].h, 16)
                sop[dt % 2].n += 16
            vS.wait(T)
        t_lastuse = tS.n

    if dbg:
        SY.wait_ge(vS.h, vS.n)
        SY.dma_start(out=d_wtu[:, :], in_=wtup[:, :]).then_inc(sop[0].h, 16)
        sop[0].n += 16
        G.wait_ge(vS.h, vS.n)
        G.dma_start(out=d_xn[:, :], in_=xnb[0][:, :]).then_inc(sop[0].h, 16)
        sop[0].n += 16
        G.dma_start(out=d_h[:, :], in_=hbuf[:, 0:2 * CH]).then_inc(sop[0].h, 16)
        sop[0].n += 16
    SY.wait_ge(sop[0].h, sop[0].n)
    SY.wait_ge(sop[1].h, sop[1].n)
    if ctx is None:
        stack.close()
        return nc
    return sop


def build_fused():
    nc = bass.Bass("TRN2", target_bir_lowering=False)
    xT = nc.dram_tensor("xT", [D, N], F32, kind="ExternalInput").ap()
    x1T = nc.dram_tensor("x1s", [D, N], F32, kind="Internal").ap()
    x2T = nc.dram_tensor("x2s", [D, N], F32, kind="Internal").ap()
    x3T = nc.dram_tensor("x3s", [D, N], F32, kind="Internal").ap()
    outT = nc.dram_tensor("outT", [D, N], F32, kind="ExternalOutput").ap()
    waits = []
    with contextlib.ExitStack() as bgst:
        bg = TernBG(nc, bgst)
        with contextlib.ExitStack() as st1:
            waits = list(build_l0_attn(ctx=(nc, st1, xT, x1T, waits)))
        with contextlib.ExitStack() as st2:
            waits = list(build_l0_ffn(ctx=(nc, st2, x1T, x2T, waits)))
        with contextlib.ExitStack() as st3:
            waits = list(build_l1_attn(ctx=(nc, st3, x2T, x3T, waits), bg=bg))
        with contextlib.ExitStack() as st4:
            waits = list(build_l1_ffn(ctx=(nc, st4, x3T, outT, waits), bg=bg))
    return nc



def _sel_consts():
    sel16x = np.zeros((1, 256), np.float32)
    for h in range(16):
        sel16x[0, 16 * h + h] = 1.0
    selcol = np.zeros((16, 1024), np.float32)
    for h in range(16):
        selcol[h, 64 * h:64 * (h + 1)] = 1.0
    stk = np.zeros((64, 256), np.float32)
    for k in range(64):
        stk[k, k] = 1.0
        stk[k, 128 + 64 + k] = 1.0
    return sel16x, selcol, stk


def l0a_inputs(x_b, wq, bq, wk, bk, wv, bv, wo, bo):
    sel16x, selcol, stk = _sel_consts()
    return {
        "xT": np.ascontiguousarray(x_b.T),
        "wqT": np.ascontiguousarray(wq.T),
        "wkT": np.ascontiguousarray(wk.T),
        "wvT": np.ascontiguousarray(wv.T),
        "woT": np.ascontiguousarray(wo.T),
        "bq": np.ascontiguousarray(bq.reshape(16, 64).T),
        "bk": np.ascontiguousarray(bk.reshape(4, 64).T),
        "bv": np.ascontiguousarray(bv.reshape(1, 256)),
        "bo": np.ascontiguousarray(bo.reshape(8, 128).T),
        "sel16x": sel16x,
        "selcol": selcol,
        "stk": stk,
    }


def l0f_consts(G1_R, G2_R, G1_I, G2_I):
    m1bd = np.zeros((8 * 128, 128), np.float32)
    for RI, G2 in ((0, G2_R), (1, G2_I)):
        for c in range(4):
            B = np.asarray(G2)[c, :, 0, :]
            blk = np.zeros((128, 128), np.float32)
            for dd in range(8):
                blk[dd * 16:(dd + 1) * 16, dd * 16:(dd + 1) * 16] = B
            m1bd[(RI * 4 + c) * 128:(RI * 4 + c) * 128 + 128] = blk
    m2arr = np.zeros((32 * 128, 512), np.float32)
    for RI, G1 in ((0, G1_R), (1, G1_I)):
        for c in range(4):
            A = np.asarray(G1)[0, :, c, :]
            for tau in range(4):
                blk = np.zeros((128, 512), np.float32)
                for dl in range(8):
                    for e in range(16):
                        kk = dl * 16 + e
                        blk[kk, e::16] = A[:, 8 * tau + dl]
                m2arr[(RI * 16 + c * 4 + tau) * 128:(RI * 16 + c * 4 + tau) * 128 + 128] = blk
    return m1bd, m2arr


def l0f_inputs(x1, G1_R, G2_R, G1_I, G2_I, bias_real, bias_imag):
    m1bd, m2arr = l0f_consts(G1_R, G2_R, G1_I, G2_I)
    brow = np.concatenate([np.asarray(bias_real).reshape(4, 128).T,
                           np.asarray(bias_imag).reshape(4, 128).T], axis=1)
    return {
        "xT": np.ascontiguousarray(x1.T),
        "m1bd": m1bd,
        "m2arr": m2arr,
        "brow": np.ascontiguousarray(brow.astype(np.float32)),
    }


def l1a_inputs(x2, Wq, bq, Wk, bk, Wv, bv, Wo, bo, W1, b1, W2, b2, g1, be1, g2, be2):
    sel16x, selcol, stk = _sel_consts()
    col8 = lambda a: np.ascontiguousarray(np.asarray(a).reshape(8, 128).T.astype(np.float32))
    return {
        "xT": np.ascontiguousarray(x2.T),
        "Wq": np.ascontiguousarray(Wq), "Wk": np.ascontiguousarray(Wk),
        "Wv": np.ascontiguousarray(Wv), "Wo": np.ascontiguousarray(Wo),
        "W1": np.ascontiguousarray(W1), "W2": np.ascontiguousarray(W2),
        "bqc": np.ascontiguousarray(np.asarray(bq).reshape(16, 64).T.astype(np.float32)),
        "bkr": np.ascontiguousarray(np.asarray(bk).reshape(1, 1024)),
        "bvr": np.ascontiguousarray(np.asarray(bv).reshape(1, 1024)),
        "boc": col8(bo),
        "b1c": np.ascontiguousarray(np.asarray(b1).reshape(32, 128).T.astype(np.float32)),
        "b2c": col8(b2),
        "g1c": col8(g1), "be1c": col8(be1),
        "g2c": col8(g2), "be2c": col8(be2),
        "sel16x1": sel16x, "selcol1": selcol, "stk1": stk,
    }


def l1f_inputs(x3, w_up, w_down):
    return {
        "xT": np.ascontiguousarray(x3.T),
        "wupT": np.ascontiguousarray(np.asarray(w_up).T),
        "wdnT": np.ascontiguousarray(np.asarray(w_down).T),
    }


_NC_CACHE = {}


def _get_fused():
    if "fused" not in _NC_CACHE:
        _NC_CACHE["fused"] = build_fused()
    return _NC_CACHE["fused"]


def kernel(**inputs):
    ncores = 8
    x = np.asarray(inputs["L0_attn_x"])
    nc = _get_fused()
    zero = np.zeros((N, D), np.float32)
    base = {}
    base.update(l0a_inputs(zero,
                           inputs["L0_attn_wq"], inputs["L0_attn_bq"],
                           inputs["L0_attn_wk"], inputs["L0_attn_bk"],
                           inputs["L0_attn_wv"], inputs["L0_attn_bv"],
                           inputs["L0_attn_wo"], inputs["L0_attn_bo"]))
    base.update(l0f_inputs(zero,
                           inputs["L0_ffn_G1_R"], inputs["L0_ffn_G2_R"],
                           inputs["L0_ffn_G1_I"], inputs["L0_ffn_G2_I"],
                           inputs["L0_ffn_bias_real"], inputs["L0_ffn_bias_imag"]))
    base.update(l1a_inputs(zero,
                           inputs["L1_attn_Wq"], inputs["L1_attn_bq"],
                           inputs["L1_attn_Wk"], inputs["L1_attn_bk"],
                           inputs["L1_attn_Wv"], inputs["L1_attn_bv"],
                           inputs["L1_attn_Wo"], inputs["L1_attn_bo"],
                           inputs["L1_attn_W1"], inputs["L1_attn_b1"],
                           inputs["L1_attn_W2"], inputs["L1_attn_b2"],
                           inputs["L1_attn_g1"], inputs["L1_attn_be1"],
                           inputs["L1_attn_g2"], inputs["L1_attn_be2"]))
    base.update(l1f_inputs(zero,
                           inputs["L1_ffn_w_up"], inputs["L1_ffn_w_down"]))
    in_maps = []
    for b in range(ncores):
        m = dict(base)
        m["xT"] = np.ascontiguousarray(np.asarray(x[b]).T)
        in_maps.append(m)
    res = run_bass_kernel_spmd(nc, in_maps, core_ids=list(range(ncores)))
    out = np.stack([res.results[b]["outT"].T for b in range(ncores)], axis=0)
    return np.ascontiguousarray(out.astype(np.float32))


# version 16
# speedup vs baseline: 1.9156x; 1.0123x over previous
import contextlib
import numpy as np
import ml_dtypes
import concourse.bass as bass
import concourse.mybir as mybir
import concourse.bass_utils as _BU
from concourse.bass_utils import run_bass_kernel_spmd

_orig_run_command = _BU.run_command


def _run_command_ldwopt(cmd, *a, **kw):
    if isinstance(cmd, list) and any(str(c).startswith("--enable-ldw-opt") for c in cmd):
        cmd = ["--enable-ldw-opt=true" if str(c) == "--enable-ldw-opt=false" else c
               for c in cmd]
    return _orig_run_command(cmd, *a, **kw)


_BU.run_command = _run_command_ldwopt

F32 = mybir.dt.float32
BF16 = mybir.dt.bfloat16
AF = mybir.ActivationFunctionType
ALU = mybir.AluOpType
AX = mybir.AxisListType

N = 4096
D = 1024
NT = 8
CH = 512
NCH = N // CH
BF = ml_dtypes.bfloat16


_UID = [0]


def _uid():
    _UID[0] += 1
    return _UID[0]


class Sem:
    def __init__(self, nc, stack, name):
        self.h = stack.enter_context(nc.semaphore(f"{name}_u{_UID[0]}"))
        self.n = 0

    def inc(self, instr, k=1):
        instr.then_inc(self.h, k)
        self.n += k
        return self.n

    def wait(self, eng, val=None):
        eng.wait_ge(self.h, self.n if val is None else val)


class PBank:
    def __init__(self, ap):
        self.ap = ap
        self.fresh = True

    def start(self):
        s = self.fresh
        self.fresh = False
        return s

    def reset(self):
        self.fresh = True



def make_serial(eng, sem, auto_inc=True):
    class _W:
        def __getattr__(self, name):
            fn = getattr(eng, name)
            def call(*a, **kw):
                eng.wait_ge(sem.h, sem.n)
                ins = fn(*a, **kw)
                if auto_inc:
                    sem.inc(ins)
                return ins
            return call
    return _W()

class TernBG:

    def __init__(self, nc, stack):
        self.nc = nc
        self._p = f"tbg{_uid()}_"
        self.wupT = nc.dram_tensor("wupT", [D, 4096], F32, kind="ExternalInput").ap()
        self.wdnT = nc.dram_tensor("wdnT", [4096, D], F32, kind="ExternalInput").ap()
        self.wtup = nc.dram_tensor("wtup", [128, 32 * 1024], BF16, kind="Internal").ap()
        self.wtdn = nc.dram_tensor("wtdn", [4096, D], BF16, kind="Internal").ap()
        self.scr = [nc.dram_tensor(f"{self._p}scr{k}", [1, 128], F32, kind="Internal").ap()
                    for k in range(4)]
        self.lwT = Sem(nc, stack, "tbg_lw")
        self.vT = Sem(nc, stack, "tbg_v")
        self.sT = Sem(nc, stack, "tbg_s")
        self.fz = Sem(nc, stack, "tbg_fz")
        self.swp = [Sem(nc, stack, "tbg_swA"), Sem(nc, stack, "tbg_swB")]
        self._gen = None
        self.done = False

    def alloc(self, stack):
        nc = self.nc
        sb = lambda name, shape, dt: stack.enter_context(nc.sbuf_tensor(self._p + name, shape, dt))
        self.wq = sb("wq", [128, 1024], F32)
        self.ab = self.wq[:, 0:512]
        self.sgn = sb("sgn", [128, 512], BF16)
        self.m = sb("m", [128, 512], BF16)
        self.wt = sb("wt", [128, 512], BF16)
        self.acc = sb("acc", [128, 66], F32)
        self.row = sb("row", [1, 128], F32)
        self.thr = [sb(f"thr{k}", [128, 1], F32) for k in range(2)]
        self._gen = self._emit()

    def _src(self, pi):
        if pi < 32:
            t, q = pi // 4, pi % 4
            return self.wupT[128 * t:128 * (t + 1), 1024 * q:1024 * (q + 1)]
        p = pi - 32
        return self.wdnT[128 * p:128 * (p + 1), :]

    def _emit(self):
        nc = self.nc
        V, S, SY = nc.vector, nc.scalar, nc.sync
        lwT, vT, sT, fz = self.lwT, self.vT, self.sT, self.fz
        red_done = {}
        for k in range(65):
            if k >= 1:
                pi = k - 1
                V.wait_ge(lwT.h, 16 * k)
                ins = V.tensor_reduce(out=self.acc[:, pi:pi + 1], in_=self.wq[:, :],
                                      axis=AX.X, op=ALU.add, apply_absolute_value=True)
                vT.inc(ins)
                red_done[pi] = vT.n
            if k < 64:
                if k >= 1:
                    SY.wait_ge(vT.h, red_done[k - 1])
                SY.dma_start(out=self.wq[:, :], in_=self._src(k)).then_inc(lwT.h, 16)
                lwT.n += 16
            yield
        for wi, nel in ((0, float(D) * 4096), (1, 4096.0 * float(D))):
            V.wait_ge(vT.h, vT.n)
            vT.inc(V.tensor_reduce(out=self.acc[:, 64 + wi:65 + wi],
                                   in_=self.acc[:, 32 * wi:32 * (wi + 1)],
                                   axis=AX.X, op=ALU.add))
            SY.wait_ge(vT.h, vT.n)
            SY.dma_start(out=self.scr[wi].rearrange("a b -> b a"),
                         in_=self.acc[:, 64 + wi:65 + wi]).then_inc(fz.h, 16)
            fz.n += 16
            SY.wait_ge(fz.h, fz.n)
            SY.dma_start(out=self.row[:, :], in_=self.scr[wi][:, :]).then_inc(fz.h, 16)
            fz.n += 16
            yield
            V.wait_ge(fz.h, fz.n)
            vT.inc(V.tensor_reduce(out=self.acc[0:1, 64 + wi:65 + wi], in_=self.row[:, :],
                                   axis=AX.X, op=ALU.add))
            S.wait_ge(vT.h, vT.n)
            sT.inc(S.activation(out=self.row[0:1, 0:1], in_=self.acc[0:1, 64 + wi:65 + wi],
                                func=AF.Identity, scale=-0.5 / nel))
            V.wait_ge(sT.h, sT.n)
            for rep in range(7):
                w0 = 1 << rep
                V.wait_ge(vT.h, vT.n)
                vT.inc(V.tensor_copy(out=self.row[0:1, w0:2 * w0], in_=self.row[0:1, 0:w0]))
            SY.wait_ge(vT.h, vT.n)
            SY.dma_start(out=self.scr[2 + wi][:, :], in_=self.row[:, :]).then_inc(fz.h, 16)
            fz.n += 16
            SY.wait_ge(fz.h, fz.n)
            SY.dma_start(out=self.thr[wi][:, :],
                         in_=self.scr[2 + wi].rearrange("a b -> b a")).then_inc(fz.h, 16)
            fz.n += 16
            yield
        thr_gate = fz.n
        ab_done = {}
        for k in range(65):
            if k >= 1:
                pi = k - 1
                wslot = pi % 2
                S.wait_ge(lwT.h, 16 * (64 + k))
                if pi == 0:
                    S.wait_ge(fz.h, thr_gate)
                S.wait_ge(vT.h, vT.n)
                wi = 0 if pi < 32 else 1
                for hh in range(2):
                    if hh == 1:
                        S.wait_ge(vT.h, vT.n)
                    sT.inc(S.activation(out=self.sgn[:, :],
                                        in_=self.wq[:, 512 * hh:512 * (hh + 1)], func=AF.Sign))
                    sT.inc(S.activation(out=self.ab[:, :], in_=self.wq[:, 512 * hh:512 * (hh + 1)],
                                        func=AF.Abs))
                    S.wait_ge(sT.h, sT.n)
                    sT.inc(S.activation(out=self.m[:, :], in_=self.ab[:, :], func=AF.Sign,
                                        bias=self.thr[wi][:, 0:1], scale=1.0))
                    if hh == 1:
                        ab_done[pi] = sT.n
                    V.wait_ge(sT.h, sT.n)
                    vT.inc(V.tensor_scalar_max(out=self.m[:, :], in0=self.m[:, :], scalar1=0.0))
                    V.wait_ge(vT.h, vT.n)
                    V.wait_ge(self.swp[0].h, self.swp[0].n)
                    vT.inc(V.tensor_mul(out=self.wt[:, :], in0=self.m[:, :],
                                        in1=self.sgn[:, :]))
                    SY.wait_ge(vT.h, vT.n)
                    if pi < 32:
                        t, q = pi // 4, pi % 4
                        for fl in range(4 * hh, 4 * hh + 4):
                            o0 = ((8 * q + fl) * 8 + t) * 128
                            SY.dma_start(out=self.wtup[:, o0:o0 + 128],
                                         in_=self.wt[:, 128 * (fl - 4 * hh):128 * (fl - 4 * hh + 1)]
                                         ).then_inc(self.swp[0].h, 16)
                            self.swp[0].n += 16
                    else:
                        p = pi - 32
                        SY.dma_start(out=self.wtdn[128 * p:128 * (p + 1),
                                                   512 * hh:512 * (hh + 1)],
                                     in_=self.wt[:, :]).then_inc(self.swp[0].h, 16)
                        self.swp[0].n += 16
            if k < 64:
                if k >= 1:
                    SY.wait_ge(sT.h, ab_done[k - 1])
                SY.dma_start(out=self.wq[:, :], in_=self._src(k)).then_inc(lwT.h, 16)
                lwT.n += 16
            yield

    def step(self):
        if not self.done and self._gen is not None:
            try:
                next(self._gen)
            except StopIteration:
                self.done = True

    def finish(self):
        while not self.done:
            self.step()



def build_l0_attn(N=N, NCH=NCH, dbg=False, ctx=None, bg=None):
    bg_step = (lambda: None) if bg is None else bg.step
    if ctx is None:
        nc = bass.Bass("TRN2", target_bir_lowering=False)
        stack = contextlib.ExitStack()
        xT = nc.dram_tensor("xT", [D, N], F32, kind="ExternalInput").ap()
    else:
        nc, stack, xT, _out, _waits = ctx
    wqT = nc.dram_tensor("wqT", [D, D], F32, kind="ExternalInput").ap()
    wkT = nc.dram_tensor("wkT", [D, 256], F32, kind="ExternalInput").ap()
    wvT = nc.dram_tensor("wvT", [D, 256], F32, kind="ExternalInput").ap()
    woT = nc.dram_tensor("woT", [D, D], F32, kind="ExternalInput").ap()
    bq = nc.dram_tensor("bq", [64, 16], F32, kind="ExternalInput").ap()
    bk = nc.dram_tensor("bk", [64, 4], F32, kind="ExternalInput").ap()
    bv = nc.dram_tensor("bv", [1, 256], F32, kind="ExternalInput").ap()
    bo = nc.dram_tensor("bo", [128, 8], F32, kind="ExternalInput").ap()
    sel16x = nc.dram_tensor("sel16x", [1, 256], F32, kind="ExternalInput").ap()
    selcol = nc.dram_tensor("selcol", [16, 1024], F32, kind="ExternalInput").ap()
    stk = nc.dram_tensor("stk", [64, 256], F32, kind="ExternalInput").ap()
    if ctx is None:
        x1T = nc.dram_tensor("x1T", [D, N], F32, kind="ExternalOutput").ap()
    else:
        x1T = _out
    if dbg:
        NSd = N // 128
        d_r = nc.dram_tensor("d_r", [1, N], F32, kind="ExternalOutput").ap()
        d_kh = nc.dram_tensor("d_kh", [65, N], F32, kind="ExternalOutput").ap()
        d_va = nc.dram_tensor("d_va", [128, NSd * 260], F32, kind="ExternalOutput").ap()
        d_ob = nc.dram_tensor("d_ob", [64, 16 * CH], F32, kind="ExternalOutput").ap()
        d_rd = nc.dram_tensor("d_rd", [16, CH], F32, kind="ExternalOutput").ap()
        d_op = nc.dram_tensor("d_op", [128, 8 * CH], F32, kind="ExternalOutput").ap()
        d_km = nc.dram_tensor("d_km", [64, 4], F32, kind="ExternalOutput").ap()
        d_qp = nc.dram_tensor("d_qp", [65, CH], F32, kind="ExternalOutput").ap()

    _p = f"u{_uid()}_"
    sb = lambda name, shape, dt: stack.enter_context(nc.sbuf_tensor(_p + name, shape, dt))
    psa = lambda name, shape: stack.enter_context(nc.psum_tensor(_p + name, shape, F32))

    NS = N // 128
    NP_G = NS // 2

    xbc = [[sb(f"xbc{k}_{t}", [128, CH], BF16) for t in range(NT)] for k in range(2)]
    wqb = [sb(f"wqb{t}", [128, D], BF16) for t in range(NT)]
    wkb = [sb(f"wkb{t}", [128, 256], BF16) for t in range(NT)]
    wvb = [sb(f"wvb{t}", [128, 256], BF16) for t in range(NT)]
    wob = [sb(f"wob{t}", [128, D], BF16) for t in range(NT)]
    khead = [sb(f"khead{h}", [65, N], BF16) for h in range(4)]
    v_all = sb("v_all", [128, NS * 260], BF16)
    qplus = [sb(f"qplus{k}", [65, CH], BF16) for k in range(2)]
    qabs = [sb(f"qabs{k}", [64, CH], BF16) for k in range(2)]
    Ebuf = [sb(f"Ebuf{k}", [128, 1024], BF16) for k in range(2)]
    Obuf = sb("Obuf", [64, 16 * CH], BF16)
    Opair = sb("Opair", [128, 8 * CH], BF16)
    stkb = sb("stkb", [64, 256], BF16)
    den2 = sb("den2", [1, 2 * CH], F32)
    rd16 = sb("rd16", [16, CH], F32)
    denbp = [sb(f"denbp{k}", [64, CH], F32) for k in range(2)]
    kmax4 = sb("kmax4", [64, 4], BF16)
    s1row = sb("s1row", [1, CH], F32)
    s2row = sb("s2row", [1, CH], F32)
    mrow = sb("mrow", [1, CH], F32)
    rrows = sb("rrows", [1, N], F32)
    rb = sb("rb", [128, CH], F32)
    S1b = sb("S1b", [1, N], BF16)
    rcolf = sb("rcolf", [128, 1], F32)
    xsq2 = [sb(f"xsq2{k}", [128, CH], BF16) for k in range(2)]
    tev = sb("tev", [128, CH], F32)
    tevv = sb("tevv", [128, 256], F32)
    eps30 = sb("eps30", [128, 1], F32)
    tev2p = [sb(f"tev2p{k}", [128, CH], F32) for k in range(2)]
    x1sb = [sb(f"x1sb{k}", [128, CH], F32) for k in range(2)]
    xres = [sb(f"xres{k}", [128, CH], F32) for k in range(2)]
    wqcs = sb("wqcs", [1, D], BF16)
    wkcs = sb("wkcs", [1, 256], BF16)
    wvcs = sb("wvcs", [1, 256], BF16)
    bvb = sb("bvb", [128, 256], F32)
    ones128 = sb("ones128", [128, 1], BF16)
    onesrow_f = sb("onesrow_f", [1, 128], F32)
    one1f = sb("one1f", [1, 1], F32)
    eps5 = sb("eps5", [1, 1], F32)
    sel16s = sb("sel16s", [1, 256], F32)
    selcs = sb("selcs", [16, 1024], F32)
    bq64s = sb("bq64s", [64, 16], F32)
    bk64 = sb("bk64", [64, 4], F32)
    bvr = sb("bvr", [1, 256], F32)
    bo_c = sb("bo_c", [128, 8], F32)

    pst = [psa(f"pst{k}", [128, 1024]) for k in range(2)]
    pOt = psa("pO", [128, CH])
    pO = pOt
    pdent = psa("pden", [128, CH])
    pden = pdent
    pm1 = psa("pm1", [128, CH])
    pm2 = psa("pm2", [128, CH])

    lw = Sem(nc, stack, "lw")
    lb = Sem(nc, stack, "lb")
    lxp = [Sem(nc, stack, "lxA"), Sem(nc, stack, "lxB")]
    lrp = [Sem(nc, stack, "lrA"), Sem(nc, stack, "lrB")]
    so = Sem(nc, stack, "so")
    sop = [Sem(nc, stack, "soA"), Sem(nc, stack, "soB")]
    tS = Sem(nc, stack, "tS")
    vS = Sem(nc, stack, "vS")
    sS = Sem(nc, stack, "sS")
    t_st = Sem(nc, stack, "t_st")
    s_ex = Sem(nc, stack, "s_ex")
    t_pv = Sem(nc, stack, "t_pv")
    t_pvh = Sem(nc, stack, "t_pvh")
    v_oe = Sem(nc, stack, "v_oe")
    t_pk = Sem(nc, stack, "t_pk")
    v_rd = Sem(nc, stack, "v_rd")
    v_nc = Sem(nc, stack, "v_nc")
    t_cm = Sem(nc, stack, "t_cm")
    v_sq = Sem(nc, stack, "v_sq")
    t_sq = Sem(nc, stack, "t_sq")
    t_std = Sem(nc, stack, "t_std")
    t_pvd = Sem(nc, stack, "t_pvd")
    t_pkd = Sem(nc, stack, "t_pkd")

    T, V, S, G, SY = nc.tensor, nc.vector, nc.scalar, nc.gpsimd, nc.sync
    Vw = make_serial(V, vS)
    Sw = make_serial(S, sS)
    Vn = make_serial(V, vS, auto_inc=False)
    Sn = make_serial(S, sS, auto_inc=False)
    if ctx is not None:
        for ws in _waits:
            G.wait_ge(ws.h, ws.n)
            SY.wait_ge(ws.h, ws.n)

    for t in range(NT):
        G.dma_start(out=wqb[t][:, :], in_=wqT[128 * t:128 * (t + 1), :]).then_inc(lw.h, 16)
        G.dma_start(out=wkb[t][:, :], in_=wkT[128 * t:128 * (t + 1), :]).then_inc(lw.h, 16)
        G.dma_start(out=wvb[t][:, :], in_=wvT[128 * t:128 * (t + 1), :]).then_inc(lw.h, 16)
        G.dma_start(out=wob[t][:, :], in_=woT[128 * t:128 * (t + 1), :]).then_inc(lw.h, 16)
    lw.n += 32 * 16
    for srcap, dst in ((bq, bq64s), (bk, bk64), (bv, bvr), (bo, bo_c),
                       (sel16x, sel16s), (selcol, selcs)):
        SY.dma_start(out=dst[:, :], in_=srcap[:, :]).then_inc(lb.h, 16)
    G.dma_start(out=stkb[:, :], in_=stk[:, :]).then_inc(lw.h, 16)
    lw.n += 16
    lb.n += 6 * 16

    Vw.memset(ones128[:, :], 1.0)
    Vw.memset(onesrow_f[:, :], 1.0)
    Vw.memset(one1f[:, :], 1.0)
    Vw.memset(eps5[:, :], 1e-5)
    Vw.memset(eps30[:, :], 1e-30)
    Vw.memset(v_all[:, :], 1.0)
    for h in range(3):
        Vw.memset(khead[h][64:65, :], 1.0)
    vS.inc(Vn.memset(khead[3][64:65, :], 1.0))
    vS.wait(T)
    vS.wait(S)
    lb.wait(V)
    lb.wait(S)
    vS.inc(Vn.tensor_scalar(out=bq64s[:, :], in0=bq64s[:, :], scalar1=0.125,
                           scalar2=None, op0=ALU.mult))

    lw.wait(T)
    for (wtiles, csb, wid) in ((wqb, wqcs, D), (wkb, wkcs, 256), (wvb, wvcs, 256)):
        for c0 in range(0, wid, CH):
            w = min(CH, wid - c0)
            for t in range(NT):
                T.matmul(pm1[0:1, 0:w], ones128[:, :], wtiles[t][:, c0:c0 + w],
                         start=(t == 0), stop=(t == NT - 1))
            tS.inc(T.drain(fusable=False))
            tS.wait(S)
            sS.inc(Sn.activation(out=csb[0:1, c0:c0 + w], in_=pm1[0:1, 0:w],
                                func=AF.Identity, scale=-1.0 / D))
            sS.wait(T)
    lb.wait(T)
    T.matmul(pm2[:, 0:256], onesrow_f[:, :], bvr[:, :], start=True, stop=True)
    tS.inc(T.drain(fusable=False))
    tS.wait(V)
    vS.inc(Vn.tensor_copy(out=bvb[:, :], in_=pm2[:, 0:256]))
    vS.wait(T)

    t_lastuse = [0, 0]

    def load_chunk(c):
        slot = c % 2
        G.wait_ge(tS.h, t_lastuse[slot])
        for t in range(NT):
            G.dma_start(out=xbc[slot][t][:, :],
                        in_=xT[128 * t:128 * (t + 1), c * CH:(c + 1) * CH]).then_inc(lxp[slot].h, 16)
        lxp[slot].n += NT * 16
        return (slot, lxp[slot].n)

    def stats(c, xc):
        cc = slice(c * CH, (c + 1) * CH)
        lsl, lval = loaded[c]
        V.wait_ge(lxp[lsl].h, lval)
        V.wait_ge(sS.h, sS.n)
        T.wait_ge(lxp[lsl].h, lval)
        sq0, tq0 = v_sq.n, t_sq.n
        for t in range(NT):
            if t >= 2:
                V.wait_ge(t_sq.h, tq0 + t - 1)
            v_sq.inc(Vn.tensor_mul(out=xsq2[t % 2][:, :], in0=xc[t][:, :], in1=xc[t][:, :]))
            T.wait_ge(v_sq.h, sq0 + t + 1)
            T.matmul(pm1[0:1, :], ones128[:, :], xc[t][:, :],
                     start=(t == 0), stop=(t == NT - 1))
            t_sq.inc(T.matmul(pm2[0:1, :], ones128[:, :], xsq2[t % 2][:, :],
                              start=(t == 0), stop=(t == NT - 1)))
        tS.inc(T.drain(fusable=False))
        tS.wait(V)
        Vw.tensor_copy(out=s1row[:, :], in_=pm1[0:1, :])
        Vw.tensor_copy(out=s2row[:, :], in_=pm2[0:1, :])
        Vw.tensor_copy(out=S1b[0:1, cc], in_=s1row[:, :])
        Vw.tensor_scalar(out=mrow[:, :], in0=s1row[:, :], scalar1=1.0 / D,
                        scalar2=None, op0=ALU.mult)
        Vw.tensor_mul(out=mrow[:, :], in0=mrow[:, :], in1=mrow[:, :])
        vS.inc(Vn.scalar_tensor_tensor(out=s1row[:, :], in0=s2row[:, :], scalar=1.0 / D,
                                      in1=mrow[:, :], op0=ALU.mult, op1=ALU.subtract))
        vS.wait(S)
        Sw.activation(out=s2row[:, :], in_=s1row[:, :], func=AF.Ln, bias=eps5[0:1, 0:1], scale=1.0)
        sS.inc(Sn.activation(out=rrows[0:1, cc], in_=s2row[:, :], func=AF.Exp,
                            bias=0.0, scale=-0.5))
        sS.wait(T)
        T.matmul(pm1[:, :], onesrow_f[:, :], rrows[0:1, cc], start=True, stop=True)
        tS.inc(T.drain(fusable=False))
        tS.wait(V)
        vS.inc(Vn.tensor_copy(out=rb[:, :], in_=pm1[:, :]))
        vS.wait(T)

    loaded = [load_chunk(0)]
    for c in range(NCH):
        slot = c % 2
        xc = xbc[slot]
        cc = slice(c * CH, (c + 1) * CH)
        stats(c, xc)

        for h in range(4):
            bg_step()
            for t in range(NT):
                T.matmul(pm1[0:64, :], wkb[t][:, 64 * h:64 * (h + 1)], xc[t][:, :],
                         start=(t == 0), stop=False)
            T.matmul(pm1[0:64, :], wkcs[0:1, 64 * h:64 * (h + 1)],
                     S1b[0:1, cc], start=False, stop=True)
            tS.inc(T.drain(fusable=False))
            tS.wait(V)
            V.wait_ge(sS.h, sS.n)
            vS.inc(Vn.tensor_tensor(out=tev[0:64, :], in0=pm1[0:64, :],
                                   in1=rb[0:64, :], op=ALU.mult))
            vS.wait(S)
            sS.inc(Sn.activation(out=khead[h][0:64, cc], in_=tev[0:64, :],
                                func=AF.Identity, bias=bk64[:, h:h + 1], scale=1.0))
            sS.wait(T)

        for j in range(4):
            s0 = c * CH + j * 128
            o0 = j * 128
            T.matmul(pm1[:, 0:1], rrows[0:1, s0:s0 + 128], one1f[:, :],
                     start=True, stop=True)
            for t in range(NT):
                T.matmul(pm2[:, 0:256], xc[t][:, o0:o0 + 128], wvb[t][:, :],
                         start=(t == 0), stop=False)
            T.matmul(pm2[:, 0:256], S1b[0:1, s0:s0 + 128], wvcs[0:1, :],
                     start=False, stop=True)
            tS.inc(T.drain(fusable=False))
            tS.wait(V)
            Vw.tensor_copy(out=rcolf[:, :], in_=pm1[:, 0:1])
            Vw.tensor_copy(out=tevv[:, :], in_=pm2[:, 0:256])
            st = c * 4 + j
            for kvh in range(4):
                Vw.scalar_tensor_tensor(
                    out=v_all[:, st * 260 + 65 * kvh: st * 260 + 65 * kvh + 64],
                    in0=tevv[:, 64 * kvh:64 * (kvh + 1)], scalar=rcolf[:, 0:1],
                    in1=bvb[:, 64 * kvh:64 * (kvh + 1)], op0=ALU.mult, op1=ALU.add)
            vS.wait(T)
        t_lastuse[slot] = tS.n
        if c + 1 < NCH:
            loaded.append(load_chunk(c + 1))

    V.wait_ge(sS.h, sS.n)
    for h in range(4):
        Vw.tensor_reduce(out=kmax4[:, h:h + 1], in_=khead[h][0:64, :], axis=AX.X,
                         op=ALU.max, apply_absolute_value=True)
    vS.wait(T)

    t_lastuse = [tS.n, tS.n]
    loaded = [load_chunk(0)]
    adds_done = 0
    qtt_done = {}

    def emit_qbuild(c, i):
        H = c * 16 + i
        cc = slice(c * CH, (c + 1) * CH)
        xc = xbc[c % 2]
        kv = i // 4
        qp = qplus[H % 2]
        qa = qabs[H % 2]
        lsl, lval = loaded[c]
        T.wait_ge(lxp[lsl].h, lval)
        if H >= 1:
            T.wait_ge(vS.h, qtt_done[H - 1])
        for t in range(NT):
            T.matmul(pm1[0:64, :], wqb[t][:, 64 * i:64 * (i + 1)], xc[t][:, :],
                     start=(t == 0), stop=False)
        T.matmul(pm1[0:64, :], wqcs[0:1, 64 * i:64 * (i + 1)],
                 S1b[0:1, cc], start=False, stop=True)
        tS.inc(T.drain(fusable=False))
        V.wait_ge(tS.h, tS.n)
        V.wait_ge(sS.h, sS.n)
        vS.inc(Vn.tensor_tensor(out=tev[0:64, :], in0=pm1[0:64, :],
                               in1=rb[0:64, :], op=ALU.mult))
        qtt_done[H] = vS.n
        S.wait_ge(vS.h, qtt_done[H])
        if H >= 2:
            S.wait_ge(t_pv.h, NP_G * (H - 1))
            S.wait_ge(t_cm.h, H - 1)
        Sw.activation(out=qp[0:64, :], in_=tev[0:64, :], func=AF.Identity,
                     bias=bq64s[:, i:i + 1], scale=0.125)
        sS.inc(Sn.activation(out=qa[0:64, :], in_=qp[0:64, :], func=AF.Abs))
        sS.wait(T)
        T.wait_ge(v_nc.h, v_nc.n)
        T.matmul(pm2[0:1, :], kmax4[:, kv:kv + 1], qa[0:64, :],
                 start=True, stop=True)
        t_cm.inc(T.drain(fusable=False))
        t_cm.wait(S)
        v_nc.inc(Sn.activation(out=qp[64:65, :], in_=pm2[0:1, :],
                              func=AF.Identity, scale=-1.0))

    qa_done = {}

    def emit_q_tmms(c, i):
        H = c * 16 + i
        cc = slice(c * CH, (c + 1) * CH)
        xc = xbc[c % 2]
        lsl, lval = loaded[c]
        T.wait_ge(lxp[lsl].h, lval)
        if H >= 1:
            T.wait_ge(vS.h, qtt_done[H - 1])
        qcov = None
        for t in range(NT):
            mm = T.matmul(pm1[0:64, :], wqb[t][:, 64 * i:64 * (i + 1)], xc[t][:, :],
                          start=(t == 0), stop=False)
            if t == 1:
                tS.inc(mm)
                qcov = tS.n
        T.matmul(pm1[0:64, :], wqcs[0:1, 64 * i:64 * (i + 1)],
                 S1b[0:1, cc], start=False, stop=True)
        V.wait_ge(t_st.h, t_st.n + 1)
        V.wait_ge(sS.h, sS.n)
        vS.inc(Vn.tensor_tensor(out=tev[0:64, :], in0=pm1[0:64, :],
                               in1=rb[0:64, :], op=ALU.mult))
        qtt_done[H] = vS.n
        return qcov

    def emit_q_sacts(c, i):
        H = c * 16 + i
        qp = qplus[H % 2]
        qa = qabs[H % 2]
        S.wait_ge(vS.h, qtt_done[H])
        if H >= 2:
            S.wait_ge(t_pv.h, NP_G * (H - 1))
            S.wait_ge(t_cm.h, H - 1)
        Sw.activation(out=qp[0:64, :], in_=tev[0:64, :], func=AF.Identity,
                     bias=bq64s[:, i:i + 1], scale=0.125)
        sS.inc(Sn.activation(out=qa[0:64, :], in_=qp[0:64, :], func=AF.Abs))
        qa_done[H] = sS.n

    def emit_q_cmm(c, i):
        H = c * 16 + i
        kv = i // 4
        qp = qplus[H % 2]
        qa = qabs[H % 2]
        T.wait_ge(sS.h, qa_done[H])
        T.wait_ge(v_nc.h, v_nc.n)
        t_cm.inc(T.matmul(pm2[0:1, :], kmax4[:, kv:kv + 1], qa[0:64, :],
                          start=True, stop=True))
        S.wait_ge(t_st.h, t_st.n + 1)
        v_nc.inc(Sn.activation(out=qp[64:65, :], in_=pm2[0:1, :],
                              func=AF.Identity, scale=-1.0))

    for c in range(NCH):
        slot = c % 2
        cc = slice(c * CH, (c + 1) * CH)
        sS.wait(T)
        T.matmul(pm1[:, :], onesrow_f[:, :], rrows[0:1, cc], start=True, stop=True)
        tS.inc(T.drain(fusable=False))
        tS.wait(V)
        vS.inc(Vn.tensor_copy(out=rb[:, :], in_=pm1[:, :]))
        vS.wait(T)
        emit_qbuild(c, 0)

        for i in range(16):
            bg_step()
            H = c * 16 + i
            kv = i // 4
            qp = qplus[H % 2]
            trigger = min(4, NP_G - 1)

            split = (NP_G == 16)
            T.wait_ge(v_nc.h, H + 1)
            for p in range(NP_G):
                g = NP_G * H + p
                buf = g % 2
                if g >= 2:
                    T.wait_ge(s_ex.h, g - 1)
                for j in range(2):
                    stile = 2 * p + j
                    mm = T.matmul(pst[buf][:, 512 * j:512 * (j + 1)],
                                  khead[kv][:, stile * 128:(stile + 1) * 128],
                                  qp[:, :], start=True, stop=True)
                t_st.inc(mm)
                if p == NP_G - 1 and not split:
                    t_std.inc(T.drain(fusable=False))
                if p == trigger:
                    if i >= 1:
                        T.wait_ge(v_oe.h, H)
                        if i == 1:
                            T.wait_ge(v_rd.h, c)
                        t_pk.inc(T.matmul(pden[0:16, :],
                                          sel16s[0:1, 16 * (i - 1):16 * i],
                                          den2[0:1, ((H - 1) % 2) * CH:((H - 1) % 2 + 1) * CH],
                                          start=(i == 1), stop=False))
                    if i + 1 < 16:
                        if split:
                            emit_q_tmms(c, i + 1)
                        else:
                            emit_qbuild(c, i + 1)
                if split and p == 6 and i + 1 < 16:
                    emit_q_sacts(c, i + 1)
                if p == 0:
                    S.wait_ge(t_st.h, g + 2)
                else:
                    S.wait_ge(t_pv.h, g)
                if g >= 2:
                    S.wait_ge(t_pv.h, g - 1)
                s_ex.inc(Sn.activation(out=Ebuf[buf][:, :], in_=pst[buf][:, :],
                                      func=AF.Exp))
                if p >= 1:
                    gp = g - 1
                    T.wait_ge(s_ex.h, gp + 1)
                    if p == 1:
                        T.wait_ge(v_oe.h, H)
                    for j in range(2):
                        stile = 2 * (p - 1) + j
                        mm = T.matmul(pO[0:65, :],
                                      v_all[:, stile * 260 + 65 * kv:stile * 260 + 65 * (kv + 1)],
                                      Ebuf[gp % 2][:, 512 * j:512 * (j + 1)],
                                      start=(p == 1 and j == 0), stop=False)
                    t_pv.inc(mm)
                if split and p == 8 and i + 1 < 16:
                    emit_q_cmm(c, i + 1)
            g = NP_G * H + (NP_G - 1)
            T.wait_ge(s_ex.h, g + 1)
            if NP_G == 1:
                T.wait_ge(v_oe.h, H)
            for j in range(2):
                stile = 2 * (NP_G - 1) + j
                mm = T.matmul(pO[0:65, :],
                              v_all[:, stile * 260 + 65 * kv:stile * 260 + 65 * (kv + 1)],
                              Ebuf[g % 2][:, 512 * j:512 * (j + 1)],
                              start=(NP_G == 1 and j == 0), stop=(j == 1))
            t_pv.inc(mm)
            if i == 15:
                t_pvd.inc(T.drain(fusable=False))

            V.wait_ge(t_pv.h, NP_G * (H + 1))
            if i == 15:
                V.wait_ge(t_pvd.h, t_pvd.n)
            else:
                V.wait_ge(t_st.h, NP_G * (H + 1) + 1)
            if H >= 2:
                V.wait_ge(t_pk.h, H - 1)
            dslot = H % 2
            Vw.tensor_copy(out=den2[0:1, dslot * CH:(dslot + 1) * CH], in_=pO[64:65, :])
            v_oe.inc(Vn.tensor_copy(out=Obuf[:, i * CH:(i + 1) * CH], in_=pO[0:64, :]))
        T.wait_ge(v_oe.h, v_oe.n)
        t_pk.inc(T.matmul(pden[0:16, :], sel16s[0:1, 240:256],
                          den2[0:1, ((c * 16 + 15) % 2) * CH:((c * 16 + 15) % 2 + 1) * CH],
                          start=False, stop=True))
        t_pkd.inc(T.drain(fusable=False))

        S.wait_ge(t_pkd.h, c + 1)
        sS.inc(Sn.activation(out=denbp[0][0:16, :], in_=pden[0:16, :],
                            func=AF.Identity, bias=eps30[0:16, 0:1], scale=1.0))
        sS.wait(V)
        v_rd.inc(Vn.reciprocal(out=rd16[:, :], in_=denbp[0][0:16, :]))
        nbanks = [pm1, pm2]
        nmm_done = []
        nb_done = []
        mult_done = []
        T.wait_ge(v_rd.h, c + 1)
        T.wait_ge(v_nc.h, v_nc.n)

        def norm_evac(i, cover):
            nslot = i % 2
            V.wait_ge(tS.h, cover)
            if i >= 2:
                V.wait_ge(vS.h, mult_done[i - 2])
            vS.inc(Vn.tensor_copy(out=denbp[nslot][:, :], in_=nbanks[nslot][0:64, :]))
            nb_done.append(vS.n)
            vS.inc(Vn.tensor_tensor(out=Obuf[:, i * CH:(i + 1) * CH],
                                   in0=Obuf[:, i * CH:(i + 1) * CH],
                                   in1=denbp[nslot][:, :], op=ALU.mult))
            mult_done.append(vS.n)

        for i in range(16):
            nslot = i % 2
            if i >= 2:
                T.wait_ge(vS.h, nb_done[i - 2])
            tS.inc(T.matmul(nbanks[nslot][0:64, :], selcs[:, 64 * i:64 * (i + 1)],
                            rd16[:, :], start=True, stop=True))
            nmm_done.append(tS.n)
            if i >= 1:
                norm_evac(i - 1, nmm_done[i])
        tS.inc(T.drain(fusable=False))
        norm_evac(15, tS.n)

        pmm_done = []
        pr_done = []
        for pr in range(8):
            pslot = pr % 2
            if pr < 2:
                T.wait_ge(vS.h, mult_done[15])
            else:
                T.wait_ge(vS.h, pr_done[pr - 2])
            T.matmul(nbanks[pslot][:, :], stkb[:, 0:128],
                     Obuf[:, (2 * pr) * CH:(2 * pr + 1) * CH], start=True, stop=False)
            tS.inc(T.matmul(nbanks[pslot][:, :], stkb[:, 128:256],
                            Obuf[:, (2 * pr + 1) * CH:(2 * pr + 2) * CH],
                            start=False, stop=True))
            pmm_done.append(tS.n)
            if pr >= 1:
                V.wait_ge(tS.h, pmm_done[pr])
                vS.inc(Vn.tensor_copy(out=Opair[:, (pr - 1) * CH:pr * CH],
                                      in_=nbanks[(pr - 1) % 2][:, :]))
                pr_done.append(vS.n)
        tS.inc(T.drain(fusable=False))
        V.wait_ge(tS.h, tS.n)
        vS.inc(Vn.tensor_copy(out=Opair[:, 7 * CH:8 * CH], in_=nbanks[1][:, :]))
        pr_done.append(vS.n)

        omm_done = []
        op_act_done = []
        op_add_done = []
        lr_done = []

        def oproj_tail(dt, cover):
            oslot = dt % 2
            S.wait_ge(tS.h, cover)
            if dt >= 2:
                S.wait_ge(vS.h, op_add_done[dt - 2])
            sS.inc(Sn.activation(out=tev2p[oslot][:, :], in_=nbanks[oslot][:, :],
                                func=AF.Identity, bias=bo_c[:, dt:dt + 1], scale=1.0))
            op_act_done.append(sS.n)
            V.wait_ge(sS.h, op_act_done[dt])
            V.wait_ge(lrp[dt % 2].h, lr_done[dt])
            V.wait_ge(sop[oslot].h, sop[oslot].n)
            vS.inc(Vn.tensor_add(out=x1sb[oslot][:, :], in0=tev2p[oslot][:, :],
                                in1=xres[oslot][:, :]))
            op_add_done.append(vS.n)
            vS.wait(SY)
            SY.dma_start(out=x1T[128 * dt:128 * (dt + 1), cc],
                         in_=x1sb[oslot][:, :]).then_inc(sop[oslot].h, 16)
            sop[oslot].n += 16

        for dt in range(NT):
            oslot = dt % 2
            SY.wait_ge(vS.h, adds_done)
            SY.dma_start(out=xres[oslot][:, :],
                         in_=xT[128 * dt:128 * (dt + 1), cc]).then_inc(lrp[oslot].h, 16)
            lrp[oslot].n += 16
            lr_done.append(lrp[oslot].n)
            if dt == 0:
                T.wait_ge(vS.h, pr_done[7])
                T.wait_ge(sS.h, sS.n)
            if dt >= 2:
                T.wait_ge(sS.h, op_act_done[dt - 2])
            for pr in range(8):
                mm = T.matmul(nbanks[oslot][:, :], wob[pr][:, 128 * dt:128 * (dt + 1)],
                              Opair[:, pr * CH:(pr + 1) * CH],
                              start=(pr == 0), stop=(pr == 7))
            tS.inc(mm)
            omm_done.append(tS.n)
            if dt >= 1:
                oproj_tail(dt - 1, omm_done[dt])
                adds_done = op_add_done[dt - 1]
        tS.inc(T.drain(fusable=False))
        oproj_tail(7, tS.n)
        adds_done = op_add_done[7]
        t_lastuse[slot] = tS.n
        if c + 1 < NCH:
            loaded.append(load_chunk(c + 1))

    SY.wait_ge(sop[0].h, sop[0].n)
    SY.wait_ge(sop[1].h, sop[1].n)
    if ctx is None:
        stack.close()
        return nc
    return sop



def build_l0_ffn(N=N, NCH=NCH, ctx=None):
    if ctx is None:
        nc = bass.Bass("TRN2", target_bir_lowering=False)
        stack = contextlib.ExitStack()
        xT = nc.dram_tensor("xT", [D, N], F32, kind="ExternalInput").ap()
    else:
        nc, stack, xT, _out, _waits = ctx
    m1bd = nc.dram_tensor("m1bd", [8 * 128, 128], F32, kind="ExternalInput").ap()
    m2arr = nc.dram_tensor("m2arr", [32 * 128, 512], F32, kind="ExternalInput").ap()
    brow = nc.dram_tensor("brow", [128, 8], F32, kind="ExternalInput").ap()
    if ctx is None:
        x2T = nc.dram_tensor("x2T", [D, N], F32, kind="ExternalOutput").ap()
    else:
        x2T = _out

    _p = f"u{_uid()}_"
    sb = lambda name, shape, dt: stack.enter_context(nc.sbuf_tensor(_p + name, shape, dt))
    psa = lambda name, shape: stack.enter_context(nc.psum_tensor(_p + name, shape, F32))

    xbc = [[sb(f"xbc{k}_{t}", [128, CH], BF16) for t in range(NT)] for k in range(2)]
    m1b = sb("m1b", [128, 8 * 128], F32)
    m2b = [sb(f"m2b{k}", [128, 512], F32) for k in range(4)]
    wR = sb("wR", [128, 4 * 512], BF16)
    wI = sb("wI", [128, 4 * 512], BF16)
    wIn = sb("wIn", [128, 4 * 512], BF16)
    rsr = sb("rsr", [1, 512], BF16)
    rsi = sb("rsi", [1, 512], BF16)
    t1row = sb("t1row", [1, 512], F32)
    t2row = sb("t2row", [1, 512], F32)
    s1row = sb("s1row", [1, CH], F32)
    s2row = sb("s2row", [1, CH], F32)
    mrow = sb("mrow", [1, CH], F32)
    rrow = sb("rrow", [1, CH], F32)
    rb = sb("rb", [128, CH], F32)
    S1b = sb("S1b", [1, CH], BF16)
    xsq2 = [sb(f"xsq2{k}", [128, CH], BF16) for k in range(2)]
    tev = sb("tev", [128, CH], F32)
    tevp = [sb(f"tevp{k}", [128, CH], F32) for k in range(2)]
    x2sb = [sb(f"x2sb{k}", [128, CH], F32) for k in range(2)]
    xres = [sb(f"xres{k}", [128, CH], F32) for k in range(2)]
    ones128 = sb("ones128", [128, 1], BF16)
    onesrow_f = sb("onesrow_f", [1, 128], F32)
    eps5 = sb("eps5", [1, 1], F32)
    brc = sb("brc", [128, 8], F32)

    pm1 = psa("pm1", [128, CH])
    pm2 = psa("pm2", [128, CH])
    pmw = psa("pmw", [128, CH])

    lw = Sem(nc, stack, "lw")
    lx = Sem(nc, stack, "lx")
    lr = Sem(nc, stack, "lr")
    so = Sem(nc, stack, "so")
    sop = [Sem(nc, stack, "soA"), Sem(nc, stack, "soB")]
    tS = Sem(nc, stack, "tS")
    vS = Sem(nc, stack, "vS")
    sS = Sem(nc, stack, "sS")
    v_sq = Sem(nc, stack, "v_sq")
    t_sq = Sem(nc, stack, "t_sq")

    T, V, S, G, SY = nc.tensor, nc.vector, nc.scalar, nc.gpsimd, nc.sync
    Vw = make_serial(V, vS)
    Sw = make_serial(S, sS)
    Vn = make_serial(V, vS, auto_inc=False)
    Sn = make_serial(S, sS, auto_inc=False)
    if ctx is not None:
        for ws in _waits:
            G.wait_ge(ws.h, ws.n)
            SY.wait_ge(ws.h, ws.n)

    SY.dma_start(out=m1b.rearrange("p (a j) -> p a j", a=8),
                 in_=m1bd.rearrange("(a p) j -> p a j", p=128)).then_inc(lw.h, 16)
    SY.dma_start(out=brc[:, :], in_=brow[:, :]).then_inc(lw.h, 16)
    lw.n += 32
    Vw.memset(ones128[:, :], 1.0)
    Vw.memset(onesrow_f[:, :], 1.0)
    vS.inc(Vn.memset(eps5[:, :], 1e-5))
    vS.wait(T)
    vS.wait(S)

    m2ld = 0
    for RI in range(2):
        for tau in range(4):
            for c in range(4):
                slot = c
                G.wait_ge(tS.h, tS.n)
                G.dma_start(out=m2b[slot][:, :],
                            in_=m2arr[(RI * 16 + c * 4 + tau) * 128:
                                      (RI * 16 + c * 4 + tau) * 128 + 128, :]).then_inc(lx.h, 16)
                lx.n += 16
                T.wait_ge(lx.h, lx.n)
                T.wait_ge(lw.h, 32)
                if c == 0:
                    T.wait_ge(sS.h, sS.n)
                T.matmul(pmw[:, :], m1b[:, (RI * 4 + c) * 128:(RI * 4 + c) * 128 + 128],
                         m2b[slot][:, :], start=(c == 0), stop=(c == 3))
            tS.inc(T.drain(fusable=False))
            tS.wait(S)
            dst = wR if RI == 0 else wI
            Sw.activation(out=dst[:, tau * 512:(tau + 1) * 512], in_=pmw[:, :],
                         func=AF.Identity, scale=1.0)
            if RI == 1:
                sS.inc(Sn.activation(out=wIn[:, tau * 512:(tau + 1) * 512], in_=pmw[:, :],
                                    func=AF.Identity, scale=-1.0))
            else:
                sS.inc(Sn.nop())
    sS.wait(T)
    for tau in range(4):
        T.matmul(pm1[0:1, :], ones128[:, :], wR[:, tau * 512:(tau + 1) * 512],
                 start=(tau == 0), stop=(tau == 3))
    for tau in range(4):
        T.matmul(pm2[0:1, :], ones128[:, :], wI[:, tau * 512:(tau + 1) * 512],
                 start=(tau == 0), stop=(tau == 3))
    tS.inc(T.drain(fusable=False))
    tS.wait(V)
    Vw.tensor_copy(out=t1row[:, :], in_=pm1[0:1, :])
    vS.inc(Vn.tensor_copy(out=t2row[:, :], in_=pm2[0:1, :]))
    vS.wait(S)
    Vw.tensor_sub(out=s1row[0:1, :], in0=t1row[:, :], in1=t2row[:, :])
    Vw.tensor_add(out=s2row[0:1, :], in0=t1row[:, :], in1=t2row[:, :])
    vS.inc(Vn.nop())
    vS.wait(S)
    Sw.activation(out=rsr[0:1, :], in_=s1row[0:1, :], func=AF.Identity, scale=-1.0 / D)
    sS.inc(Sn.activation(out=rsi[0:1, :], in_=s2row[0:1, :],
                        func=AF.Identity, scale=-1.0 / D))
    sS.wait(T)
    sS.wait(V)

    t_lastuse = [tS.n, tS.n]

    def load_chunk(c):
        slot = c % 2
        G.wait_ge(tS.h, t_lastuse[slot])
        for t in range(NT):
            G.dma_start(out=xbc[slot][t][:, :],
                        in_=xT[128 * t:128 * (t + 1), c * CH:(c + 1) * CH]).then_inc(lx.h, 16)
        lx.n += NT * 16
        return lx.n

    loaded = {0: load_chunk(0)}
    adds_done = 0
    for c in range(NCH):
        slot = c % 2
        xc = xbc[slot]
        V.wait_ge(lx.h, loaded[c])
        V.wait_ge(sS.h, sS.n)
        T.wait_ge(lx.h, loaded[c])
        sq0, tq0 = v_sq.n, t_sq.n
        for t in range(NT):
            if t >= 2:
                V.wait_ge(t_sq.h, tq0 + t - 1)
            v_sq.inc(Vn.tensor_mul(out=xsq2[t % 2][:, :], in0=xc[t][:, :], in1=xc[t][:, :]))
            T.wait_ge(v_sq.h, sq0 + t + 1)
            T.matmul(pm1[0:1, :], ones128[:, :], xc[t][:, :], start=(t == 0), stop=False)
            t_sq.inc(T.matmul(pm2[0:1, :], ones128[:, :], xsq2[t % 2][:, :],
                              start=(t == 0), stop=(t == NT - 1)))
        tS.inc(T.drain(fusable=False))
        tS.wait(V)
        Vw.tensor_copy(out=s1row[:, :], in_=pm1[0:1, :])
        Vw.tensor_copy(out=s2row[:, :], in_=pm2[0:1, :])
        Vw.tensor_copy(out=S1b[0:1, :], in_=s1row[:, :])
        Vw.tensor_scalar(out=mrow[:, :], in0=s1row[:, :], scalar1=1.0 / D,
                        scalar2=None, op0=ALU.mult)
        Vw.tensor_mul(out=mrow[:, :], in0=mrow[:, :], in1=mrow[:, :])
        vS.inc(Vn.scalar_tensor_tensor(out=s1row[:, :], in0=s2row[:, :], scalar=1.0 / D,
                                      in1=mrow[:, :], op0=ALU.mult, op1=ALU.subtract))
        vS.wait(S)
        Sw.activation(out=s2row[:, :], in_=s1row[:, :], func=AF.Ln, bias=eps5[0:1, 0:1], scale=1.0)
        sS.inc(Sn.activation(out=rrow[0:1, :], in_=s2row[:, :], func=AF.Exp,
                            bias=0.0, scale=-0.5))
        sS.wait(T)
        T.matmul(pm1[:, :], onesrow_f[:, :], rrow[0:1, :], start=True, stop=True)
        tS.inc(T.drain(fusable=False))
        tS.wait(V)
        vS.inc(Vn.tensor_copy(out=rb[:, :], in_=pm1[:, :]))
        vS.wait(T)

        sbanks = [pm1, pmw]
        tt_done = []
        act_done = []
        add_done_l = []
        for sig in range(8):
            im = sig >= 4
            tpart = sig % 4
            slot = sig % 2
            SY.wait_ge(vS.h, adds_done)
            SY.dma_start(out=xres[slot][:, :],
                         in_=xT[128 * sig:128 * (sig + 1),
                                c * CH:(c + 1) * CH]).then_inc(lr.h, 16)
            lr.n += 16
            if sig >= 2:
                T.wait_ge(vS.h, tt_done[sig - 2])
            wA = wI if im else wR
            wB = wR if im else wIn
            for tau in range(4):
                T.matmul(sbanks[slot][:, :],
                         wA[:, tau * 512 + tpart * 128:tau * 512 + tpart * 128 + 128],
                         xc[tau][:, :], start=(tau == 0), stop=False)
            for tau in range(4):
                T.matmul(sbanks[slot][:, :],
                         wB[:, tau * 512 + tpart * 128:tau * 512 + tpart * 128 + 128],
                         xc[4 + tau][:, :], start=False, stop=False)
            rs = rsi if im else rsr
            T.matmul(sbanks[slot][:, :], rs[0:1, 128 * tpart:128 * (tpart + 1)], S1b[0:1, :],
                     start=False, stop=True)
            tS.inc(T.drain(fusable=False))
            dcount = tS.n
            V.wait_ge(tS.h, dcount)
            if sig >= 2:
                V.wait_ge(sS.h, act_done[sig - 2])
            vS.inc(Vn.tensor_tensor(out=tevp[slot][:, :], in0=sbanks[slot][:, :],
                                    in1=rb[:, :], op=ALU.mult))
            tt_done.append(vS.n)
            S.wait_ge(vS.h, tt_done[sig])
            if sig >= 2:
                S.wait_ge(vS.h, add_done_l[sig - 2])
            sS.inc(Sn.activation(out=xsq2[slot][:, :], in_=tevp[slot][:, :], func=AF.Identity,
                                 bias=brc[:, sig:sig + 1], scale=1.0))
            act_done.append(sS.n)
            V.wait_ge(sS.h, act_done[sig])
            V.wait_ge(lr.h, lr.n)
            V.wait_ge(sop[slot].h, sop[slot].n)
            vS.inc(Vn.tensor_add(out=x2sb[slot][:, :], in0=xsq2[slot][:, :],
                                 in1=xres[slot][:, :]))
            adds_done = vS.n
            add_done_l.append(vS.n)
            vS.wait(SY)
            SY.dma_start(out=x2T[128 * sig:128 * (sig + 1), c * CH:(c + 1) * CH],
                         in_=x2sb[slot][:, :]).then_inc(sop[slot].h, 16)
            sop[slot].n += 16
        t_lastuse[slot] = tS.n
        if c + 1 < NCH:
            loaded[c + 1] = load_chunk(c + 1)

    SY.wait_ge(sop[0].h, sop[0].n)
    SY.wait_ge(sop[1].h, sop[1].n)
    if ctx is None:
        stack.close()
        return nc
    return sop



def build_l1_attn(N=N, NCH=NCH, stage=9, ctx=None, bg=None):
    bg_step = (lambda: None) if bg is None else bg.step
    if ctx is None:
        nc = bass.Bass("TRN2", target_bir_lowering=False)
        stack = contextlib.ExitStack()
        xT = nc.dram_tensor("xT", [D, N], F32, kind="ExternalInput").ap()
    else:
        nc, stack, xT, _out, _waits = ctx
    Wq = nc.dram_tensor("Wq", [D, D], F32, kind="ExternalInput").ap()
    Wk = nc.dram_tensor("Wk", [D, D], F32, kind="ExternalInput").ap()
    Wv = nc.dram_tensor("Wv", [D, D], F32, kind="ExternalInput").ap()
    Wo = nc.dram_tensor("Wo", [D, D], F32, kind="ExternalInput").ap()
    W1 = nc.dram_tensor("W1", [D, 4096], F32, kind="ExternalInput").ap()
    W2 = nc.dram_tensor("W2", [4096, D], F32, kind="ExternalInput").ap()
    bqc = nc.dram_tensor("bqc", [64, 16], F32, kind="ExternalInput").ap()
    bkr = nc.dram_tensor("bkr", [1, D], F32, kind="ExternalInput").ap()
    bvr = nc.dram_tensor("bvr", [1, D], F32, kind="ExternalInput").ap()
    boc = nc.dram_tensor("boc", [128, 8], F32, kind="ExternalInput").ap()
    b1c = nc.dram_tensor("b1c", [128, 32], F32, kind="ExternalInput").ap()
    b2c = nc.dram_tensor("b2c", [128, 8], F32, kind="ExternalInput").ap()
    g1c = nc.dram_tensor("g1c", [128, 8], F32, kind="ExternalInput").ap()
    be1c = nc.dram_tensor("be1c", [128, 8], F32, kind="ExternalInput").ap()
    g2c = nc.dram_tensor("g2c", [128, 8], F32, kind="ExternalInput").ap()
    be2c = nc.dram_tensor("be2c", [128, 8], F32, kind="ExternalInput").ap()
    sel16x = nc.dram_tensor("sel16x1", [1, 256], F32, kind="ExternalInput").ap()
    selcol = nc.dram_tensor("selcol1", [16, 1024], F32, kind="ExternalInput").ap()
    stk = nc.dram_tensor("stk1", [64, 256], F32, kind="ExternalInput").ap()
    W1c = nc.dram_tensor("W1c", [128, 32 * 1024], BF16, kind="Internal").ap()
    W2c = nc.dram_tensor("W2c", [4096, 1024], BF16, kind="Internal").ap()
    if ctx is None:
        x3T = nc.dram_tensor("x3T", [D, N], F32, kind="ExternalOutput").ap()
    else:
        x3T = _out

    _p = f"u{_uid()}_"
    sb = lambda name, shape, dt: stack.enter_context(nc.sbuf_tensor(_p + name, shape, dt))
    psa = lambda name, shape: stack.enter_context(nc.psum_tensor(_p + name, shape, F32))
    if bg is not None:
        bg.alloc(stack)

    NSC = CH // 128

    wqb = [sb(f"wqb{t}", [128, D], BF16) for t in range(NT)]
    wkb = [sb(f"wkb{t}", [128, D], BF16) for t in range(NT)]
    wvb = [sb(f"wvb{t}", [128, D], BF16) for t in range(NT)]
    wob = [sb(f"wob{t}", [128, D], BF16) for t in range(NT)]
    arena = sb("arena", [128, 32 * CH], BF16)
    hbuf = arena
    xbc = [arena[:, 512 * t:512 * (t + 1)] for t in range(NT)]
    srcb = [sb(f"srcb{t}", [128, CH], BF16) for t in range(NT)]
    w1b = [sb(f"w1b{k}", [128, 1024], BF16) for k in range(4)]
    w2b = [sb(f"w2b{k}", [128, 512], BF16) for k in range(4)]
    kf = sb("kf", [128, 1024], F32)
    ebk = sb("ebk", [128, 1024], BF16)
    Ktok = sb("Ktok", [128, 1024], BF16)
    Vtok = sb("Vtok", [128, 1024], BF16)
    KVb = sb("KVb", [64, 1024], BF16)
    Ksumr = sb("Ksumr", [1, 1024], F32)
    KsumColb = sb("KsumColb", [64, 16], BF16)
    qe = [sb(f"qe{k}", [64, CH], BF16) for k in range(2)]
    Obuf = arena[0:64, 16 * CH:32 * CH]
    Opair = arena[:, 8 * CH:16 * CH]
    stkb = sb("stkb", [64, 256], BF16)
    den2 = sb("den2", [1, 2 * CH], F32)
    rd16 = sb("rd16", [16, CH], F32)
    denbf = sb("denbf", [64, CH], F32)
    uc = srcb
    xln = [sb(f"xln{t}", [128, CH], BF16) for t in range(NT)]
    u2c = xln
    tev = sb("tev", [128, CH], F32)
    tevb = sb("tevb", [128, CH], F32)
    ta = sb("ta", [64, CH], F32)
    x3sb = [sb(f"x3sb{k}", [128, CH], F32) for k in range(2)]
    xres = [sb(f"xres{k}", [128, CH], F32) for k in range(2)]
    s1row = sb("s1row", [1, CH], F32)
    s2row = sb("s2row", [1, CH], F32)
    mrow = sb("mrow", [1, CH], F32)
    mr2 = sb("mr2", [1, CH], F32)
    rrow = sb("rrow", [1, CH], F32)
    rb = sb("rb", [128, CH], F32)
    mrb = sb("mrb", [128, CH], F32)
    xsq2 = [sb(f"xsq2{k}", [128, CH], BF16) for k in range(2)]
    bkb = sb("bkb", [128, D], BF16)
    bvb = sb("bvb", [128, D], BF16)
    ones128 = sb("ones128", [128, 1], BF16)
    onesrow_f = sb("onesrow_f", [1, 128], F32)
    one1f = sb("one1f", [1, 1], F32)
    eps5 = sb("eps5", [1, 1], F32)
    eps6 = sb("eps6", [128, 1], F32)
    bq64 = sb("bq64", [64, 16], F32)
    bkrr = kf[0:1, :]
    bo_c = sb("bo_c", [128, 8], F32)
    b1_c = sb("b1_c", [128, 32], F32)
    b2_c = sb("b2_c", [128, 8], F32)
    g1_c = sb("g1_c", [128, 8], F32)
    be1_c = sb("be1_c", [128, 8], F32)
    g2_c = sb("g2_c", [128, 8], F32)
    be2_c = sb("be2_c", [128, 8], F32)
    sel16s = sb("sel16s", [1, 256], F32)
    selcs = sb("selcs", [16, 1024], F32)

    pk = [psa(f"pk{k}", [128, CH]) for k in range(2)]
    pkv = [psa(f"pkv{k}", [128, CH]) for k in range(2)]
    pks = psa("pks", [128, CH])
    pm1 = psa("pm1", [128, CH])
    pm2 = psa("pm2", [128, CH])
    pm3 = psa("pm3", [128, CH])

    lw = Sem(nc, stack, "lw")
    lb1 = Sem(nc, stack, "lb1")
    lx = Sem(nc, stack, "lx")
    lfp = [Sem(nc, stack, f"lf{k}") for k in range(4)]
    lfd = [Sem(nc, stack, f"lfd{k}") for k in range(4)]
    w1cw = Sem(nc, stack, "w1cw")
    w2cw = Sem(nc, stack, "w2cw")
    lr = Sem(nc, stack, "lr")
    so = Sem(nc, stack, "so")
    sop = [Sem(nc, stack, "soA"), Sem(nc, stack, "soB")]
    tS = Sem(nc, stack, "tS")
    vS = Sem(nc, stack, "vS")
    sS = Sem(nc, stack, "sS")
    t_pk = Sem(nc, stack, "t_pk")
    t_pkd = Sem(nc, stack, "t_pkd")
    v_rd = Sem(nc, stack, "v_rd")

    T, V, S, G, SY = nc.tensor, nc.vector, nc.scalar, nc.gpsimd, nc.sync
    Vw = make_serial(V, vS)
    Sw = make_serial(S, sS)
    Vn = make_serial(V, vS, auto_inc=False)
    Sn = make_serial(S, sS, auto_inc=False)
    if ctx is not None:
        for ws in _waits:
            G.wait_ge(ws.h, ws.n)
            SY.wait_ge(ws.h, ws.n)

    for t in range(NT):
        G.dma_start(out=wqb[t][:, :], in_=Wq[128 * t:128 * (t + 1), :]).then_inc(lw.h, 16)
        G.dma_start(out=wkb[t][:, :], in_=Wk[128 * t:128 * (t + 1), :]).then_inc(lw.h, 16)
        G.dma_start(out=wvb[t][:, :], in_=Wv[128 * t:128 * (t + 1), :]).then_inc(lw.h, 16)
        G.dma_start(out=wob[t][:, :], in_=Wo[128 * t:128 * (t + 1), :]).then_inc(lw.h, 16)
    G.dma_start(out=stkb[:, :], in_=stk[:, :]).then_inc(lw.h, 16)
    lw.n += 33 * 16
    for srcap, dst in ((bqc, bq64), (bkr, bkrr), (boc, bo_c), (b1c, b1_c),
                       (b2c, b2_c), (g1c, g1_c), (be1c, be1_c), (g2c, g2_c), (be2c, be2_c),
                       (sel16x, sel16s), (selcol, selcs)):
        SY.dma_start(out=dst[:, :], in_=srcap[:, :]).then_inc(lb1.h, 16)
    lb1.n += 11 * 16
    Vw.memset(ones128[:, :], 1.0)
    Vw.memset(onesrow_f[:, :], 1.0)
    Vw.memset(one1f[:, :], 1.0)
    Vw.memset(eps5[:, :], 1e-5)
    vS.inc(Vn.memset(eps6[:, :], 1e-6))
    vS.wait(T)
    vS.wait(S)
    lw.wait(T)
    lb1.wait(T)
    lb1.wait(S)
    lb1.wait(V)
    T.matmul(pm1[:, :], onesrow_f[:, :], bkrr[0:1, 0:512], start=True, stop=True)
    tS.inc(T.matmul(pm2[:, :], onesrow_f[:, :], bkrr[0:1, 512:1024], start=True, stop=True))
    SY.wait_ge(tS.h, tS.n)
    SY.dma_start(out=bkrr[:, :], in_=bvr[:, :]).then_inc(lb1.h, 16)
    lb1.n += 16
    lb1.wait(T)
    T.matmul(pm3[:, :], onesrow_f[:, :], bkrr[0:1, 0:512], start=True, stop=True)
    T.matmul(pk[0][:, :], onesrow_f[:, :], bkrr[0:1, 512:1024], start=True, stop=True)
    tS.inc(T.drain(fusable=False))
    tS.wait(V)
    Vw.tensor_copy(out=bkb[:, 0:512], in_=pm1[:, :])
    Vw.tensor_copy(out=bkb[:, 512:1024], in_=pm2[:, :])
    Vw.tensor_copy(out=bvb[:, 0:512], in_=pm3[:, :])
    vS.inc(Vn.tensor_copy(out=bvb[:, 512:1024], in_=pk[0][:, :]))
    vS.wait(T)

    def load_chunk(c, gate):
        G.wait_ge(tS.h, gate)
        for t in range(NT):
            G.dma_start(out=xbc[t][:, :],
                        in_=xT[128 * t:128 * (t + 1), c * CH:(c + 1) * CH]).then_inc(lx.h, 16)
        lx.n += NT * 16
        return lx.n

    def stats_rows(tiles, into_s1b=None):
        sq0 = vS.n
        for t in range(NT):
            if t >= 2:
                V.wait_ge(tS.h, tS.n)
            vS.inc(Vn.tensor_mul(out=xsq2[t % 2][:, :], in0=tiles[t][:, :], in1=tiles[t][:, :]))
            V_cnt = vS.n
            T.wait_ge(vS.h, V_cnt)
            T.matmul(pm1[0:1, :], ones128[:, :], tiles[t][:, :],
                     start=(t == 0), stop=(t == NT - 1))
            tS.inc(T.matmul(pm2[0:1, :], ones128[:, :], xsq2[t % 2][:, :],
                            start=(t == 0), stop=(t == NT - 1)))
        tS.inc(T.drain(fusable=False))
        tS.wait(V)
        Vw.tensor_copy(out=s1row[:, :], in_=pm1[0:1, :])
        Vw.tensor_copy(out=s2row[:, :], in_=pm2[0:1, :])
        Vw.tensor_scalar(out=mrow[:, :], in0=s1row[:, :], scalar1=1.0 / D,
                        scalar2=None, op0=ALU.mult)
        Vw.tensor_mul(out=mr2[:, :], in0=mrow[:, :], in1=mrow[:, :])
        vS.inc(Vn.scalar_tensor_tensor(out=s1row[:, :], in0=s2row[:, :], scalar=1.0 / D,
                                      in1=mr2[:, :], op0=ALU.mult, op1=ALU.subtract))
        vS.wait(S)
        Sw.activation(out=s2row[:, :], in_=s1row[:, :], func=AF.Ln, bias=eps5[0:1, 0:1], scale=1.0)
        sS.inc(Sn.activation(out=rrow[0:1, :], in_=s2row[:, :], func=AF.Exp,
                            bias=0.0, scale=-0.5))
        sS.wait(V)
        vS.inc(Vn.tensor_mul(out=mr2[:, :], in0=mrow[:, :], in1=rrow[:, :]))
        vS.wait(T)
        sS.wait(T)
        T.matmul(pm1[:, :], onesrow_f[:, :], rrow[0:1, :], start=True, stop=True)
        T.matmul(pm2[:, :], onesrow_f[:, :], mr2[0:1, :], start=True, stop=True)
        tS.inc(T.drain(fusable=False))
        tS.wait(V)
        Vw.tensor_copy(out=rb[:, :], in_=pm1[:, :])
        vS.inc(Vn.tensor_copy(out=mrb[:, :], in_=pm2[:, :]))
        vS.wait(T)

    def ln_apply(src_tiles, out_tiles, gcol, bcol):
        for t in range(NT):
            V.wait_ge(sS.h, sS.n)
            Vw.tensor_tensor(out=tev[:, :], in0=src_tiles[t][:, :], in1=rb[:, :], op=ALU.mult)
            vS.inc(Vn.tensor_sub(out=tevb[:, :], in0=tev[:, :], in1=mrb[:, :]))
            vS.wait(S)
            sS.inc(Sn.activation(out=out_tiles[t][:, :], in_=tevb[:, :], func=AF.Identity,
                                scale=gcol[:, t:t + 1] if gcol is not None else 1.0,
                                bias=bcol[:, t:t + 1] if bcol is not None else 0.0))
        sS.wait(V)
        sS.wait(T)

    KV_first = True
    adds_done = 0
    w1_last = [0, 0, 0, 0]
    w2_last = [0, 0, 0, 0]
    gate = tS.n
    for c in range(NCH):
        loaded = load_chunk(c, gate)
        V.wait_ge(lx.h, loaded)
        T.wait_ge(lx.h, loaded)
        stats_rows(xbc)
        ln_apply(xbc, srcb, None, None)

        for j in range(NSC if stage >= 2 else 0):
            bg_step()
            o0 = j * 128
            for half in range(2):
                for t in range(NT):
                    T.matmul(pk[half][:, :], srcb[t][:, o0:o0 + 128],
                             wkb[t][:, 512 * half:512 * (half + 1)],
                             start=(t == 0), stop=(t == NT - 1))
            tS.inc(T.drain(fusable=False))
            tS.wait(V)
            Vw.tensor_copy(out=kf[:, 0:512], in_=pk[0][:, :])
            vS.inc(Vn.tensor_copy(out=kf[:, 512:1024], in_=pk[1][:, :]))
            vS.wait(T)
            if stage < 2.5:
                continue
            for half in range(2):
                for t in range(NT):
                    T.matmul(pk[half][:, :], srcb[t][:, o0:o0 + 128],
                             wvb[t][:, 512 * half:512 * (half + 1)],
                             start=(t == 0), stop=(t == NT - 1))
            tS.inc(T.drain(fusable=False))
            if stage < 2.7:
                continue
            Vw.tensor_add(out=kf[:, :], in0=kf[:, :], in1=bkb[:, :])
            Vw.tensor_scalar_min(out=ebk[:, :], in0=kf[:, :], scalar1=0.0)
            vS.inc(Vn.nop())
            vS.wait(S)
            sS.inc(Sn.activation(out=ebk[:, :], in_=ebk[:, :], func=AF.Exp))
            Vw.tensor_scalar_max(out=kf[:, :], in0=kf[:, :], scalar1=0.0)
            sS.wait(V)
            vS.inc(Vn.tensor_add(out=Ktok[:, :], in0=kf[:, :], in1=ebk[:, :]))
            tS.wait(V)
            Vw.tensor_copy(out=kf[:, 0:512], in_=pk[0][:, :])
            Vw.tensor_copy(out=kf[:, 512:1024], in_=pk[1][:, :])
            vS.inc(Vn.tensor_add(out=Vtok[:, :], in0=kf[:, :], in1=bvb[:, :]))
            vS.wait(T)
            if stage < 2.9:
                continue
            KV_last = (c == NCH - 1 and j == NSC - 1)
            for i in range(16):
                bank = pkv[i // 8]
                T.matmul(bank[0:64, 64 * (i % 8):64 * (i % 8) + 64],
                         Ktok[:, 64 * i:64 * (i + 1)], Vtok[:, 64 * i:64 * (i + 1)],
                         start=KV_first, stop=KV_last)
            T.matmul(pks[0:1, :], ones128[:, :], Ktok[:, 0:512],
                     start=KV_first, stop=KV_last)
            T.matmul(pm3[0:1, :], ones128[:, :], Ktok[:, 512:1024],
                     start=KV_first, stop=KV_last)
            KV_first = False
        gate = tS.n
    if stage < 3:
        SY.wait_ge(so.h, so.n)
        stack.close()
        return nc
    tS.inc(T.drain(fusable=False))
    tS.wait(V)
    Vw.tensor_copy(out=KVb[:, 0:512], in_=pkv[0][0:64, :])
    Vw.tensor_copy(out=KVb[:, 512:1024], in_=pkv[1][0:64, :])
    Vw.tensor_copy(out=Ksumr[0:1, 0:512], in_=pks[0:1, :])
    vS.inc(Vn.tensor_copy(out=Ksumr[0:1, 512:1024], in_=pm3[0:1, :]))
    vS.wait(T)
    for i in range(16):
        T.matmul(pm1[0:64, i:i + 1], Ksumr[0:1, 64 * i:64 * (i + 1)], one1f[:, :],
                 start=True, stop=True)
    tS.inc(T.drain(fusable=False))
    tS.wait(S)
    sS.inc(Sn.activation(out=KsumColb[:, :], in_=pm1[0:64, 0:16], func=AF.Identity, scale=1.0))
    sS.wait(T)
    sS.wait(V)

    gate2 = tS.n
    for c in range(NCH):
        loaded = load_chunk(c, gate2)
        V.wait_ge(lx.h, loaded)
        T.wait_ge(lx.h, loaded)
        stats_rows(xbc)
        ln_apply(xbc, srcb, None, None)

        for i in range(16 if stage >= 4 else 0):
            bg_step()
            H = c * 16 + i
            T.wait_ge(vS.h, vS.n)
            for t in range(NT):
                T.matmul(pm1[0:64, :], wqb[t][:, 64 * i:64 * (i + 1)], srcb[t][:, :],
                         start=(t == 0), stop=(t == NT - 1))
            tS.inc(T.drain(fusable=False))
            tS.wait(S)
            if H >= 2:
                S.wait_ge(t_pk.h, H - 1)
            Sw.activation(out=tev[0:64, :], in_=pm1[0:64, :], func=AF.Identity,
                         bias=bq64[:, i:i + 1], scale=1.0)
            sS.inc(Sn.nop())
            sS.wait(V)
            Vw.tensor_scalar_min(out=ta[:, :], in0=tev[0:64, :], scalar1=0.0)
            vS.inc(Vn.nop())
            vS.wait(S)
            sS.inc(Sn.activation(out=ta[:, :], in_=ta[:, :], func=AF.Exp))
            Vw.tensor_scalar_max(out=tev[0:64, :], in0=tev[0:64, :], scalar1=0.0)
            sS.wait(V)
            vS.inc(Vn.tensor_add(out=qe[H % 2][:, :], in0=tev[0:64, :], in1=ta[:, :]))
            vS.wait(T)
            dslot = H % 2
            T.matmul(pm2[0:1, :], KsumColb[:, i:i + 1], qe[H % 2][:, :],
                     start=True, stop=True)
            T.matmul(pm3[0:64, :], KVb[:, 64 * i:64 * (i + 1)], qe[H % 2][:, :],
                     start=True, stop=True)
            t_pk.inc(T.drain(fusable=False))
            V.wait_ge(t_pk.h, t_pk.n)
            Vw.tensor_copy(out=den2[0:1, dslot * CH:(dslot + 1) * CH], in_=pm2[0:1, :])
            vS.inc(Vn.tensor_copy(out=Obuf[:, i * CH:(i + 1) * CH], in_=pm3[0:64, :]))
            vS.wait(T)
            if i == 0:
                T.wait_ge(v_rd.h, c)
            T.matmul(pk[0][0:16, :], sel16s[0:1, 16 * i:16 * (i + 1)],
                     den2[0:1, dslot * CH:(dslot + 1) * CH],
                     start=(i == 0), stop=(i == 15))
            if i == 15:
                t_pkd.inc(T.drain(fusable=False))
        if stage < 5:
            gate2 = tS.n
            continue
        S.wait_ge(t_pkd.h, c + 1)
        sS.inc(Sn.activation(out=denbf[0:16, :], in_=pk[0][0:16, :], func=AF.Identity,
                            bias=eps6[0:16, 0:1], scale=1.0))
        sS.wait(V)
        v_rd.inc(Vn.reciprocal(out=rd16[:, :], in_=denbf[0:16, :]))
        for i in range(16):
            T.wait_ge(v_rd.h, c + 1)
            T.wait_ge(vS.h, vS.n)
            T.matmul(pm1[0:64, :], selcs[:, 64 * i:64 * (i + 1)], rd16[:, :],
                     start=True, stop=True)
            tS.inc(T.drain(fusable=False))
            tS.wait(V)
            Vw.tensor_copy(out=denbf[:, :], in_=pm1[0:64, :])
            vS.inc(Vn.tensor_tensor(out=Obuf[:, i * CH:(i + 1) * CH],
                                   in0=Obuf[:, i * CH:(i + 1) * CH],
                                   in1=denbf[:, :], op=ALU.mult))
        for pr in range(8):
            T.wait_ge(vS.h, vS.n)
            T.matmul(pm1[:, :], stkb[:, 0:128], Obuf[:, (2 * pr) * CH:(2 * pr + 1) * CH],
                     start=True, stop=False)
            T.matmul(pm1[:, :], stkb[:, 128:256],
                     Obuf[:, (2 * pr + 1) * CH:(2 * pr + 2) * CH],
                     start=False, stop=True)
            tS.inc(T.drain(fusable=False))
            tS.wait(V)
            vS.inc(Vn.tensor_copy(out=Opair[:, pr * CH:(pr + 1) * CH], in_=pm1[:, :]))
        for dt in range(NT):
            T.wait_ge(vS.h, vS.n)
            for pr in range(8):
                T.matmul(pm2[:, :], wob[pr][:, 128 * dt:128 * (dt + 1)],
                         Opair[:, pr * CH:(pr + 1) * CH],
                         start=(pr == 0), stop=(pr == 7))
            tS.inc(T.drain(fusable=False))
            tS.wait(S)
            sS.inc(Sn.activation(out=tev[:, :], in_=pm2[:, :], func=AF.Identity,
                                bias=bo_c[:, dt:dt + 1], scale=1.0))
            sS.wait(V)
            vS.inc(Vn.tensor_add(out=uc[dt][:, :], in0=tev[:, :], in1=srcb[dt][:, :]))
        vS.wait(T)
        vS.wait(S)

        if stage < 6:
            gate2 = tS.n
            continue
        stats_rows(uc)
        ln_apply(uc, xln, g1_c, be1_c)

        hbanks = [pm1, pk[0]]
        gelu_done = []
        mm_done = []
        for ft in range(32):
            slot = ft % 4
            G.wait_ge(tS.h, mm_done[ft - 4] if ft >= 4 else w1_last[slot])
            if c == 0:
                if ft >= 4:
                    G.wait_ge(w1cw.h, 16 * (ft - 3))
                G.dma_start(out=w1b[slot].rearrange("p (t j) -> p t j", t=8),
                            in_=W1.rearrange("(t p) f -> p t f", p=128)[:, :, 128 * ft:128 * (ft + 1)]
                            ).then_inc(lfp[slot].h, 16)
            else:
                if c == 1 and ft == 0:
                    G.wait_ge(w1cw.h, 16 * 32)
                G.dma_start(out=w1b[slot][:, :],
                            in_=W1c[:, 1024 * ft:1024 * (ft + 1)]).then_inc(lfp[slot].h, 16)
            lfp[slot].n += 16
            if c == 0:
                S.wait_ge(lfp[slot].h, lfp[slot].n)
                S.dma_start(out=W1c[:, 1024 * ft:1024 * (ft + 1)],
                            in_=w1b[slot][:, :]).then_inc(w1cw.h, 16)
            T.wait_ge(lfp[slot].h, lfp[slot].n)
            if ft >= 2:
                T.wait_ge(sS.h, gelu_done[ft - 2])
            for t in range(NT):
                mm = T.matmul(hbanks[ft % 2][:, :], w1b[slot][:, 128 * t:128 * (t + 1)],
                              xln[t][:, :], start=(t == 0), stop=(t == NT - 1))
            tS.inc(mm)
            mm_done.append(tS.n)
            w1_last[slot] = tS.n
            if ft >= 1:
                S.wait_ge(tS.h, mm_done[ft])
                sS.inc(Sn.activation(out=hbuf[:, (ft - 1) * CH:ft * CH],
                                     in_=hbanks[(ft - 1) % 2][:, :],
                                     func=AF.Gelu, bias=b1_c[:, ft - 1:ft], scale=1.0))
                gelu_done.append(sS.n)
        tS.inc(T.drain(fusable=False))
        S.wait_ge(tS.h, tS.n)
        sS.inc(Sn.activation(out=hbuf[:, 31 * CH:32 * CH], in_=hbanks[31 % 2][:, :],
                             func=AF.Gelu, bias=b1_c[:, 31:32], scale=1.0))
        gelu_done.append(sS.n)
        sS.wait(T)
        ybanks = [pm2, pm3, pk[0], pk[1]]
        if stage < 7:
            gate2 = tS.n
            continue
        for half in range(2):
            dn_done = []
            for ft in range(32):
                slot = ft % 4
                G.wait_ge(tS.h, dn_done[ft - 4] if ft >= 4 else w2_last[slot])
                if c == 0:
                    nw = half * 32 + ft
                    if nw >= 4:
                        G.wait_ge(w2cw.h, 16 * (nw - 3))
                    G.dma_start(out=w2b[slot][:, :],
                                in_=W2[128 * ft:128 * (ft + 1),
                                       512 * half:512 * (half + 1)]).then_inc(lfd[slot].h, 16)
                else:
                    if c == 1 and half == 0 and ft == 0:
                        G.wait_ge(w2cw.h, 16 * 64)
                    G.dma_start(out=w2b[slot][:, :],
                                in_=W2c[128 * ft:128 * (ft + 1),
                                        512 * half:512 * (half + 1)]).then_inc(lfd[slot].h, 16)
                lfd[slot].n += 16
                if c == 0:
                    S.wait_ge(lfd[slot].h, lfd[slot].n)
                    S.dma_start(out=W2c[128 * ft:128 * (ft + 1),
                                        512 * half:512 * (half + 1)],
                                in_=w2b[slot][:, :]).then_inc(w2cw.h, 16)
                T.wait_ge(lfd[slot].h, lfd[slot].n)
                for dl in range(4):
                    mm = T.matmul(ybanks[dl][:, :], w2b[slot][:, 128 * dl:128 * (dl + 1)],
                                  hbuf[:, ft * CH:(ft + 1) * CH],
                                  start=(ft == 0), stop=(ft == 31))
                tS.inc(mm)
                dn_done.append(tS.n)
                w2_last[slot] = tS.n
            tS.inc(T.drain(fusable=False))
            tS.wait(S)
            for dl in range(4):
                dt = half * 4 + dl
                S.wait_ge(vS.h, vS.n)
                sS.inc(Sn.activation(out=tev[:, :], in_=ybanks[dl][:, :], func=AF.Identity,
                                    bias=b2_c[:, dt:dt + 1], scale=1.0))
                sS.wait(V)
                vS.inc(Vn.tensor_add(out=u2c[dt][:, :], in0=tev[:, :], in1=xln[dt][:, :]))
            vS.wait(T)
        vS.wait(S)

        if stage < 8:
            gate2 = tS.n
            continue
        stats_rows(u2c)
        for dt in range(NT):
            SY.wait_ge(vS.h, adds_done)
            SY.dma_start(out=xres[dt % 2][:, :],
                         in_=xT[128 * dt:128 * (dt + 1), c * CH:(c + 1) * CH]).then_inc(lr.h, 16)
            lr.n += 16
            V.wait_ge(sS.h, sS.n)
            Vw.tensor_tensor(out=tev[:, :], in0=u2c[dt][:, :], in1=rb[:, :], op=ALU.mult)
            vS.inc(Vn.tensor_sub(out=tevb[:, :], in0=tev[:, :], in1=mrb[:, :]))
            vS.wait(S)
            sS.inc(Sn.activation(out=tev[:, :], in_=tevb[:, :], func=AF.Identity,
                                scale=g2_c[:, dt:dt + 1], bias=be2_c[:, dt:dt + 1]))
            sS.wait(V)
            V.wait_ge(lr.h, lr.n)
            V.wait_ge(sop[dt % 2].h, sop[dt % 2].n)
            vS.inc(Vn.tensor_add(out=x3sb[dt % 2][:, :], in0=tev[:, :], in1=xres[dt % 2][:, :]))
            adds_done = vS.n
            vS.wait(SY)
            SY.dma_start(out=x3T[128 * dt:128 * (dt + 1), c * CH:(c + 1) * CH],
                         in_=x3sb[dt % 2][:, :]).then_inc(sop[dt % 2].h, 16)
            sop[dt % 2].n += 16
        vS.wait(T)
        gate2 = tS.n

    if bg is not None:
        bg.finish()
    SY.wait_ge(sop[0].h, sop[0].n)
    SY.wait_ge(sop[1].h, sop[1].n)
    if ctx is None:
        stack.close()
        return nc
    return sop



def build_l1_ffn(N=N, NCH=NCH, dbg=False, ctx=None, bg=None):
    if ctx is None:
        nc = bass.Bass("TRN2", target_bir_lowering=False)
        stack = contextlib.ExitStack()
        xT = nc.dram_tensor("xT", [D, N], F32, kind="ExternalInput").ap()
    else:
        nc, stack, xT, _out, _waits = ctx
    if bg is None:
        wupT = nc.dram_tensor("wupT", [D, 4096], F32, kind="ExternalInput").ap()
        wdnT = nc.dram_tensor("wdnT", [4096, D], F32, kind="ExternalInput").ap()
    else:
        wupT, wdnT = bg.wupT, bg.wdnT
    if ctx is None:
        outT = nc.dram_tensor("outT", [D, N], F32, kind="ExternalOutput").ap()
    else:
        outT = _out
    if bg is None:
        wtup = nc.dram_tensor("wtup", [128, 32 * 1024], BF16, kind="Internal").ap()
        wtdn = nc.dram_tensor("wtdn", [4096, D], BF16, kind="Internal").ap()
    else:
        wtup, wtdn = bg.wtup, bg.wtdn
    if dbg:
        d_wtu = nc.dram_tensor("d_wtu", [D, 4096], mybir.dt.bfloat16, kind="ExternalOutput").ap()
        d_xn = nc.dram_tensor("d_xn", [128, CH], mybir.dt.bfloat16, kind="ExternalOutput").ap()
        d_h = nc.dram_tensor("d_h", [128, 2 * CH], mybir.dt.bfloat16, kind="ExternalOutput").ap()

    _p = f"u{_uid()}_"
    sb = lambda name, shape, dt: stack.enter_context(nc.sbuf_tensor(_p + name, shape, dt))
    psa = lambda name, shape: stack.enter_context(nc.psum_tensor(_p + name, shape, F32))

    arena = sb("arena", [128, 32 * CH], BF16)
    hbuf = arena
    xbc = [arena[:, 512 * t:512 * (t + 1)] for t in range(NT)]
    xnb = [sb(f"xnb{t}", [128, CH], BF16) for t in range(NT)]
    if bg is None:
        wq1 = [sb(f"wq1{k}", [128, 1024], F32) for k in range(2)]
        sgn = sb("sgn", [128, 1024], F32)
        ab = sb("ab", [128, 1024], F32)
        wtb = [sb(f"wtb{k}", [128, 1024], BF16) for k in range(2)]
        acc = sb("acc", [128, 33], F32)
        thr1 = sb("thr1", [1, 1], F32)
        negthrb = sb("negthrb", [128, 1024], F32)
        ab2 = sb("ab2", [128, 1024], F32)
    wupb = [sb(f"wupb{k}", [128, 1024], BF16) for k in range(4)]
    wdnb = [sb(f"wdnb{k}", [128, 512], BF16) for k in range(4)]
    hsq2p = [sb(f"hsq2p{k}", [128, CH], BF16) for k in range(2)]
    onescol_f = sb("onescol_f", [128, 1], F32)
    s1row = sb("s1row", [1, CH], F32)
    s2row = sb("s2row", [1, CH], F32)
    mrow = sb("mrow", [1, CH], F32)
    mr2 = sb("mr2", [1, CH], F32)
    rhrow = sb("rhrow", [1, CH], F32)
    r2row = sb("r2row", [1, CH], F32)
    rb = sb("rb", [128, CH], F32)
    mrb = sb("mrb", [128, CH], F32)
    r2b = sb("r2b", [128, CH], F32)
    xsq2 = [sb(f"xsq2{k}", [128, CH], BF16) for k in range(2)]
    tev = sb("tev", [128, CH], F32)
    tevb = sb("tevb", [128, CH], F32)
    x4sb = [sb(f"x4sb{k}", [128, CH], F32) for k in range(2)]
    xres = [sb(f"xres{k}", [128, CH], F32) for k in range(2)]
    ones128 = sb("ones128", [128, 1], BF16)
    onesrow_f = sb("onesrow_f", [1, 128], F32)
    eps5 = sb("eps5", [1, 1], F32)

    pm1 = psa("pm1", [128, CH])
    pm2 = psa("pm2", [128, CH])
    ybanks = [psa(f"py{k}", [128, CH]) for k in range(4)]

    lw = Sem(nc, stack, "lw")
    lx = Sem(nc, stack, "lx")
    lfp = [Sem(nc, stack, f"lf{k}") for k in range(4)]
    lfd = [Sem(nc, stack, f"lfd{k}") for k in range(4)]
    lr = Sem(nc, stack, "lr")
    swp = [Sem(nc, stack, "swA"), Sem(nc, stack, "swB")]
    sop = [Sem(nc, stack, "soA"), Sem(nc, stack, "soB")]
    tS = Sem(nc, stack, "tS")
    vS = Sem(nc, stack, "vS")
    sS = Sem(nc, stack, "sS")

    T, V, S, G, SY = nc.tensor, nc.vector, nc.scalar, nc.gpsimd, nc.sync
    Vw = make_serial(V, vS)
    Sw = make_serial(S, sS)
    Vn = make_serial(V, vS, auto_inc=False)
    Sn = make_serial(S, sS, auto_inc=False)
    if ctx is not None:
        for ws in _waits:
            G.wait_ge(ws.h, ws.n)
            SY.wait_ge(ws.h, ws.n)

    Vw.memset(ones128[:, :], 1.0)
    Vw.memset(onesrow_f[:, :], 1.0)
    Vw.memset(onescol_f[:, :], 1.0)
    Vw.memset(eps5[:, :], 1e-5)
    vS.wait(T)
    vS.wait(S)

    def ternarize(wsrc, wdst, ntiles, width, permuted=False):
        nq = width // 1024
        pieces = []
        for t in range(ntiles):
            for q in range(nq):
                pieces.append((t, q))
        for pi, (t, q) in enumerate(pieces):
            slot = pi % 2
            SY.wait_ge(sS.h, sS.n)
            if pi >= 2:
                SY.wait_ge(vS.h, vS.n)
            SY.dma_start(out=wq1[slot][:, :],
                         in_=wsrc[128 * t:128 * (t + 1), 1024 * q:1024 * (q + 1)]
                         ).then_inc(lw.h, 16)
            lw.n += 16
            V.wait_ge(lw.h, lw.n)
            Vw.tensor_reduce(out=acc[:, pi % 32:pi % 32 + 1], in_=wq1[slot][:, :],
                             axis=AX.X, op=ALU.add, apply_absolute_value=True)
            if (pi + 1) % 32 == 0 or pi == len(pieces) - 1:
                k = (pi % 32) + 1
                if pi < 32:
                    Vw.tensor_reduce(out=acc[:, 32:33], in_=acc[:, 0:k], axis=AX.X,
                                     op=ALU.add)
                else:
                    Vw.tensor_reduce(out=acc[:, 0:1], in_=acc[:, 0:k], axis=AX.X,
                                     op=ALU.add)
                    Vw.tensor_add(out=acc[:, 32:33], in0=acc[:, 32:33], in1=acc[:, 0:1])
        vS.wait(T)
        T.matmul(pm1[0:1, 0:1], acc[:, 32:33], onescol_f[:, :], start=True, stop=True)
        tS.inc(T.drain(fusable=False))
        tS.wait(S)
        Sw.activation(out=thr1[:, :], in_=pm1[0:1, 0:1], func=AF.Identity,
                      scale=-0.5 / (ntiles * 128 * width))
        sS.wait(T)
        T.matmul(pm2[:, 0:1], onesrow_f[:, :], thr1[:, :], start=True, stop=True)
        tS.inc(T.drain(fusable=False))
        tS.wait(V)
        vS.inc(Vn.tensor_copy(out=negthrb[:, 0:1], in_=pm2[:, 0:1]))
        for rep in range(10):
            w0 = 1 << rep
            w1 = min(1024 - w0, w0)
            if w1 <= 0:
                break
            Vw.tensor_copy(out=negthrb[:, w0:w0 + w1], in_=negthrb[:, 0:w1])
        for pi, (t, q) in enumerate(pieces):
            slot = pi % 2
            SY.wait_ge(sS.h, sS.n)
            SY.dma_start(out=wq1[slot][:, :],
                         in_=wsrc[128 * t:128 * (t + 1), 1024 * q:1024 * (q + 1)]
                         ).then_inc(lw.h, 16)
            lw.n += 16
            S.wait_ge(lw.h, lw.n)
            S.wait_ge(vS.h, vS.n)
            Sw.activation(out=sgn[:, :], in_=wq1[slot][:, :], func=AF.Sign)
            Sw.activation(out=ab[:, :], in_=wq1[slot][:, :], func=AF.Abs)
            sS.wait(V)
            Vw.tensor_add(out=ab[:, :], in0=ab[:, :], in1=negthrb[:, :])
            vS.wait(S)
            Sw.activation(out=ab2[:, :], in_=ab[:, :], func=AF.Sign)
            sS.wait(V)
            Vw.tensor_scalar_max(out=ab2[:, :], in0=ab2[:, :], scalar1=0.0)
            V.wait_ge(swp[pi % 2].h, swp[pi % 2].n)
            vS.inc(Vn.tensor_mul(out=wtb[pi % 2][:, :], in0=ab2[:, :], in1=sgn[:, :]))
            vS.wait(G)
            if permuted:
                for fl in range(8):
                    o0 = ((8 * q + fl) * 8 + t) * 128
                    G.dma_start(out=wdst[:, o0:o0 + 128],
                                in_=wtb[pi % 2][:, 128 * fl:128 * (fl + 1)]
                                ).then_inc(swp[pi % 2].h, 16)
                    swp[pi % 2].n += 16
            else:
                G.dma_start(out=wdst[128 * t:128 * (t + 1), 1024 * q:1024 * (q + 1)],
                            in_=wtb[pi % 2][:, :]).then_inc(swp[pi % 2].h, 16)
                swp[pi % 2].n += 16

    if bg is None:
        ternarize(wupT, wtup, NT, 4096, permuted=True)
        ternarize(wdnT, wtdn, 32, 1024)
        G.wait_ge(swp[0].h, swp[0].n)
        G.wait_ge(swp[1].h, swp[1].n)
    else:
        bg.finish()
        G.wait_ge(bg.swp[0].h, bg.swp[0].n)
        G.wait_ge(bg.swp[1].h, bg.swp[1].n)

    t_lastuse = 0
    adds_done = 0
    wu_last = [0, 0, 0, 0]
    wd_last = [0, 0, 0, 0]
    for c in range(NCH):
        G.wait_ge(tS.h, t_lastuse)
        for t in range(NT):
            G.dma_start(out=xbc[t][:, :],
                        in_=xT[128 * t:128 * (t + 1), c * CH:(c + 1) * CH]).then_inc(lx.h, 16)
        lx.n += NT * 16
        V.wait_ge(lx.h, lx.n)
        T.wait_ge(lx.h, lx.n)
        for t in range(NT):
            if t >= 2:
                V.wait_ge(tS.h, tS.n)
            vS.inc(Vn.tensor_mul(out=xsq2[t % 2][:, :], in0=xbc[t][:, :], in1=xbc[t][:, :]))
            T.wait_ge(vS.h, vS.n)
            T.matmul(pm1[0:1, :], ones128[:, :], xbc[t][:, :],
                     start=(t == 0), stop=(t == NT - 1))
            tS.inc(T.matmul(pm2[0:1, :], ones128[:, :], xsq2[t % 2][:, :],
                            start=(t == 0), stop=(t == NT - 1)))
        tS.inc(T.drain(fusable=False))
        tS.wait(V)
        Vw.tensor_copy(out=s1row[:, :], in_=pm1[0:1, :])
        Vw.tensor_copy(out=s2row[:, :], in_=pm2[0:1, :])
        Vw.tensor_scalar(out=mrow[:, :], in0=s1row[:, :], scalar1=1.0 / D,
                         scalar2=None, op0=ALU.mult)
        Vw.tensor_mul(out=mr2[:, :], in0=mrow[:, :], in1=mrow[:, :])
        Vw.scalar_tensor_tensor(out=s1row[:, :], in0=s2row[:, :], scalar=1.0 / D,
                                in1=mr2[:, :], op0=ALU.mult, op1=ALU.subtract)
        vS.wait(S)
        Sw.activation(out=s2row[:, :], in_=s1row[:, :], func=AF.Ln, bias=eps5[0:1, 0:1], scale=1.0)
        Sw.activation(out=mr2[:, :], in_=s1row[:, :], func=AF.Ln, bias=0.0, scale=1.0)
        sS.wait(V)
        Vw.tensor_sub(out=mr2[:, :], in0=mr2[:, :], in1=s2row[:, :])
        vS.wait(S)
        Sw.activation(out=mr2[:, :], in_=mr2[:, :], func=AF.Exp, bias=0.0, scale=0.5)
        Sw.activation(out=s2row[:, :], in_=s2row[:, :], func=AF.Exp, bias=0.0, scale=0.5)
        sS.wait(V)
        Vw.tensor_scalar(out=mr2[:, :], in0=mr2[:, :], scalar1=1e-8, scalar2=None, op0=ALU.add)
        Vw.tensor_mul(out=mr2[:, :], in0=mr2[:, :], in1=s2row[:, :])
        Vw.reciprocal(out=rhrow[:, :], in_=mr2[:, :])
        Vw.tensor_mul(out=mr2[:, :], in0=mrow[:, :], in1=rhrow[:, :])
        vS.wait(T)
        T.matmul(pm1[:, :], onesrow_f[:, :], rhrow[0:1, :], start=True, stop=True)
        T.matmul(pm2[:, :], onesrow_f[:, :], mr2[0:1, :], start=True, stop=True)
        tS.inc(T.drain(fusable=False))
        tS.wait(V)
        Vw.tensor_copy(out=rb[:, :], in_=pm1[:, :])
        Vw.tensor_copy(out=mrb[:, :], in_=pm2[:, :])
        for t in range(NT):
            Vw.tensor_tensor(out=tev[:, :], in0=xbc[t][:, :], in1=rb[:, :], op=ALU.mult)
            Vw.tensor_sub(out=xnb[t][:, :], in0=tev[:, :], in1=mrb[:, :])
        vS.wait(T)
        vS.wait(S)

        ubanks = [pm1, ybanks[0]]
        gelu_done = []
        mm_done = []
        hsq_mm_done = [0] * 33
        sq_done = []

        def _l1f_gelu(ftc):
            slot2 = ftc % 2
            if ftc >= 2:
                S.wait_ge(vS.h, sq_done[ftc - 2])
            sS.inc(Sn.activation(out=hbuf[:, ftc * CH:(ftc + 1) * CH],
                                 in_=ubanks[slot2][:, :],
                                 func=AF.Gelu, bias=0.0, scale=1.0))
            gelu_done.append(sS.n)
            V.wait_ge(sS.h, gelu_done[ftc])
            if ftc >= 1:
                V.wait_ge(tS.h, hsq_mm_done[ftc - 1])
            vS.inc(Vn.tensor_mul(out=hsq2p[slot2][:, :], in0=hbuf[:, ftc * CH:(ftc + 1) * CH],
                                 in1=hbuf[:, ftc * CH:(ftc + 1) * CH]))
            sq_done.append(vS.n)
            T.wait_ge(vS.h, sq_done[ftc])
            tS.inc(T.matmul(pm2[0:1, :], ones128[:, :], hsq2p[slot2][:, :],
                            start=(ftc == 0), stop=(ftc == 31)))
            hsq_mm_done[ftc] = tS.n

        for ft in range(32):
            slot = ft % 4
            G.wait_ge(tS.h, mm_done[ft - 4] if ft >= 4 else wu_last[slot])
            G.dma_start(out=wupb[slot][:, :],
                        in_=wtup[:, 1024 * ft:1024 * (ft + 1)]).then_inc(lfp[slot].h, 16)
            lfp[slot].n += 16
            T.wait_ge(lfp[slot].h, lfp[slot].n)
            if ft >= 2:
                T.wait_ge(sS.h, gelu_done[ft - 2])
            for t in range(NT):
                mm = T.matmul(ubanks[ft % 2][:, :], wupb[slot][:, 128 * t:128 * (t + 1)],
                              xnb[t][:, :], start=(t == 0), stop=(t == NT - 1))
            tS.inc(mm)
            mm_done.append(tS.n)
            wu_last[slot] = tS.n
            if ft >= 1:
                S.wait_ge(tS.h, mm_done[ft])
                _l1f_gelu(ft - 1)
        tS.inc(T.drain(fusable=False))
        S.wait_ge(tS.h, tS.n)
        _l1f_gelu(31)
        tS.inc(T.drain(fusable=False))
        tS.wait(S)
        Sw.activation(out=s2row[:, :], in_=pm2[0:1, :], func=AF.Ln, bias=0.0, scale=1.0 / 4096)
        Sw.activation(out=s2row[:, :], in_=s2row[:, :], func=AF.Exp, bias=0.0, scale=0.5)
        sS.wait(V)
        Vw.tensor_scalar(out=s2row[:, :], in0=s2row[:, :], scalar1=1e-8, scalar2=None, op0=ALU.add)
        Vw.reciprocal(out=r2row[:, :], in_=s2row[:, :])
        vS.wait(T)
        T.matmul(pm1[:, :], onesrow_f[:, :], r2row[0:1, :], start=True, stop=True)
        tS.inc(T.drain(fusable=False))
        tS.wait(V)
        Vw.tensor_copy(out=r2b[:, :], in_=pm1[:, :])
        vS.wait(T)

        for half in range(2):
            dn_done = []
            for ft in range(32):
                slot = ft % 4
                G.wait_ge(tS.h, dn_done[ft - 4] if ft >= 4 else wd_last[slot])
                G.dma_start(out=wdnb[slot][:, :],
                            in_=wtdn[128 * ft:128 * (ft + 1),
                                     512 * half:512 * (half + 1)]).then_inc(lfd[slot].h, 16)
                lfd[slot].n += 16
                T.wait_ge(lfd[slot].h, lfd[slot].n)
                for dl in range(4):
                    mm = T.matmul(ybanks[dl][:, :], wdnb[slot][:, 128 * dl:128 * (dl + 1)],
                                  hbuf[:, ft * CH:(ft + 1) * CH],
                                  start=(ft == 0), stop=(ft == 31))
                tS.inc(mm)
                dn_done.append(tS.n)
                wd_last[slot] = tS.n
            tS.inc(T.drain(fusable=False))
            tS.wait(V)
            for dl in range(4):
                dt = half * 4 + dl
                SY.wait_ge(vS.h, adds_done)
                SY.dma_start(out=xres[dt % 2][:, :],
                             in_=xT[128 * dt:128 * (dt + 1),
                                    c * CH:(c + 1) * CH]).then_inc(lr.h, 16)
                lr.n += 16
                Vw.tensor_tensor(out=tev[:, :], in0=ybanks[dl][:, :], in1=r2b[:, :],
                                 op=ALU.mult)
                V.wait_ge(lr.h, lr.n)
                V.wait_ge(sop[dt % 2].h, sop[dt % 2].n)
                vS.inc(Vn.tensor_add(out=x4sb[dt % 2][:, :], in0=tev[:, :],
                                     in1=xres[dt % 2][:, :]))
                adds_done = vS.n
                vS.wait(SY)
                SY.dma_start(out=outT[128 * dt:128 * (dt + 1), c * CH:(c + 1) * CH],
                             in_=x4sb[dt % 2][:, :]).then_inc(sop[dt % 2].h, 16)
                sop[dt % 2].n += 16
            vS.wait(T)
        t_lastuse = tS.n

    if dbg:
        SY.wait_ge(vS.h, vS.n)
        SY.dma_start(out=d_wtu[:, :], in_=wtup[:, :]).then_inc(sop[0].h, 16)
        sop[0].n += 16
        G.wait_ge(vS.h, vS.n)
        G.dma_start(out=d_xn[:, :], in_=xnb[0][:, :]).then_inc(sop[0].h, 16)
        sop[0].n += 16
        G.dma_start(out=d_h[:, :], in_=hbuf[:, 0:2 * CH]).then_inc(sop[0].h, 16)
        sop[0].n += 16
    SY.wait_ge(sop[0].h, sop[0].n)
    SY.wait_ge(sop[1].h, sop[1].n)
    if ctx is None:
        stack.close()
        return nc
    return sop


def build_fused():
    nc = bass.Bass("TRN2", target_bir_lowering=False)
    xT = nc.dram_tensor("xT", [D, N], F32, kind="ExternalInput").ap()
    x1T = nc.dram_tensor("x1s", [D, N], F32, kind="Internal").ap()
    x2T = nc.dram_tensor("x2s", [D, N], F32, kind="Internal").ap()
    x3T = nc.dram_tensor("x3s", [D, N], F32, kind="Internal").ap()
    outT = nc.dram_tensor("outT", [D, N], F32, kind="ExternalOutput").ap()
    waits = []
    with contextlib.ExitStack() as bgst:
        bg = TernBG(nc, bgst)
        with contextlib.ExitStack() as st1:
            waits = list(build_l0_attn(ctx=(nc, st1, xT, x1T, waits)))
        with contextlib.ExitStack() as st2:
            waits = list(build_l0_ffn(ctx=(nc, st2, x1T, x2T, waits)))
        with contextlib.ExitStack() as st3:
            waits = list(build_l1_attn(ctx=(nc, st3, x2T, x3T, waits), bg=bg))
        with contextlib.ExitStack() as st4:
            waits = list(build_l1_ffn(ctx=(nc, st4, x3T, outT, waits), bg=bg))
    return nc



def _sel_consts():
    sel16x = np.zeros((1, 256), np.float32)
    for h in range(16):
        sel16x[0, 16 * h + h] = 1.0
    selcol = np.zeros((16, 1024), np.float32)
    for h in range(16):
        selcol[h, 64 * h:64 * (h + 1)] = 1.0
    stk = np.zeros((64, 256), np.float32)
    for k in range(64):
        stk[k, k] = 1.0
        stk[k, 128 + 64 + k] = 1.0
    return sel16x, selcol, stk


def l0a_inputs(x_b, wq, bq, wk, bk, wv, bv, wo, bo):
    sel16x, selcol, stk = _sel_consts()
    return {
        "xT": np.ascontiguousarray(x_b.T),
        "wqT": np.ascontiguousarray(wq.T),
        "wkT": np.ascontiguousarray(wk.T),
        "wvT": np.ascontiguousarray(wv.T),
        "woT": np.ascontiguousarray(wo.T),
        "bq": np.ascontiguousarray(bq.reshape(16, 64).T),
        "bk": np.ascontiguousarray(bk.reshape(4, 64).T),
        "bv": np.ascontiguousarray(bv.reshape(1, 256)),
        "bo": np.ascontiguousarray(bo.reshape(8, 128).T),
        "sel16x": sel16x,
        "selcol": selcol,
        "stk": stk,
    }


def l0f_consts(G1_R, G2_R, G1_I, G2_I):
    m1bd = np.zeros((8 * 128, 128), np.float32)
    for RI, G2 in ((0, G2_R), (1, G2_I)):
        for c in range(4):
            B = np.asarray(G2)[c, :, 0, :]
            blk = np.zeros((128, 128), np.float32)
            for dd in range(8):
                blk[dd * 16:(dd + 1) * 16, dd * 16:(dd + 1) * 16] = B
            m1bd[(RI * 4 + c) * 128:(RI * 4 + c) * 128 + 128] = blk
    m2arr = np.zeros((32 * 128, 512), np.float32)
    for RI, G1 in ((0, G1_R), (1, G1_I)):
        for c in range(4):
            A = np.asarray(G1)[0, :, c, :]
            for tau in range(4):
                blk = np.zeros((128, 512), np.float32)
                for dl in range(8):
                    for e in range(16):
                        kk = dl * 16 + e
                        blk[kk, e::16] = A[:, 8 * tau + dl]
                m2arr[(RI * 16 + c * 4 + tau) * 128:(RI * 16 + c * 4 + tau) * 128 + 128] = blk
    return m1bd, m2arr


def l0f_inputs(x1, G1_R, G2_R, G1_I, G2_I, bias_real, bias_imag):
    m1bd, m2arr = l0f_consts(G1_R, G2_R, G1_I, G2_I)
    brow = np.concatenate([np.asarray(bias_real).reshape(4, 128).T,
                           np.asarray(bias_imag).reshape(4, 128).T], axis=1)
    return {
        "xT": np.ascontiguousarray(x1.T),
        "m1bd": m1bd,
        "m2arr": m2arr,
        "brow": np.ascontiguousarray(brow.astype(np.float32)),
    }


def l1a_inputs(x2, Wq, bq, Wk, bk, Wv, bv, Wo, bo, W1, b1, W2, b2, g1, be1, g2, be2):
    sel16x, selcol, stk = _sel_consts()
    col8 = lambda a: np.ascontiguousarray(np.asarray(a).reshape(8, 128).T.astype(np.float32))
    return {
        "xT": np.ascontiguousarray(x2.T),
        "Wq": np.ascontiguousarray(Wq), "Wk": np.ascontiguousarray(Wk),
        "Wv": np.ascontiguousarray(Wv), "Wo": np.ascontiguousarray(Wo),
        "W1": np.ascontiguousarray(W1), "W2": np.ascontiguousarray(W2),
        "bqc": np.ascontiguousarray(np.asarray(bq).reshape(16, 64).T.astype(np.float32)),
        "bkr": np.ascontiguousarray(np.asarray(bk).reshape(1, 1024)),
        "bvr": np.ascontiguousarray(np.asarray(bv).reshape(1, 1024)),
        "boc": col8(bo),
        "b1c": np.ascontiguousarray(np.asarray(b1).reshape(32, 128).T.astype(np.float32)),
        "b2c": col8(b2),
        "g1c": col8(g1), "be1c": col8(be1),
        "g2c": col8(g2), "be2c": col8(be2),
        "sel16x1": sel16x, "selcol1": selcol, "stk1": stk,
    }


def l1f_inputs(x3, w_up, w_down):
    return {
        "xT": np.ascontiguousarray(x3.T),
        "wupT": np.ascontiguousarray(np.asarray(w_up).T),
        "wdnT": np.ascontiguousarray(np.asarray(w_down).T),
    }


_NC_CACHE = {}


def _get_fused():
    if "fused" not in _NC_CACHE:
        _NC_CACHE["fused"] = build_fused()
    return _NC_CACHE["fused"]


def kernel(**inputs):
    ncores = 8
    x = np.asarray(inputs["L0_attn_x"])
    nc = _get_fused()
    zero = np.zeros((N, D), np.float32)
    base = {}
    base.update(l0a_inputs(zero,
                           inputs["L0_attn_wq"], inputs["L0_attn_bq"],
                           inputs["L0_attn_wk"], inputs["L0_attn_bk"],
                           inputs["L0_attn_wv"], inputs["L0_attn_bv"],
                           inputs["L0_attn_wo"], inputs["L0_attn_bo"]))
    base.update(l0f_inputs(zero,
                           inputs["L0_ffn_G1_R"], inputs["L0_ffn_G2_R"],
                           inputs["L0_ffn_G1_I"], inputs["L0_ffn_G2_I"],
                           inputs["L0_ffn_bias_real"], inputs["L0_ffn_bias_imag"]))
    base.update(l1a_inputs(zero,
                           inputs["L1_attn_Wq"], inputs["L1_attn_bq"],
                           inputs["L1_attn_Wk"], inputs["L1_attn_bk"],
                           inputs["L1_attn_Wv"], inputs["L1_attn_bv"],
                           inputs["L1_attn_Wo"], inputs["L1_attn_bo"],
                           inputs["L1_attn_W1"], inputs["L1_attn_b1"],
                           inputs["L1_attn_W2"], inputs["L1_attn_b2"],
                           inputs["L1_attn_g1"], inputs["L1_attn_be1"],
                           inputs["L1_attn_g2"], inputs["L1_attn_be2"]))
    base.update(l1f_inputs(zero,
                           inputs["L1_ffn_w_up"], inputs["L1_ffn_w_down"]))
    in_maps = []
    for b in range(ncores):
        m = dict(base)
        m["xT"] = np.ascontiguousarray(np.asarray(x[b]).T)
        in_maps.append(m)
    res = run_bass_kernel_spmd(nc, in_maps, core_ids=list(range(ncores)))
    out = np.stack([res.results[b]["outT"].T for b in range(ncores)], axis=0)
    return np.ascontiguousarray(out.astype(np.float32))
